# version 10
# baseline (speedup 1.0000x reference)
import contextlib
import numpy as np
import ml_dtypes

import concourse.bass as bass
import concourse.mybir as mybir
from concourse.bass_utils import run_bass_kernel_spmd
from concourse.aot_env import allow_ici_all_to_all

F32 = mybir.dt.float32
BF16 = mybir.dt.bfloat16
I8 = mybir.dt.int8
AF = mybir.ActivationFunctionType
ALU = mybir.AluOpType
AX = mybir.AxisListType

B, S, D = 2, 8192, 1024
TPC = 2048
NT = TPC // 128
NH, HD = 16, 64
DFF = 4096
RANK = 256
DI3 = 2816
LN_EPS = 1e-5
C0_L0 = 6.0
C0_L3 = 20.0

PHASES = None
ALL_PHASES = ["l0a", "l0f", "l1a", "l1f", "l2a", "l2f", "l3a", "l3p", "l3f"]
N_CORES = 8


def bf16(x):
    return np.asarray(x, dtype=ml_dtypes.bfloat16)



def _host_constants():
    c = {}
    c["c_ident"] = bf16(np.eye(128, dtype=np.float32))
    c["c_onesb"] = bf16(np.ones((1, 2048), np.float32))
    c["c_onesf"] = np.ones((33, 512), np.float32)
    mfull = np.zeros((4, 128, 512), np.float32)
    for r in range(4):
        kpos = 128 * r + np.arange(128)[:, None]
        qpos = np.arange(512)[None, :]
        mfull[r] = np.where(kpos > qpos, -1e9, 0.0)
    lo = np.concatenate([mfull[:, 0:64], mfull[:, 0:64]], axis=1)
    hi = np.concatenate([mfull[:, 64:128], mfull[:, 64:128]], axis=1)
    c["c_cmask_lo"] = bf16(np.concatenate(list(lo), axis=1))
    c["c_cmask_hi"] = bf16(np.concatenate(list(hi), axis=1))
    j1 = np.zeros((64, 128), np.float32)
    j1[np.arange(64), np.arange(64)] = 1.0
    j2 = np.zeros((64, 128), np.float32)
    j2[np.arange(64), 64 + np.arange(64)] = 1.0
    c["c_jlo"] = bf16(np.concatenate([j1, j1], axis=0))
    c["c_jhi"] = bf16(np.concatenate([j2, j2], axis=0))
    inv = 1.0 / (10000.0 ** (np.arange(0, HD, 2, dtype=np.float64) / HD))
    freqs = np.outer(np.arange(S, dtype=np.float64), inv)
    emb = np.concatenate([freqs, freqs], axis=-1)
    cosf = np.cos(emb).T
    sinf = np.sin(emb).T
    c["c_cosd"] = bf16(np.concatenate([cosf, cosf], axis=0))
    c["c_sind"] = bf16(np.concatenate([sinf, sinf], axis=0))
    rotm = np.zeros((128, 128), np.float32)
    for b0 in (0, 64):
        for j in range(32):
            rotm[b0 + 32 + j, b0 + j] = -1.0
            rotm[b0 + j, b0 + 32 + j] = 1.0
    c["c_rotm"] = bf16(rotm)
    return c


def _host_prep(inp):
    T = lambda a: np.ascontiguousarray(np.asarray(a).T)
    A = lambda a: np.ascontiguousarray(np.asarray(a))
    base = {}
    base["w_l0_wq"] = A(inp["L0_attn_Wq"])
    base["w_l0_wk"] = A(inp["L0_attn_Wk"])
    base["w_l0_wv"] = A(inp["L0_attn_Wv"])
    base["w_l0_wo"] = A(inp["L0_attn_Wo"])
    base["w_l0_bo"] = A(inp["L0_attn_bo"]).reshape(1, D)
    base["w_l0_u1"] = A(inp["L0_ffn_U1"])
    base["w_l0_v1"] = A(inp["L0_ffn_V1"])
    base["w_l0_b1"] = A(inp["L0_ffn_b1"]).reshape(1, DFF)
    base["w_l0_u2"] = A(inp["L0_ffn_U2"])
    base["w_l0_v2"] = A(inp["L0_ffn_V2"])
    base["w_l0_b2"] = A(inp["L0_ffn_b2"]).reshape(1, D)
    base["w_l2_wq"] = A(inp["L2_attn_Wq"])
    base["w_l2_wk"] = A(inp["L2_attn_Wk"])
    base["w_l2_wvT"] = T(inp["L2_attn_Wv"])
    base["w_l2_woT"] = T(inp["L2_attn_Wo"])

    def _bcol(b):
        arr = np.zeros((128, 8), np.float32)
        b = np.asarray(b)
        for h in range(16):
            arr[64 * (h % 2):64 * (h % 2) + 64, h // 2] = b[64 * h:64 * (h + 1)]
        return arr
    base["w_l2_bqc"] = _bcol(inp["L2_attn_bq"])
    base["w_l2_bkc"] = _bcol(inp["L2_attn_bk"])
    base["w_l2_bv"] = A(inp["L2_attn_bv"]).reshape(1, D)
    base["w_l2_bo"] = A(inp["L2_attn_bo"]).reshape(1, D)
    pt_ = np.ascontiguousarray(np.asarray(inp["L2_attn_P"]).T)
    base["w_l2_pT"] = np.concatenate([pt_, pt_], axis=0).copy()
    base["w_l2_w1"] = A(inp["L2_ffn_w1"])
    base["w_l2_b1"] = A(inp["L2_ffn_b1"]).reshape(1, DFF)
    base["w_l2_w2"] = A(inp["L2_ffn_w2"])
    base["w_l2_b2"] = A(inp["L2_ffn_b2"]).reshape(1, D)
    base["w_l3_wqT"] = T(inp["L3_attn_Wq"])
    base["w_l3_wkT"] = T(inp["L3_attn_Wk"])
    base["w_l3_wvT"] = T(inp["L3_attn_Wv"])
    base["w_l3_woT"] = T(inp["L3_attn_Wo"])
    base["w_l3_bo"] = np.zeros((1, D), np.float32)
    wg_ = np.asarray(inp["L3_ffn_w_gate"])
    wu_ = np.asarray(inp["L3_ffn_w_up"])
    wd_ = np.asarray(inp["L3_ffn_w_down"])
    base["w_l1_wqT"] = T(inp["L1_attn_Wq"])
    base["w_l1_wkT"] = T(inp["L1_attn_Wk"])
    base["w_l1_wvT"] = T(inp["L1_attn_Wv"])
    base["w_l1_woT"] = T(inp["L1_attn_Wo"])
    base["w_l1_bo"] = np.zeros((1, D), np.float32)
    w1T_ = np.ascontiguousarray(np.asarray(inp["L1_ffn_w1"]).T)
    w2T_ = np.ascontiguousarray(np.asarray(inp["L1_ffn_w2"]).T)
    base["c_n1rep"] = np.broadcast_to(np.asarray(inp["L1_ffn_norm1_w"], np.float32), (128, D)).copy()
    base["c_n2rep"] = np.broadcast_to(np.asarray(inp["L1_ffn_norm2_w"], np.float32), (128, DFF)).copy()
    sc_ = np.zeros((128, 2), np.float32)
    sc_[:, 0] = np.float32(np.asarray(inp["L1_ffn_act_scale"]).reshape(-1)[0])
    sc_[:, 1] = np.float32(np.asarray(inp["L1_ffn_act_bias"]).reshape(-1)[0])
    base["c_l1s"] = sc_
    base.update(_host_constants())
    in_maps = []
    for j in range(N_CORES):
        m = dict(base)
        b, r = j // 4, (j % 4) * TPC
        m["x"] = A(np.asarray(inp["L0_attn_x"])[b, r:r + TPC, :])
        rk = j % 4
        m["w_l3_wg_part"] = A(wg_[:, 256 * rk:256 * (rk + 1)])
        m["w_l1_w1T_part"] = A(w1T_[256 * rk:256 * (rk + 1), :])
        m["w_l1_w2T_part"] = A(w2T_[1024 * rk:1024 * (rk + 1), :])
        m["w_l3_wu_part"] = A(wu_[:, 256 * rk:256 * (rk + 1)])
        if rk < 3:
            m["w_l3_wd_part"] = A(wd_[:, 768 * rk:768 * (rk + 1)])
        else:
            pad = np.zeros((D, 768), wd_.dtype)
            pad[:, 0:512] = wd_[:, 2304:2816]
            m["w_l3_wd_part"] = pad
        t_ = np.arange(128, dtype=np.float64)
        maskT = np.zeros((128, 256), np.float32)
        qdec = np.zeros((128, 128), np.float32)
        kdec = np.zeros((128, 2), np.float32)
        bdec = np.zeros((128, 2), np.float32)
        for hh in range(2):
            h = 2 * j + hh
            sl = 2.0 ** (-8.0 / 16 * (h + 1))
            mT = np.where(t_[None, :] >= t_[:, None],
                          np.exp(-sl * (t_[None, :] - t_[:, None])), 0.0)
            maskT[:, 128 * hh:128 * (hh + 1)] = mT
            qdec[64 * hh:64 * (hh + 1), :] = np.exp(-sl * (t_ + 1.0))[None, :]
            kdec[:, hh] = np.exp(-sl * (127.0 - t_))
            bdec[:, hh] = np.exp(-sl * 128.0)
        m["c_l1maskT"] = bf16(maskT)
        m["c_l1qdec"] = bf16(qdec)
        m["c_l1kdec"] = kdec.astype(np.float32)
        m["c_l1bdec"] = bdec.astype(np.float32)
        in_maps.append(m)
    return in_maps



class Ctx:
    def __init__(self, nc, es):
        self.nc = nc
        self.cnt = {}
        self.sems = {}
        for name in ["ld", "mm", "pv", "ev", "sc", "st", "tr", "cc"]:
            self.sems[name] = es.enter_context(nc.semaphore("s_" + name))

    def s(self, name):
        return self.sems[name]

    def add(self, name, n=1):
        self.cnt[name] = self.cnt.get(name, 0) + n
        return self.cnt[name]

    def v(self, name):
        return self.cnt.get(name, 0)


def build_program(phases):
    nc = bass.Bass("TRN2", target_bir_lowering=False)
    es = contextlib.ExitStack()
    es.__enter__()
    ctx = Ctx(nc, es)
    E = es.enter_context

    dram = {}

    def din(name, shape, dtype=F32):
        dram[name] = nc.dram_tensor(name, list(shape), dtype, kind="ExternalInput")

    def dint(name, shape, dtype=BF16):
        dram[name] = nc.dram_tensor(name, list(shape), dtype)

    din("x", [TPC, D])
    din("w_l0_wq", [D, D]); din("w_l0_wk", [D, D]); din("w_l0_wv", [D, D])
    din("w_l0_wo", [D, D]); din("w_l0_bo", [1, D])
    din("w_l0_u1", [D, RANK]); din("w_l0_v1", [RANK, DFF]); din("w_l0_b1", [1, DFF])
    din("w_l0_u2", [DFF, RANK]); din("w_l0_v2", [RANK, D]); din("w_l0_b2", [1, D])
    din("c_ident", [128, 128], BF16)
    din("c_onesb", [1, 2048], BF16)
    din("c_onesf", [33, 512], F32)
    din("c_cmask_lo", [128, 4 * 512], BF16)
    din("c_cmask_hi", [128, 4 * 512], BF16)
    din("c_jlo", [128, 128], BF16)
    din("c_jhi", [128, 128], BF16)
    din("c_cosd", [128, S], BF16)
    din("c_sind", [128, S], BF16)
    din("c_rotm", [128, 128], BF16)
    for nm in ("w_l3_wqT", "w_l3_wkT", "w_l3_wvT", "w_l3_woT"):
        din(nm, [D, D])
    din("w_l3_bo", [1, D])
    din("w_l3_wg_part", [DI3, 256]); din("w_l3_wu_part", [DI3, 256])
    din("w_l3_wd_part", [D, 768])
    din("w_l1_w1T_part", [256, DFF]); din("w_l1_w2T_part", [1024, D])
    din("w_l1_wqT", [D, D]); din("w_l1_wkT", [D, 256]); din("w_l1_wvT", [D, 256])
    din("w_l1_woT", [D, D]); din("w_l1_bo", [1, D])
    din("c_l1maskT", [128, 256], BF16)
    din("c_l1qdec", [128, 128], BF16)
    din("c_l1kdec", [128, 2])
    din("c_l1bdec", [128, 2])
    din("c_n1rep", [128, D]); din("c_n2rep", [128, DFF]); din("c_l1s", [128, 2])

    for nm in ("w_l2_wq", "w_l2_wk", "w_l2_wvT", "w_l2_woT"):
        din(nm, [D, D])
    for nm in ("w_l2_bv", "w_l2_bo", "w_l2_b2"):
        din(nm, [1, D])
    din("w_l2_bqc", [128, 8]); din("w_l2_bkc", [128, 8])
    din("w_l2_pT", [128, HD])
    din("w_l2_w1", [D, DFF]); din("w_l2_b1", [1, DFF]); din("w_l2_w2", [DFF, D])

    y_out = nc.dram_tensor("y", [TPC, D], F32, kind="ExternalOutput")
    dbg_out = nc.dram_tensor("dbg", [8, 786432], BF16, kind="ExternalOutput")

    dint("stage_xn", [TPC, D])
    SEG_Q = 128 * 2048
    SEG_V = 2048 * 128
    BLK_QKV = SEG_Q + SEG_Q + SEG_V
    dint("a2a_qkv_in", [8, BLK_QKV])
    dint("a2a_qkv_out", [8, BLK_QKV])
    dint("a2a_ao_in", [8, SEG_Q])
    dint("a2a_ao_out", [8, SEG_Q])
    dint("ar_in", [NH, HD, 65], F32)
    dint("ar_out", [NH, HD, 65], F32)
    dint("wstage", [DI3 * D])
    dint("stage_xn32", [TPC, D], F32)
    dint("stage_hq", [TPC, DI3])
    dint("wq1q", [D, DFF]); dint("wq2q", [DFF, D])
    dint("stage_hq1", [TPC, DFF])
    dint("wgqT", [D, DI3]); dint("wuqT", [D, DI3]); dint("wdqT", [3072, D])
    dint("wtpartA", [256, DI3]); dint("wtpartB", [768, D])
    dint("wtpartC", [256, DI3])
    dint("w1qpart", [256, DFF]); dint("w2qpart", [1024, D])
    dint("l1sum_io", [1, 2], F32); dint("l1sum_out", [1, 2], F32)
    dint("stage_h", [TPC, DI3], F32)

    sb = {}
    sb["stream"] = E(nc.sbuf_tensor("stream", [128, NT * D], F32))
    sb["scr"] = E(nc.sbuf_tensor("scr", [128, 2048], BF16))
    sb["stat"] = E(nc.sbuf_tensor("stat", [128, 10 * NT], F32))
    sb["rec"] = E(nc.sbuf_tensor("rec", [33, 1024], F32))
    sb["bcb"] = E(nc.sbuf_tensor("bcb", [128, 2 * 512], F32))
    sb["ident"] = E(nc.sbuf_tensor("ident", [128, 128], BF16))
    sb["onesb"] = E(nc.sbuf_tensor("onesb", [1, 2048], BF16))
    sb["onesf"] = E(nc.sbuf_tensor("onesf", [33, 512], F32))
    sb["cml"] = E(nc.sbuf_tensor("cml", [128, 4 * 512], BF16))
    sb["cmh"] = E(nc.sbuf_tensor("cmh", [128, 4 * 512], BF16))
    sb["jlo"] = E(nc.sbuf_tensor("jlo", [128, 128], BF16))
    sb["jhi"] = E(nc.sbuf_tensor("jhi", [128, 128], BF16))
    sb["bias"] = E(nc.sbuf_tensor("bias", [1, DFF], BF16))
    sb["bias2"] = E(nc.sbuf_tensor("bias2", [1, D], BF16))
    sb["cst"] = E(nc.sbuf_tensor("cst", [128, 8], F32))
    sb["pl2"] = E(nc.sbuf_tensor("pl2", [128, HD], BF16))
    sb["kvs"] = E(nc.sbuf_tensor("kvs", [64, 1040], F32))
    sb["bcl"] = E(nc.sbuf_tensor("bcl", [128, 8], BF16))
    sb["kvf"] = E(nc.sbuf_tensor("kvf", [128, 128], F32))
    sb["kvp"] = E(nc.sbuf_tensor("kvp", [128, 128], BF16))
    sb["l1m"] = E(nc.sbuf_tensor("l1m", [128, 256], BF16))
    sb["l1q"] = E(nc.sbuf_tensor("l1q", [128, 128], BF16))
    sb["l1k"] = E(nc.sbuf_tensor("l1k", [128, 2], F32))
    sb["l1b"] = E(nc.sbuf_tensor("l1b", [128, 2], F32))
    sb["swc"] = E(nc.sbuf_tensor("swc", [128, 4], F32))
    sb["l1s"] = E(nc.sbuf_tensor("l1s", [128, 2], F32))
    pool = E(nc.sbuf_tensor("pool", [128, 51216], BF16))
    pool_map = {
        "xnT": (0, 16384),
        "wa": (16384, 8192), "wb": (24576, 8192), "wc": (32768, 8192),
        "qtp": (16384, 8192), "ktp": (24576, 8192), "vt": (32768, 8320),
        "pt": (41088, 4096), "aot2": (45184, 1024),
        "cosr": (0, 8192), "sinr": (8192, 8192),
        "qdt": (0, 8192),
        "kdt": (8192, 8192),
        "vt1": (32768, 4096), "ktok": (36864, 4096),
        "st1": (41088, 256),
        "rs1": (44160, 2048), "rs2": (46208, 2048),
        "aoT": (16384, 16384), "wd": (32768, 8192),
        "u1": (16384, 2048), "v1": (18432, 8192), "u2": (26624, 8192),
        "v2": (34816, 2048), "ht": (36864, 8192), "pt2": (45056, 4096),
        "qt2": (49152, 1024),
        "aotl2": (0, 16384),
        "qft": (16384, 16384),
        "l2wv": (32768, 8192),
        "wraw": (32768, 8192),
        "wcmb": (40960, 8192),
        "kf2": (49152, 1024), "v22": (50176, 1040), "kvb": (49152, 520),
        "aoq2": (32768, 2048),
        "wfl1": (0, 8192), "qfl1": (8192, 8192),
        "i8l1": (16384, 2048), "qbl1": (18432, 4096),
        "xb1": (16384, 2048), "xh1": (18432, 2048), "i81": (20480, 512),
        "xqb1": (20992, 1024), "n1r": (22016, 2048),
        "w1sl": (16384, 8192), "a0": (24576, 8192), "a1": (32768, 8192),
        "n2r": (40960, 8192), "i8h": (49152, 2048),
        "hq1s": (16384, 4096), "w2sl": (20480, 4096),
        "wf32": (0, 5632), "i8a": (5632, 1408), "i8b": (7040, 1408),
        "i8c": (8448, 1408), "qf1": (9856, 5632), "qf2": (15488, 5632),
        "bmx": (21120, 352), "sclq": (21472, 352), "rcpq": (21824, 352),
        "qbf": (22176, 2816), "trs": (24992, 2816),
        "wgs": (16384, 4096), "wus": (20480, 4096),
        "sg1": (24576, 1024), "sg2": (25600, 1024),
        "hqs": (16384, 2048), "wds": (18432, 2048),
        "w1s": (16384, 2048),
        "w2s": (18432, 2048),
        "htl2": (20480, 16384),
    }

    class SBuf(dict):
        def __getitem__(self, k):
            v = dict.get(self, k, None)
            if v is None and k in pool_map:
                off, n = pool_map[k]
                return pool[:, off:off + n]
            return v

    sb = SBuf(sb)

    ps = E(nc.psum_tensor("ps", [128, 4096], F32))

    def pbank(i, r0=0, r1=128, w=512):
        return ps[r0:r1, 512 * i:512 * i + w]

    def run_phase(fn, *a, **kw):
        with nc.Block() as blk:
            fn(blk, *a, **kw)
        nc.all_engine_barrier()

    do_zero = sorted(set(phases)) != sorted(set(ALL_PHASES))

    def ph_init(blk):
        nl = [0]

        @blk.sync
        def _(s):
            for tt in range(NT):
                s.dma_start(out=sb["stream"][:, 1024 * tt:1024 * (tt + 1)],
                            in_=dram["x"].ap()[128 * tt:128 * (tt + 1), :]).then_inc(ctx.s("ld"), 16)
            s.dma_start(out=sb["ident"][:, :], in_=dram["c_ident"].ap()).then_inc(ctx.s("ld"), 16)
            s.dma_start(out=sb["onesb"][:, :], in_=dram["c_onesb"].ap()).then_inc(ctx.s("ld"), 16)
            s.dma_start(out=sb["onesf"][:, :], in_=dram["c_onesf"].ap()).then_inc(ctx.s("ld"), 16)
            s.dma_start(out=sb["cml"][:, :], in_=dram["c_cmask_lo"].ap()).then_inc(ctx.s("ld"), 16)
            s.dma_start(out=sb["cmh"][:, :], in_=dram["c_cmask_hi"].ap()).then_inc(ctx.s("ld"), 16)
            s.dma_start(out=sb["jlo"][:, :], in_=dram["c_jlo"].ap()).then_inc(ctx.s("ld"), 16)
            s.dma_start(out=sb["jhi"][:, :], in_=dram["c_jhi"].ap()).then_inc(ctx.s("ld"), 16)
            nl[0] = NT + 7

        @blk.vector
        def _(v):
            v.memset(sb["vt"].rearrange("p (c w) -> p c w", w=130)[:, :, 64:130:65], 1.0)
            v.memset(sb["scr"][:, :], 0.0).then_inc(ctx.s("ev"), 1)
        ctx.add("ev", 1)

        @blk.gpsimd
        def _(g):
            if not do_zero:
                return
            g.wait_ge(ctx.s("ev"), ctx.v("ev"))
            nz = 0
            for name in ("a2a_qkv_in", "a2a_qkv_out", "a2a_ao_in", "a2a_ao_out", "stage_xn"):
                flat = dram[name].ap().rearrange("a b -> (a b)")
                tot = flat.shape[0]
                CH = 128 * 2048
                for off in range(0, tot, CH):
                    n = min(CH, tot - off)
                    g.dma_start(out=flat[off:off + n].rearrange("(p n) -> p n", p=128),
                                in_=sb["scr"][:, 0:n // 128]).then_inc(ctx.s("st"), 16)
                    nz += 1
            g.wait_ge(ctx.s("st"), ctx.v("st") + 16 * nz)
            ctx.add("st", 16 * nz)

        @blk.gpsimd
        def _(g):
            g.wait_ge(ctx.s("ld"), ctx.v("ld") + 16 * nl[0])

        ctx.add("ld", 16 * nl[0])

    def ph_ln(blk, mode="T"):
        ev0, sc0, st0, mm0, tr0 = (ctx.v("ev"), ctx.v("sc"), ctx.v("st"),
                                   ctx.v("mm"), ctx.v("tr"))
        sums = sb["stat"][:, 0:NT]
        sumsq = sb["stat"][:, NT:2 * NT]
        mcol = sb["stat"][:, 2 * NT:3 * NT]
        e2 = sb["stat"][:, 3 * NT:4 * NT]
        var0 = sb["stat"][:, 4 * NT:5 * NT]
        var = sb["stat"][:, 5 * NT:6 * NT]
        stdc = sb["stat"][:, 6 * NT:7 * NT]
        rstd = sb["stat"][:, 7 * NT:8 * NT]
        nb = sb["stat"][:, 8 * NT:9 * NT]

        def psB(dc):
            return ps.bitcast(BF16)[:, 1024 * dc:1024 * dc + 128]

        @blk.scalar
        def _(sc):
            for tt in range(NT):
                st = sb["stream"][:, 1024 * tt:1024 * (tt + 1)]
                ins = sc.activation(sb["scr"][:, 0:1024], st, AF.Square,
                                    accum_out=sumsq[:, tt:tt + 1])
                if tt == NT - 1:
                    ins.then_inc(ctx.s("sc"), 1)
            sc.wait_ge(ctx.s("ev"), ev0 + 4)
            sc.activation(stdc, var, AF.Sqrt, bias=np.float32(LN_EPS),
                          scale=1.0).then_inc(ctx.s("sc"), 1)

        @blk.vector
        def _(v):
            for tt in range(NT):
                st = sb["stream"][:, 1024 * tt:1024 * (tt + 1)]
                ins = v.reduce_sum(sums[:, tt:tt + 1], st, axis=AX.X)
                if tt == NT - 1:
                    ins.then_inc(ctx.s("ev"), 1)
            v.wait_ge(ctx.s("ev"), ev0 + 1)
            v.tensor_scalar(mcol, sums, 1.0 / D, None, ALU.mult).then_inc(ctx.s("ev"), 1)
            v.wait_ge(ctx.s("sc"), sc0 + 1)
            v.tensor_scalar(e2, sumsq, 1.0 / D, None, ALU.mult)
            v.wait_ge(ctx.s("ev"), ev0 + 2)
            v.tensor_mul(var0, mcol, mcol).then_inc(ctx.s("ev"), 1)
            v.wait_ge(ctx.s("ev"), ev0 + 3)
            v.tensor_sub(var, e2, var0).then_inc(ctx.s("ev"), 1)
            v.wait_ge(ctx.s("sc"), sc0 + 2)
            v.reciprocal(rstd, stdc).then_inc(ctx.s("ev"), 1)
            v.wait_ge(ctx.s("ev"), ev0 + 5)
            v.scalar_tensor_tensor(nb, mcol, -1.0, rstd, ALU.mult, ALU.mult).then_inc(ctx.s("ev"), 1)
            v.wait_ge(ctx.s("ev"), ev0 + 6)
            if mode == "T":
                for tt in range(NT):
                    if tt >= 2:
                        v.wait_ge(ctx.s("mm"), mm0 + 8 * (tt - 1))
                    st = sb["stream"][:, 1024 * tt:1024 * (tt + 1)]
                    half = sb["scr"][:, 1024 * (tt % 2):1024 * (tt % 2) + 1024]
                    v.tensor_scalar(half, st, rstd[:, tt:tt + 1],
                                    nb[:, tt:tt + 1], ALU.mult,
                                    ALU.add).then_inc(ctx.s("ev"), 1)
                    for dc in range(8):
                        v.wait_ge(ctx.s("mm"), mm0 + 8 * tt + dc + 1)
                        v.tensor_copy(sb["xnT"][:, 2048 * dc + 128 * tt:2048 * dc + 128 * (tt + 1)],
                                      psB(dc)).then_inc(ctx.s("tr"), 1)
            else:
                for tt in range(NT):
                    if tt >= 2:
                        v.wait_ge(ctx.s("st"), st0 + 16 * (tt - 1))
                    st = sb["stream"][:, 1024 * tt:1024 * (tt + 1)]
                    dst32 = (sb["scr"].bitcast(F32)[:, 0:1024] if tt % 2 == 0
                             else sb["bcb"][:, 0:1024])
                    v.tensor_scalar(dst32, st,
                                    rstd[:, tt:tt + 1], nb[:, tt:tt + 1],
                                    ALU.mult, ALU.add).then_inc(ctx.s("ev"), 1)

        if mode == "T":
            @blk.tensor
            def _(t):
                for tt in range(NT):
                    t.wait_ge(ctx.s("ev"), ev0 + 7 + tt)
                    half = sb["scr"][:, 1024 * (tt % 2):1024 * (tt % 2) + 1024]
                    for dc in range(8):
                        if tt >= 1:
                            t.wait_ge(ctx.s("tr"), tr0 + 8 * (tt - 1) + dc + 1)
                        t.transpose(psB(dc), half[:, 128 * dc:128 * (dc + 1)],
                                    sb["ident"][:, :]).then_inc(ctx.s("mm"), 1)
            ctx.add("mm", 8 * NT)
            ctx.add("tr", 8 * NT)
        else:
            @blk.gpsimd
            def _(g):
                for tt in range(NT):
                    g.wait_ge(ctx.s("ev"), ev0 + 6 + tt + 1)
                    src32 = (sb["scr"].bitcast(F32)[:, 0:1024] if tt % 2 == 0
                             else sb["bcb"][:, 0:1024])
                    g.dma_start(out=dram["stage_xn32"].ap()[128 * tt:128 * (tt + 1), :],
                                in_=src32).then_inc(ctx.s("st"), 16)
            ctx.add("st", 16 * NT)

        ctx.add("ev", 6 + NT)
        ctx.add("sc", 2)

    def ph_l0_qkv(blk, wq="w_l0_wq", wk="w_l0_wk", wv="w_l0_wv"):
        ld0, mm0, ev0, st0 = ctx.v("ld"), ctx.v("mm"), ctx.v("ev"), ctx.v("st")
        tr_ready = ctx.v("tr")
        qkv_in = dram["a2a_qkv_in"].ap()

        @blk.gpsimd
        def _(g):
            for w, dst in ((wq, "wa"), (wk, "wb"), (wv, "wc")):
                for dc in range(8):
                    g.dma_start(out=sb[dst][:, 1024 * dc:1024 * (dc + 1)],
                                in_=dram[w].ap()[128 * dc:128 * (dc + 1), :]).then_inc(ctx.s("ld"), 16)
        ctx.add("ld", 16 * 24)

        cum_st = []
        tot = 0
        for g in range(96):
            tot += 16 if g < 64 else 64
            cum_st.append(tot)

        @blk.tensor
        def _(t):
            t.wait_ge(ctx.s("tr"), tr_ready)
            g = 0
            for which, wt in (("q", "wa"), ("k", "wb")):
                t.wait_ge(ctx.s("ld"), ld0 + 16 * 8 * (1 if which == "q" else 2))
                for ic in range(8):
                    for t4 in range(4):
                        bk = g % 2
                        if g >= 2:
                            t.wait_ge(ctx.s("ev"), ev0 + g - 1)
                        for dc in range(8):
                            last = t.matmul(pbank(bk),
                                            sb[wt][:, 1024 * dc + 128 * ic:1024 * dc + 128 * (ic + 1)],
                                            sb["xnT"][:, 2048 * dc + 512 * t4:2048 * dc + 512 * (t4 + 1)],
                                            start=(dc == 0), stop=(dc == 7))
                        last.then_inc(ctx.s("mm"), 1)
                        g += 1
            t.wait_ge(ctx.s("ld"), ld0 + 16 * 24)
            for tt in range(NT):
                for vc in range(2):
                    bk = g % 2
                    t.wait_ge(ctx.s("ev"), ev0 + g - 1)
                    for dc in range(8):
                        last = t.matmul(pbank(bk),
                                        sb["xnT"][:, 2048 * dc + 128 * tt:2048 * dc + 128 * (tt + 1)],
                                        sb["wc"][:, 1024 * dc + 512 * vc:1024 * dc + 512 * (vc + 1)],
                                        start=(dc == 0), stop=(dc == 7))
                    last.then_inc(ctx.s("mm"), 1)
                    g += 1

        @blk.vector
        def _(v):
            for g in range(96):
                v.wait_ge(ctx.s("mm"), mm0 + g + 1)
                if g >= 4:
                    v.wait_ge(ctx.s("st"), st0 + cum_st[g - 4])
                q = 512 * (g % 4)
                v.tensor_copy(sb["scr"][:, q:q + 512], pbank(g % 2)).then_inc(ctx.s("ev"), 1)

        @blk.sync
        def _(s):
            g = 0
            for which in ("q", "k"):
                off0 = 0 if which == "q" else SEG_Q
                for ic in range(8):
                    for t4 in range(4):
                        s.wait_ge(ctx.s("ev"), ev0 + g + 1)
                        q = 512 * (g % 4)
                        dst = qkv_in[ic, off0:off0 + SEG_Q].rearrange("(p n) -> p n", p=128)[:, 512 * t4:512 * (t4 + 1)]
                        s.dma_start(out=dst, in_=sb["scr"][:, q:q + 512]).then_inc(ctx.s("st"), 16)
                        g += 1
            for tt in range(NT):
                for vc in range(2):
                    s.wait_ge(ctx.s("ev"), ev0 + g + 1)
                    q = 512 * (g % 4)
                    for blki in range(4):
                        p = 4 * vc + blki
                        dst = qkv_in[p, 2 * SEG_Q + 16384 * tt:2 * SEG_Q + 16384 * (tt + 1)]
                        dst = dst.rearrange("(p n) -> p n", p=128)
                        s.dma_start(out=dst,
                                    in_=sb["scr"][:, q + 128 * blki:q + 128 * (blki + 1)]).then_inc(ctx.s("st"), 16)
                    g += 1

        ctx.add("mm", 96)
        ctx.add("ev", 96)
        ctx.add("st", 16 * (64 + 32 * 4))

    def ph_a2a(blk, name_in, name_out):
        st_all = ctx.v("st")

        @blk.gpsimd
        def _(g):
            g.wait_ge(ctx.s("st"), st_all)
            with allow_ici_all_to_all():
                g.collective_compute(
                    "AllToAll", ALU.bypass,
                    replica_groups=[list(range(8))],
                    ins=[dram[name_in].ap().opt()],
                    outs=[dram[name_out].ap().opt()],
                ).then_inc(ctx.s("cc"), 1)
            g.wait_ge(ctx.s("cc"), ctx.add("cc"))

    def emit_attn_loads(s, b):
        qkv_out = dram["a2a_qkv_out"].ap()
        for i in range(4):
            srcb = qkv_out[4 * b + i, 0:SEG_Q].rearrange("(p n) -> p n", p=128)
            s.dma_start(out=sb["qtp"][:, 2048 * i:2048 * (i + 1)], in_=srcb).then_inc(ctx.s("ld"), 16)
        for i in range(4):
            srcb = qkv_out[4 * b + i, SEG_Q:2 * SEG_Q].rearrange("(p n) -> p n", p=128)
            s.dma_start(out=sb["ktp"][:, 2048 * i:2048 * (i + 1)], in_=srcb).then_inc(ctx.s("ld"), 16)
        vtv = sb["vt"].rearrange("p (c w) -> p c w", w=130)
        for i in range(4):
            srcb = qkv_out[4 * b + i, 2 * SEG_Q:].rearrange("(c p n) -> c p n", c=16, p=128)
            s.dma_start(out=vtv[:, 16 * i:16 * (i + 1), 0:64],
                        in_=srcb[:, :, 0:64].transpose([1, 0, 2])).then_inc(ctx.s("ld"), 16)
            s.dma_start(out=vtv[:, 16 * i:16 * (i + 1), 65:129],
                        in_=srcb[:, :, 64:128].transpose([1, 0, 2])).then_inc(ctx.s("ld"), 16)

    def ph_rope(blk, b):
        ld0, mm0, ev0 = ctx.v("ld"), ctx.v("mm"), ctx.v("ev")

        @blk.sync
        def _(s):
            emit_attn_loads(s, b)
            s.dma_start(out=sb["cosr"][:, :], in_=dram["c_cosd"].ap()).then_inc(ctx.s("ld"), 16)
            s.dma_start(out=sb["sinr"][:, :], in_=dram["c_sind"].ap()).then_inc(ctx.s("ld"), 16)
            s.dma_start(out=sb["rs2"][:, 0:128], in_=dram["c_rotm"].ap()).then_inc(ctx.s("ld"), 16)
        ctx.add("ld", 16 * 19)

        chunks = [(tgt, c) for tgt in ("qtp", "ktp") for c in range(16)]

        @blk.tensor
        def _(t):
            t.wait_ge(ctx.s("ld"), ld0 + 16 * 19)
            for r, (tgt, c) in enumerate(chunks):
                bk = r % 2
                if r >= 2:
                    t.wait_ge(ctx.s("ev"), ev0 + 3 * (r - 2) + 1)
                t.matmul(pbank(bk), sb["rs2"][:, 0:128],
                         sb[tgt][:, 512 * c:512 * (c + 1)],
                         start=True, stop=True).then_inc(ctx.s("mm"), 1)

        @blk.vector
        def _(v):
            for r, (tgt, c) in enumerate(chunks):
                bk = r % 2
                cs = slice(512 * c, 512 * (c + 1))
                v.wait_ge(ctx.s("mm"), mm0 + r + 1)
                v.tensor_mul(sb["rs1"][:, 0:512], pbank(bk), sb["sinr"][:, cs]).then_inc(ctx.s("ev"), 1)
                v.tensor_mul(sb["rs1"][:, 512:1024], sb[tgt][:, cs],
                             sb["cosr"][:, cs]).then_inc(ctx.s("ev"), 1)
                v.wait_ge(ctx.s("ev"), ev0 + 3 * r + 2)
                v.tensor_add(sb[tgt][:, cs], sb["rs1"][:, 0:512],
                             sb["rs1"][:, 512:1024]).then_inc(ctx.s("ev"), 1)

        ctx.add("mm", 32)
        ctx.add("ev", 96)

    def ph_l0_attn_batch(blk, b, c0, do_loads=True):
        ld0, mm0, pv0, ev0, sc0, st0, tr0 = (ctx.v("ld"), ctx.v("mm"), ctx.v("pv"),
                                             ctx.v("ev"), ctx.v("sc"), ctx.v("st"),
                                             ctx.v("tr"))
        cc0 = ctx.v("cc")

        @blk.vector
        def _(v):
            v.memset(sb["vt"].rearrange("p (c w) -> p c w", w=130)[:, :, 64:130:65],
                     1.0).then_inc(ctx.s("cc"), 1)
        ctx.add("cc", 1)

        if do_loads:
            @blk.sync
            def _(s):
                emit_attn_loads(s, b)
            ctx.add("ld", 16 * 16)

        qts = list(range(16))
        nks = [4 * (qt + 1) for qt in qts]
        total_exp = sum(nks)
        cum_pv = np.cumsum(nks)

        @blk.tensor
        def _(t):
            t.wait_ge(ctx.s("ld"), ctx.v("ld"))
            t.wait_ge(ctx.s("cc"), cc0 + 1)
            t.wait_ge(ctx.s("ev"), ev0)
            e = 0
            for qi, qt in enumerate(qts):
                nk = nks[qt]
                for kc in range(nk):
                    bA, bB = 2 * (kc % 2), 2 * (kc % 2) + 1
                    if e >= 2:
                        t.wait_ge(ctx.s("sc"), sc0 + e - 1)
                    last = t.matmul(pbank(bA), sb["ktp"][0:64, 128 * kc:128 * (kc + 1)],
                                    sb["qtp"][0:64, 512 * qt:512 * (qt + 1)],
                                    start=True, stop=(kc < 4 * qt))
                    last = t.matmul(pbank(bB), sb["ktp"][64:128, 128 * kc:128 * (kc + 1)],
                                    sb["qtp"][64:128, 512 * qt:512 * (qt + 1)],
                                    start=True, stop=(kc < 4 * qt))
                    if kc >= 4 * qt:
                        r = kc - 4 * qt
                        t.matmul(pbank(bA), sb["jlo"][0:64, :], sb["cml"][0:64, 512 * r:512 * (r + 1)],
                                 start=False, stop=False)
                        t.matmul(pbank(bA), sb["jhi"][0:64, :], sb["cmh"][0:64, 512 * r:512 * (r + 1)],
                                 start=False, stop=True)
                        t.matmul(pbank(bB), sb["jlo"][64:128, :], sb["cml"][64:128, 512 * r:512 * (r + 1)],
                                 start=False, stop=False)
                        last = t.matmul(pbank(bB), sb["jhi"][64:128, :],
                                        sb["cmh"][64:128, 512 * r:512 * (r + 1)],
                                        start=False, stop=True)
                    last.then_inc(ctx.s("mm"), 1)
                    if kc >= 1:
                        t.wait_ge(ctx.s("sc"), sc0 + e)
                        pc = kc - 1
                        if pc == 0 and qi > 0:
                            t.wait_ge(ctx.s("ev"), ev0 + 3 * qi)
                        half = 1024 * (pc % 4)
                        t.matmul(pbank(4, 0, 65), sb["vt"][:, 130 * pc:130 * pc + 65],
                                 sb["pt"][:, half:half + 512], start=(pc == 0), stop=False)
                        t.matmul(pbank(5, 0, 65), sb["vt"][:, 130 * pc + 65:130 * pc + 130],
                                 sb["pt"][:, half + 512:half + 1024],
                                 start=(pc == 0), stop=False).then_inc(ctx.s("pv"), 1)
                    e += 1
                t.wait_ge(ctx.s("sc"), sc0 + e)
                pc = nk - 1
                if pc == 0 and qi > 0:
                    t.wait_ge(ctx.s("ev"), ev0 + 3 * qi)
                half = 1024 * (pc % 4)
                t.matmul(pbank(4, 0, 65), sb["vt"][:, 130 * pc:130 * pc + 65],
                         sb["pt"][:, half:half + 512], start=(pc == 0), stop=True)
                t.matmul(pbank(5, 0, 65), sb["vt"][:, 130 * pc + 65:130 * pc + 130],
                         sb["pt"][:, half + 512:half + 1024],
                         start=(pc == 0), stop=True).then_inc(ctx.s("pv"), 1)
                t.wait_ge(ctx.s("ev"), ev0 + 3 * qi + 1)
                t.matmul(pbank(6, 0, 64), sb["onesf"][0:1, 0:64], sb["rec"][0:1, 0:512],
                         start=True, stop=True)
                t.matmul(pbank(7, 0, 64), sb["onesf"][32:33, 0:64], sb["rec"][32:33, 0:512],
                         start=True, stop=True).then_inc(ctx.s("tr"), 1)

        @blk.scalar
        def _(sc):
            e = 0
            for qi, qt in enumerate(qts):
                nk = nks[qt]
                for kc in range(nk):
                    sc.wait_ge(ctx.s("mm"), mm0 + e + 1)
                    if e >= 4:
                        sc.wait_ge(ctx.s("pv"), pv0 + e - 3)
                    half = 1024 * (kc % 4)
                    stp = kc % 2
                    sc.activation(sb["pt"][:, half:half + 1024],
                                  ps[:, 1024 * stp:1024 * stp + 1024],
                                  AF.Exp, bias=np.float32(-c0),
                                  scale=0.125).then_inc(ctx.s("sc"), 1)
                    e += 1

        @blk.vector
        def _(v):
            for qi, qt in enumerate(qts):
                v.wait_ge(ctx.s("pv"), pv0 + int(cum_pv[qi]))
                v.tensor_copy(sb["rec"][0:1, 0:512], ps[64:65, 512 * 4:512 * 4 + 512])
                v.tensor_copy(sb["rec"][32:33, 0:512], ps[64:65, 512 * 5:512 * 5 + 512]).then_inc(ctx.s("ev"), 1)
                v.wait_ge(ctx.s("tr"), tr0 + qi + 1)
                v.reciprocal(sb["bcb"][0:64, 0:512], pbank(6, 0, 64))
                v.reciprocal(sb["bcb"][0:64, 512:1024], pbank(7, 0, 64)).then_inc(ctx.s("ev"), 1)
                v.wait_ge(ctx.s("ev"), ev0 + 3 * qi + 2)
                if qi >= 2:
                    v.wait_ge(ctx.s("st"), st0 + 16 * (qi - 1))
                h = 512 * (qi % 2)
                v.tensor_mul(sb["aot2"][0:64, h:h + 512],
                             pbank(4, 0, 64), sb["bcb"][0:64, 0:512])
                v.tensor_mul(sb["aot2"][64:128, h:h + 512],
                             pbank(5, 0, 64), sb["bcb"][0:64, 512:1024]).then_inc(ctx.s("ev"), 1)

        @blk.gpsimd
        def _(g):
            for qi, qt in enumerate(qts):
                g.wait_ge(ctx.s("ev"), ev0 + 3 * qi + 3)
                h = 512 * (qi % 2)
                blkv = dram["a2a_ao_in"].ap()[4 * b + qt // 4, :].rearrange("(p n) -> p n", p=128)
                dst = blkv[:, 512 * (qt % 4):512 * (qt % 4 + 1)]
                g.dma_start(out=dst, in_=sb["aot2"][:, h:h + 512]).then_inc(ctx.s("st"), 16)

        ctx.add("mm", total_exp)
        ctx.add("sc", total_exp)
        ctx.add("pv", total_exp)
        ctx.add("tr", 16)
        ctx.add("ev", 3 * 16)
        ctx.add("st", 16 * 16)

    def ph_l0_wo(blk, wo="w_l0_wo", bo="w_l0_bo"):
        ld0, mm0, ev0 = ctx.v("ld"), ctx.v("mm"), ctx.v("ev")
        ao = dram["a2a_ao_out"].ap()

        @blk.gpsimd
        def _(g):
            for dc in range(8):
                g.dma_start(out=sb["wd"][:, 1024 * dc:1024 * (dc + 1)],
                            in_=dram[wo].ap()[128 * dc:128 * (dc + 1), :]).then_inc(ctx.s("ld"), 16)
            g.dma_start(out=sb["bias"][0:1, 0:D], in_=dram[bo].ap()).then_inc(ctx.s("ld"), 16)

        @blk.sync
        def _(s):
            s.wait_ge(ctx.s("cc"), ctx.v("cc"))
            for ic in range(8):
                src = ao[ic, :].rearrange("(p n) -> p n", p=128)
                s.dma_start(out=sb["aoT"][:, 2048 * ic:2048 * (ic + 1)], in_=src).then_inc(ctx.s("ld"), 16)
        ctx.add("ld", 16 * 17)

        @blk.tensor
        def _(t):
            t.wait_ge(ctx.s("ld"), ld0 + 16 * 17)
            g = 0
            for tt in range(NT):
                for ds in range(2):
                    bk = g % 2
                    if g >= 2:
                        t.wait_ge(ctx.s("ev"), ev0 + g - 1)
                    for ic in range(8):
                        t.matmul(pbank(bk),
                                 sb["aoT"][:, 2048 * ic + 128 * tt:2048 * ic + 128 * (tt + 1)],
                                 sb["wd"][:, 1024 * ic + 512 * ds:1024 * ic + 512 * (ds + 1)],
                                 start=(ic == 0), stop=False)
                    t.matmul(pbank(bk), sb["onesb"][0:1, 0:128],
                             sb["bias"][0:1, 512 * ds:512 * (ds + 1)],
                             start=False, stop=True).then_inc(ctx.s("mm"), 1)
                    g += 1

        @blk.vector
        def _(v):
            for g in range(32):
                v.wait_ge(ctx.s("mm"), mm0 + g + 1)
                tt, ds = g // 2, g % 2
                dst = sb["stream"][:, 1024 * tt + 512 * ds:1024 * tt + 512 * (ds + 1)]
                v.tensor_add(dst, dst, pbank(g % 2)).then_inc(ctx.s("ev"), 1)

        ctx.add("mm", 32)
        ctx.add("ev", 32)

    def ph_l0_ffn(blk):
        ld0, mm0, ev0, sc0 = ctx.v("ld"), ctx.v("mm"), ctx.v("ev"), ctx.v("sc")
        tr_ready = ctx.v("tr")

        events = []
        ev_idx = {}
        n = 0
        for g in range(8):
            n += 1
            ev_idx[("pt", g)] = n
        for t4 in range(4):
            for fc in range(16):
                n += 1
                ev_idx[("ht", t4, fc)] = n
            n += 1
            ev_idx[("qpart", t4)] = n
            for fc in range(16, 32):
                n += 1
                ev_idx[("ht", t4, fc)] = n
            for rc in range(2):
                n += 1
                ev_idx[("qfin", t4, rc)] = n
            for tt4 in range(4):
                for ds in range(2):
                    n += 1
                    ev_idx[("y", t4, tt4, ds)] = n
        n_mm_total = n

        ve_idx = {}
        n = 0
        for g in range(8):
            n += 1
            ve_idx[("pt", g)] = n
        for t4 in range(4):
            for rc in range(2):
                n += 1
                ve_idx[("qt2", t4, rc)] = n
            for tt4 in range(4):
                for ds in range(2):
                    n += 1
                    ve_idx[("y", t4, tt4, ds)] = n
        n_ev_total = n

        sc_idx = {}
        n = 0
        for t4 in range(4):
            for fc in range(32):
                n += 1
                sc_idx[(t4, fc)] = n
        n_sc_total = n

        @blk.gpsimd
        def _(g):
            for dc in range(8):
                g.dma_start(out=sb["u1"][:, 256 * dc:256 * (dc + 1)],
                            in_=dram["w_l0_u1"].ap()[128 * dc:128 * (dc + 1), :]).then_inc(ctx.s("ld"), 16)
            for rc in range(2):
                g.dma_start(out=sb["v1"][:, 4096 * rc:4096 * (rc + 1)],
                            in_=dram["w_l0_v1"].ap()[128 * rc:128 * (rc + 1), :]).then_inc(ctx.s("ld"), 16)
            for fc in range(32):
                g.dma_start(out=sb["u2"][:, 256 * fc:256 * (fc + 1)],
                            in_=dram["w_l0_u2"].ap()[128 * fc:128 * (fc + 1), :]).then_inc(ctx.s("ld"), 16)
            for rc in range(2):
                g.dma_start(out=sb["v2"][:, 1024 * rc:1024 * (rc + 1)],
                            in_=dram["w_l0_v2"].ap()[128 * rc:128 * (rc + 1), :]).then_inc(ctx.s("ld"), 16)
            g.dma_start(out=sb["bias"][0:1, :], in_=dram["w_l0_b1"].ap()).then_inc(ctx.s("ld"), 16)
            g.dma_start(out=sb["bias2"][0:1, 0:D], in_=dram["w_l0_b2"].ap()).then_inc(ctx.s("ld"), 16)
        NLD = 8 + 2 + 32 + 2 + 2
        ctx.add("ld", 16 * NLD)

        @blk.tensor
        def _(t):
            t.wait_ge(ctx.s("tr"), tr_ready)
            t.wait_ge(ctx.s("ld"), ld0 + 16 * NLD)
            for g in range(8):
                rc, t4 = g // 4, g % 4
                bk = g % 2
                if g >= 2:
                    t.wait_ge(ctx.s("ev"), ev0 + ve_idx[("pt", g - 2)])
                last = None
                for dc in range(8):
                    last = t.matmul(pbank(bk),
                                    sb["u1"][:, 256 * dc + 128 * rc:256 * dc + 128 * (rc + 1)],
                                    sb["xnT"][:, 2048 * dc + 512 * t4:2048 * dc + 512 * (t4 + 1)],
                                    start=(dc == 0), stop=(dc == 7))
                last.then_inc(ctx.s("mm"), 1)
            for t4 in range(4):
                if t4 == 0:
                    t.wait_ge(ctx.s("ev"), ev0 + ve_idx[("pt", 7)])
                else:
                    t.wait_ge(ctx.s("ev"), ev0 + ve_idx[("qt2", t4 - 1, 1)])
                for half in range(2):
                    for fci in range(16):
                        fc = 16 * half + fci
                        bk = 2 + (fc % 4)
                        if fc >= 4:
                            t.wait_ge(ctx.s("sc"), sc0 + sc_idx[(t4, fc)] - 4)
                        for rc in range(2):
                            t.matmul(pbank(bk),
                                     sb["v1"][:, 4096 * rc + 128 * fc:4096 * rc + 128 * (fc + 1)],
                                     sb["pt2"][:, 2048 * rc + 512 * t4:2048 * rc + 512 * (t4 + 1)],
                                     start=(rc == 0), stop=False)
                        t.matmul(pbank(bk), sb["bias"][0:1, 128 * fc:128 * (fc + 1)],
                                 sb["onesb"][0:1, 0:512], start=False,
                                 stop=True).then_inc(ctx.s("mm"), 1)
                    t.wait_ge(ctx.s("sc"), sc0 + sc_idx[(t4, 16 * half + 15)])
                    last = None
                    for rc in range(2):
                        for fci in range(16):
                            fc = 16 * half + fci
                            last = t.matmul(pbank(rc),
                                            sb["u2"][:, 256 * fc + 128 * rc:256 * fc + 128 * (rc + 1)],
                                            sb["ht"][:, 512 * fci:512 * (fci + 1)],
                                            start=(fc == 0), stop=(fc == 31))
                        if half == 1:
                            last.then_inc(ctx.s("mm"), 1)
                    if half == 0:
                        last.then_inc(ctx.s("mm"), 1)
                for tt4 in range(4):
                    for ds in range(2):
                        gy = 2 * tt4 + ds
                        bk = 6 + (gy % 2)
                        t.wait_ge(ctx.s("ev"), ev0 + ve_idx[("qt2", t4, 1)])
                        if gy >= 2:
                            t.wait_ge(ctx.s("ev"), ev0 + ve_idx[("y", t4, (gy - 2) // 2, (gy - 2) % 2)])
                        elif t4 > 0:
                            t.wait_ge(ctx.s("ev"), ev0 + ve_idx[("y", t4 - 1, 3, 1)])
                        for rc in range(2):
                            t.matmul(pbank(bk),
                                     sb["qt2"][:, 512 * rc + 128 * tt4:512 * rc + 128 * (tt4 + 1)],
                                     sb["v2"][:, 1024 * rc + 512 * ds:1024 * rc + 512 * (ds + 1)],
                                     start=(rc == 0), stop=False)
                        t.matmul(pbank(bk), sb["onesb"][0:1, 0:128],
                                 sb["bias2"][0:1, 512 * ds:512 * (ds + 1)], start=False,
                                 stop=True).then_inc(ctx.s("mm"), 1)

        @blk.scalar
        def _(sc):
            for t4 in range(4):
                for fc in range(32):
                    sc.wait_ge(ctx.s("mm"), mm0 + ev_idx[("ht", t4, fc)])
                    if fc == 16:
                        sc.wait_ge(ctx.s("mm"), mm0 + ev_idx[("qpart", t4)])
                    bk = 2 + (fc % 4)
                    sc.activation(sb["ht"][:, 512 * (fc % 16):512 * (fc % 16 + 1)], pbank(bk),
                                  AF.Gelu_apprx_tanh, bias=0.0, scale=1.0).then_inc(ctx.s("sc"), 1)

        @blk.vector
        def _(v):
            for g in range(8):
                v.wait_ge(ctx.s("mm"), mm0 + ev_idx[("pt", g)])
                rc, t4 = g // 4, g % 4
                v.tensor_copy(sb["pt2"][:, 2048 * rc + 512 * t4:2048 * rc + 512 * (t4 + 1)],
                              pbank(g % 2)).then_inc(ctx.s("ev"), 1)
            for t4 in range(4):
                for rc in range(2):
                    v.wait_ge(ctx.s("mm"), mm0 + ev_idx[("qfin", t4, rc)])
                    v.tensor_copy(sb["qt2"][:, 512 * rc:512 * (rc + 1)], pbank(rc)).then_inc(ctx.s("ev"), 1)
                for tt4 in range(4):
                    for ds in range(2):
                        v.wait_ge(ctx.s("mm"), mm0 + ev_idx[("y", t4, tt4, ds)])
                        bk = 6 + ((2 * tt4 + ds) % 2)
                        tt = 4 * t4 + tt4
                        dst = sb["stream"][:, 1024 * tt + 512 * ds:1024 * tt + 512 * (ds + 1)]
                        v.tensor_add(dst, dst, pbank(bk)).then_inc(ctx.s("ev"), 1)

        ctx.add("mm", n_mm_total)
        ctx.add("sc", n_sc_total)
        ctx.add("ev", n_ev_total)

    def ph_store(blk):
        st0 = ctx.v("st")

        @blk.sync
        def _(s):
            for tt in range(NT):
                s.dma_start(out=y_out.ap()[128 * tt:128 * (tt + 1), :],
                            in_=sb["stream"][:, 1024 * tt:1024 * (tt + 1)]).then_inc(ctx.s("st"), 16)
            s.wait_ge(ctx.s("st"), st0 + 16 * NT)
        ctx.add("st", 16 * NT)


    NORM_L2 = float(1.0 / np.sqrt(np.sqrt(64.0)))

    def ph_l2_comb(blk, wname, bcname, bias_off):
        ld0, mm0, ev0 = ctx.v("ld"), ctx.v("mm"), ctx.v("ev")

        @blk.gpsimd
        def _(g):
            for rc in range(8):
                g.dma_start(out=sb["wraw"][:, 1024 * rc:1024 * (rc + 1)],
                            in_=dram[wname].ap()[128 * rc:128 * (rc + 1), :]).then_inc(ctx.s("ld"), 16)
            g.dma_start(out=sb["bcl"][:, :], in_=dram[bcname].ap()).then_inc(ctx.s("ld"), 16)
            g.dma_start(out=sb["pl2"][:, :], in_=dram["w_l2_pT"].ap()).then_inc(ctx.s("ld"), 16)
        ctx.add("ld", 16 * 10)

        @blk.tensor
        def _(t):
            t.wait_ge(ctx.s("ld"), ld0 + 16 * 10)
            for dc in range(8):
                bkE, bkO = (0, 1) if dc % 2 == 0 else (2, 3)
                if dc >= 2:
                    t.wait_ge(ctx.s("ev"), ev0 + dc - 1)
                last = None
                for h in range(16):
                    base = 64 * (h % 2)
                    bk = bkE if h % 2 == 0 else bkO
                    lhs = sb["wraw"][base:base + 64,
                                     1024 * (h // 2) + 128 * dc:1024 * (h // 2) + 128 * (dc + 1)]
                    last = t.matmul(pbank(bk)[:, 64 * (h // 2):64 * (h // 2) + 64],
                                    lhs, sb["pl2"][base:base + 64, :],
                                    start=True, stop=True)
                last.then_inc(ctx.s("mm"), 1)
            last = None
            for h in range(16):
                base = 64 * (h % 2)
                bkb = 6 if h % 2 == 0 else 7
                last = t.matmul(ps[0:1, 512 * bkb + 64 * (h // 2):512 * bkb + 64 * (h // 2) + 64],
                                sb["bcl"][base:base + 64, h // 2:h // 2 + 1],
                                sb["pl2"][base:base + 64, :], start=True, stop=True)
            last.then_inc(ctx.s("mm"), 1)

        @blk.vector
        def _(v):
            for dc in range(8):
                v.wait_ge(ctx.s("mm"), mm0 + dc + 1)
                bkE, bkO = (0, 1) if dc % 2 == 0 else (2, 3)
                dst3 = sb["wcmb"].rearrange("p (c w) -> p c w", w=64)
                psE = pbank(bkE).rearrange("p (c w) -> p c w", w=64)
                psO = pbank(bkO).rearrange("p (c w) -> p c w", w=64)
                v.tensor_copy(dst3[:, 16 * dc:16 * (dc + 1):2, :], psE[:, 0:8, :])
                v.tensor_copy(dst3[:, 16 * dc + 1:16 * (dc + 1):2, :],
                              psO[:, 0:8, :]).then_inc(ctx.s("ev"), 1)
            v.wait_ge(ctx.s("mm"), mm0 + 9)
            bias3 = sb["bias"].rearrange("o (c w) -> o c w", w=64)
            ps3a = ps.rearrange("p (b w) -> p b w", w=64)
            v.tensor_copy(bias3[0:1, bias_off // 64:bias_off // 64 + 16:2, :],
                          ps3a[0:1, 48:56, :])
            v.tensor_copy(bias3[0:1, bias_off // 64 + 1:bias_off // 64 + 16:2, :],
                          ps3a[0:1, 56:64, :]).then_inc(ctx.s("ev"), 1)

        ctx.add("mm", 9)
        ctx.add("ev", 9)

    def ph_l2_qf(blk):
        mm0, ev0 = ctx.v("mm"), ctx.v("ev")
        tr_ready = ctx.v("tr")

        @blk.tensor
        def _(t):
            t.wait_ge(ctx.s("tr"), tr_ready)
            g = 0
            for ic in range(8):
                for t4 in range(4):
                    bk = g % 2
                    if g >= 2:
                        t.wait_ge(ctx.s("ev"), ev0 + 2 * (g - 1))
                    for dc in range(8):
                        t.matmul(pbank(bk),
                                 sb["wcmb"][:, 1024 * dc + 128 * ic:1024 * dc + 128 * (ic + 1)],
                                 sb["xnT"][:, 2048 * dc + 512 * t4:2048 * dc + 512 * (t4 + 1)],
                                 start=(dc == 0), stop=False)
                    t.matmul(pbank(bk), sb["bias"][0:1, 128 * ic:128 * (ic + 1)],
                             sb["onesb"][0:1, 0:512], start=False,
                             stop=True).then_inc(ctx.s("mm"), 1)
                    g += 1

        @blk.vector
        def _(v):
            for g in range(32):
                ic, t4 = g // 4, g % 4
                v.wait_ge(ctx.s("mm"), mm0 + g + 1)
                dst = sb["qft"][:, 2048 * ic + 512 * t4:2048 * ic + 512 * (t4 + 1)]
                v.tensor_scalar(dst, pbank(g % 2), NORM_L2, 0.0,
                                ALU.mult, ALU.max).then_inc(ctx.s("ev"), 1)
                v.wait_ge(ctx.s("ev"), ev0 + 2 * g + 1)
                v.tensor_scalar(dst, dst, 1e-6, None, ALU.add).then_inc(ctx.s("ev"), 1)

        ctx.add("mm", 32)
        ctx.add("ev", 64)

    def ph_l2_kv(blk):
        ld0, mm0, ev0 = ctx.v("ld"), ctx.v("mm"), ctx.v("ev")

        @blk.gpsimd
        def _(g):
            for dc in range(8):
                g.dma_start(out=sb["l2wv"][:, 1024 * dc:1024 * (dc + 1)],
                            in_=dram["w_l2_wvT"].ap()[128 * dc:128 * (dc + 1), :]).then_inc(ctx.s("ld"), 16)
            g.dma_start(out=sb["bias"][0:1, 2048:3072],
                        in_=dram["w_l2_bv"].ap()).then_inc(ctx.s("ld"), 16)
            g.dma_start(out=sb["bias"][0:1, 3072:4096],
                        in_=dram["w_l2_bo"].ap()).then_inc(ctx.s("ld"), 16)
        ctx.add("ld", 16 * 10)

        @blk.vector
        def _(v):
            v.memset(sb["v22"].rearrange("p (c w) -> p c w", w=65)[:, :, 64:65],
                     1.0).then_inc(ctx.s("ev"), 1)
        ev1 = ev0 + 1
        ctx.add("ev", 1)

        @blk.tensor
        def _(t):
            t.wait_ge(ctx.s("ld"), ld0 + 16 * 10)
            t.wait_ge(ctx.s("ev"), ev1)
            for tt in range(NT):
                for which in range(2):
                    for vc in range(2):
                        if tt >= 1:
                            t.wait_ge(ctx.s("ev"), ev1 + 5 * tt)
                        bk = 2 * which + vc
                        wsrc = "l2wv" if which == 0 else "wcmb"
                        for dc in range(8):
                            t.matmul(pbank(bk),
                                     sb["xnT"][:, 2048 * dc + 128 * tt:2048 * dc + 128 * (tt + 1)],
                                     sb[wsrc][:, 1024 * dc + 512 * vc:1024 * dc + 512 * (vc + 1)],
                                     start=(dc == 0), stop=False)
                        boff = 2048 if which == 0 else 1024
                        t.matmul(pbank(bk), sb["onesb"][0:1, 0:128],
                                 sb["bias"][0:1, boff + 512 * vc:boff + 512 * (vc + 1)],
                                 start=False, stop=True).then_inc(ctx.s("mm"), 1)
                t.wait_ge(ctx.s("ev"), ev1 + 5 * tt + 5)
                last = None
                for h in range(16):
                    bkv = 5 + h // 7
                    coff = 65 * (h % 7)
                    last = t.matmul(ps[0:64, 512 * bkv + coff:512 * bkv + coff + 65],
                                    sb["kf2"][:, 64 * h:64 * (h + 1)],
                                    sb["v22"][:, 65 * h:65 * (h + 1)],
                                    start=(tt == 0), stop=(tt == NT - 1))
                last.then_inc(ctx.s("mm"), 1)

        @blk.vector
        def _(v):
            v2v = sb["v22"].rearrange("p (c w) -> p c w", w=65)
            for tt in range(NT):
                for vc in range(2):
                    v.wait_ge(ctx.s("mm"), mm0 + 5 * tt + vc + 1)
                    v.tensor_copy(v2v[:, 8 * vc:8 * (vc + 1), 0:64],
                                  pbank(vc)).then_inc(ctx.s("ev"), 1)
                for vc in range(2):
                    v.wait_ge(ctx.s("mm"), mm0 + 5 * tt + 2 + vc + 1)
                    v.tensor_scalar(sb["kf2"][:, 512 * vc:512 * (vc + 1)],
                                    pbank(2 + vc), NORM_L2, 0.0,
                                    ALU.mult, ALU.max).then_inc(ctx.s("ev"), 1)
                v.wait_ge(ctx.s("ev"), ev1 + 5 * tt + 4)
                v.tensor_scalar(sb["kf2"][:, :], sb["kf2"][:, :], 1e-6, None,
                                ALU.add).then_inc(ctx.s("ev"), 1)

        ctx.add("mm", NT * 5)
        ctx.add("ev", NT * 5)

    def ph_l2_ar(blk):
        mm_all, ev0, st0 = ctx.v("mm"), ctx.v("ev"), ctx.v("st")

        @blk.vector
        def _(v):
            v.wait_ge(ctx.s("mm"), mm_all)
            v.tensor_copy(sb["kvs"][0:64, 0:455], ps[0:64, 512 * 5:512 * 5 + 455])
            v.tensor_copy(sb["kvs"][0:64, 455:910], ps[0:64, 512 * 6:512 * 6 + 455])
            v.tensor_copy(sb["kvs"][0:64, 910:1040],
                          ps[0:64, 512 * 7:512 * 7 + 130]).then_inc(ctx.s("ev"), 1)

        @blk.gpsimd
        def _(g):
            g.wait_ge(ctx.s("ev"), ev0 + 1)
            for h in range(16):
                g.dma_start(out=dram["ar_in"].ap()[h],
                            in_=sb["kvs"][0:64, 65 * h:65 * (h + 1)]).then_inc(ctx.s("st"), 16)
            g.wait_ge(ctx.s("st"), st0 + 16 * 16)
            g.collective_compute(
                "AllReduce", ALU.add,
                replica_groups=[[0, 1, 2, 3], [4, 5, 6, 7]],
                ins=[dram["ar_in"].ap().opt()],
                outs=[dram["ar_out"].ap().opt()],
            ).then_inc(ctx.s("cc"), 1)
            g.wait_ge(ctx.s("cc"), ctx.add("cc"))

        ctx.add("ev", 1)
        ctx.add("st", 16 * 16)

    def ph_l2_out(blk):
        ld0, mm0, ev0, tr0 = ctx.v("ld"), ctx.v("mm"), ctx.v("ev"), ctx.v("tr")
        dn = sb["stat"][:, 0:16]
        rc = sb["stat"][:, 16:32]

        def slot(h):
            p, half = h // 2, h % 2
            bank = (0 if half == 0 else 2) + (p // 6)
            return 512 * bank + 66 * (p % 6)

        @blk.gpsimd
        def _(g):
            for p in range(8):
                g.dma_start(out=sb["kvb"][0:64, 65 * p:65 * (p + 1)],
                            in_=dram["ar_out"].ap()[2 * p]).then_inc(ctx.s("ld"), 16)
                g.dma_start(out=sb["kvb"][64:128, 65 * p:65 * (p + 1)],
                            in_=dram["ar_out"].ap()[2 * p + 1]).then_inc(ctx.s("ld"), 16)
        ctx.add("ld", 16 * 16)

        @blk.tensor
        def _(t):
            t.wait_ge(ctx.s("ld"), ld0 + 16 * 16)
            for tt in range(NT):
                if tt >= 1:
                    t.wait_ge(ctx.s("ev"), ev0 + 5 * (tt - 1) + 4)
                last = None
                for h in range(16):
                    p, half = h // 2, h % 2
                    r0 = 64 * half
                    last = t.matmul(ps[:, slot(h):slot(h) + 65],
                                    sb["qft"][r0:r0 + 64,
                                              2048 * p + 128 * tt:2048 * p + 128 * (tt + 1)],
                                    sb["kvb"][r0:r0 + 64, 65 * p:65 * (p + 1)],
                                    start=True, stop=True)
                last.then_inc(ctx.s("mm"), 1)
                t.wait_ge(ctx.s("ev"), ev0 + 5 * tt + 4)
                if tt >= 1:
                    t.wait_ge(ctx.s("ev"), ev0 + 5 * (tt - 1) + 5)
                hq = 1024 * (tt % 2)
                ins = None
                for ic in range(8):
                    ins = t.transpose(ps.bitcast(BF16)[:, 6144 + 128 * ic:6144 + 128 * (ic + 1)],
                                      sb["aoq2"][:, hq + 128 * ic:hq + 128 * (ic + 1)],
                                      sb["ident"][:, :])
                ins.then_inc(ctx.s("tr"), 1)

        @blk.vector
        def _(v):
            for tt in range(NT):
                v.wait_ge(ctx.s("mm"), mm0 + tt + 1)
                ins = None
                for h in range(16):
                    ins = v.tensor_copy(dn[:, h:h + 1], ps[:, slot(h) + 64:slot(h) + 65])
                ins.then_inc(ctx.s("ev"), 1)
                v.wait_ge(ctx.s("ev"), ev0 + 5 * tt + 1)
                v.tensor_scalar(dn, dn, 1e-6, None, ALU.add).then_inc(ctx.s("ev"), 1)
                v.wait_ge(ctx.s("ev"), ev0 + 5 * tt + 2)
                v.reciprocal(rc, dn).then_inc(ctx.s("ev"), 1)
                v.wait_ge(ctx.s("ev"), ev0 + 5 * tt + 3)
                if tt >= 2:
                    v.wait_ge(ctx.s("tr"), tr0 + tt - 1)
                hq = 1024 * (tt % 2)
                ins = None
                for h in range(16):
                    p, half = h // 2, h % 2
                    ins = v.tensor_scalar(sb["aoq2"][:, hq + 128 * p + 64 * half:hq + 128 * p + 64 * (half + 1)],
                                          ps[:, slot(h):slot(h) + 64],
                                          rc[:, h:h + 1], None, ALU.mult)
                ins.then_inc(ctx.s("ev"), 1)
                v.wait_ge(ctx.s("tr"), tr0 + tt + 1)
                ins = None
                for ic in range(8):
                    ins = v.tensor_copy(sb["aotl2"][:, 2048 * ic + 128 * tt:2048 * ic + 128 * (tt + 1)],
                                        ps.bitcast(BF16)[:, 6144 + 128 * ic:6144 + 128 * (ic + 1)])
                ins.then_inc(ctx.s("ev"), 1)

        ctx.add("mm", NT)
        ctx.add("tr", NT)
        ctx.add("ev", 5 * NT)

    def ph_l2_wo(blk):
        ld0, mm0, ev0 = ctx.v("ld"), ctx.v("mm"), ctx.v("ev")

        @blk.gpsimd
        def _(g):
            for dc in range(8):
                g.dma_start(out=sb["wd"][:, 1024 * dc:1024 * (dc + 1)],
                            in_=dram["w_l2_woT"].ap()[128 * dc:128 * (dc + 1), :]).then_inc(ctx.s("ld"), 16)
        ctx.add("ld", 16 * 8)

        @blk.tensor
        def _(t):
            t.wait_ge(ctx.s("ld"), ld0 + 16 * 8)
            g = 0
            for tt in range(NT):
                for ds in range(2):
                    bk = g % 2
                    if g >= 2:
                        t.wait_ge(ctx.s("ev"), ev0 + g - 1)
                    for ic in range(8):
                        t.matmul(pbank(bk),
                                 sb["aotl2"][:, 2048 * ic + 128 * tt:2048 * ic + 128 * (tt + 1)],
                                 sb["wd"][:, 1024 * ic + 512 * ds:1024 * ic + 512 * (ds + 1)],
                                 start=(ic == 0), stop=False)
                    t.matmul(pbank(bk), sb["onesb"][0:1, 0:128],
                             sb["bias"][0:1, 3072 + 512 * ds:3072 + 512 * (ds + 1)],
                             start=False, stop=True).then_inc(ctx.s("mm"), 1)
                    g += 1

        @blk.vector
        def _(v):
            for g in range(32):
                v.wait_ge(ctx.s("mm"), mm0 + g + 1)
                tt, ds = g // 2, g % 2
                dst = sb["stream"][:, 1024 * tt + 512 * ds:1024 * tt + 512 * (ds + 1)]
                v.tensor_add(dst, dst, pbank(g % 2)).then_inc(ctx.s("ev"), 1)

        ctx.add("mm", 32)
        ctx.add("ev", 32)

    def ph_l2_ffn(blk):
        ld0, mm0, ev0, sc0 = ctx.v("ld"), ctx.v("mm"), ctx.v("ev"), ctx.v("sc")
        tr_ready = ctx.v("tr")

        def ht_mm_idx(t4, fc):
            return t4 * 65 + fc + 1

        def y_mm_idx(t4, fc):
            return t4 * 65 + 32 + fc + 1

        def ybias_mm_idx(t4):
            return t4 * 65 + 65

        @blk.gpsimd
        def _(g):
            g.dma_start(out=sb["bias"][0:1, :], in_=dram["w_l2_b1"].ap()).then_inc(ctx.s("ld"), 16)
            g.dma_start(out=sb["bias2"][0:1, :], in_=dram["w_l2_b2"].ap()).then_inc(ctx.s("ld"), 16)
            for t4 in range(4):
                for fc in range(32):
                    if t4 * 32 + fc >= 2:
                        pf = t4 * 32 + fc - 2
                        g.wait_ge(ctx.s("mm"), mm0 + ht_mm_idx(pf // 32, pf % 32))
                    slot = sb["w1s"][:, 1024 * (fc % 2):1024 * (fc % 2 + 1)]
                    srcw = dram["w_l2_w1"].ap()[:, 128 * fc:128 * (fc + 1)]
                    srcw = srcw.rearrange("(c p) n -> c p n", p=128).transpose([1, 0, 2])
                    g.dma_start(out=slot.rearrange("p (c n) -> p c n", n=128),
                                in_=srcw).then_inc(ctx.s("ld"), 16)
                for fc in range(32):
                    if t4 * 32 + fc >= 2:
                        pf = t4 * 32 + fc - 2
                        g.wait_ge(ctx.s("mm"), mm0 + y_mm_idx(pf // 32, pf % 32))
                    slot = sb["w2s"][:, 1024 * (fc % 2):1024 * (fc % 2 + 1)]
                    g.dma_start(out=slot,
                                in_=dram["w_l2_w2"].ap()[128 * fc:128 * (fc + 1), :]).then_inc(ctx.s("ld"), 16)

        NLD_T = 2 + 4 * 64
        ctx.add("ld", 16 * NLD_T)

        def ld_cnt_w1(t4, fc):
            return 2 + t4 * 64 + fc + 1

        def ld_cnt_w2(t4, fc):
            return 2 + t4 * 64 + 32 + fc + 1

        @blk.tensor
        def _(t):
            t.wait_ge(ctx.s("tr"), tr_ready)
            t.wait_ge(ctx.s("ld"), ld0 + 16 * 2)
            for t4 in range(4):
                if t4 >= 1:
                    t.wait_ge(ctx.s("ev"), ev0 + 8 * t4)
                for fc in range(32):
                    t.wait_ge(ctx.s("ld"), ld0 + 16 * ld_cnt_w1(t4, fc))
                    bk = 2 + (fc % 4)
                    if t4 * 32 + fc >= 4:
                        t.wait_ge(ctx.s("sc"), sc0 + t4 * 32 + fc - 3)
                    slot = sb["w1s"][:, 1024 * (fc % 2):1024 * (fc % 2 + 1)]
                    for dc in range(8):
                        t.matmul(pbank(bk), slot[:, 128 * dc:128 * (dc + 1)],
                                 sb["xnT"][:, 2048 * dc + 512 * t4:2048 * dc + 512 * (t4 + 1)],
                                 start=(dc == 0), stop=False)
                    t.matmul(pbank(bk), sb["bias"][0:1, 128 * fc:128 * (fc + 1)],
                             sb["onesb"][0:1, 0:512], start=False,
                             stop=True).then_inc(ctx.s("mm"), 1)
                t.wait_ge(ctx.s("sc"), sc0 + 32 * (t4 + 1))
                if t4 >= 1:
                    t.wait_ge(ctx.s("ev"), ev0 + 8 * t4)
                for fc in range(32):
                    t.wait_ge(ctx.s("ld"), ld0 + 16 * ld_cnt_w2(t4, fc))
                    slot = sb["w2s"][:, 1024 * (fc % 2):1024 * (fc % 2 + 1)]
                    last = None
                    for tt4 in range(4):
                        for ds in range(2):
                            bk = 2 * tt4 + ds
                            last = t.matmul(pbank(bk),
                                            sb["htl2"][:, 512 * fc + 128 * tt4:512 * fc + 128 * (tt4 + 1)],
                                            slot[:, 512 * ds:512 * (ds + 1)],
                                            start=(fc == 0), stop=False)
                    last.then_inc(ctx.s("mm"), 1)
                last = None
                for tt4 in range(4):
                    for ds in range(2):
                        bk = 2 * tt4 + ds
                        last = t.matmul(pbank(bk), sb["onesb"][0:1, 0:128],
                                        sb["bias2"][0:1, 512 * ds:512 * (ds + 1)],
                                        start=False, stop=True)
                last.then_inc(ctx.s("mm"), 1)

        @blk.scalar
        def _(sc):
            for t4 in range(4):
                for fc in range(32):
                    sc.wait_ge(ctx.s("mm"), mm0 + ht_mm_idx(t4, fc))
                    bk = 2 + (fc % 4)
                    sc.activation(sb["htl2"][:, 512 * fc:512 * (fc + 1)], pbank(bk),
                                  AF.Gelu, bias=0.0, scale=1.0).then_inc(ctx.s("sc"), 1)

        @blk.vector
        def _(v):
            for t4 in range(4):
                v.wait_ge(ctx.s("mm"), mm0 + ybias_mm_idx(t4))
                for tt4 in range(4):
                    for ds in range(2):
                        bk = 2 * tt4 + ds
                        tt = 4 * t4 + tt4
                        dst = sb["stream"][:, 1024 * tt + 512 * ds:1024 * tt + 512 * (ds + 1)]
                        v.tensor_add(dst, dst, pbank(bk)).then_inc(ctx.s("ev"), 1)

        ctx.add("mm", 4 * 65)
        ctx.add("sc", 4 * 32)
        ctx.add("ev", 4 * 8)



    def ph_l1_qkv(blk):
        ld0, mm0, ev0, st0 = ctx.v("ld"), ctx.v("mm"), ctx.v("ev"), ctx.v("st")
        tr_ready = ctx.v("tr")
        qkv_in = dram["a2a_qkv_in"].ap()

        @blk.gpsimd
        def _(g):
            for dc in range(8):
                g.dma_start(out=sb["wa"][:, 1024 * dc:1024 * (dc + 1)],
                            in_=dram["w_l1_wqT"].ap()[128 * dc:128 * (dc + 1), :]).then_inc(ctx.s("ld"), 16)
            for dc in range(8):
                g.dma_start(out=sb["wb"][:, 256 * dc:256 * (dc + 1)],
                            in_=dram["w_l1_wkT"].ap()[128 * dc:128 * (dc + 1), :]).then_inc(ctx.s("ld"), 16)
            for dc in range(8):
                g.dma_start(out=sb["wb"][:, 2048 + 256 * dc:2048 + 256 * (dc + 1)],
                            in_=dram["w_l1_wvT"].ap()[128 * dc:128 * (dc + 1), :]).then_inc(ctx.s("ld"), 16)
        ctx.add("ld", 16 * 24)

        cum_st = []
        tot = 0
        for gq in range(32):
            tot += 16
            cum_st.append(tot)
        for gk in range(8):
            tot += 16 * 4
            cum_st.append(tot)
        for gv in range(16):
            tot += 16 * 8
            cum_st.append(tot)

        @blk.tensor
        def _(t):
            t.wait_ge(ctx.s("tr"), tr_ready)
            t.wait_ge(ctx.s("ld"), ld0 + 16 * 8)
            g = 0
            for ic in range(8):
                for t4 in range(4):
                    bk = g % 2
                    if g >= 2:
                        t.wait_ge(ctx.s("ev"), ev0 + g - 1)
                    last = None
                    for dc in range(8):
                        last = t.matmul(pbank(bk),
                                        sb["wa"][:, 1024 * dc + 128 * ic:1024 * dc + 128 * (ic + 1)],
                                        sb["xnT"][:, 2048 * dc + 512 * t4:2048 * dc + 512 * (t4 + 1)],
                                        start=(dc == 0), stop=(dc == 7))
                    last.then_inc(ctx.s("mm"), 1)
                    g += 1
            t.wait_ge(ctx.s("ld"), ld0 + 16 * 16)
            for ic2 in range(2):
                for t4 in range(4):
                    bk = g % 2
                    t.wait_ge(ctx.s("ev"), ev0 + g - 1)
                    last = None
                    for dc in range(8):
                        last = t.matmul(pbank(bk),
                                        sb["wb"][:, 256 * dc + 128 * ic2:256 * dc + 128 * (ic2 + 1)],
                                        sb["xnT"][:, 2048 * dc + 512 * t4:2048 * dc + 512 * (t4 + 1)],
                                        start=(dc == 0), stop=(dc == 7))
                    last.then_inc(ctx.s("mm"), 1)
                    g += 1
            t.wait_ge(ctx.s("ld"), ld0 + 16 * 24)
            for tt in range(NT):
                bk = g % 2
                t.wait_ge(ctx.s("ev"), ev0 + g - 1)
                last = None
                for dc in range(8):
                    last = t.matmul(pbank(bk, 0, 128, 256),
                                    sb["xnT"][:, 2048 * dc + 128 * tt:2048 * dc + 128 * (tt + 1)],
                                    sb["wb"][:, 2048 + 256 * dc:2048 + 256 * (dc + 1)],
                                    start=(dc == 0), stop=(dc == 7))
                last.then_inc(ctx.s("mm"), 1)
                g += 1

        @blk.vector
        def _(v):
            for g in range(56):
                v.wait_ge(ctx.s("mm"), mm0 + g + 1)
                if g >= 4:
                    v.wait_ge(ctx.s("st"), st0 + cum_st[g - 4])
                q = 512 * (g % 4)
                w = 512 if g < 40 else 256
                v.tensor_copy(sb["scr"][:, q:q + w], pbank(g % 2, 0, 128, w)).then_inc(ctx.s("ev"), 1)

        @blk.sync
        def _(s):
            g = 0
            for ic in range(8):
                for t4 in range(4):
                    s.wait_ge(ctx.s("ev"), ev0 + g + 1)
                    q = 512 * (g % 4)
                    dst = qkv_in[ic, 0:SEG_Q].rearrange("(p n) -> p n", p=128)[:, 512 * t4:512 * (t4 + 1)]
                    s.dma_start(out=dst, in_=sb["scr"][:, q:q + 512]).then_inc(ctx.s("st"), 16)
                    g += 1
            for ic2 in range(2):
                for t4 in range(4):
                    s.wait_ge(ctx.s("ev"), ev0 + g + 1)
                    q = 512 * (g % 4)
                    for half in range(2):
                        for dup in range(2):
                            blkid = 4 * ic2 + 2 * half + dup
                            dst = qkv_in[blkid, SEG_Q:SEG_Q + 64 * 2048]
                            dst = dst.rearrange("(p n) -> p n", p=64)[:, 512 * t4:512 * (t4 + 1)]
                            s.dma_start(out=dst,
                                        in_=sb["scr"][64 * half:64 * (half + 1), q:q + 512]).then_inc(ctx.s("st"), 16)
                    g += 1
            for tt in range(NT):
                s.wait_ge(ctx.s("ev"), ev0 + g + 1)
                q = 512 * (g % 4)
                for gv in range(4):
                    for dup in range(2):
                        blkid = 2 * gv + dup
                        dst = qkv_in[blkid, SEG_Q + 64 * 2048 + 8192 * tt:
                                     SEG_Q + 64 * 2048 + 8192 * (tt + 1)]
                        dst = dst.rearrange("(p n) -> p n", p=128)
                        s.dma_start(out=dst,
                                    in_=sb["scr"][:, q + 64 * gv:q + 64 * (gv + 1)]).then_inc(ctx.s("st"), 16)
                g += 1

        ctx.add("mm", 56)
        ctx.add("ev", 56)
        ctx.add("st", 16 * (32 + 32 + 128))

    def ph_l1_rope(blk, b):
        ld0, mm0, ev0 = ctx.v("ld"), ctx.v("mm"), ctx.v("ev")
        qkv_out = dram["a2a_qkv_out"].ap()

        @blk.sync
        def _(s):
            for i in range(4):
                srcb = qkv_out[4 * b + i, 0:SEG_Q].rearrange("(p n) -> p n", p=128)
                s.dma_start(out=sb["qtp"][:, 2048 * i:2048 * (i + 1)], in_=srcb).then_inc(ctx.s("ld"), 16)
            for i in range(4):
                srcb = qkv_out[4 * b + i, SEG_Q:SEG_Q + 64 * 2048].rearrange("(p n) -> p n", p=64)
                s.dma_start(out=sb["ktp"][0:64, 2048 * i:2048 * (i + 1)], in_=srcb).then_inc(ctx.s("ld"), 16)
                s.dma_start(out=sb["ktp"][64:128, 2048 * i:2048 * (i + 1)], in_=srcb).then_inc(ctx.s("ld"), 16)
            vtv = sb["vt1"].rearrange("p (c w) -> p c w", w=64)
            for i in range(4):
                srcb = qkv_out[4 * b + i, SEG_Q + 64 * 2048:SEG_Q + 64 * 2048 + 16 * 8192]
                srcb = srcb.rearrange("(c p n) -> c p n", c=16, p=128)
                s.dma_start(out=vtv[:, 16 * i:16 * (i + 1), :],
                            in_=srcb.transpose([1, 0, 2])).then_inc(ctx.s("ld"), 16)
            s.dma_start(out=sb["cosr"][:, :], in_=dram["c_cosd"].ap()).then_inc(ctx.s("ld"), 16)
            s.dma_start(out=sb["sinr"][:, :], in_=dram["c_sind"].ap()).then_inc(ctx.s("ld"), 16)
            s.dma_start(out=sb["rs2"][:, 0:128], in_=dram["c_rotm"].ap()).then_inc(ctx.s("ld"), 16)
        ctx.add("ld", 16 * 19)

        chunks = [(tgt, c) for tgt in ("qtp", "ktp") for c in range(16)]

        @blk.tensor
        def _(t):
            t.wait_ge(ctx.s("ld"), ld0 + 16 * 19)
            for r, (tgt, c) in enumerate(chunks):
                bk = r % 2
                if r >= 2:
                    t.wait_ge(ctx.s("ev"), ev0 + 3 * (r - 2) + 1)
                t.matmul(pbank(bk), sb["rs2"][:, 0:128],
                         sb[tgt][:, 512 * c:512 * (c + 1)],
                         start=True, stop=True).then_inc(ctx.s("mm"), 1)

        @blk.vector
        def _(v):
            for r, (tgt, c) in enumerate(chunks):
                bk = r % 2
                cs = slice(512 * c, 512 * (c + 1))
                v.wait_ge(ctx.s("mm"), mm0 + r + 1)
                v.tensor_mul(sb["rs1"][:, 0:512], pbank(bk), sb["sinr"][:, cs]).then_inc(ctx.s("ev"), 1)
                v.tensor_mul(sb["rs1"][:, 512:1024], sb[tgt][:, cs],
                             sb["cosr"][:, cs]).then_inc(ctx.s("ev"), 1)
                v.wait_ge(ctx.s("ev"), ev0 + 3 * r + 2)
                v.tensor_add(sb[tgt][:, cs], sb["rs1"][:, 0:512],
                             sb["rs1"][:, 512:1024]).then_inc(ctx.s("ev"), 1)

        ctx.add("mm", 32)
        ctx.add("ev", 96)

    def ph_l1_attn(blk, b):
        ld0, mm0, ev0, st0 = ctx.v("ld"), ctx.v("mm"), ctx.v("ev"), ctx.v("st")

        @blk.gpsimd
        def _(g):
            g.dma_start(out=sb["l1m"][:, :], in_=dram["c_l1maskT"].ap()).then_inc(ctx.s("ld"), 16)
            g.dma_start(out=sb["l1q"][:, :], in_=dram["c_l1qdec"].ap()).then_inc(ctx.s("ld"), 16)
            g.dma_start(out=sb["l1k"][:, :], in_=dram["c_l1kdec"].ap()).then_inc(ctx.s("ld"), 16)
            g.dma_start(out=sb["l1b"][:, :], in_=dram["c_l1bdec"].ap()).then_inc(ctx.s("ld"), 16)
        ctx.add("ld", 16 * 4)
        ld_all = ctx.v("ld")

        @blk.tensor
        def _(t):
            t.wait_ge(ctx.s("ld"), ld_all)
            for c in range(64):
                bk = c % 2
                if c >= 2:
                    t.wait_ge(ctx.s("ev"), ev0 + (c - 2) + 1)
                t.transpose(pbank(bk, 0, 128, 32).bitcast(BF16),
                            sb["ktp"][0:64, 128 * c:128 * (c + 1)],
                            sb["ident"][0:64, 0:64]).then_inc(ctx.s("mm"), 1)

        @blk.vector
        def _(v):
            for c in range(64):
                v.wait_ge(ctx.s("mm"), mm0 + c + 1)
                v.tensor_copy(sb["ktok"][:, 64 * c:64 * (c + 1)],
                              pbank(c % 2, 0, 128, 32).bitcast(BF16)).then_inc(ctx.s("ev"), 1)
            v.wait_ge(ctx.s("ev"), ev0 + 64)
            qd3 = sb["l1q"][:, :].unsqueeze(1).broadcast_to([128, 64, 128])
            v.tensor_mul(sb["qdt"].rearrange("p (c w) -> p c w", w=128),
                         sb["qtp"].rearrange("p (c w) -> p c w", w=128), qd3)
            v.tensor_scalar(sb["kdt"][:, 0:4096], sb["ktok"][:, :],
                            sb["l1k"][:, 0:1], None, ALU.mult)
            v.tensor_scalar(sb["kdt"][:, 4096:8192], sb["ktok"][:, :],
                            sb["l1k"][:, 1:2], None, ALU.mult).then_inc(ctx.s("ev"), 1)
            v.memset(sb["kvf"][:, :], 0.0)
            v.memset(sb["kvp"][:, :], 0.0).then_inc(ctx.s("ev"), 1)

        ctx.add("mm", 64)
        ctx.add("ev", 66)
        pre_ev = ctx.v("ev")
        pre_mm = ctx.v("mm")

        @blk.tensor
        def _(t):
            t.wait_ge(ctx.s("ev"), pre_ev)
            for c in range(64):
                cs = slice(128 * c, 128 * (c + 1))
                vs = slice(64 * c, 64 * (c + 1))
                if c >= 1:
                    t.wait_ge(ctx.s("ev"), pre_ev + 4 * (c - 1) + 1)
                t.matmul(pbank(0, 0, 128, 128), sb["ktp"][0:64, cs], sb["qtp"][0:64, cs],
                         start=True, stop=True)
                t.matmul(pbank(1, 0, 128, 128), sb["ktp"][64:128, cs], sb["qtp"][64:128, cs],
                         start=True, stop=True).then_inc(ctx.s("mm"), 1)
                if c >= 1:
                    t.wait_ge(ctx.s("ev"), pre_ev + 4 * (c - 1) + 2)
                t.matmul(pbank(2, 0, 64, 64), sb["kdt"][:, vs], sb["vt1"][:, vs],
                         start=True, stop=True)
                t.matmul(pbank(3, 0, 64, 64), sb["kdt"][:, 4096 + 64 * c:4096 + 64 * (c + 1)],
                         sb["vt1"][:, vs], start=True, stop=True).then_inc(ctx.s("mm"), 1)
                t.wait_ge(ctx.s("ev"), pre_ev + 4 * c + 1)
                if c >= 1:
                    t.wait_ge(ctx.s("ev"), pre_ev + 4 * (c - 1) + 3)
                    t.wait_ge(ctx.s("ev"), pre_ev + 4 * (c - 1) + 4)
                t.matmul(pbank(4, 0, 64, 128), sb["vt1"][:, vs], sb["st1"][:, 0:128],
                         start=True, stop=False)
                t.matmul(pbank(4, 0, 64, 128), sb["kvp"][:, 0:64],
                         sb["qdt"][:, cs], start=False, stop=True)
                t.matmul(pbank(5, 0, 64, 128), sb["vt1"][:, vs], sb["st1"][:, 128:256],
                         start=True, stop=False)
                t.matmul(pbank(5, 0, 64, 128), sb["kvp"][:, 64:128],
                         sb["qdt"][:, cs], start=False, stop=True).then_inc(ctx.s("mm"), 1)

        @blk.vector
        def _(v):
            nst = 0
            for c in range(64):
                v.wait_ge(ctx.s("mm"), pre_mm + 3 * c + 1)
                if c >= 1:
                    v.wait_ge(ctx.s("mm"), pre_mm + 3 * (c - 1) + 3)
                v.tensor_mul(sb["st1"][:, 0:128], pbank(0, 0, 128, 128), sb["l1m"][:, 0:128])
                v.tensor_mul(sb["st1"][:, 128:256], pbank(1, 0, 128, 128),
                             sb["l1m"][:, 128:256]).then_inc(ctx.s("ev"), 1)
                v.wait_ge(ctx.s("mm"), pre_mm + 3 * c + 2)
                v.scalar_tensor_tensor(sb["kvf"][0:64, 0:64], sb["kvf"][0:64, 0:64],
                                       sb["l1b"][0:64, 0:1], pbank(2, 0, 64, 64),
                                       ALU.mult, ALU.add)
                v.scalar_tensor_tensor(sb["kvf"][64:128, 64:128], sb["kvf"][64:128, 64:128],
                                       sb["l1b"][64:128, 1:2], pbank(3, 0, 64, 64),
                                       ALU.mult, ALU.add).then_inc(ctx.s("ev"), 1)
                v.wait_ge(ctx.s("ev"), pre_ev + 4 * c + 2)
                v.tensor_copy(sb["kvp"][0:64, 0:64], sb["kvf"][0:64, 0:64])
                v.tensor_copy(sb["kvp"][64:128, 64:128],
                              sb["kvf"][64:128, 64:128]).then_inc(ctx.s("ev"), 1)
                v.wait_ge(ctx.s("mm"), pre_mm + 3 * c + 3)
                h = 512 * ((c // 4) % 2)
                if c >= 8 and c % 4 == 0:
                    v.wait_ge(ctx.s("st"), st0 + 16 * (c // 4 - 1))
                v.tensor_copy(sb["aot2"][0:64, h + 128 * (c % 4):h + 128 * (c % 4 + 1)],
                              pbank(4, 0, 64, 128))
                v.tensor_copy(sb["aot2"][64:128, h + 128 * (c % 4):h + 128 * (c % 4 + 1)],
                              pbank(5, 0, 64, 128)).then_inc(ctx.s("ev"), 1)

        @blk.gpsimd
        def _(g):
            for c4 in range(16):
                g.wait_ge(ctx.s("ev"), pre_ev + 4 * (4 * c4 + 3) + 4)
                h = 512 * (c4 % 2)
                blkv = dram["a2a_ao_in"].ap()[4 * b + c4 // 4, :].rearrange("(p n) -> p n", p=128)
                dst = blkv[:, 512 * (c4 % 4):512 * (c4 % 4 + 1)]
                g.dma_start(out=dst, in_=sb["aot2"][:, h:h + 512]).then_inc(ctx.s("st"), 16)

        ctx.add("mm", 3 * 64)
        ctx.add("ev", 4 * 64)
        ctx.add("st", 16 * 16)


    def ph_l1_sum(blk):
        ld0, mm0, ev0, st0 = ctx.v("ld"), ctx.v("mm"), ctx.v("ev"), ctx.v("st")
        parts = [("w_l1_w1T_part", 2, DFF), ("w_l1_w2T_part", 8, D)]
        abscol = sb["stat"][:, 0:10]
        par2 = sb["stat"][:, 10:12]

        @blk.gpsimd
        def _(g):
            k = 0
            for wname, n, W in parts:
                for t in range(n):
                    if k >= 1:
                        g.wait_ge(ctx.s("ev"), ev0 + 1 + k)
                    g.dma_start(out=sb["wfl1"].bitcast(F32)[:, 0:W],
                                in_=dram[wname].ap()[128 * t:128 * (t + 1), :]).then_inc(ctx.s("ld"), 16)
                    k += 1
            g.wait_ge(ctx.s("ev"), ev0 + 13)
            g.dma_start(out=dram["l1sum_io"].ap(),
                        in_=sb["rec"][0:1, 8:10]).then_inc(ctx.s("st"), 16)
            g.wait_ge(ctx.s("st"), st0 + 16)

        @blk.vector
        def _(v):
            v.memset(sb["cst"][:, 3:4], 1.0).then_inc(ctx.s("ev"), 1)
            k = 0
            for wname, n, W in parts:
                for t in range(n):
                    v.wait_ge(ctx.s("ld"), ld0 + 16 * (k + 1))
                    v.reduce_sum(abscol[:, k:k + 1], sb["wfl1"].bitcast(F32)[:, 0:W],
                                 axis=AX.X, apply_absolute_value=True).then_inc(ctx.s("ev"), 1)
                    k += 1
            v.wait_ge(ctx.s("ev"), ev0 + 11)
            v.reduce_sum(par2[:, 0:1], abscol[:, 0:2], axis=AX.X)
            v.reduce_sum(par2[:, 1:2], abscol[:, 2:10], axis=AX.X).then_inc(ctx.s("ev"), 1)
            v.wait_ge(ctx.s("mm"), mm0 + 1)
            v.tensor_copy(sb["rec"][0:1, 8:10], ps[0:1, 0:2]).then_inc(ctx.s("ev"), 1)

        @blk.tensor
        def _(t):
            t.wait_ge(ctx.s("ev"), ev0 + 12)
            t.matmul(ps[0:1, 0:2], sb["cst"][:, 3:4], par2,
                     start=True, stop=True).then_inc(ctx.s("mm"), 1)

        ctx.add("ld", 16 * 10)
        ctx.add("st", 16)
        ctx.add("mm", 1)
        ctx.add("ev", 13)

    def ph_l1_ar(blk):
        st_all = ctx.v("st")

        @blk.gpsimd
        def _(g):
            g.wait_ge(ctx.s("st"), st_all)
            with allow_ici_all_to_all():
                g.collective_compute(
                    "AllReduce", ALU.add,
                    replica_groups=[[0, 1, 2, 3], [4, 5, 6, 7]],
                    ins=[dram["l1sum_io"].ap().opt()],
                    outs=[dram["l1sum_out"].ap().opt()],
                ).then_inc(ctx.s("cc"), 1)
            g.wait_ge(ctx.s("cc"), ctx.add("cc"))

    def ph_l1_tern(blk, wpart, n_tiles, W, qpart, sw_col, sum_col):
        ld0, mm0, ev0, st0 = ctx.v("ld"), ctx.v("mm"), ctx.v("ev"), ctx.v("st")
        n = n_tiles
        swb = sb["stat"][:, 12:13]
        wf = sb["wfl1"].bitcast(F32)[:, 0:W]
        qf = sb["qfl1"].bitcast(F32)[:, 0:W]
        i8 = sb["i8l1"].bitcast(I8)[:, 0:W]
        qb = sb["qbl1"][:, 0:W]
        base2 = ev0 + 4

        def p1_mark(t):
            return base2 + 3 * t + 1

        def p2_mark(t):
            return base2 + 3 * t + 2

        def cp_mark(t):
            return base2 + 3 * t + 3

        @blk.gpsimd
        def _(g):
            g.dma_start(out=sb["rec"][0:1, 12:14],
                        in_=dram["l1sum_out"].ap()).then_inc(ctx.s("ld"), 16)
            for t in range(n):
                if t >= 1:
                    g.wait_ge(ctx.s("ev"), p1_mark(t - 1))
                g.dma_start(out=wf,
                            in_=dram[wpart].ap()[128 * t:128 * (t + 1), :]).then_inc(ctx.s("ld"), 16)

        @blk.sync
        def _(s):
            for t in range(n):
                s.wait_ge(ctx.s("ev"), cp_mark(t))
                s.dma_start(out=dram[qpart].ap()[128 * t:128 * (t + 1), :],
                            in_=qb).then_inc(ctx.s("st"), 16)

        @blk.tensor
        def _(t):
            t.wait_ge(ctx.s("ev"), ev0 + 3)
            t.matmul(pbank(1, 0, 128, 1), sb["onesf"][0:1, 0:128],
                     sb["rec"][0:1, 0:1], start=True, stop=True)
            t.matmul(pbank(2, 0, 128, 1), sb["onesf"][0:1, 0:128],
                     sb["rec"][0:1, 1:2], start=True, stop=True).then_inc(ctx.s("mm"), 1)

        @blk.vector
        def _(v):
            v.wait_ge(ctx.s("ld"), ld0 + 16)
            v.tensor_scalar(sb["rec"][0:1, 1:2],
                            sb["rec"][0:1, 12 + sum_col:13 + sum_col],
                            1.0 / (D * DFF), 1e-5, ALU.mult,
                            ALU.max).then_inc(ctx.s("ev"), 1)
            v.wait_ge(ctx.s("ev"), ev0 + 1)
            v.reciprocal(sb["rec"][0:1, 0:1], sb["rec"][0:1, 1:2]).then_inc(ctx.s("ev"), 1)
            v.wait_ge(ctx.s("ev"), ev0 + 2)
            v.engine_nop().then_inc(ctx.s("ev"), 1)
            v.wait_ge(ctx.s("mm"), mm0 + 1)
            v.tensor_copy(swb, pbank(1, 0, 128, 1))
            v.tensor_copy(sb["swc"][:, sw_col:sw_col + 1],
                          pbank(2, 0, 128, 1)).then_inc(ctx.s("ev"), 1)
            v.wait_ge(ctx.s("ev"), base2)
            for t in range(n):
                v.wait_ge(ctx.s("ld"), ld0 + 16 * (t + 2))
                v.tensor_scalar(qf, wf, swb, -1.0, ALU.mult,
                                ALU.max).then_inc(ctx.s("ev"), 1)
                v.wait_ge(ctx.s("ev"), p1_mark(t))
                v.tensor_scalar(i8, qf, 1.0, None, ALU.min).then_inc(ctx.s("ev"), 1)
                v.wait_ge(ctx.s("ev"), p2_mark(t))
                if t >= 1:
                    v.wait_ge(ctx.s("st"), st0 + 16 * t)
                v.tensor_copy(qb, i8).then_inc(ctx.s("ev"), 1)

        ctx.add("ld", 16 * (n + 1))
        ctx.add("st", 16 * n)
        ctx.add("mm", 1)
        ctx.add("ev", 4 + 3 * n)

    def ph_l1_agq(blk, part, dstq, prev=None):
        st_all = ctx.v("st")

        @blk.gpsimd
        def _(g):
            g.wait_ge(ctx.s("st"), st_all)
            if prev is not None:
                g.wait_ge(ctx.s("cc"), cc_marks[prev])
            with allow_ici_all_to_all():
                g.collective_compute(
                    "AllGather", ALU.bypass,
                    replica_groups=[[0, 1, 2, 3], [4, 5, 6, 7]],
                    ins=[dram[part].ap().opt()],
                    outs=[dram[dstq].ap().opt()],
                ).then_inc(ctx.s("cc"), 1)
            cc_marks[dstq] = ctx.add("cc")
            g.wait_ge(ctx.s("cc"), cc_marks[dstq])

    def ph_l1_quantx(blk):
        ld0, ev0, sc0, st0 = ctx.v("ld"), ctx.v("ev"), ctx.v("sc"), ctx.v("st")
        ss = sb["stat"][:, 64:80]
        rstd1 = sb["stat"][:, 80:96]
        sA = sb["stat"][:, 96:112]
        sq1 = sb["stat"][:, 112:128]
        mcol = sb["stat"][:, 128:144]
        xb = sb["xb1"].bitcast(F32)[:, 0:D]
        xh = sb["xh1"].bitcast(F32)[:, 0:D]
        i8 = sb["i81"].bitcast(I8)[:, 0:D]
        xq = sb["xqb1"][:, 0:D]
        n1 = sb["n1r"].bitcast(F32)[:, 0:D]

        def e_(t):
            return ev0 + 10 * t

        @blk.gpsimd
        def _(g):
            g.dma_start(out=n1, in_=dram["c_n1rep"].ap()).then_inc(ctx.s("ld"), 16)
            g.dma_start(out=sb["l1s"][:, :], in_=dram["c_l1s"].ap()).then_inc(ctx.s("ld"), 16)
            for t in range(NT):
                if t >= 1:
                    g.wait_ge(ctx.s("ev"), e_(t - 1) + 3)
                g.dma_start(out=xb,
                            in_=dram["stage_xn32"].ap()[128 * t:128 * (t + 1), :]).then_inc(ctx.s("ld"), 16)

        @blk.scalar
        def _(sc):
            for t in range(NT):
                sc.wait_ge(ctx.s("ld"), ld0 + 16 * (t + 3))
                sc.activation(ps[:, 2048:3072], xb, AF.Square,
                              accum_out=ss[:, t:t + 1]).then_inc(ctx.s("sc"), 1)
                sc.wait_ge(ctx.s("ev"), e_(t) + 1)
                sc.activation(mcol[:, t:t + 1], rstd1[:, t:t + 1], AF.Sqrt,
                              bias=np.float32(0.0), scale=1.0).then_inc(ctx.s("sc"), 1)

        @blk.vector
        def _(v):
            for t in range(NT):
                v.wait_ge(ctx.s("sc"), sc0 + 2 * t + 1)
                v.tensor_scalar(rstd1[:, t:t + 1], ss[:, t:t + 1], 1.0 / D, 1e-8,
                                ALU.mult, ALU.add).then_inc(ctx.s("ev"), 1)
                v.wait_ge(ctx.s("sc"), sc0 + 2 * t + 2)
                v.reciprocal(rstd1[:, t:t + 1], mcol[:, t:t + 1]).then_inc(ctx.s("ev"), 1)
                v.wait_ge(ctx.s("ev"), e_(t) + 2)
                v.scalar_tensor_tensor(xh, xb, rstd1[:, t:t + 1], n1,
                                       ALU.mult, ALU.mult).then_inc(ctx.s("ev"), 1)
                v.wait_ge(ctx.s("ev"), e_(t) + 3)
                v.reduce_max(mcol[:, t:t + 1], xh, axis=AX.X,
                             apply_absolute_value=True).then_inc(ctx.s("ev"), 1)
                v.wait_ge(ctx.s("ev"), e_(t) + 4)
                v.tensor_scalar(mcol[:, t:t + 1], mcol[:, t:t + 1], 1e-5, None,
                                ALU.max).then_inc(ctx.s("ev"), 1)
                v.wait_ge(ctx.s("ev"), e_(t) + 5)
                v.reciprocal(sq1[:, t:t + 1], mcol[:, t:t + 1]).then_inc(ctx.s("ev"), 1)
                v.wait_ge(ctx.s("ev"), e_(t) + 6)
                v.tensor_scalar(sq1[:, t:t + 1], sq1[:, t:t + 1], 127.0, None,
                                ALU.mult)
                v.scalar_tensor_tensor(mcol[:, t:t + 1], mcol[:, t:t + 1],
                                       1.0 / 127.0, sb["swc"][:, 0:1],
                                       ALU.mult, ALU.mult).then_inc(ctx.s("ev"), 1)
                v.wait_ge(ctx.s("ev"), e_(t) + 7)
                v.tensor_mul(sA[:, t:t + 1], mcol[:, t:t + 1],
                             mcol[:, t:t + 1]).then_inc(ctx.s("ev"), 1)
                v.wait_ge(ctx.s("ev"), e_(t) + 8)
                v.tensor_scalar(sA[:, t:t + 1], sA[:, t:t + 1],
                                sb["l1s"][:, 0:1], None, ALU.mult)
                v.tensor_scalar(i8, xh, sq1[:, t:t + 1], None,
                                ALU.mult).then_inc(ctx.s("ev"), 1)
                v.wait_ge(ctx.s("ev"), e_(t) + 9)
                if t >= 1:
                    v.wait_ge(ctx.s("st"), st0 + 16 * t)
                v.tensor_copy(xq, i8).then_inc(ctx.s("ev"), 1)

        @blk.sync
        def _(s):
            for t in range(NT):
                s.wait_ge(ctx.s("ev"), e_(t) + 10)
                s.dma_start(out=dram["stage_xn"].ap()[128 * t:128 * (t + 1), :],
                            in_=xq).then_inc(ctx.s("st"), 16)
            s.wait_ge(ctx.s("st"), st0 + 16 * NT)
            for dc in range(8):
                s.dma_start(out=sb["xnT"][:, 2048 * dc:2048 * (dc + 1)],
                            in_=dram["stage_xn"].ap()[:, 128 * dc:128 * (dc + 1)],
                            transpose=True).then_inc(ctx.s("tr"), 16)

        ctx.add("ld", 16 * (NT + 2))
        ctx.add("st", 16 * NT)
        ctx.add("sc", 2 * NT)
        ctx.add("ev", 10 * NT)
        ctx.add("tr", 16 * 8)

    def ph_l1_mm1(blk):
        ld0, mm0, ev0, sc0, st0 = (ctx.v("ld"), ctx.v("mm"), ctx.v("ev"),
                                   ctx.v("sc"), ctx.v("st"))
        tr_ready = ctx.v("tr")
        ss2a = sb["stat"][:, 0:16]
        rstd2 = sb["stat"][:, 16:32]
        m2 = sb["stat"][:, 32:48]
        sq2 = sb["stat"][:, 48:64]
        ss2b = sb["stat"][:, 64:80]
        sA = sb["stat"][:, 96:112]
        mcol = sb["stat"][:, 128:144]
        rs2 = sb["stat"][:, 144:160]
        n2 = sb["n2r"].bitcast(F32)[:, 0:DFF]
        i8 = sb["i8h"].bitcast(I8)[:, 0:DFF]

        def aF(sub):
            return (sb["a0"] if sub == 0 else sb["a1"]).bitcast(F32)[:, 0:DFF]

        def araw(sub):
            return (sb["a0"] if sub == 0 else sb["a1"])[:, 0:DFF]

        def evtp0(tp):
            return ev0 + 68 * tp

        def qb_(tp, sub):
            return evtp0(tp) + 48 + 10 * sub

        @blk.gpsimd
        def _(g):
            g.dma_start(out=n2, in_=dram["c_n2rep"].ap()).then_inc(ctx.s("ld"), 16)
            if "wq1q" in cc_marks:
                g.wait_ge(ctx.s("cc"), cc_marks["wq1q"])
            for k in range(64):
                if k >= 2:
                    g.wait_ge(ctx.s("mm"), mm0 + 2 * (k - 1))
                hs = k % 8
                slab = sb["w1sl"].rearrange("p (c n) -> p c n", n=512)
                srcw = dram["wq1q"].ap()[:, 512 * hs:512 * (hs + 1)]
                srcw = srcw.rearrange("(c p) n -> p c n", p=128)
                g.dma_start(out=slab[:, 8 * (k % 2):8 * (k % 2) + 8, :],
                            in_=srcw).then_inc(ctx.s("ld"), 16)

        @blk.tensor
        def _(t):
            t.wait_ge(ctx.s("tr"), tr_ready)
            for tp in range(8):
                for hs in range(8):
                    k = 8 * tp + hs
                    t.wait_ge(ctx.s("ld"), ld0 + 16 * (k + 2))
                    slab = sb["w1sl"][:, 4096 * (k % 2):4096 * (k % 2) + 4096]
                    for sub in range(2):
                        tt = 2 * tp + sub
                        gidx = 16 * tp + 2 * hs + sub
                        bk = gidx % 4
                        if gidx >= 4:
                            g4 = gidx - 4
                            t.wait_ge(ctx.s("ev"),
                                      evtp0(g4 // 16) + 3 * (g4 % 16) + 1)
                        last = None
                        for dc in range(8):
                            last = t.matmul(pbank(bk),
                                            sb["xnT"][:, 2048 * dc + 128 * tt:2048 * dc + 128 * (tt + 1)],
                                            slab[:, 512 * dc:512 * (dc + 1)],
                                            start=(dc == 0), stop=(dc == 7))
                        last.then_inc(ctx.s("mm"), 1)

        @blk.scalar
        def _(sc):
            for tp in range(8):
                for sub in range(2):
                    tt = 2 * tp + sub
                    w = evtp0(tp) + 45 + 3 * sub
                    if sub == 1:
                        w = max(w, qb_(tp, 0) + 10)
                    elif tp >= 1:
                        w = max(w, qb_(tp - 1, 1) + 10)
                    sc.wait_ge(ctx.s("ev"), w)
                    sc.activation(ps[:, 2048:4096], aF(sub)[:, 0:2048], AF.Square,
                                  accum_out=ss2a[:, tt:tt + 1]).then_inc(ctx.s("sc"), 1)
                    sc.activation(ps[:, 2048:4096], aF(sub)[:, 2048:DFF], AF.Square,
                                  accum_out=ss2b[:, tt:tt + 1]).then_inc(ctx.s("sc"), 1)
                    sc.wait_ge(ctx.s("ev"), qb_(tp, sub) + 2)
                    sc.activation(m2[:, tt:tt + 1], rstd2[:, tt:tt + 1], AF.Sqrt,
                                  bias=np.float32(0.0), scale=1.0).then_inc(ctx.s("sc"), 1)

        @blk.vector
        def _(v):
            for tp in range(8):
                for hs in range(8):
                    for sub in range(2):
                        tt = 2 * tp + sub
                        gidx = 16 * tp + 2 * hs + sub
                        bk = gidx % 4
                        v.wait_ge(ctx.s("mm"), mm0 + gidx + 1)
                        if hs == 0 and tp >= 1:
                            v.wait_ge(ctx.s("st"), st0 + 16 * (2 * tp - 1 + sub))
                        asl = aF(sub)[:, 512 * hs:512 * (hs + 1)]
                        p0 = evtp0(tp) + 3 * (2 * hs + sub)
                        v.tensor_copy(asl, pbank(bk)).then_inc(ctx.s("ev"), 1)
                        v.wait_ge(ctx.s("ev"), p0 + 1)
                        v.scalar_tensor_tensor(asl, asl, 0.0, asl, ALU.max,
                                               ALU.mult).then_inc(ctx.s("ev"), 1)
                        v.wait_ge(ctx.s("ev"), p0 + 2)
                        v.tensor_scalar(asl, asl, sA[:, tt:tt + 1],
                                        sb["l1s"][:, 1:2], ALU.mult,
                                        ALU.add).then_inc(ctx.s("ev"), 1)
                for sub in range(2):
                    tt = 2 * tp + sub
                    q0 = qb_(tp, sub)
                    v.wait_ge(ctx.s("sc"), sc0 + 6 * tp + (2 if sub == 0 else 5))
                    v.scalar_tensor_tensor(rstd2[:, tt:tt + 1], ss2a[:, tt:tt + 1],
                                           1.0, ss2b[:, tt:tt + 1], ALU.mult,
                                           ALU.add).then_inc(ctx.s("ev"), 1)
                    v.wait_ge(ctx.s("ev"), q0 + 1)
                    v.tensor_scalar(rstd2[:, tt:tt + 1], rstd2[:, tt:tt + 1],
                                    1.0 / DFF, 1e-8, ALU.mult,
                                    ALU.add).then_inc(ctx.s("ev"), 1)
                    v.wait_ge(ctx.s("sc"), sc0 + 6 * tp + (3 if sub == 0 else 6))
                    v.reciprocal(rstd2[:, tt:tt + 1], m2[:, tt:tt + 1]).then_inc(ctx.s("ev"), 1)
                    v.wait_ge(ctx.s("ev"), q0 + 3)
                    v.scalar_tensor_tensor(aF(sub), aF(sub), rstd2[:, tt:tt + 1],
                                           n2, ALU.mult, ALU.mult).then_inc(ctx.s("ev"), 1)
                    v.wait_ge(ctx.s("ev"), q0 + 4)
                    v.reduce_max(m2[:, tt:tt + 1], aF(sub), axis=AX.X,
                                 apply_absolute_value=True).then_inc(ctx.s("ev"), 1)
                    v.wait_ge(ctx.s("ev"), q0 + 5)
                    v.tensor_scalar(m2[:, tt:tt + 1], m2[:, tt:tt + 1], 1e-5,
                                    None, ALU.max).then_inc(ctx.s("ev"), 1)
                    v.wait_ge(ctx.s("ev"), q0 + 6)
                    v.scalar_tensor_tensor(rs2[:, tt:tt + 1], m2[:, tt:tt + 1],
                                           1.0 / 127.0, sb["swc"][:, 1:2],
                                           ALU.mult, ALU.mult)
                    v.reciprocal(mcol[:, tt:tt + 1], m2[:, tt:tt + 1]).then_inc(ctx.s("ev"), 1)
                    v.wait_ge(ctx.s("ev"), q0 + 7)
                    v.tensor_scalar(sq2[:, tt:tt + 1], mcol[:, tt:tt + 1], 127.0,
                                    None, ALU.mult).then_inc(ctx.s("ev"), 1)
                    v.wait_ge(ctx.s("ev"), q0 + 8)
                    v.tensor_scalar(i8, aF(sub), sq2[:, tt:tt + 1], None,
                                    ALU.mult).then_inc(ctx.s("ev"), 1)
                    v.wait_ge(ctx.s("ev"), q0 + 9)
                    v.tensor_copy(araw(sub)[:, 0:4096], i8).then_inc(ctx.s("ev"), 1)

        @blk.sync
        def _(s):
            for tp in range(8):
                for sub in range(2):
                    tt = 2 * tp + sub
                    s.wait_ge(ctx.s("ev"), qb_(tp, sub) + 10)
                    s.dma_start(out=dram["stage_hq1"].ap()[128 * tt:128 * (tt + 1), :],
                                in_=araw(sub)[:, 0:4096]).then_inc(ctx.s("st"), 16)

        ctx.add("ld", 16 * 65)
        ctx.add("mm", 128)
        ctx.add("sc", 48)
        ctx.add("st", 16 * NT)
        ctx.add("ev", 68 * 8)

    def ph_l1_mm2(blk):
        ld0, mm0, ev0, tr0 = ctx.v("ld"), ctx.v("mm"), ctx.v("ev"), ctx.v("tr")
        rs2 = sb["stat"][:, 144:160]

        @blk.sync
        def _(s):
            for grp in range(4):
                for fc in range(32):
                    k = 32 * grp + fc
                    if k >= 4:
                        s.wait_ge(ctx.s("mm"), mm0 + k - 3)
                    s.dma_start(out=sb["hq1s"][:, 1024 * (k % 4):1024 * (k % 4) + 512],
                                in_=dram["stage_hq1"].ap()[512 * grp:512 * (grp + 1),
                                                           128 * fc:128 * (fc + 1)],
                                transpose=True).then_inc(ctx.s("tr"), 16)

        @blk.gpsimd
        def _(g):
            if "wq2q" in cc_marks:
                g.wait_ge(ctx.s("cc"), cc_marks["wq2q"])
            for grp in range(4):
                for fc in range(32):
                    k = 32 * grp + fc
                    if k >= 4:
                        g.wait_ge(ctx.s("mm"), mm0 + k - 3)
                    g.dma_start(out=sb["w2sl"][:, 1024 * (k % 4):1024 * (k % 4 + 1)],
                                in_=dram["wq2q"].ap()[128 * fc:128 * (fc + 1), :]).then_inc(ctx.s("ld"), 16)

        @blk.tensor
        def _(t):
            for grp in range(4):
                if grp >= 1:
                    t.wait_ge(ctx.s("ev"), ev0 + 8 * grp)
                for fc in range(32):
                    k = 32 * grp + fc
                    t.wait_ge(ctx.s("tr"), tr0 + 16 * (k + 1))
                    t.wait_ge(ctx.s("ld"), ld0 + 16 * (k + 1))
                    hq = sb["hq1s"][:, 1024 * (k % 4):1024 * (k % 4) + 512]
                    w2_ = sb["w2sl"][:, 1024 * (k % 4):1024 * (k % 4 + 1)]
                    last = None
                    for tt4 in range(4):
                        for ds in range(2):
                            last = t.matmul(pbank(2 * tt4 + ds),
                                            hq[:, 128 * tt4:128 * (tt4 + 1)],
                                            w2_[:, 512 * ds:512 * (ds + 1)],
                                            start=(fc == 0), stop=(fc == 31))
                    last.then_inc(ctx.s("mm"), 1)

        @blk.vector
        def _(v):
            for grp in range(4):
                v.wait_ge(ctx.s("mm"), mm0 + 32 * (grp + 1))
                for tt4 in range(4):
                    tt = 4 * grp + tt4
                    for ds in range(2):
                        dst = sb["stream"][:, 1024 * tt + 512 * ds:1024 * tt + 512 * (ds + 1)]
                        v.scalar_tensor_tensor(dst, pbank(2 * tt4 + ds),
                                               rs2[:, tt:tt + 1], dst,
                                               ALU.mult, ALU.add).then_inc(ctx.s("ev"), 1)

        ctx.add("mm", 128)
        ctx.add("tr", 16 * 128)
        ctx.add("ld", 16 * 128)
        ctx.add("ev", 32)

    def emit_fp4_quant(v, sc_ops, W, pairs):
        nb = W // 16
        ns = len(pairs)
        st = []
        for k, (src_ap, out_ap) in enumerate(pairs):
            st.append(dict(
                src3=src_ap.rearrange("p (b w) -> p b w", w=16),
                out3=out_ap.rearrange("p (b w) -> p b w", w=16),
                qf1=sb["qf1"].bitcast(F32)[:, k * W:(k + 1) * W],
                qf2=sb["qf2"].bitcast(F32)[:, k * W:(k + 1) * W],
                bm=sb["bmx"].bitcast(F32)[:, k * nb:(k + 1) * nb],
                scl=sb["sclq"].bitcast(F32)[:, k * nb:(k + 1) * nb],
                rcp=sb["rcpq"].bitcast(F32)[:, k * nb:(k + 1) * nb],
                i8a=sb["i8a"].bitcast(I8)[:, k * W:(k + 1) * W],
                i8b=sb["i8b"].bitcast(I8)[:, k * W:(k + 1) * W],
                i8c=sb["i8c"].bitcast(I8)[:, k * W:(k + 1) * W],
                mark=[0],
            ))

        def step(fn, wait_prev=True):
            for k in range(ns):
                s = st[k]
                if wait_prev and s["mark"][0]:
                    v.wait_ge(ctx.s("ev"), s["mark"][0])
                fn(s).then_inc(ctx.s("ev"), 1)
                s["mark"][0] = ctx.add("ev")

        step(lambda s: v.reduce_max(s["bm"], s["src3"], axis=AX.X,
                                    apply_absolute_value=True), wait_prev=False)
        step(lambda s: v.tensor_scalar(s["scl"], s["bm"], 1.0 / 6.0, 1e-12,
                                       ALU.mult, ALU.max))
        step(lambda s: v.reciprocal(s["rcp"], s["scl"]))

        def _xn(s):
            rcp3 = s["rcp"].unsqueeze(2).broadcast_to([128, nb, 16])
            return v.tensor_mul(s["qf1"].rearrange("p (b w) -> p b w", w=16),
                                s["src3"], rcp3)
        step(_xn)
        xns = []
        for k in range(ns):
            s = st[k]
            sc_ops.append((s["mark"][0], s["qf1"], s["i8a"], s["i8b"], s["i8c"]))
            s["sc_tgt"] = ctx.add("sc")
            xns.append(s["mark"][0])

        def _c1(s):
            v.wait_ge(ctx.s("sc"), s["sc_tgt"])
            v.tensor_scalar(s["qf1"], s["i8b"], -4.0, 4.0, ALU.max, ALU.min)
            return v.tensor_scalar(s["qf2"], s["i8b"], -2.0, 2.0, ALU.max, ALU.min)
        step(_c1, wait_prev=False)

        def _c2(s):
            v.tensor_sub(s["qf1"], s["qf1"], s["qf2"])
            return v.tensor_scalar(s["qf2"], s["i8c"], -3.0, 3.0, ALU.max, ALU.min)
        step(_c2)

        def _c3(s):
            v.scalar_tensor_tensor(s["qf1"], s["qf2"], 2.0, s["qf1"], ALU.mult, ALU.add)
            return v.tensor_scalar(s["qf2"], s["i8c"], -2.0, 2.0, ALU.max, ALU.min)
        step(_c3)

        def _c4(s):
            v.scalar_tensor_tensor(s["qf1"], s["qf2"], -2.0, s["qf1"], ALU.mult, ALU.add)
            return v.tensor_scalar(s["qf2"], s["i8a"], -4.0, 4.0, ALU.max, ALU.min)
        step(_c4)
        step(lambda s: v.scalar_tensor_tensor(s["qf1"], s["qf2"], 0.5, s["qf1"],
                                              ALU.mult, ALU.add))

        def _out(s):
            scl3 = s["scl"].unsqueeze(2).broadcast_to([128, nb, 16])
            return v.tensor_mul(s["out3"], s["qf1"].rearrange("p (b w) -> p b w", w=16),
                                scl3)
        step(_out)
        return [(xns[k], st[k]["mark"][0]) for k in range(ns)]

    TIE_EPS = 1.0 - 2.0 ** -11

    def emit_fp4_scalar(sc, sc_ops):
        for (xn_ev, qf1, i8a, i8b, i8c) in sc_ops:
            sc.wait_ge(ctx.s("ev"), xn_ev)
            sc.mul(i8a, qf1, 2.0 * TIE_EPS)
            sc.mul(i8b, qf1, 1.0 * TIE_EPS)
            sc.mul(i8c, qf1, 0.5 * TIE_EPS).then_inc(ctx.s("sc"), 1)

    def ph_l3_prep(blk, wname, n_tiles, Wl, _unused=0):
        ld0, st0 = ctx.v("ld"), ctx.v("st")
        sc_ops = []
        ev_marks = []
        wst = dram["wstage"].ap()

        npair = n_tiles // 2

        @blk.vector
        def _(v):
            for tp in range(npair):
                for k in (0, 1):
                    v.wait_ge(ctx.s("ld"), ld0 + 16 * (2 * tp + k + 1))
                    if tp >= 1:
                        v.wait_ge(ctx.s("st"), st0 + 16 * (2 * tp - 1 + k))
                wf = sb["wf32"].bitcast(F32)
                marks = emit_fp4_quant(v, sc_ops, Wl,
                                       [(wf[:, 0:Wl], sb["qbf"][:, 0:Wl]),
                                        (wf[:, Wl:2 * Wl], sb["qbf"][:, Wl:2 * Wl])])
                ev_marks.append(marks)

        @blk.scalar
        def _(sc):
            emit_fp4_scalar(sc, sc_ops)

        @blk.gpsimd
        def _(g):
            for tp in range(npair):
                for k in (0, 1):
                    t = 2 * tp + k
                    if tp >= 1:
                        g.wait_ge(ctx.s("ev"), ev_marks[tp - 1][k][0])
                    g.dma_start(out=sb["wf32"].bitcast(F32)[:, Wl * k:Wl * (k + 1)],
                                in_=dram[wname].ap()[128 * t:128 * (t + 1), :]).then_inc(ctx.s("ld"), 16)
                for k in (0, 1):
                    t = 2 * tp + k
                    g.wait_ge(ctx.s("ev"), ev_marks[tp][k][1])
                    dst = wst[128 * Wl * t: 128 * Wl * (t + 1)]
                    g.dma_start(out=dst.rearrange("(p n) -> p n", p=128),
                                in_=sb["qbf"][:, Wl * k:Wl * (k + 1)]).then_inc(ctx.s("st"), 16)

        ctx.add("ld", 16 * n_tiles)
        ctx.add("st", 16 * n_tiles)

    def ph_l3_prepT(blk, part, R, Wl):
        tr0, st0 = ctx.v("tr"), ctx.v("st")
        wst = dram["wstage"].ap()[0:R * Wl].rearrange("(r c) -> r c", c=Wl)
        chunks = []
        c0 = 0
        while c0 < Wl:
            chunks.append((c0, min(128, Wl - c0)))
            c0 += 128

        @blk.sync
        def _(s):
            s.wait_ge(ctx.s("st"), st0)
            for i, (c0, w) in enumerate(chunks):
                if i >= 1:
                    s.wait_ge(ctx.s("st"), st0 + 16 * i)
                s.dma_start(out=sb["trs"][0:w, 0:R], in_=wst[:, c0:c0 + w],
                            transpose=True).then_inc(ctx.s("tr"), 16)

        @blk.gpsimd
        def _(g):
            for i, (c0, w) in enumerate(chunks):
                g.wait_ge(ctx.s("tr"), tr0 + 16 * (i + 1))
                g.dma_start(out=dram[part].ap()[c0:c0 + w, :],
                            in_=sb["trs"][0:w, 0:R]).then_inc(ctx.s("st"), 16)
            g.wait_ge(ctx.s("st"), st0 + 16 * len(chunks))

        ctx.add("tr", 16 * len(chunks))
        ctx.add("st", 16 * len(chunks))

    cc_marks = {}

    def ph_l3_ag(blk, part, dstT, prev=None):
        st_all = ctx.v("st")

        @blk.gpsimd
        def _(g):
            g.wait_ge(ctx.s("st"), st_all)
            if prev is not None:
                g.wait_ge(ctx.s("cc"), cc_marks[prev])
            with allow_ici_all_to_all():
                g.collective_compute(
                    "AllGather", ALU.bypass,
                    replica_groups=[[0, 1, 2, 3], [4, 5, 6, 7]],
                    ins=[dram[part].ap().opt()],
                    outs=[dram[dstT].ap().opt()],
                ).then_inc(ctx.s("cc"), 1)
            cc_marks[dstT] = ctx.add("cc")
            g.wait_ge(ctx.s("cc"), cc_marks[dstT])

    def ph_l3_quantx(blk):
        ld0, st0 = ctx.v("ld"), ctx.v("st")
        sc_ops = []
        ev_marks = []

        @blk.vector
        def _(v):
            for tp in range(NT // 2):
                for k in (0, 1):
                    v.wait_ge(ctx.s("ld"), ld0 + 16 * (2 * tp + k + 1))
                    if tp >= 1:
                        v.wait_ge(ctx.s("st"), st0 + 16 * (2 * tp - 1 + k))
                wf = sb["wf32"].bitcast(F32)
                marks = emit_fp4_quant(v, sc_ops, 1024,
                                       [(wf[:, 0:1024], sb["qbf"][:, 0:1024]),
                                        (wf[:, 1024:2048], sb["qbf"][:, 1024:2048])])
                ev_marks.append(marks)

        @blk.scalar
        def _(sc):
            emit_fp4_scalar(sc, sc_ops)

        @blk.gpsimd
        def _(g):
            for tp in range(NT // 2):
                for k in (0, 1):
                    t = 2 * tp + k
                    if tp >= 1:
                        g.wait_ge(ctx.s("ev"), ev_marks[tp - 1][k][0])
                    g.dma_start(out=sb["wf32"].bitcast(F32)[:, 1024 * k:1024 * (k + 1)],
                                in_=dram["stage_xn32"].ap()[128 * t:128 * (t + 1), :]).then_inc(ctx.s("ld"), 16)
                for k in (0, 1):
                    t = 2 * tp + k
                    g.wait_ge(ctx.s("ev"), ev_marks[tp][k][1])
                    g.dma_start(out=dram["stage_xn"].ap()[128 * t:128 * (t + 1), :],
                                in_=sb["qbf"][:, 1024 * k:1024 * (k + 1)]).then_inc(ctx.s("st"), 16)

        @blk.sync
        def _(s):
            s.wait_ge(ctx.s("st"), st0 + 16 * NT)
            for dc in range(8):
                s.dma_start(out=sb["xnT"][:, 2048 * dc:2048 * (dc + 1)],
                            in_=dram["stage_xn"].ap()[:, 128 * dc:128 * (dc + 1)],
                            transpose=True).then_inc(ctx.s("tr"), 16)

        ctx.add("ld", 16 * NT)
        ctx.add("st", 16 * NT)
        ctx.add("tr", 16 * 8)

    def ph_l3_gateup(blk):
        ld0, mm0, ev0, sc0, st0 = (ctx.v("ld"), ctx.v("mm"), ctx.v("ev"),
                                   ctx.v("sc"), ctx.v("st"))
        tr_ready = ctx.v("tr")
        GS = [(i * 512, 512) for i in range(5)] + [(2560, 256)]

        @blk.sync
        def _(s):
            if "wuqT" in cc_marks:
                s.wait_ge(ctx.s("cc"), cc_marks["wuqT"])
            n = 0
            for gi, (goff, gw) in enumerate(GS):
                for src_t, dst in (("wgqT", "wgs"), ("wuqT", "wus")):
                    for dc in range(8):
                        if gi >= 1:
                            s.wait_ge(ctx.s("mm"), mm0 + 16 * gi)
                        s.dma_start(out=sb[dst][:, 512 * dc:512 * dc + gw],
                                    in_=dram[src_t].ap()[128 * dc:128 * (dc + 1), goff:goff + gw]).then_inc(ctx.s("ld"), 16)
                        n += 1
        NLD = 6 * 16
        ctx.add("ld", 16 * NLD)

        @blk.tensor
        def _(t):
            t.wait_ge(ctx.s("tr"), tr_ready)
            for gi, (goff, gw) in enumerate(GS):
                t.wait_ge(ctx.s("ld"), ld0 + 16 * 16 * (gi + 1))
                for tt in range(NT):
                    gidx = gi * NT + tt
                    bkG = 2 * (tt % 2)
                    bkU = bkG + 1
                    if gidx >= 2:
                        t.wait_ge(ctx.s("ev"), ev0 + gidx - 1)
                    for dc in range(8):
                        t.matmul(pbank(bkG, 0, 128, gw),
                                 sb["xnT"][:, 2048 * dc + 128 * tt:2048 * dc + 128 * (tt + 1)],
                                 sb["wgs"][:, 512 * dc:512 * dc + gw],
                                 start=(dc == 0), stop=(dc == 7))
                    last = None
                    for dc in range(8):
                        last = t.matmul(pbank(bkU, 0, 128, gw),
                                        sb["xnT"][:, 2048 * dc + 128 * tt:2048 * dc + 128 * (tt + 1)],
                                        sb["wus"][:, 512 * dc:512 * dc + gw],
                                        start=(dc == 0), stop=(dc == 7))
                    last.then_inc(ctx.s("mm"), 1)

        @blk.scalar
        def _(sc):
            for gi, (goff, gw) in enumerate(GS):
                for tt in range(NT):
                    gidx = gi * NT + tt
                    sc.wait_ge(ctx.s("mm"), mm0 + gidx + 1)
                    if gidx >= 1:
                        sc.wait_ge(ctx.s("ev"), ev0 + gidx)
                    bkG = 2 * (tt % 2)
                    sc.activation(sb["sg1"].bitcast(F32)[:, 0:gw], pbank(bkG, 0, 128, gw),
                                  AF.Silu, bias=0.0, scale=1.0).then_inc(ctx.s("sc"), 1)

        @blk.vector
        def _(v):
            for gi, (goff, gw) in enumerate(GS):
                for tt in range(NT):
                    gidx = gi * NT + tt
                    v.wait_ge(ctx.s("sc"), sc0 + gidx + 1)
                    if gidx >= 1:
                        v.wait_ge(ctx.s("st"), st0 + 16 * gidx)
                    bkU = 2 * (tt % 2) + 1
                    v.tensor_mul(sb["sg2"].bitcast(F32)[:, 0:gw],
                                 sb["sg1"].bitcast(F32)[:, 0:gw],
                                 pbank(bkU, 0, 128, gw)).then_inc(ctx.s("ev"), 1)

        @blk.gpsimd
        def _(g):
            for gi, (goff, gw) in enumerate(GS):
                for tt in range(NT):
                    gidx = gi * NT + tt
                    g.wait_ge(ctx.s("ev"), ev0 + gidx + 1)
                    g.dma_start(out=dram["stage_h"].ap()[128 * tt:128 * (tt + 1), goff:goff + gw],
                                in_=sb["sg2"].bitcast(F32)[:, 0:gw]).then_inc(ctx.s("st"), 16)

        NG = 6 * NT
        ctx.add("mm", NG)
        ctx.add("sc", NG)
        ctx.add("ev", NG)
        ctx.add("st", 16 * NG)

    def ph_l3_quanth(blk):
        ld0, st0 = ctx.v("ld"), ctx.v("st")
        sc_ops = []
        ev_marks = []

        @blk.vector
        def _(v):
            for t in range(NT):
                v.wait_ge(ctx.s("ld"), ld0 + 16 * (t + 1))
                if t >= 1:
                    v.wait_ge(ctx.s("st"), st0 + 16 * t)
                wf = sb["wf32"].bitcast(F32)
                marks = emit_fp4_quant(v, sc_ops, 1408,
                                       [(wf[:, 0:1408], sb["qbf"][:, 0:1408]),
                                        (wf[:, 1408:2816], sb["qbf"][:, 1408:2816])])
                ev_marks.append(marks)

        @blk.scalar
        def _(sc):
            emit_fp4_scalar(sc, sc_ops)

        @blk.gpsimd
        def _(g):
            for t in range(NT):
                if t >= 1:
                    g.wait_ge(ctx.s("ev"), max(ev_marks[t - 1][0][0], ev_marks[t - 1][1][0]))
                g.dma_start(out=sb["wf32"].bitcast(F32)[:, 0:2816],
                            in_=dram["stage_h"].ap()[128 * t:128 * (t + 1), :]).then_inc(ctx.s("ld"), 16)
                g.wait_ge(ctx.s("ev"), max(ev_marks[t][0][1], ev_marks[t][1][1]))
                g.dma_start(out=dram["stage_hq"].ap()[128 * t:128 * (t + 1), :],
                            in_=sb["qbf"][:, 0:2816]).then_inc(ctx.s("st"), 16)

        ctx.add("ld", 16 * NT)
        ctx.add("st", 16 * NT)

    def ph_l3_down(blk):
        ld0, mm0, ev0, tr0 = ctx.v("ld"), ctx.v("mm"), ctx.v("ev"), ctx.v("tr")

        @blk.sync
        def _(s):
            for grp in range(4):
                for dd in range(22):
                    k = grp * 22 + dd
                    if k >= 2:
                        s.wait_ge(ctx.s("mm"), mm0 + k - 1)
                    s.dma_start(out=sb["hqs"][:, 1024 * (dd % 2):1024 * (dd % 2) + 512],
                                in_=dram["stage_hq"].ap()[512 * grp:512 * (grp + 1),
                                                          128 * dd:128 * (dd + 1)],
                                transpose=True).then_inc(ctx.s("tr"), 16)

        @blk.gpsimd
        def _(g):
            if "wdqT" in cc_marks:
                g.wait_ge(ctx.s("cc"), cc_marks["wdqT"])
            for grp in range(4):
                for dd in range(22):
                    k = grp * 22 + dd
                    if k >= 2:
                        g.wait_ge(ctx.s("mm"), mm0 + k - 1)
                    g.dma_start(out=sb["wds"][:, 1024 * (dd % 2):1024 * (dd % 2 + 1)],
                                in_=dram["wdqT"].ap()[128 * dd:128 * (dd + 1), :]).then_inc(ctx.s("ld"), 16)

        @blk.tensor
        def _(t):
            for grp in range(4):
                if grp >= 1:
                    t.wait_ge(ctx.s("ev"), ev0 + 8 * grp)
                for dd in range(22):
                    k = grp * 22 + dd
                    t.wait_ge(ctx.s("tr"), tr0 + 16 * (k + 1))
                    t.wait_ge(ctx.s("ld"), ld0 + 16 * (k + 1))
                    hq = sb["hqs"][:, 1024 * (dd % 2):1024 * (dd % 2) + 512]
                    wd_ = sb["wds"][:, 1024 * (dd % 2):1024 * (dd % 2 + 1)]
                    last = None
                    for tt4 in range(4):
                        for ds in range(2):
                            bk = 2 * tt4 + ds
                            last = t.matmul(pbank(bk),
                                            hq[:, 128 * tt4:128 * (tt4 + 1)],
                                            wd_[:, 512 * ds:512 * (ds + 1)],
                                            start=(dd == 0), stop=(dd == 21))
                    last.then_inc(ctx.s("mm"), 1)

        @blk.vector
        def _(v):
            for grp in range(4):
                v.wait_ge(ctx.s("mm"), mm0 + 22 * (grp + 1))
                for tt4 in range(4):
                    for ds in range(2):
                        bk = 2 * tt4 + ds
                        tt = 4 * grp + tt4
                        dst = sb["stream"][:, 1024 * tt + 512 * ds:1024 * tt + 512 * (ds + 1)]
                        v.tensor_add(dst, dst, pbank(bk)).then_inc(ctx.s("ev"), 1)

        ctx.add("mm", 88)
        ctx.add("tr", 16 * 88)
        ctx.add("ld", 16 * 88)
        ctx.add("ev", 32)

    def ph_dump(blk, name, nelem):
        st0 = ctx.v("st")

        @blk.sync
        def _(s):
            s.wait_ge(ctx.s("st"), st0)
            src_ap = dram[name].ap()
            flat = src_ap
            if len(src_ap.shape) > 1:
                flat = src_ap.rearrange("a b -> (a b)") if len(src_ap.shape) == 2 else src_ap
            s.dma_start(out=dbg_out.ap().rearrange("a b -> (a b)")[0:nelem],
                        in_=flat[0:nelem] if len(flat.shape) == 1 else flat).then_inc(ctx.s("st"), 16)
            s.wait_ge(ctx.s("st"), st0 + 16)
        ctx.add("st", 16)

    def ph_dump_sbuf(blk):
        st0 = ctx.v("st")

        @blk.gpsimd
        def _(g):
            flat = dbg_out.ap().rearrange("a b -> (a b)")
            o = [2621440]

            def dump(t, n):
                g.dma_start(out=flat[o[0]:o[0] + 128 * n].rearrange("(p n) -> p n", p=128),
                            in_=t).then_inc(ctx.s("st"), 16)
                o[0] += 128 * n
            dump(sb["qtp"][:, :], 8192)
            dump(sb["ktp"][:, :], 8192)
            dump(sb["vt"][:, :], 8320)
            dump(sb["pt"][:, :], 2048)
            g.wait_ge(ctx.s("st"), st0 + 64)
        ctx.add("st", 16 * 4)

    def ph_dump_wx(blk):
        st0 = ctx.v("st")

        @blk.gpsimd
        def _(g):
            flat = dbg_out.ap().rearrange("a b -> (a b)")
            g.dma_start(out=flat[3948544:3948544 + 128 * 8192].rearrange("(p n) -> p n", p=128),
                        in_=sb["wa"][:, :]).then_inc(ctx.s("st"), 16)
            g.dma_start(out=flat[4997120:4997120 + 128 * 8192].rearrange("(p n) -> p n", p=128),
                        in_=sb["xnT"][:, 0:8192]).then_inc(ctx.s("st"), 16)
            g.wait_ge(ctx.s("st"), st0 + 32)
        ctx.add("st", 32)

    def ph_dump_flat(blk, name, nelem, off):
        st0 = ctx.v("st")

        @blk.sync
        def _(s):
            s.wait_ge(ctx.s("st"), st0)
            s.wait_ge(ctx.s("cc"), ctx.v("cc"))
            flat = dbg_out.ap().rearrange("a b -> (a b)")
            srcf = dram[name].ap().rearrange("a b -> (a b)")
            s.dma_start(out=flat[off:off + nelem], in_=srcf[0:nelem]).then_inc(ctx.s("st"), 16)
            s.wait_ge(ctx.s("st"), st0 + 16)
        ctx.add("st", 16)

    def ph_dump_stat(blk):
        st0 = ctx.v("st")

        @blk.gpsimd
        def _(g):
            g.dma_start(out=dbg_out.ap()[7, 0:128 * 160].rearrange("(p n) -> p n", p=128),
                        in_=sb["stat"][:, :]).then_inc(ctx.s("st"), 16)
            g.wait_ge(ctx.s("st"), st0 + 16)
        ctx.add("st", 16)

    run_phase(ph_init)
    if "ln" in phases:
        run_phase(ph_ln)
        run_phase(ph_dump, "stage_xn", TPC * D)
        run_phase(ph_dump_stat)
    if "qkv" in phases:
        run_phase(ph_ln)
        run_phase(ph_l0_qkv)
        run_phase(ph_dump, "a2a_qkv_in", 8 * BLK_QKV)
    if "attn" in phases:
        run_phase(ph_ln)
        run_phase(ph_l0_qkv)
        run_phase(ph_dump_flat, "a2a_qkv_in", 262144, 2097152)
        run_phase(ph_dump_wx)
        run_phase(ph_a2a, "a2a_qkv_in", "a2a_qkv_out")
        run_phase(ph_dump_flat, "a2a_qkv_out", 262144, 2359296)
        run_phase(ph_l0_attn_batch, 0, C0_L0)
        run_phase(ph_dump_sbuf)
        run_phase(ph_l0_attn_batch, 1, C0_L0)
        run_phase(ph_dump, "a2a_ao_in", 8 * SEG_Q)
    if "l0a" in phases:
        run_phase(ph_ln)
        run_phase(ph_l0_qkv)
        run_phase(ph_a2a, "a2a_qkv_in", "a2a_qkv_out")
        run_phase(ph_l0_attn_batch, 0, C0_L0)
        run_phase(ph_l0_attn_batch, 1, C0_L0)
        run_phase(ph_a2a, "a2a_ao_in", "a2a_ao_out")
        run_phase(ph_l0_wo)
    if "l0f" in phases:
        run_phase(ph_ln)
        run_phase(ph_l0_ffn)
    if "l1a" in phases:
        run_phase(ph_ln)
        run_phase(ph_l1_qkv)
        run_phase(ph_a2a, "a2a_qkv_in", "a2a_qkv_out")
        run_phase(ph_l1_rope, 0)
        run_phase(ph_l1_attn, 0)
        run_phase(ph_l1_rope, 1)
        run_phase(ph_l1_attn, 1)
        run_phase(ph_a2a, "a2a_ao_in", "a2a_ao_out")
        run_phase(ph_l0_wo, "w_l1_woT", "w_l1_bo")
    if "l1fdbg" in phases:
        run_phase(ph_l1_sum)
        run_phase(ph_l1_ar)
        run_phase(ph_l1_tern, "w_l1_w1T_part", 2, DFF, "w1qpart", 0, 0)
        run_phase(ph_l1_tern, "w_l1_w2T_part", 8, D, "w2qpart", 1, 1)
        run_phase(ph_l1_agq, "w1qpart", "wq1q")
        run_phase(ph_l1_agq, "w2qpart", "wq2q", prev="wq1q")
        run_phase(ph_ln, "32")
        run_phase(ph_l1_quantx)
        run_phase(ph_l1_mm1)
        run_phase(ph_l1_mm2)
        run_phase(ph_dump_flat, "wq1q", 1024 * 4096, 262144)
    if "l1f" in phases:
        run_phase(ph_l1_sum)
        run_phase(ph_l1_ar)
        run_phase(ph_l1_tern, "w_l1_w1T_part", 2, DFF, "w1qpart", 0, 0)
        run_phase(ph_l1_tern, "w_l1_w2T_part", 8, D, "w2qpart", 1, 1)
        run_phase(ph_l1_agq, "w1qpart", "wq1q")
        run_phase(ph_l1_agq, "w2qpart", "wq2q", prev="wq1q")
        run_phase(ph_ln, "32")
        run_phase(ph_l1_quantx)
        run_phase(ph_l1_mm1)
        run_phase(ph_l1_mm2)
    if "l3p" in phases:
        run_phase(ph_l3_prep, "w_l3_wg_part", 22, 256, 0)
        run_phase(ph_l3_prepT, "wtpartA", DI3, 256)
        run_phase(ph_l3_ag, "wtpartA", "wgqT")
        run_phase(ph_l3_prep, "w_l3_wu_part", 22, 256, 0)
        run_phase(ph_l3_prepT, "wtpartC", DI3, 256)
        run_phase(ph_l3_ag, "wtpartC", "wuqT", prev="wgqT")
        run_phase(ph_l3_prep, "w_l3_wd_part", 8, 768, 0)
        run_phase(ph_l3_prepT, "wtpartB", D, 768)
        run_phase(ph_l3_ag, "wtpartB", "wdqT", prev="wuqT")
    if "l3f" in phases:
        run_phase(ph_ln, "32")
        run_phase(ph_l3_quantx)
        run_phase(ph_l3_gateup)
        run_phase(ph_l3_quanth)
        run_phase(ph_l3_down)
    if "l3f1" in phases:
        run_phase(ph_ln, "32")
        run_phase(ph_l3_quantx)
        run_phase(ph_dump, "stage_xn", TPC * D)
    if "l3f3" in phases:
        run_phase(ph_ln, "32")
        run_phase(ph_l3_quantx)
        run_phase(ph_l3_gateup)
        run_phase(ph_l3_quanth)
    if "l3a" in phases:
        run_phase(ph_ln)
        run_phase(ph_l0_qkv, "w_l3_wqT", "w_l3_wkT", "w_l3_wvT")
        run_phase(ph_a2a, "a2a_qkv_in", "a2a_qkv_out")
        run_phase(ph_rope, 0)
        run_phase(ph_l0_attn_batch, 0, C0_L3, False)
        run_phase(ph_rope, 1)
        run_phase(ph_l0_attn_batch, 1, C0_L3, False)
        run_phase(ph_a2a, "a2a_ao_in", "a2a_ao_out")
        run_phase(ph_l0_wo, "w_l3_woT", "w_l3_bo")
    if "l2a" in phases:
        run_phase(ph_ln)
        run_phase(ph_l2_comb, "w_l2_wq", "w_l2_bqc", 0)
        run_phase(ph_l2_qf)
        run_phase(ph_l2_comb, "w_l2_wk", "w_l2_bkc", 1024)
        run_phase(ph_l2_kv)
        run_phase(ph_l2_ar)
        run_phase(ph_l2_out)
        run_phase(ph_l2_wo)
    if "l2f" in phases:
        run_phase(ph_ln)
        run_phase(ph_l2_ffn)
    run_phase(ph_store)

    es.__exit__(None, None, None)
    return nc


LAST_RESULTS = None


def kernel(**inp):
    global LAST_RESULTS
    in_maps = _host_prep(inp)
    nc = build_program(ALL_PHASES if PHASES is None else PHASES)
    res = run_bass_kernel_spmd(nc, in_maps, core_ids=list(range(N_CORES)))
    LAST_RESULTS = res.results
    out = np.zeros((B, S, D), np.float32)
    for j in range(N_CORES):
        b, r = j // 4, (j % 4) * TPC
        out[b, r:r + TPC, :] = res.results[j]["y"]
    return out


# version 11
# speedup vs baseline: 1.1139x; 1.1139x over previous
import contextlib
import numpy as np
import ml_dtypes

import concourse.bass as bass
import concourse.mybir as mybir
from concourse.bass_utils import run_bass_kernel_spmd
from concourse.aot_env import allow_ici_all_to_all

F32 = mybir.dt.float32
BF16 = mybir.dt.bfloat16
I8 = mybir.dt.int8
AF = mybir.ActivationFunctionType
ALU = mybir.AluOpType
AX = mybir.AxisListType

B, S, D = 2, 8192, 1024
TPC = 2048
NT = TPC // 128
NH, HD = 16, 64
DFF = 4096
RANK = 256
DI3 = 2816
LN_EPS = 1e-5
C0_L0 = 6.0
C0_L3 = 20.0

PHASES = None
ALL_PHASES = ["l0a", "l0f", "l1a", "l1f", "l2a", "l2f", "l3a", "l3p", "l3f"]
N_CORES = 8


def bf16(x):
    return np.asarray(x, dtype=ml_dtypes.bfloat16)



def _host_constants():
    c = {}
    c["c_ident"] = bf16(np.eye(128, dtype=np.float32))
    c["c_onesb"] = bf16(np.ones((1, 2048), np.float32))
    c["c_onesf"] = np.ones((33, 512), np.float32)
    mfull = np.zeros((4, 128, 512), np.float32)
    for r in range(4):
        kpos = 128 * r + np.arange(128)[:, None]
        qpos = np.arange(512)[None, :]
        mfull[r] = np.where(kpos > qpos, -1e9, 0.0)
    lo = np.concatenate([mfull[:, 0:64], mfull[:, 0:64]], axis=1)
    hi = np.concatenate([mfull[:, 64:128], mfull[:, 64:128]], axis=1)
    c["c_cmask_lo"] = bf16(np.concatenate(list(lo), axis=1))
    c["c_cmask_hi"] = bf16(np.concatenate(list(hi), axis=1))
    j1 = np.zeros((64, 128), np.float32)
    j1[np.arange(64), np.arange(64)] = 1.0
    j2 = np.zeros((64, 128), np.float32)
    j2[np.arange(64), 64 + np.arange(64)] = 1.0
    c["c_jlo"] = bf16(np.concatenate([j1, j1], axis=0))
    c["c_jhi"] = bf16(np.concatenate([j2, j2], axis=0))
    inv = 1.0 / (10000.0 ** (np.arange(0, HD, 2, dtype=np.float64) / HD))
    freqs = np.outer(np.arange(S, dtype=np.float64), inv)
    emb = np.concatenate([freqs, freqs], axis=-1)
    cosf = np.cos(emb).T
    sinf = np.sin(emb).T
    c["c_cosd"] = bf16(np.concatenate([cosf, cosf], axis=0))
    c["c_sind"] = bf16(np.concatenate([sinf, sinf], axis=0))
    rotm = np.zeros((128, 128), np.float32)
    for b0 in (0, 64):
        for j in range(32):
            rotm[b0 + 32 + j, b0 + j] = -1.0
            rotm[b0 + j, b0 + 32 + j] = 1.0
    c["c_rotm"] = bf16(rotm)
    return c


def _host_prep(inp):
    T = lambda a: np.ascontiguousarray(np.asarray(a).T)
    A = lambda a: np.ascontiguousarray(np.asarray(a))
    base = {}
    base["w_l0_wq"] = A(inp["L0_attn_Wq"])
    base["w_l0_wk"] = A(inp["L0_attn_Wk"])
    base["w_l0_wv"] = A(inp["L0_attn_Wv"])
    base["w_l0_wo"] = A(inp["L0_attn_Wo"])
    base["w_l0_bo"] = A(inp["L0_attn_bo"]).reshape(1, D)
    base["w_l0_u1"] = A(inp["L0_ffn_U1"])
    base["w_l0_v1"] = A(inp["L0_ffn_V1"])
    base["w_l0_b1"] = A(inp["L0_ffn_b1"]).reshape(1, DFF)
    base["w_l0_u2"] = A(inp["L0_ffn_U2"])
    base["w_l0_v2"] = A(inp["L0_ffn_V2"])
    base["w_l0_b2"] = A(inp["L0_ffn_b2"]).reshape(1, D)
    base["w_l2_wq"] = A(inp["L2_attn_Wq"])
    base["w_l2_wk"] = A(inp["L2_attn_Wk"])
    base["w_l2_wvT"] = T(inp["L2_attn_Wv"])
    base["w_l2_woT"] = T(inp["L2_attn_Wo"])

    def _bcol(b):
        arr = np.zeros((128, 8), np.float32)
        b = np.asarray(b)
        for h in range(16):
            arr[64 * (h % 2):64 * (h % 2) + 64, h // 2] = b[64 * h:64 * (h + 1)]
        return arr
    base["w_l2_bqc"] = _bcol(inp["L2_attn_bq"])
    base["w_l2_bkc"] = _bcol(inp["L2_attn_bk"])
    base["w_l2_bv"] = A(inp["L2_attn_bv"]).reshape(1, D)
    base["w_l2_bo"] = A(inp["L2_attn_bo"]).reshape(1, D)
    pt_ = np.ascontiguousarray(np.asarray(inp["L2_attn_P"]).T)
    base["w_l2_pT"] = np.concatenate([pt_, pt_], axis=0).copy()
    base["w_l2_w1"] = A(inp["L2_ffn_w1"])
    base["w_l2_b1"] = A(inp["L2_ffn_b1"]).reshape(1, DFF)
    base["w_l2_w2"] = A(inp["L2_ffn_w2"])
    base["w_l2_b2"] = A(inp["L2_ffn_b2"]).reshape(1, D)
    base["w_l3_wqT"] = T(inp["L3_attn_Wq"])
    base["w_l3_wkT"] = T(inp["L3_attn_Wk"])
    base["w_l3_wvT"] = T(inp["L3_attn_Wv"])
    base["w_l3_woT"] = T(inp["L3_attn_Wo"])
    base["w_l3_bo"] = np.zeros((1, D), np.float32)
    wg_ = np.asarray(inp["L3_ffn_w_gate"])
    wu_ = np.asarray(inp["L3_ffn_w_up"])
    wd_ = np.asarray(inp["L3_ffn_w_down"])
    base["w_l1_wqT"] = T(inp["L1_attn_Wq"])
    base["w_l1_wkT"] = T(inp["L1_attn_Wk"])
    base["w_l1_wvT"] = T(inp["L1_attn_Wv"])
    base["w_l1_woT"] = T(inp["L1_attn_Wo"])
    base["w_l1_bo"] = np.zeros((1, D), np.float32)
    w1T_ = np.ascontiguousarray(np.asarray(inp["L1_ffn_w1"]).T)
    w2T_ = np.ascontiguousarray(np.asarray(inp["L1_ffn_w2"]).T)
    base["c_n1rep"] = np.broadcast_to(np.asarray(inp["L1_ffn_norm1_w"], np.float32), (128, D)).copy()
    base["c_n2rep"] = np.broadcast_to(np.asarray(inp["L1_ffn_norm2_w"], np.float32), (128, DFF)).copy()
    sc_ = np.zeros((128, 2), np.float32)
    sc_[:, 0] = np.float32(np.asarray(inp["L1_ffn_act_scale"]).reshape(-1)[0])
    sc_[:, 1] = np.float32(np.asarray(inp["L1_ffn_act_bias"]).reshape(-1)[0])
    base["c_l1s"] = sc_
    base.update(_host_constants())
    in_maps = []
    for j in range(N_CORES):
        m = dict(base)
        b, r = j // 4, (j % 4) * TPC
        m["x"] = A(np.asarray(inp["L0_attn_x"])[b, r:r + TPC, :])
        rk = j % 4
        m["w_l3_wg_part"] = A(wg_[:, 256 * rk:256 * (rk + 1)])
        m["w_l1_w1T_part"] = A(w1T_[256 * rk:256 * (rk + 1), :])
        m["w_l1_w2T_part"] = A(w2T_[1024 * rk:1024 * (rk + 1), :])
        m["w_l3_wu_part"] = A(wu_[:, 256 * rk:256 * (rk + 1)])
        if rk < 3:
            m["w_l3_wd_part"] = A(wd_[:, 768 * rk:768 * (rk + 1)])
        else:
            pad = np.zeros((D, 768), wd_.dtype)
            pad[:, 0:512] = wd_[:, 2304:2816]
            m["w_l3_wd_part"] = pad
        t_ = np.arange(128, dtype=np.float64)
        maskT = np.zeros((128, 256), np.float32)
        qdec = np.zeros((128, 128), np.float32)
        kdec = np.zeros((128, 2), np.float32)
        bdec = np.zeros((128, 2), np.float32)
        for hh in range(2):
            h = 2 * j + hh
            sl = 2.0 ** (-8.0 / 16 * (h + 1))
            mT = np.where(t_[None, :] >= t_[:, None],
                          np.exp(-sl * (t_[None, :] - t_[:, None])), 0.0)
            maskT[:, 128 * hh:128 * (hh + 1)] = mT
            qdec[64 * hh:64 * (hh + 1), :] = np.exp(-sl * (t_ + 1.0))[None, :]
            kdec[:, hh] = np.exp(-sl * (127.0 - t_))
            bdec[:, hh] = np.exp(-sl * 128.0)
        m["c_l1maskT"] = bf16(maskT)
        m["c_l1qdec"] = bf16(qdec)
        m["c_l1kdec"] = kdec.astype(np.float32)
        m["c_l1bdec"] = bdec.astype(np.float32)
        in_maps.append(m)
    return in_maps



class Ctx:
    def __init__(self, nc, es):
        self.nc = nc
        self.cnt = {}
        self.sems = {}
        for name in ["ld", "mm", "pv", "ev", "sc", "st", "tr", "cc"]:
            self.sems[name] = es.enter_context(nc.semaphore("s_" + name))

    def s(self, name):
        return self.sems[name]

    def add(self, name, n=1):
        self.cnt[name] = self.cnt.get(name, 0) + n
        return self.cnt[name]

    def v(self, name):
        return self.cnt.get(name, 0)


def build_program(phases):
    nc = bass.Bass("TRN2", target_bir_lowering=False)
    es = contextlib.ExitStack()
    es.__enter__()
    ctx = Ctx(nc, es)
    E = es.enter_context

    dram = {}

    def din(name, shape, dtype=F32):
        dram[name] = nc.dram_tensor(name, list(shape), dtype, kind="ExternalInput")

    def dint(name, shape, dtype=BF16):
        dram[name] = nc.dram_tensor(name, list(shape), dtype)

    din("x", [TPC, D])
    din("w_l0_wq", [D, D]); din("w_l0_wk", [D, D]); din("w_l0_wv", [D, D])
    din("w_l0_wo", [D, D]); din("w_l0_bo", [1, D])
    din("w_l0_u1", [D, RANK]); din("w_l0_v1", [RANK, DFF]); din("w_l0_b1", [1, DFF])
    din("w_l0_u2", [DFF, RANK]); din("w_l0_v2", [RANK, D]); din("w_l0_b2", [1, D])
    din("c_ident", [128, 128], BF16)
    din("c_onesb", [1, 2048], BF16)
    din("c_onesf", [33, 512], F32)
    din("c_cmask_lo", [128, 4 * 512], BF16)
    din("c_cmask_hi", [128, 4 * 512], BF16)
    din("c_jlo", [128, 128], BF16)
    din("c_jhi", [128, 128], BF16)
    din("c_cosd", [128, S], BF16)
    din("c_sind", [128, S], BF16)
    din("c_rotm", [128, 128], BF16)
    for nm in ("w_l3_wqT", "w_l3_wkT", "w_l3_wvT", "w_l3_woT"):
        din(nm, [D, D])
    din("w_l3_bo", [1, D])
    din("w_l3_wg_part", [DI3, 256]); din("w_l3_wu_part", [DI3, 256])
    din("w_l3_wd_part", [D, 768])
    din("w_l1_w1T_part", [256, DFF]); din("w_l1_w2T_part", [1024, D])
    din("w_l1_wqT", [D, D]); din("w_l1_wkT", [D, 256]); din("w_l1_wvT", [D, 256])
    din("w_l1_woT", [D, D]); din("w_l1_bo", [1, D])
    din("c_l1maskT", [128, 256], BF16)
    din("c_l1qdec", [128, 128], BF16)
    din("c_l1kdec", [128, 2])
    din("c_l1bdec", [128, 2])
    din("c_n1rep", [128, D]); din("c_n2rep", [128, DFF]); din("c_l1s", [128, 2])

    for nm in ("w_l2_wq", "w_l2_wk", "w_l2_wvT", "w_l2_woT"):
        din(nm, [D, D])
    for nm in ("w_l2_bv", "w_l2_bo", "w_l2_b2"):
        din(nm, [1, D])
    din("w_l2_bqc", [128, 8]); din("w_l2_bkc", [128, 8])
    din("w_l2_pT", [128, HD])
    din("w_l2_w1", [D, DFF]); din("w_l2_b1", [1, DFF]); din("w_l2_w2", [DFF, D])

    y_out = nc.dram_tensor("y", [TPC, D], F32, kind="ExternalOutput")
    dbg_out = nc.dram_tensor("dbg", [8, 786432], BF16, kind="ExternalOutput")

    dint("stage_xn", [TPC, D])
    SEG_Q = 128 * 2048
    SEG_V = 2048 * 128
    BLK_QKV = SEG_Q + SEG_Q + SEG_V
    dint("a2a_qkv_in", [8, BLK_QKV])
    dint("a2a_qkv_out", [8, BLK_QKV])
    dint("a2a_ao_in", [8, SEG_Q])
    dint("a2a_ao_out", [8, SEG_Q])
    dint("ar_in", [NH, HD, 65], F32)
    dint("ar_out", [NH, HD, 65], F32)
    dint("wstage", [DI3 * D])
    dint("stage_xn32", [TPC, D], F32)
    dint("stage_hq", [TPC, DI3])
    dint("wq1q", [D, DFF]); dint("wq2q", [DFF, D])
    dint("stage_hq1", [TPC, DFF])
    dint("wgqT", [D, DI3]); dint("wuqT", [D, DI3]); dint("wdqT", [3072, D])
    dint("wtpartA", [256, DI3]); dint("wtpartB", [768, D])
    dint("wtpartC", [256, DI3])
    dint("w1qpart", [256, DFF]); dint("w2qpart", [1024, D])
    dint("l1sum_io", [1, 2], F32); dint("l1sum_out", [1, 2], F32)
    dint("stage_h", [TPC, DI3], F32)

    sb = {}
    sb["stream"] = E(nc.sbuf_tensor("stream", [128, NT * D], F32))
    sb["scr"] = E(nc.sbuf_tensor("scr", [128, 2048], BF16))
    sb["stat"] = E(nc.sbuf_tensor("stat", [128, 10 * NT], F32))
    sb["rec"] = E(nc.sbuf_tensor("rec", [33, 1024], F32))
    sb["bcb"] = E(nc.sbuf_tensor("bcb", [128, 2 * 512], F32))
    sb["ident"] = E(nc.sbuf_tensor("ident", [128, 128], BF16))
    sb["onesb"] = E(nc.sbuf_tensor("onesb", [1, 2048], BF16))
    sb["onesf"] = E(nc.sbuf_tensor("onesf", [33, 512], F32))
    sb["cml"] = E(nc.sbuf_tensor("cml", [128, 4 * 512], BF16))
    sb["cmh"] = E(nc.sbuf_tensor("cmh", [128, 4 * 512], BF16))
    sb["jlo"] = E(nc.sbuf_tensor("jlo", [128, 128], BF16))
    sb["jhi"] = E(nc.sbuf_tensor("jhi", [128, 128], BF16))
    sb["bias"] = E(nc.sbuf_tensor("bias", [1, DFF], BF16))
    sb["bias2"] = E(nc.sbuf_tensor("bias2", [1, D], BF16))
    sb["cst"] = E(nc.sbuf_tensor("cst", [128, 8], F32))
    sb["pl2"] = E(nc.sbuf_tensor("pl2", [128, HD], BF16))
    sb["kvs"] = E(nc.sbuf_tensor("kvs", [64, 1040], F32))
    sb["bcl"] = E(nc.sbuf_tensor("bcl", [128, 8], BF16))
    sb["kvf"] = E(nc.sbuf_tensor("kvf", [128, 128], F32))
    sb["kvp"] = E(nc.sbuf_tensor("kvp", [128, 128], BF16))
    sb["l1m"] = E(nc.sbuf_tensor("l1m", [128, 256], BF16))
    sb["l1q"] = E(nc.sbuf_tensor("l1q", [128, 128], BF16))
    sb["l1k"] = E(nc.sbuf_tensor("l1k", [128, 2], F32))
    sb["l1b"] = E(nc.sbuf_tensor("l1b", [128, 2], F32))
    sb["swc"] = E(nc.sbuf_tensor("swc", [128, 4], F32))
    sb["l1s"] = E(nc.sbuf_tensor("l1s", [128, 2], F32))
    pool = E(nc.sbuf_tensor("pool", [128, 51216], BF16))
    pool_map = {
        "xnT": (0, 16384),
        "wa": (16384, 8192), "wb": (24576, 8192), "wc": (32768, 8192),
        "qtp": (16384, 8192), "ktp": (24576, 8192), "vt": (32768, 8320),
        "pt": (41088, 4096), "aot2": (45184, 1024),
        "cosr": (0, 8192), "sinr": (8192, 8192),
        "qdt": (0, 8192),
        "kdt": (8192, 8192),
        "vt1": (32768, 4096), "ktok": (36864, 4096),
        "st1": (41088, 256),
        "rs1": (44160, 2048), "rs2": (46208, 2048),
        "aoT": (16384, 16384), "wd": (32768, 8192),
        "u1": (16384, 2048), "v1": (18432, 8192), "u2": (26624, 8192),
        "v2": (34816, 2048), "ht": (36864, 8192), "pt2": (45056, 4096),
        "qt2": (49152, 1024),
        "aotl2": (0, 16384),
        "qft": (16384, 16384),
        "l2wv": (32768, 8192),
        "wraw": (32768, 8192),
        "wcmb": (40960, 8192),
        "kf2": (49152, 1024), "v22": (50176, 1040), "kvb": (49152, 520),
        "aoq2": (32768, 2048),
        "wfl1": (0, 8192), "qfl1": (8192, 8192),
        "i8l1": (16384, 2048), "qbl1": (18432, 4096),
        "xb1": (16384, 2048), "xh1": (18432, 2048), "i81": (20480, 512),
        "xqb1": (20992, 1024), "n1r": (22016, 2048),
        "w1sl": (16384, 8192), "a0": (24576, 8192), "a1": (32768, 8192),
        "n2r": (40960, 8192), "i8h": (49152, 2048),
        "hq1s": (16384, 4096), "w2sl": (20480, 4096),
        "wf32": (0, 5632), "i8a": (5632, 1408), "i8b": (7040, 1408),
        "i8c": (8448, 1408), "qf1": (9856, 5632), "qf2": (15488, 5632),
        "bmx": (21120, 352), "sclq": (21472, 352), "rcpq": (21824, 352),
        "qbf": (22176, 2816), "trs": (24992, 2816),
        "wgs": (16384, 4096), "wus": (20480, 4096),
        "sg1": (24576, 1024), "sg2": (25600, 1024),
        "hqs": (16384, 2048), "wds": (18432, 2048),
        "w1s": (16384, 2048),
        "w2s": (18432, 2048),
        "htl2": (20480, 16384),
    }

    class SBuf(dict):
        def __getitem__(self, k):
            v = dict.get(self, k, None)
            if v is None and k in pool_map:
                off, n = pool_map[k]
                return pool[:, off:off + n]
            return v

    sb = SBuf(sb)

    ps = E(nc.psum_tensor("ps", [128, 4096], F32))

    def pbank(i, r0=0, r1=128, w=512):
        return ps[r0:r1, 512 * i:512 * i + w]

    def run_phase(fn, *a, **kw):
        with nc.Block() as blk:
            fn(blk, *a, **kw)
        nc.all_engine_barrier()

    do_zero = sorted(set(phases)) != sorted(set(ALL_PHASES))

    def ph_init(blk):
        nl = [0]

        @blk.sync
        def _(s):
            for tt in range(NT):
                s.dma_start(out=sb["stream"][:, 1024 * tt:1024 * (tt + 1)],
                            in_=dram["x"].ap()[128 * tt:128 * (tt + 1), :]).then_inc(ctx.s("ld"), 16)
            s.dma_start(out=sb["ident"][:, :], in_=dram["c_ident"].ap()).then_inc(ctx.s("ld"), 16)
            s.dma_start(out=sb["onesb"][:, :], in_=dram["c_onesb"].ap()).then_inc(ctx.s("ld"), 16)
            s.dma_start(out=sb["onesf"][:, :], in_=dram["c_onesf"].ap()).then_inc(ctx.s("ld"), 16)
            s.dma_start(out=sb["cml"][:, :], in_=dram["c_cmask_lo"].ap()).then_inc(ctx.s("ld"), 16)
            s.dma_start(out=sb["cmh"][:, :], in_=dram["c_cmask_hi"].ap()).then_inc(ctx.s("ld"), 16)
            s.dma_start(out=sb["jlo"][:, :], in_=dram["c_jlo"].ap()).then_inc(ctx.s("ld"), 16)
            s.dma_start(out=sb["jhi"][:, :], in_=dram["c_jhi"].ap()).then_inc(ctx.s("ld"), 16)
            nl[0] = NT + 7

        @blk.vector
        def _(v):
            v.memset(sb["vt"].rearrange("p (c w) -> p c w", w=130)[:, :, 64:130:65], 1.0)
            v.memset(sb["scr"][:, :], 0.0).then_inc(ctx.s("ev"), 1)
        ctx.add("ev", 1)

        @blk.gpsimd
        def _(g):
            if not do_zero:
                return
            g.wait_ge(ctx.s("ev"), ctx.v("ev"))
            nz = 0
            for name in ("a2a_qkv_in", "a2a_qkv_out", "a2a_ao_in", "a2a_ao_out", "stage_xn"):
                flat = dram[name].ap().rearrange("a b -> (a b)")
                tot = flat.shape[0]
                CH = 128 * 2048
                for off in range(0, tot, CH):
                    n = min(CH, tot - off)
                    g.dma_start(out=flat[off:off + n].rearrange("(p n) -> p n", p=128),
                                in_=sb["scr"][:, 0:n // 128]).then_inc(ctx.s("st"), 16)
                    nz += 1
            g.wait_ge(ctx.s("st"), ctx.v("st") + 16 * nz)
            ctx.add("st", 16 * nz)

        @blk.gpsimd
        def _(g):
            g.wait_ge(ctx.s("ld"), ctx.v("ld") + 16 * nl[0])

        ctx.add("ld", 16 * nl[0])

    def ph_ln(blk, mode="T"):
        ev0, sc0, st0, mm0, tr0 = (ctx.v("ev"), ctx.v("sc"), ctx.v("st"),
                                   ctx.v("mm"), ctx.v("tr"))
        sums = sb["stat"][:, 0:NT]
        sumsq = sb["stat"][:, NT:2 * NT]
        mcol = sb["stat"][:, 2 * NT:3 * NT]
        e2 = sb["stat"][:, 3 * NT:4 * NT]
        var0 = sb["stat"][:, 4 * NT:5 * NT]
        var = sb["stat"][:, 5 * NT:6 * NT]
        stdc = sb["stat"][:, 6 * NT:7 * NT]
        rstd = sb["stat"][:, 7 * NT:8 * NT]
        nb = sb["stat"][:, 8 * NT:9 * NT]

        def psB(dc):
            return ps.bitcast(BF16)[:, 1024 * dc:1024 * dc + 128]

        @blk.scalar
        def _(sc):
            for tt in range(NT):
                st = sb["stream"][:, 1024 * tt:1024 * (tt + 1)]
                ins = sc.activation(sb["scr"][:, 0:1024], st, AF.Square,
                                    accum_out=sumsq[:, tt:tt + 1])
                if tt == NT - 1:
                    ins.then_inc(ctx.s("sc"), 1)
            sc.wait_ge(ctx.s("ev"), ev0 + 4)
            sc.activation(stdc, var, AF.Sqrt, bias=np.float32(LN_EPS),
                          scale=1.0).then_inc(ctx.s("sc"), 1)

        @blk.vector
        def _(v):
            for tt in range(NT):
                st = sb["stream"][:, 1024 * tt:1024 * (tt + 1)]
                ins = v.reduce_sum(sums[:, tt:tt + 1], st, axis=AX.X)
                if tt == NT - 1:
                    ins.then_inc(ctx.s("ev"), 1)
            v.wait_ge(ctx.s("ev"), ev0 + 1)
            v.tensor_scalar(mcol, sums, 1.0 / D, None, ALU.mult).then_inc(ctx.s("ev"), 1)
            v.wait_ge(ctx.s("sc"), sc0 + 1)
            v.tensor_scalar(e2, sumsq, 1.0 / D, None, ALU.mult)
            v.wait_ge(ctx.s("ev"), ev0 + 2)
            v.tensor_mul(var0, mcol, mcol).then_inc(ctx.s("ev"), 1)
            v.wait_ge(ctx.s("ev"), ev0 + 3)
            v.tensor_sub(var, e2, var0).then_inc(ctx.s("ev"), 1)
            v.wait_ge(ctx.s("sc"), sc0 + 2)
            v.reciprocal(rstd, stdc).then_inc(ctx.s("ev"), 1)
            v.wait_ge(ctx.s("ev"), ev0 + 5)
            v.scalar_tensor_tensor(nb, mcol, -1.0, rstd, ALU.mult, ALU.mult).then_inc(ctx.s("ev"), 1)
            v.wait_ge(ctx.s("ev"), ev0 + 6)
            if mode == "T":
                for tt in range(NT):
                    if tt >= 2:
                        v.wait_ge(ctx.s("mm"), mm0 + 8 * (tt - 1))
                    st = sb["stream"][:, 1024 * tt:1024 * (tt + 1)]
                    half = sb["scr"][:, 1024 * (tt % 2):1024 * (tt % 2) + 1024]
                    v.tensor_scalar(half, st, rstd[:, tt:tt + 1],
                                    nb[:, tt:tt + 1], ALU.mult,
                                    ALU.add).then_inc(ctx.s("ev"), 1)
                    for dc in range(8):
                        v.wait_ge(ctx.s("mm"), mm0 + 8 * tt + dc + 1)
                        v.tensor_copy(sb["xnT"][:, 2048 * dc + 128 * tt:2048 * dc + 128 * (tt + 1)],
                                      psB(dc)).then_inc(ctx.s("tr"), 1)
            else:
                for tt in range(NT):
                    if tt >= 2:
                        v.wait_ge(ctx.s("st"), st0 + 16 * (tt - 1))
                    st = sb["stream"][:, 1024 * tt:1024 * (tt + 1)]
                    dst32 = (sb["scr"].bitcast(F32)[:, 0:1024] if tt % 2 == 0
                             else sb["bcb"][:, 0:1024])
                    v.tensor_scalar(dst32, st,
                                    rstd[:, tt:tt + 1], nb[:, tt:tt + 1],
                                    ALU.mult, ALU.add).then_inc(ctx.s("ev"), 1)

        if mode == "T":
            @blk.tensor
            def _(t):
                for tt in range(NT):
                    t.wait_ge(ctx.s("ev"), ev0 + 7 + tt)
                    half = sb["scr"][:, 1024 * (tt % 2):1024 * (tt % 2) + 1024]
                    for dc in range(8):
                        if tt >= 1:
                            t.wait_ge(ctx.s("tr"), tr0 + 8 * (tt - 1) + dc + 1)
                        t.transpose(psB(dc), half[:, 128 * dc:128 * (dc + 1)],
                                    sb["ident"][:, :]).then_inc(ctx.s("mm"), 1)
            ctx.add("mm", 8 * NT)
            ctx.add("tr", 8 * NT)
        else:
            @blk.gpsimd
            def _(g):
                for tt in range(NT):
                    g.wait_ge(ctx.s("ev"), ev0 + 6 + tt + 1)
                    src32 = (sb["scr"].bitcast(F32)[:, 0:1024] if tt % 2 == 0
                             else sb["bcb"][:, 0:1024])
                    g.dma_start(out=dram["stage_xn32"].ap()[128 * tt:128 * (tt + 1), :],
                                in_=src32).then_inc(ctx.s("st"), 16)
            ctx.add("st", 16 * NT)

        ctx.add("ev", 6 + NT)
        ctx.add("sc", 2)

    def ph_l0_qkv(blk, wq="w_l0_wq", wk="w_l0_wk", wv="w_l0_wv"):
        ld0, mm0, ev0, st0 = ctx.v("ld"), ctx.v("mm"), ctx.v("ev"), ctx.v("st")
        tr_ready = ctx.v("tr")
        qkv_in = dram["a2a_qkv_in"].ap()

        @blk.gpsimd
        def _(g):
            for w, dst in ((wq, "wa"), (wk, "wb"), (wv, "wc")):
                for dc in range(8):
                    g.dma_start(out=sb[dst][:, 1024 * dc:1024 * (dc + 1)],
                                in_=dram[w].ap()[128 * dc:128 * (dc + 1), :]).then_inc(ctx.s("ld"), 16)
        ctx.add("ld", 16 * 24)

        cum_st = []
        tot = 0
        for g in range(96):
            tot += 16 if g < 64 else 64
            cum_st.append(tot)

        @blk.tensor
        def _(t):
            t.wait_ge(ctx.s("tr"), tr_ready)
            g = 0
            for which, wt in (("q", "wa"), ("k", "wb")):
                t.wait_ge(ctx.s("ld"), ld0 + 16 * 8 * (1 if which == "q" else 2))
                for ic in range(8):
                    for t4 in range(4):
                        bk = g % 2
                        if g >= 2:
                            t.wait_ge(ctx.s("ev"), ev0 + g - 1)
                        for dc in range(8):
                            last = t.matmul(pbank(bk),
                                            sb[wt][:, 1024 * dc + 128 * ic:1024 * dc + 128 * (ic + 1)],
                                            sb["xnT"][:, 2048 * dc + 512 * t4:2048 * dc + 512 * (t4 + 1)],
                                            start=(dc == 0), stop=(dc == 7))
                        last.then_inc(ctx.s("mm"), 1)
                        g += 1
            t.wait_ge(ctx.s("ld"), ld0 + 16 * 24)
            for tt in range(NT):
                for vc in range(2):
                    bk = g % 2
                    t.wait_ge(ctx.s("ev"), ev0 + g - 1)
                    for dc in range(8):
                        last = t.matmul(pbank(bk),
                                        sb["xnT"][:, 2048 * dc + 128 * tt:2048 * dc + 128 * (tt + 1)],
                                        sb["wc"][:, 1024 * dc + 512 * vc:1024 * dc + 512 * (vc + 1)],
                                        start=(dc == 0), stop=(dc == 7))
                    last.then_inc(ctx.s("mm"), 1)
                    g += 1

        @blk.vector
        def _(v):
            for g in range(96):
                v.wait_ge(ctx.s("mm"), mm0 + g + 1)
                if g >= 4:
                    v.wait_ge(ctx.s("st"), st0 + cum_st[g - 4])
                q = 512 * (g % 4)
                v.tensor_copy(sb["scr"][:, q:q + 512], pbank(g % 2)).then_inc(ctx.s("ev"), 1)

        @blk.sync
        def _(s):
            g = 0
            for which in ("q", "k"):
                off0 = 0 if which == "q" else SEG_Q
                for ic in range(8):
                    for t4 in range(4):
                        s.wait_ge(ctx.s("ev"), ev0 + g + 1)
                        q = 512 * (g % 4)
                        dst = qkv_in[ic, off0:off0 + SEG_Q].rearrange("(p n) -> p n", p=128)[:, 512 * t4:512 * (t4 + 1)]
                        s.dma_start(out=dst, in_=sb["scr"][:, q:q + 512]).then_inc(ctx.s("st"), 16)
                        g += 1
            for tt in range(NT):
                for vc in range(2):
                    s.wait_ge(ctx.s("ev"), ev0 + g + 1)
                    q = 512 * (g % 4)
                    for blki in range(4):
                        p = 4 * vc + blki
                        dst = qkv_in[p, 2 * SEG_Q + 16384 * tt:2 * SEG_Q + 16384 * (tt + 1)]
                        dst = dst.rearrange("(p n) -> p n", p=128)
                        s.dma_start(out=dst,
                                    in_=sb["scr"][:, q + 128 * blki:q + 128 * (blki + 1)]).then_inc(ctx.s("st"), 16)
                    g += 1

        ctx.add("mm", 96)
        ctx.add("ev", 96)
        ctx.add("st", 16 * (64 + 32 * 4))

    def ph_a2a(blk, name_in, name_out):
        st_all = ctx.v("st")

        @blk.gpsimd
        def _(g):
            g.wait_ge(ctx.s("st"), st_all)
            with allow_ici_all_to_all():
                g.collective_compute(
                    "AllToAll", ALU.bypass,
                    replica_groups=[list(range(8))],
                    ins=[dram[name_in].ap().opt()],
                    outs=[dram[name_out].ap().opt()],
                ).then_inc(ctx.s("cc"), 1)
            g.wait_ge(ctx.s("cc"), ctx.add("cc"))

    def emit_attn_loads(s, b):
        qkv_out = dram["a2a_qkv_out"].ap()
        for i in range(4):
            srcb = qkv_out[4 * b + i, 0:SEG_Q].rearrange("(p n) -> p n", p=128)
            s.dma_start(out=sb["qtp"][:, 2048 * i:2048 * (i + 1)], in_=srcb).then_inc(ctx.s("ld"), 16)
        for i in range(4):
            srcb = qkv_out[4 * b + i, SEG_Q:2 * SEG_Q].rearrange("(p n) -> p n", p=128)
            s.dma_start(out=sb["ktp"][:, 2048 * i:2048 * (i + 1)], in_=srcb).then_inc(ctx.s("ld"), 16)
        vtv = sb["vt"].rearrange("p (c w) -> p c w", w=130)
        for i in range(4):
            srcb = qkv_out[4 * b + i, 2 * SEG_Q:].rearrange("(c p n) -> c p n", c=16, p=128)
            s.dma_start(out=vtv[:, 16 * i:16 * (i + 1), 0:64],
                        in_=srcb[:, :, 0:64].transpose([1, 0, 2])).then_inc(ctx.s("ld"), 16)
            s.dma_start(out=vtv[:, 16 * i:16 * (i + 1), 65:129],
                        in_=srcb[:, :, 64:128].transpose([1, 0, 2])).then_inc(ctx.s("ld"), 16)

    def ph_rope(blk, b):
        ld0, mm0, ev0 = ctx.v("ld"), ctx.v("mm"), ctx.v("ev")

        @blk.sync
        def _(s):
            emit_attn_loads(s, b)
            s.dma_start(out=sb["cosr"][:, :], in_=dram["c_cosd"].ap()).then_inc(ctx.s("ld"), 16)
            s.dma_start(out=sb["sinr"][:, :], in_=dram["c_sind"].ap()).then_inc(ctx.s("ld"), 16)
            s.dma_start(out=sb["rs2"][:, 0:128], in_=dram["c_rotm"].ap()).then_inc(ctx.s("ld"), 16)
        ctx.add("ld", 16 * 19)

        chunks = [(tgt, c) for tgt in ("qtp", "ktp") for c in range(16)]

        @blk.tensor
        def _(t):
            t.wait_ge(ctx.s("ld"), ld0 + 16 * 19)
            for r, (tgt, c) in enumerate(chunks):
                bk = r % 2
                if r >= 2:
                    t.wait_ge(ctx.s("ev"), ev0 + 3 * (r - 2) + 1)
                t.matmul(pbank(bk), sb["rs2"][:, 0:128],
                         sb[tgt][:, 512 * c:512 * (c + 1)],
                         start=True, stop=True).then_inc(ctx.s("mm"), 1)

        @blk.vector
        def _(v):
            for r, (tgt, c) in enumerate(chunks):
                bk = r % 2
                cs = slice(512 * c, 512 * (c + 1))
                v.wait_ge(ctx.s("mm"), mm0 + r + 1)
                v.tensor_mul(sb["rs1"][:, 0:512], pbank(bk), sb["sinr"][:, cs]).then_inc(ctx.s("ev"), 1)
                v.tensor_mul(sb["rs1"][:, 512:1024], sb[tgt][:, cs],
                             sb["cosr"][:, cs]).then_inc(ctx.s("ev"), 1)
                v.wait_ge(ctx.s("ev"), ev0 + 3 * r + 2)
                v.tensor_add(sb[tgt][:, cs], sb["rs1"][:, 0:512],
                             sb["rs1"][:, 512:1024]).then_inc(ctx.s("ev"), 1)

        ctx.add("mm", 32)
        ctx.add("ev", 96)

    def ph_l0_attn_batch(blk, b, c0, do_loads=True):
        ld0, mm0, pv0, ev0, sc0, st0, tr0 = (ctx.v("ld"), ctx.v("mm"), ctx.v("pv"),
                                             ctx.v("ev"), ctx.v("sc"), ctx.v("st"),
                                             ctx.v("tr"))
        cc0 = ctx.v("cc")

        @blk.vector
        def _(v):
            v.memset(sb["vt"].rearrange("p (c w) -> p c w", w=130)[:, :, 64:130:65],
                     1.0).then_inc(ctx.s("cc"), 1)
        ctx.add("cc", 1)

        if do_loads:
            @blk.sync
            def _(s):
                emit_attn_loads(s, b)
            ctx.add("ld", 16 * 16)

        qts = list(range(16))
        nks = [4 * (qt + 1) for qt in qts]
        total_exp = sum(nks)
        cum_pv = np.cumsum(nks)

        @blk.tensor
        def _(t):
            t.wait_ge(ctx.s("ld"), ctx.v("ld"))
            t.wait_ge(ctx.s("cc"), cc0 + 1)
            t.wait_ge(ctx.s("ev"), ev0)
            e = 0
            for qi, qt in enumerate(qts):
                nk = nks[qt]
                for kc in range(nk):
                    bA, bB = 2 * (kc % 2), 2 * (kc % 2) + 1
                    if e >= 2:
                        t.wait_ge(ctx.s("sc"), sc0 + e - 1)
                    last = t.matmul(pbank(bA), sb["ktp"][0:64, 128 * kc:128 * (kc + 1)],
                                    sb["qtp"][0:64, 512 * qt:512 * (qt + 1)],
                                    start=True, stop=(kc < 4 * qt))
                    last = t.matmul(pbank(bB), sb["ktp"][64:128, 128 * kc:128 * (kc + 1)],
                                    sb["qtp"][64:128, 512 * qt:512 * (qt + 1)],
                                    start=True, stop=(kc < 4 * qt))
                    if kc >= 4 * qt:
                        r = kc - 4 * qt
                        t.matmul(pbank(bA), sb["jlo"][0:64, :], sb["cml"][0:64, 512 * r:512 * (r + 1)],
                                 start=False, stop=False)
                        t.matmul(pbank(bA), sb["jhi"][0:64, :], sb["cmh"][0:64, 512 * r:512 * (r + 1)],
                                 start=False, stop=True)
                        t.matmul(pbank(bB), sb["jlo"][64:128, :], sb["cml"][64:128, 512 * r:512 * (r + 1)],
                                 start=False, stop=False)
                        last = t.matmul(pbank(bB), sb["jhi"][64:128, :],
                                        sb["cmh"][64:128, 512 * r:512 * (r + 1)],
                                        start=False, stop=True)
                    last.then_inc(ctx.s("mm"), 1)
                    if kc >= 1:
                        t.wait_ge(ctx.s("sc"), sc0 + e)
                        pc = kc - 1
                        if pc == 0 and qi > 0:
                            t.wait_ge(ctx.s("ev"), ev0 + 3 * qi)
                        half = 1024 * (pc % 4)
                        t.matmul(pbank(4, 0, 65), sb["vt"][:, 130 * pc:130 * pc + 65],
                                 sb["pt"][:, half:half + 512], start=(pc == 0), stop=False)
                        t.matmul(pbank(5, 0, 65), sb["vt"][:, 130 * pc + 65:130 * pc + 130],
                                 sb["pt"][:, half + 512:half + 1024],
                                 start=(pc == 0), stop=False).then_inc(ctx.s("pv"), 1)
                    e += 1
                t.wait_ge(ctx.s("sc"), sc0 + e)
                pc = nk - 1
                if pc == 0 and qi > 0:
                    t.wait_ge(ctx.s("ev"), ev0 + 3 * qi)
                half = 1024 * (pc % 4)
                t.matmul(pbank(4, 0, 65), sb["vt"][:, 130 * pc:130 * pc + 65],
                         sb["pt"][:, half:half + 512], start=(pc == 0), stop=True)
                t.matmul(pbank(5, 0, 65), sb["vt"][:, 130 * pc + 65:130 * pc + 130],
                         sb["pt"][:, half + 512:half + 1024],
                         start=(pc == 0), stop=True).then_inc(ctx.s("pv"), 1)
                t.wait_ge(ctx.s("ev"), ev0 + 3 * qi + 1)
                t.matmul(pbank(6, 0, 64), sb["onesf"][0:1, 0:64], sb["rec"][0:1, 0:512],
                         start=True, stop=True)
                t.matmul(pbank(7, 0, 64), sb["onesf"][32:33, 0:64], sb["rec"][32:33, 0:512],
                         start=True, stop=True).then_inc(ctx.s("tr"), 1)

        @blk.scalar
        def _(sc):
            e = 0
            for qi, qt in enumerate(qts):
                nk = nks[qt]
                for kc in range(nk):
                    sc.wait_ge(ctx.s("mm"), mm0 + e + 1)
                    if e >= 4:
                        sc.wait_ge(ctx.s("pv"), pv0 + e - 3)
                    half = 1024 * (kc % 4)
                    stp = kc % 2
                    sc.activation(sb["pt"][:, half:half + 1024],
                                  ps[:, 1024 * stp:1024 * stp + 1024],
                                  AF.Exp, bias=np.float32(-c0),
                                  scale=0.125).then_inc(ctx.s("sc"), 1)
                    e += 1

        @blk.vector
        def _(v):
            for qi, qt in enumerate(qts):
                v.wait_ge(ctx.s("pv"), pv0 + int(cum_pv[qi]))
                v.tensor_copy(sb["rec"][0:1, 0:512], ps[64:65, 512 * 4:512 * 4 + 512])
                v.tensor_copy(sb["rec"][32:33, 0:512], ps[64:65, 512 * 5:512 * 5 + 512]).then_inc(ctx.s("ev"), 1)
                v.wait_ge(ctx.s("tr"), tr0 + qi + 1)
                v.reciprocal(sb["bcb"][0:64, 0:512], pbank(6, 0, 64))
                v.reciprocal(sb["bcb"][0:64, 512:1024], pbank(7, 0, 64)).then_inc(ctx.s("ev"), 1)
                v.wait_ge(ctx.s("ev"), ev0 + 3 * qi + 2)
                if qi >= 2:
                    v.wait_ge(ctx.s("st"), st0 + 16 * (qi - 1))
                h = 512 * (qi % 2)
                v.tensor_mul(sb["aot2"][0:64, h:h + 512],
                             pbank(4, 0, 64), sb["bcb"][0:64, 0:512])
                v.tensor_mul(sb["aot2"][64:128, h:h + 512],
                             pbank(5, 0, 64), sb["bcb"][0:64, 512:1024]).then_inc(ctx.s("ev"), 1)

        @blk.gpsimd
        def _(g):
            for qi, qt in enumerate(qts):
                g.wait_ge(ctx.s("ev"), ev0 + 3 * qi + 3)
                h = 512 * (qi % 2)
                blkv = dram["a2a_ao_in"].ap()[4 * b + qt // 4, :].rearrange("(p n) -> p n", p=128)
                dst = blkv[:, 512 * (qt % 4):512 * (qt % 4 + 1)]
                g.dma_start(out=dst, in_=sb["aot2"][:, h:h + 512]).then_inc(ctx.s("st"), 16)

        ctx.add("mm", total_exp)
        ctx.add("sc", total_exp)
        ctx.add("pv", total_exp)
        ctx.add("tr", 16)
        ctx.add("ev", 3 * 16)
        ctx.add("st", 16 * 16)

    def ph_l0_wo(blk, wo="w_l0_wo", bo="w_l0_bo"):
        ld0, mm0, ev0 = ctx.v("ld"), ctx.v("mm"), ctx.v("ev")
        ao = dram["a2a_ao_out"].ap()

        @blk.gpsimd
        def _(g):
            for dc in range(8):
                g.dma_start(out=sb["wd"][:, 1024 * dc:1024 * (dc + 1)],
                            in_=dram[wo].ap()[128 * dc:128 * (dc + 1), :]).then_inc(ctx.s("ld"), 16)
            g.dma_start(out=sb["bias"][0:1, 0:D], in_=dram[bo].ap()).then_inc(ctx.s("ld"), 16)

        @blk.sync
        def _(s):
            s.wait_ge(ctx.s("cc"), ctx.v("cc"))
            for ic in range(8):
                src = ao[ic, :].rearrange("(p n) -> p n", p=128)
                s.dma_start(out=sb["aoT"][:, 2048 * ic:2048 * (ic + 1)], in_=src).then_inc(ctx.s("ld"), 16)
        ctx.add("ld", 16 * 17)

        @blk.tensor
        def _(t):
            t.wait_ge(ctx.s("ld"), ld0 + 16 * 17)
            g = 0
            for tt in range(NT):
                for ds in range(2):
                    bk = g % 2
                    if g >= 2:
                        t.wait_ge(ctx.s("ev"), ev0 + g - 1)
                    for ic in range(8):
                        t.matmul(pbank(bk),
                                 sb["aoT"][:, 2048 * ic + 128 * tt:2048 * ic + 128 * (tt + 1)],
                                 sb["wd"][:, 1024 * ic + 512 * ds:1024 * ic + 512 * (ds + 1)],
                                 start=(ic == 0), stop=False)
                    t.matmul(pbank(bk), sb["onesb"][0:1, 0:128],
                             sb["bias"][0:1, 512 * ds:512 * (ds + 1)],
                             start=False, stop=True).then_inc(ctx.s("mm"), 1)
                    g += 1

        @blk.vector
        def _(v):
            for g in range(32):
                v.wait_ge(ctx.s("mm"), mm0 + g + 1)
                tt, ds = g // 2, g % 2
                dst = sb["stream"][:, 1024 * tt + 512 * ds:1024 * tt + 512 * (ds + 1)]
                v.tensor_add(dst, dst, pbank(g % 2)).then_inc(ctx.s("ev"), 1)

        ctx.add("mm", 32)
        ctx.add("ev", 32)

    def ph_l0_ffn(blk):
        ld0, mm0, ev0, sc0 = ctx.v("ld"), ctx.v("mm"), ctx.v("ev"), ctx.v("sc")
        tr_ready = ctx.v("tr")

        events = []
        ev_idx = {}
        n = 0
        for g in range(8):
            n += 1
            ev_idx[("pt", g)] = n
        for t4 in range(4):
            for fc in range(16):
                n += 1
                ev_idx[("ht", t4, fc)] = n
            n += 1
            ev_idx[("qpart", t4)] = n
            for fc in range(16, 32):
                n += 1
                ev_idx[("ht", t4, fc)] = n
            for rc in range(2):
                n += 1
                ev_idx[("qfin", t4, rc)] = n
            for tt4 in range(4):
                for ds in range(2):
                    n += 1
                    ev_idx[("y", t4, tt4, ds)] = n
        n_mm_total = n

        ve_idx = {}
        n = 0
        for g in range(8):
            n += 1
            ve_idx[("pt", g)] = n
        for t4 in range(4):
            for rc in range(2):
                n += 1
                ve_idx[("qt2", t4, rc)] = n
            for tt4 in range(4):
                for ds in range(2):
                    n += 1
                    ve_idx[("y", t4, tt4, ds)] = n
        n_ev_total = n

        sc_idx = {}
        n = 0
        for t4 in range(4):
            for fc in range(32):
                n += 1
                sc_idx[(t4, fc)] = n
        n_sc_total = n

        @blk.gpsimd
        def _(g):
            for dc in range(8):
                g.dma_start(out=sb["u1"][:, 256 * dc:256 * (dc + 1)],
                            in_=dram["w_l0_u1"].ap()[128 * dc:128 * (dc + 1), :]).then_inc(ctx.s("ld"), 16)
            for rc in range(2):
                g.dma_start(out=sb["v1"][:, 4096 * rc:4096 * (rc + 1)],
                            in_=dram["w_l0_v1"].ap()[128 * rc:128 * (rc + 1), :]).then_inc(ctx.s("ld"), 16)
            for fc in range(32):
                g.dma_start(out=sb["u2"][:, 256 * fc:256 * (fc + 1)],
                            in_=dram["w_l0_u2"].ap()[128 * fc:128 * (fc + 1), :]).then_inc(ctx.s("ld"), 16)
            for rc in range(2):
                g.dma_start(out=sb["v2"][:, 1024 * rc:1024 * (rc + 1)],
                            in_=dram["w_l0_v2"].ap()[128 * rc:128 * (rc + 1), :]).then_inc(ctx.s("ld"), 16)
            g.dma_start(out=sb["bias"][0:1, :], in_=dram["w_l0_b1"].ap()).then_inc(ctx.s("ld"), 16)
            g.dma_start(out=sb["bias2"][0:1, 0:D], in_=dram["w_l0_b2"].ap()).then_inc(ctx.s("ld"), 16)
        NLD = 8 + 2 + 32 + 2 + 2
        ctx.add("ld", 16 * NLD)

        @blk.tensor
        def _(t):
            t.wait_ge(ctx.s("tr"), tr_ready)
            t.wait_ge(ctx.s("ld"), ld0 + 16 * NLD)
            for g in range(8):
                rc, t4 = g // 4, g % 4
                bk = g % 2
                if g >= 2:
                    t.wait_ge(ctx.s("ev"), ev0 + ve_idx[("pt", g - 2)])
                last = None
                for dc in range(8):
                    last = t.matmul(pbank(bk),
                                    sb["u1"][:, 256 * dc + 128 * rc:256 * dc + 128 * (rc + 1)],
                                    sb["xnT"][:, 2048 * dc + 512 * t4:2048 * dc + 512 * (t4 + 1)],
                                    start=(dc == 0), stop=(dc == 7))
                last.then_inc(ctx.s("mm"), 1)
            for t4 in range(4):
                if t4 == 0:
                    t.wait_ge(ctx.s("ev"), ev0 + ve_idx[("pt", 7)])
                else:
                    t.wait_ge(ctx.s("ev"), ev0 + ve_idx[("qt2", t4 - 1, 1)])
                for half in range(2):
                    for fci in range(16):
                        fc = 16 * half + fci
                        bk = 2 + (fc % 4)
                        if fc >= 4:
                            t.wait_ge(ctx.s("sc"), sc0 + sc_idx[(t4, fc)] - 4)
                        for rc in range(2):
                            t.matmul(pbank(bk),
                                     sb["v1"][:, 4096 * rc + 128 * fc:4096 * rc + 128 * (fc + 1)],
                                     sb["pt2"][:, 2048 * rc + 512 * t4:2048 * rc + 512 * (t4 + 1)],
                                     start=(rc == 0), stop=False)
                        t.matmul(pbank(bk), sb["bias"][0:1, 128 * fc:128 * (fc + 1)],
                                 sb["onesb"][0:1, 0:512], start=False,
                                 stop=True).then_inc(ctx.s("mm"), 1)
                    t.wait_ge(ctx.s("sc"), sc0 + sc_idx[(t4, 16 * half + 15)])
                    last = None
                    for rc in range(2):
                        for fci in range(16):
                            fc = 16 * half + fci
                            last = t.matmul(pbank(rc),
                                            sb["u2"][:, 256 * fc + 128 * rc:256 * fc + 128 * (rc + 1)],
                                            sb["ht"][:, 512 * fci:512 * (fci + 1)],
                                            start=(fc == 0), stop=(fc == 31))
                        if half == 1:
                            last.then_inc(ctx.s("mm"), 1)
                    if half == 0:
                        last.then_inc(ctx.s("mm"), 1)
                for tt4 in range(4):
                    for ds in range(2):
                        gy = 2 * tt4 + ds
                        bk = 6 + (gy % 2)
                        t.wait_ge(ctx.s("ev"), ev0 + ve_idx[("qt2", t4, 1)])
                        if gy >= 2:
                            t.wait_ge(ctx.s("ev"), ev0 + ve_idx[("y", t4, (gy - 2) // 2, (gy - 2) % 2)])
                        elif t4 > 0:
                            t.wait_ge(ctx.s("ev"), ev0 + ve_idx[("y", t4 - 1, 3, 1)])
                        for rc in range(2):
                            t.matmul(pbank(bk),
                                     sb["qt2"][:, 512 * rc + 128 * tt4:512 * rc + 128 * (tt4 + 1)],
                                     sb["v2"][:, 1024 * rc + 512 * ds:1024 * rc + 512 * (ds + 1)],
                                     start=(rc == 0), stop=False)
                        t.matmul(pbank(bk), sb["onesb"][0:1, 0:128],
                                 sb["bias2"][0:1, 512 * ds:512 * (ds + 1)], start=False,
                                 stop=True).then_inc(ctx.s("mm"), 1)

        @blk.scalar
        def _(sc):
            for t4 in range(4):
                for fc in range(32):
                    sc.wait_ge(ctx.s("mm"), mm0 + ev_idx[("ht", t4, fc)])
                    if fc == 16:
                        sc.wait_ge(ctx.s("mm"), mm0 + ev_idx[("qpart", t4)])
                    bk = 2 + (fc % 4)
                    sc.activation(sb["ht"][:, 512 * (fc % 16):512 * (fc % 16 + 1)], pbank(bk),
                                  AF.Gelu_apprx_tanh, bias=0.0, scale=1.0).then_inc(ctx.s("sc"), 1)

        @blk.vector
        def _(v):
            for g in range(8):
                v.wait_ge(ctx.s("mm"), mm0 + ev_idx[("pt", g)])
                rc, t4 = g // 4, g % 4
                v.tensor_copy(sb["pt2"][:, 2048 * rc + 512 * t4:2048 * rc + 512 * (t4 + 1)],
                              pbank(g % 2)).then_inc(ctx.s("ev"), 1)
            for t4 in range(4):
                for rc in range(2):
                    v.wait_ge(ctx.s("mm"), mm0 + ev_idx[("qfin", t4, rc)])
                    v.tensor_copy(sb["qt2"][:, 512 * rc:512 * (rc + 1)], pbank(rc)).then_inc(ctx.s("ev"), 1)
                for tt4 in range(4):
                    for ds in range(2):
                        v.wait_ge(ctx.s("mm"), mm0 + ev_idx[("y", t4, tt4, ds)])
                        bk = 6 + ((2 * tt4 + ds) % 2)
                        tt = 4 * t4 + tt4
                        dst = sb["stream"][:, 1024 * tt + 512 * ds:1024 * tt + 512 * (ds + 1)]
                        v.tensor_add(dst, dst, pbank(bk)).then_inc(ctx.s("ev"), 1)

        ctx.add("mm", n_mm_total)
        ctx.add("sc", n_sc_total)
        ctx.add("ev", n_ev_total)

    def ph_store(blk):
        st0 = ctx.v("st")

        @blk.sync
        def _(s):
            for tt in range(NT):
                s.dma_start(out=y_out.ap()[128 * tt:128 * (tt + 1), :],
                            in_=sb["stream"][:, 1024 * tt:1024 * (tt + 1)]).then_inc(ctx.s("st"), 16)
            s.wait_ge(ctx.s("st"), st0 + 16 * NT)
        ctx.add("st", 16 * NT)


    NORM_L2 = float(1.0 / np.sqrt(np.sqrt(64.0)))

    def ph_l2_comb(blk, wname, bcname, bias_off):
        ld0, mm0, ev0 = ctx.v("ld"), ctx.v("mm"), ctx.v("ev")

        @blk.gpsimd
        def _(g):
            for rc in range(8):
                g.dma_start(out=sb["wraw"][:, 1024 * rc:1024 * (rc + 1)],
                            in_=dram[wname].ap()[128 * rc:128 * (rc + 1), :]).then_inc(ctx.s("ld"), 16)
            g.dma_start(out=sb["bcl"][:, :], in_=dram[bcname].ap()).then_inc(ctx.s("ld"), 16)
            g.dma_start(out=sb["pl2"][:, :], in_=dram["w_l2_pT"].ap()).then_inc(ctx.s("ld"), 16)
        ctx.add("ld", 16 * 10)

        @blk.tensor
        def _(t):
            t.wait_ge(ctx.s("ld"), ld0 + 16 * 10)
            for dc in range(8):
                bkE, bkO = (0, 1) if dc % 2 == 0 else (2, 3)
                if dc >= 2:
                    t.wait_ge(ctx.s("ev"), ev0 + dc - 1)
                last = None
                for h in range(16):
                    base = 64 * (h % 2)
                    bk = bkE if h % 2 == 0 else bkO
                    lhs = sb["wraw"][base:base + 64,
                                     1024 * (h // 2) + 128 * dc:1024 * (h // 2) + 128 * (dc + 1)]
                    last = t.matmul(pbank(bk)[:, 64 * (h // 2):64 * (h // 2) + 64],
                                    lhs, sb["pl2"][base:base + 64, :],
                                    start=True, stop=True)
                last.then_inc(ctx.s("mm"), 1)
            last = None
            for h in range(16):
                base = 64 * (h % 2)
                bkb = 6 if h % 2 == 0 else 7
                last = t.matmul(ps[0:1, 512 * bkb + 64 * (h // 2):512 * bkb + 64 * (h // 2) + 64],
                                sb["bcl"][base:base + 64, h // 2:h // 2 + 1],
                                sb["pl2"][base:base + 64, :], start=True, stop=True)
            last.then_inc(ctx.s("mm"), 1)

        @blk.vector
        def _(v):
            for dc in range(8):
                v.wait_ge(ctx.s("mm"), mm0 + dc + 1)
                bkE, bkO = (0, 1) if dc % 2 == 0 else (2, 3)
                dst3 = sb["wcmb"].rearrange("p (c w) -> p c w", w=64)
                psE = pbank(bkE).rearrange("p (c w) -> p c w", w=64)
                psO = pbank(bkO).rearrange("p (c w) -> p c w", w=64)
                v.tensor_copy(dst3[:, 16 * dc:16 * (dc + 1):2, :], psE[:, 0:8, :])
                v.tensor_copy(dst3[:, 16 * dc + 1:16 * (dc + 1):2, :],
                              psO[:, 0:8, :]).then_inc(ctx.s("ev"), 1)
            v.wait_ge(ctx.s("mm"), mm0 + 9)
            bias3 = sb["bias"].rearrange("o (c w) -> o c w", w=64)
            ps3a = ps.rearrange("p (b w) -> p b w", w=64)
            v.tensor_copy(bias3[0:1, bias_off // 64:bias_off // 64 + 16:2, :],
                          ps3a[0:1, 48:56, :])
            v.tensor_copy(bias3[0:1, bias_off // 64 + 1:bias_off // 64 + 16:2, :],
                          ps3a[0:1, 56:64, :]).then_inc(ctx.s("ev"), 1)

        ctx.add("mm", 9)
        ctx.add("ev", 9)

    def ph_l2_qf(blk):
        mm0, ev0 = ctx.v("mm"), ctx.v("ev")
        tr_ready = ctx.v("tr")

        @blk.tensor
        def _(t):
            t.wait_ge(ctx.s("tr"), tr_ready)
            g = 0
            for ic in range(8):
                for t4 in range(4):
                    bk = g % 2
                    if g >= 2:
                        t.wait_ge(ctx.s("ev"), ev0 + 2 * (g - 1))
                    for dc in range(8):
                        t.matmul(pbank(bk),
                                 sb["wcmb"][:, 1024 * dc + 128 * ic:1024 * dc + 128 * (ic + 1)],
                                 sb["xnT"][:, 2048 * dc + 512 * t4:2048 * dc + 512 * (t4 + 1)],
                                 start=(dc == 0), stop=False)
                    t.matmul(pbank(bk), sb["bias"][0:1, 128 * ic:128 * (ic + 1)],
                             sb["onesb"][0:1, 0:512], start=False,
                             stop=True).then_inc(ctx.s("mm"), 1)
                    g += 1

        @blk.vector
        def _(v):
            for g in range(32):
                ic, t4 = g // 4, g % 4
                v.wait_ge(ctx.s("mm"), mm0 + g + 1)
                dst = sb["qft"][:, 2048 * ic + 512 * t4:2048 * ic + 512 * (t4 + 1)]
                v.tensor_scalar(dst, pbank(g % 2), NORM_L2, 0.0,
                                ALU.mult, ALU.max).then_inc(ctx.s("ev"), 1)
                v.wait_ge(ctx.s("ev"), ev0 + 2 * g + 1)
                v.tensor_scalar(dst, dst, 1e-6, None, ALU.add).then_inc(ctx.s("ev"), 1)

        ctx.add("mm", 32)
        ctx.add("ev", 64)

    def ph_l2_kv(blk):
        ld0, mm0, ev0 = ctx.v("ld"), ctx.v("mm"), ctx.v("ev")

        @blk.gpsimd
        def _(g):
            for dc in range(8):
                g.dma_start(out=sb["l2wv"][:, 1024 * dc:1024 * (dc + 1)],
                            in_=dram["w_l2_wvT"].ap()[128 * dc:128 * (dc + 1), :]).then_inc(ctx.s("ld"), 16)
            g.dma_start(out=sb["bias"][0:1, 2048:3072],
                        in_=dram["w_l2_bv"].ap()).then_inc(ctx.s("ld"), 16)
            g.dma_start(out=sb["bias"][0:1, 3072:4096],
                        in_=dram["w_l2_bo"].ap()).then_inc(ctx.s("ld"), 16)
        ctx.add("ld", 16 * 10)

        @blk.vector
        def _(v):
            v.memset(sb["v22"].rearrange("p (c w) -> p c w", w=65)[:, :, 64:65],
                     1.0).then_inc(ctx.s("ev"), 1)
        ev1 = ev0 + 1
        ctx.add("ev", 1)

        @blk.tensor
        def _(t):
            t.wait_ge(ctx.s("ld"), ld0 + 16 * 10)
            t.wait_ge(ctx.s("ev"), ev1)
            for tt in range(NT):
                for which in range(2):
                    for vc in range(2):
                        if tt >= 1:
                            t.wait_ge(ctx.s("ev"), ev1 + 5 * tt)
                        bk = 2 * which + vc
                        wsrc = "l2wv" if which == 0 else "wcmb"
                        for dc in range(8):
                            t.matmul(pbank(bk),
                                     sb["xnT"][:, 2048 * dc + 128 * tt:2048 * dc + 128 * (tt + 1)],
                                     sb[wsrc][:, 1024 * dc + 512 * vc:1024 * dc + 512 * (vc + 1)],
                                     start=(dc == 0), stop=False)
                        boff = 2048 if which == 0 else 1024
                        t.matmul(pbank(bk), sb["onesb"][0:1, 0:128],
                                 sb["bias"][0:1, boff + 512 * vc:boff + 512 * (vc + 1)],
                                 start=False, stop=True).then_inc(ctx.s("mm"), 1)
                t.wait_ge(ctx.s("ev"), ev1 + 5 * tt + 5)
                last = None
                for h in range(16):
                    bkv = 5 + h // 7
                    coff = 65 * (h % 7)
                    last = t.matmul(ps[0:64, 512 * bkv + coff:512 * bkv + coff + 65],
                                    sb["kf2"][:, 64 * h:64 * (h + 1)],
                                    sb["v22"][:, 65 * h:65 * (h + 1)],
                                    start=(tt == 0), stop=(tt == NT - 1))
                last.then_inc(ctx.s("mm"), 1)

        @blk.vector
        def _(v):
            v2v = sb["v22"].rearrange("p (c w) -> p c w", w=65)
            for tt in range(NT):
                for vc in range(2):
                    v.wait_ge(ctx.s("mm"), mm0 + 5 * tt + vc + 1)
                    v.tensor_copy(v2v[:, 8 * vc:8 * (vc + 1), 0:64],
                                  pbank(vc)).then_inc(ctx.s("ev"), 1)
                for vc in range(2):
                    v.wait_ge(ctx.s("mm"), mm0 + 5 * tt + 2 + vc + 1)
                    v.tensor_scalar(sb["kf2"][:, 512 * vc:512 * (vc + 1)],
                                    pbank(2 + vc), NORM_L2, 0.0,
                                    ALU.mult, ALU.max).then_inc(ctx.s("ev"), 1)
                v.wait_ge(ctx.s("ev"), ev1 + 5 * tt + 4)
                v.tensor_scalar(sb["kf2"][:, :], sb["kf2"][:, :], 1e-6, None,
                                ALU.add).then_inc(ctx.s("ev"), 1)

        ctx.add("mm", NT * 5)
        ctx.add("ev", NT * 5)

    def ph_l2_ar(blk):
        mm_all, ev0, st0 = ctx.v("mm"), ctx.v("ev"), ctx.v("st")

        @blk.vector
        def _(v):
            v.wait_ge(ctx.s("mm"), mm_all)
            v.tensor_copy(sb["kvs"][0:64, 0:455], ps[0:64, 512 * 5:512 * 5 + 455])
            v.tensor_copy(sb["kvs"][0:64, 455:910], ps[0:64, 512 * 6:512 * 6 + 455])
            v.tensor_copy(sb["kvs"][0:64, 910:1040],
                          ps[0:64, 512 * 7:512 * 7 + 130]).then_inc(ctx.s("ev"), 1)

        @blk.gpsimd
        def _(g):
            g.wait_ge(ctx.s("ev"), ev0 + 1)
            for h in range(16):
                g.dma_start(out=dram["ar_in"].ap()[h],
                            in_=sb["kvs"][0:64, 65 * h:65 * (h + 1)]).then_inc(ctx.s("st"), 16)
            g.wait_ge(ctx.s("st"), st0 + 16 * 16)
            g.collective_compute(
                "AllReduce", ALU.add,
                replica_groups=[[0, 1, 2, 3], [4, 5, 6, 7]],
                ins=[dram["ar_in"].ap().opt()],
                outs=[dram["ar_out"].ap().opt()],
            ).then_inc(ctx.s("cc"), 1)
            g.wait_ge(ctx.s("cc"), ctx.add("cc"))

        ctx.add("ev", 1)
        ctx.add("st", 16 * 16)

    def ph_l2_out(blk):
        ld0, mm0, ev0, tr0 = ctx.v("ld"), ctx.v("mm"), ctx.v("ev"), ctx.v("tr")
        dn = sb["stat"][:, 0:16]
        rc = sb["stat"][:, 16:32]

        def slot(h):
            p, half = h // 2, h % 2
            bank = (0 if half == 0 else 2) + (p // 6)
            return 512 * bank + 66 * (p % 6)

        @blk.gpsimd
        def _(g):
            for p in range(8):
                g.dma_start(out=sb["kvb"][0:64, 65 * p:65 * (p + 1)],
                            in_=dram["ar_out"].ap()[2 * p]).then_inc(ctx.s("ld"), 16)
                g.dma_start(out=sb["kvb"][64:128, 65 * p:65 * (p + 1)],
                            in_=dram["ar_out"].ap()[2 * p + 1]).then_inc(ctx.s("ld"), 16)
        ctx.add("ld", 16 * 16)

        @blk.tensor
        def _(t):
            t.wait_ge(ctx.s("ld"), ld0 + 16 * 16)
            for tt in range(NT):
                if tt >= 1:
                    t.wait_ge(ctx.s("ev"), ev0 + 5 * (tt - 1) + 4)
                last = None
                for h in range(16):
                    p, half = h // 2, h % 2
                    r0 = 64 * half
                    last = t.matmul(ps[:, slot(h):slot(h) + 65],
                                    sb["qft"][r0:r0 + 64,
                                              2048 * p + 128 * tt:2048 * p + 128 * (tt + 1)],
                                    sb["kvb"][r0:r0 + 64, 65 * p:65 * (p + 1)],
                                    start=True, stop=True)
                last.then_inc(ctx.s("mm"), 1)
                t.wait_ge(ctx.s("ev"), ev0 + 5 * tt + 4)
                if tt >= 1:
                    t.wait_ge(ctx.s("ev"), ev0 + 5 * (tt - 1) + 5)
                hq = 1024 * (tt % 2)
                ins = None
                for ic in range(8):
                    ins = t.transpose(ps.bitcast(BF16)[:, 6144 + 128 * ic:6144 + 128 * (ic + 1)],
                                      sb["aoq2"][:, hq + 128 * ic:hq + 128 * (ic + 1)],
                                      sb["ident"][:, :])
                ins.then_inc(ctx.s("tr"), 1)

        @blk.vector
        def _(v):
            for tt in range(NT):
                v.wait_ge(ctx.s("mm"), mm0 + tt + 1)
                ins = None
                for h in range(16):
                    ins = v.tensor_copy(dn[:, h:h + 1], ps[:, slot(h) + 64:slot(h) + 65])
                ins.then_inc(ctx.s("ev"), 1)
                v.wait_ge(ctx.s("ev"), ev0 + 5 * tt + 1)
                v.tensor_scalar(dn, dn, 1e-6, None, ALU.add).then_inc(ctx.s("ev"), 1)
                v.wait_ge(ctx.s("ev"), ev0 + 5 * tt + 2)
                v.reciprocal(rc, dn).then_inc(ctx.s("ev"), 1)
                v.wait_ge(ctx.s("ev"), ev0 + 5 * tt + 3)
                if tt >= 2:
                    v.wait_ge(ctx.s("tr"), tr0 + tt - 1)
                hq = 1024 * (tt % 2)
                ins = None
                for h in range(16):
                    p, half = h // 2, h % 2
                    ins = v.tensor_scalar(sb["aoq2"][:, hq + 128 * p + 64 * half:hq + 128 * p + 64 * (half + 1)],
                                          ps[:, slot(h):slot(h) + 64],
                                          rc[:, h:h + 1], None, ALU.mult)
                ins.then_inc(ctx.s("ev"), 1)
                v.wait_ge(ctx.s("tr"), tr0 + tt + 1)
                ins = None
                for ic in range(8):
                    ins = v.tensor_copy(sb["aotl2"][:, 2048 * ic + 128 * tt:2048 * ic + 128 * (tt + 1)],
                                        ps.bitcast(BF16)[:, 6144 + 128 * ic:6144 + 128 * (ic + 1)])
                ins.then_inc(ctx.s("ev"), 1)

        ctx.add("mm", NT)
        ctx.add("tr", NT)
        ctx.add("ev", 5 * NT)

    def ph_l2_wo(blk):
        ld0, mm0, ev0 = ctx.v("ld"), ctx.v("mm"), ctx.v("ev")

        @blk.gpsimd
        def _(g):
            for dc in range(8):
                g.dma_start(out=sb["wd"][:, 1024 * dc:1024 * (dc + 1)],
                            in_=dram["w_l2_woT"].ap()[128 * dc:128 * (dc + 1), :]).then_inc(ctx.s("ld"), 16)
        ctx.add("ld", 16 * 8)

        @blk.tensor
        def _(t):
            t.wait_ge(ctx.s("ld"), ld0 + 16 * 8)
            g = 0
            for tt in range(NT):
                for ds in range(2):
                    bk = g % 2
                    if g >= 2:
                        t.wait_ge(ctx.s("ev"), ev0 + g - 1)
                    for ic in range(8):
                        t.matmul(pbank(bk),
                                 sb["aotl2"][:, 2048 * ic + 128 * tt:2048 * ic + 128 * (tt + 1)],
                                 sb["wd"][:, 1024 * ic + 512 * ds:1024 * ic + 512 * (ds + 1)],
                                 start=(ic == 0), stop=False)
                    t.matmul(pbank(bk), sb["onesb"][0:1, 0:128],
                             sb["bias"][0:1, 3072 + 512 * ds:3072 + 512 * (ds + 1)],
                             start=False, stop=True).then_inc(ctx.s("mm"), 1)
                    g += 1

        @blk.vector
        def _(v):
            for g in range(32):
                v.wait_ge(ctx.s("mm"), mm0 + g + 1)
                tt, ds = g // 2, g % 2
                dst = sb["stream"][:, 1024 * tt + 512 * ds:1024 * tt + 512 * (ds + 1)]
                v.tensor_add(dst, dst, pbank(g % 2)).then_inc(ctx.s("ev"), 1)

        ctx.add("mm", 32)
        ctx.add("ev", 32)

    def ph_l2_ffn(blk):
        ld0, mm0, ev0, sc0 = ctx.v("ld"), ctx.v("mm"), ctx.v("ev"), ctx.v("sc")
        tr_ready = ctx.v("tr")

        def ht_mm_idx(t4, fc):
            return t4 * 65 + fc + 1

        def y_mm_idx(t4, fc):
            return t4 * 65 + 32 + fc + 1

        def ybias_mm_idx(t4):
            return t4 * 65 + 65

        @blk.gpsimd
        def _(g):
            g.dma_start(out=sb["bias"][0:1, :], in_=dram["w_l2_b1"].ap()).then_inc(ctx.s("ld"), 16)
            g.dma_start(out=sb["bias2"][0:1, :], in_=dram["w_l2_b2"].ap()).then_inc(ctx.s("ld"), 16)
            for t4 in range(4):
                for fc in range(32):
                    if t4 * 32 + fc >= 2:
                        pf = t4 * 32 + fc - 2
                        g.wait_ge(ctx.s("mm"), mm0 + ht_mm_idx(pf // 32, pf % 32))
                    slot = sb["w1s"][:, 1024 * (fc % 2):1024 * (fc % 2 + 1)]
                    srcw = dram["w_l2_w1"].ap()[:, 128 * fc:128 * (fc + 1)]
                    srcw = srcw.rearrange("(c p) n -> c p n", p=128).transpose([1, 0, 2])
                    g.dma_start(out=slot.rearrange("p (c n) -> p c n", n=128),
                                in_=srcw).then_inc(ctx.s("ld"), 16)
                for fc in range(32):
                    if t4 * 32 + fc >= 2:
                        pf = t4 * 32 + fc - 2
                        g.wait_ge(ctx.s("mm"), mm0 + y_mm_idx(pf // 32, pf % 32))
                    slot = sb["w2s"][:, 1024 * (fc % 2):1024 * (fc % 2 + 1)]
                    g.dma_start(out=slot,
                                in_=dram["w_l2_w2"].ap()[128 * fc:128 * (fc + 1), :]).then_inc(ctx.s("ld"), 16)

        NLD_T = 2 + 4 * 64
        ctx.add("ld", 16 * NLD_T)

        def ld_cnt_w1(t4, fc):
            return 2 + t4 * 64 + fc + 1

        def ld_cnt_w2(t4, fc):
            return 2 + t4 * 64 + 32 + fc + 1

        @blk.tensor
        def _(t):
            t.wait_ge(ctx.s("tr"), tr_ready)
            t.wait_ge(ctx.s("ld"), ld0 + 16 * 2)
            for t4 in range(4):
                if t4 >= 1:
                    t.wait_ge(ctx.s("ev"), ev0 + 8 * t4)
                for fc in range(32):
                    t.wait_ge(ctx.s("ld"), ld0 + 16 * ld_cnt_w1(t4, fc))
                    bk = 2 + (fc % 4)
                    if t4 * 32 + fc >= 4:
                        t.wait_ge(ctx.s("sc"), sc0 + t4 * 32 + fc - 3)
                    slot = sb["w1s"][:, 1024 * (fc % 2):1024 * (fc % 2 + 1)]
                    for dc in range(8):
                        t.matmul(pbank(bk), slot[:, 128 * dc:128 * (dc + 1)],
                                 sb["xnT"][:, 2048 * dc + 512 * t4:2048 * dc + 512 * (t4 + 1)],
                                 start=(dc == 0), stop=False)
                    t.matmul(pbank(bk), sb["bias"][0:1, 128 * fc:128 * (fc + 1)],
                             sb["onesb"][0:1, 0:512], start=False,
                             stop=True).then_inc(ctx.s("mm"), 1)
                t.wait_ge(ctx.s("sc"), sc0 + 32 * (t4 + 1))
                if t4 >= 1:
                    t.wait_ge(ctx.s("ev"), ev0 + 8 * t4)
                for fc in range(32):
                    t.wait_ge(ctx.s("ld"), ld0 + 16 * ld_cnt_w2(t4, fc))
                    slot = sb["w2s"][:, 1024 * (fc % 2):1024 * (fc % 2 + 1)]
                    last = None
                    for tt4 in range(4):
                        for ds in range(2):
                            bk = 2 * tt4 + ds
                            last = t.matmul(pbank(bk),
                                            sb["htl2"][:, 512 * fc + 128 * tt4:512 * fc + 128 * (tt4 + 1)],
                                            slot[:, 512 * ds:512 * (ds + 1)],
                                            start=(fc == 0), stop=False)
                    last.then_inc(ctx.s("mm"), 1)
                last = None
                for tt4 in range(4):
                    for ds in range(2):
                        bk = 2 * tt4 + ds
                        last = t.matmul(pbank(bk), sb["onesb"][0:1, 0:128],
                                        sb["bias2"][0:1, 512 * ds:512 * (ds + 1)],
                                        start=False, stop=True)
                last.then_inc(ctx.s("mm"), 1)

        @blk.scalar
        def _(sc):
            for t4 in range(4):
                for fc in range(32):
                    sc.wait_ge(ctx.s("mm"), mm0 + ht_mm_idx(t4, fc))
                    bk = 2 + (fc % 4)
                    sc.activation(sb["htl2"][:, 512 * fc:512 * (fc + 1)], pbank(bk),
                                  AF.Gelu, bias=0.0, scale=1.0).then_inc(ctx.s("sc"), 1)

        @blk.vector
        def _(v):
            for t4 in range(4):
                v.wait_ge(ctx.s("mm"), mm0 + ybias_mm_idx(t4))
                for tt4 in range(4):
                    for ds in range(2):
                        bk = 2 * tt4 + ds
                        tt = 4 * t4 + tt4
                        dst = sb["stream"][:, 1024 * tt + 512 * ds:1024 * tt + 512 * (ds + 1)]
                        v.tensor_add(dst, dst, pbank(bk)).then_inc(ctx.s("ev"), 1)

        ctx.add("mm", 4 * 65)
        ctx.add("sc", 4 * 32)
        ctx.add("ev", 4 * 8)



    def ph_l1_qkv(blk):
        ld0, mm0, ev0, st0 = ctx.v("ld"), ctx.v("mm"), ctx.v("ev"), ctx.v("st")
        tr_ready = ctx.v("tr")
        qkv_in = dram["a2a_qkv_in"].ap()

        @blk.gpsimd
        def _(g):
            for dc in range(8):
                g.dma_start(out=sb["wa"][:, 1024 * dc:1024 * (dc + 1)],
                            in_=dram["w_l1_wqT"].ap()[128 * dc:128 * (dc + 1), :]).then_inc(ctx.s("ld"), 16)
            for dc in range(8):
                g.dma_start(out=sb["wb"][:, 256 * dc:256 * (dc + 1)],
                            in_=dram["w_l1_wkT"].ap()[128 * dc:128 * (dc + 1), :]).then_inc(ctx.s("ld"), 16)
            for dc in range(8):
                g.dma_start(out=sb["wb"][:, 2048 + 256 * dc:2048 + 256 * (dc + 1)],
                            in_=dram["w_l1_wvT"].ap()[128 * dc:128 * (dc + 1), :]).then_inc(ctx.s("ld"), 16)
        ctx.add("ld", 16 * 24)

        cum_st = []
        tot = 0
        for gq in range(32):
            tot += 16
            cum_st.append(tot)
        for gk in range(8):
            tot += 16 * 4
            cum_st.append(tot)
        for gv in range(16):
            tot += 16 * 8
            cum_st.append(tot)

        @blk.tensor
        def _(t):
            t.wait_ge(ctx.s("tr"), tr_ready)
            t.wait_ge(ctx.s("ld"), ld0 + 16 * 8)
            g = 0
            for ic in range(8):
                for t4 in range(4):
                    bk = g % 2
                    if g >= 2:
                        t.wait_ge(ctx.s("ev"), ev0 + g - 1)
                    last = None
                    for dc in range(8):
                        last = t.matmul(pbank(bk),
                                        sb["wa"][:, 1024 * dc + 128 * ic:1024 * dc + 128 * (ic + 1)],
                                        sb["xnT"][:, 2048 * dc + 512 * t4:2048 * dc + 512 * (t4 + 1)],
                                        start=(dc == 0), stop=(dc == 7))
                    last.then_inc(ctx.s("mm"), 1)
                    g += 1
            t.wait_ge(ctx.s("ld"), ld0 + 16 * 16)
            for ic2 in range(2):
                for t4 in range(4):
                    bk = g % 2
                    t.wait_ge(ctx.s("ev"), ev0 + g - 1)
                    last = None
                    for dc in range(8):
                        last = t.matmul(pbank(bk),
                                        sb["wb"][:, 256 * dc + 128 * ic2:256 * dc + 128 * (ic2 + 1)],
                                        sb["xnT"][:, 2048 * dc + 512 * t4:2048 * dc + 512 * (t4 + 1)],
                                        start=(dc == 0), stop=(dc == 7))
                    last.then_inc(ctx.s("mm"), 1)
                    g += 1
            t.wait_ge(ctx.s("ld"), ld0 + 16 * 24)
            for tt in range(NT):
                bk = g % 2
                t.wait_ge(ctx.s("ev"), ev0 + g - 1)
                last = None
                for dc in range(8):
                    last = t.matmul(pbank(bk, 0, 128, 256),
                                    sb["xnT"][:, 2048 * dc + 128 * tt:2048 * dc + 128 * (tt + 1)],
                                    sb["wb"][:, 2048 + 256 * dc:2048 + 256 * (dc + 1)],
                                    start=(dc == 0), stop=(dc == 7))
                last.then_inc(ctx.s("mm"), 1)
                g += 1

        @blk.vector
        def _(v):
            for g in range(56):
                v.wait_ge(ctx.s("mm"), mm0 + g + 1)
                if g >= 4:
                    v.wait_ge(ctx.s("st"), st0 + cum_st[g - 4])
                q = 512 * (g % 4)
                w = 512 if g < 40 else 256
                v.tensor_copy(sb["scr"][:, q:q + w], pbank(g % 2, 0, 128, w)).then_inc(ctx.s("ev"), 1)

        @blk.sync
        def _(s):
            g = 0
            for ic in range(8):
                for t4 in range(4):
                    s.wait_ge(ctx.s("ev"), ev0 + g + 1)
                    q = 512 * (g % 4)
                    dst = qkv_in[ic, 0:SEG_Q].rearrange("(p n) -> p n", p=128)[:, 512 * t4:512 * (t4 + 1)]
                    s.dma_start(out=dst, in_=sb["scr"][:, q:q + 512]).then_inc(ctx.s("st"), 16)
                    g += 1
            for ic2 in range(2):
                for t4 in range(4):
                    s.wait_ge(ctx.s("ev"), ev0 + g + 1)
                    q = 512 * (g % 4)
                    for half in range(2):
                        for dup in range(2):
                            blkid = 4 * ic2 + 2 * half + dup
                            dst = qkv_in[blkid, SEG_Q:SEG_Q + 64 * 2048]
                            dst = dst.rearrange("(p n) -> p n", p=64)[:, 512 * t4:512 * (t4 + 1)]
                            s.dma_start(out=dst,
                                        in_=sb["scr"][64 * half:64 * (half + 1), q:q + 512]).then_inc(ctx.s("st"), 16)
                    g += 1
            for tt in range(NT):
                s.wait_ge(ctx.s("ev"), ev0 + g + 1)
                q = 512 * (g % 4)
                for gv in range(4):
                    for dup in range(2):
                        blkid = 2 * gv + dup
                        dst = qkv_in[blkid, SEG_Q + 64 * 2048 + 8192 * tt:
                                     SEG_Q + 64 * 2048 + 8192 * (tt + 1)]
                        dst = dst.rearrange("(p n) -> p n", p=128)
                        s.dma_start(out=dst,
                                    in_=sb["scr"][:, q + 64 * gv:q + 64 * (gv + 1)]).then_inc(ctx.s("st"), 16)
                g += 1

        ctx.add("mm", 56)
        ctx.add("ev", 56)
        ctx.add("st", 16 * (32 + 32 + 128))

    def ph_l1_rope(blk, b):
        ld0, mm0, ev0 = ctx.v("ld"), ctx.v("mm"), ctx.v("ev")
        qkv_out = dram["a2a_qkv_out"].ap()

        @blk.sync
        def _(s):
            for i in range(4):
                srcb = qkv_out[4 * b + i, 0:SEG_Q].rearrange("(p n) -> p n", p=128)
                s.dma_start(out=sb["qtp"][:, 2048 * i:2048 * (i + 1)], in_=srcb).then_inc(ctx.s("ld"), 16)
            for i in range(4):
                srcb = qkv_out[4 * b + i, SEG_Q:SEG_Q + 64 * 2048].rearrange("(p n) -> p n", p=64)
                s.dma_start(out=sb["ktp"][0:64, 2048 * i:2048 * (i + 1)], in_=srcb).then_inc(ctx.s("ld"), 16)
                s.dma_start(out=sb["ktp"][64:128, 2048 * i:2048 * (i + 1)], in_=srcb).then_inc(ctx.s("ld"), 16)
            vtv = sb["vt1"].rearrange("p (c w) -> p c w", w=64)
            for i in range(4):
                srcb = qkv_out[4 * b + i, SEG_Q + 64 * 2048:SEG_Q + 64 * 2048 + 16 * 8192]
                srcb = srcb.rearrange("(c p n) -> c p n", c=16, p=128)
                s.dma_start(out=vtv[:, 16 * i:16 * (i + 1), :],
                            in_=srcb.transpose([1, 0, 2])).then_inc(ctx.s("ld"), 16)
            s.dma_start(out=sb["cosr"][:, :], in_=dram["c_cosd"].ap()).then_inc(ctx.s("ld"), 16)
            s.dma_start(out=sb["sinr"][:, :], in_=dram["c_sind"].ap()).then_inc(ctx.s("ld"), 16)
            s.dma_start(out=sb["rs2"][:, 0:128], in_=dram["c_rotm"].ap()).then_inc(ctx.s("ld"), 16)
        ctx.add("ld", 16 * 19)

        chunks = [(tgt, c) for tgt in ("qtp", "ktp") for c in range(16)]

        @blk.tensor
        def _(t):
            t.wait_ge(ctx.s("ld"), ld0 + 16 * 19)
            for r, (tgt, c) in enumerate(chunks):
                bk = r % 2
                if r >= 2:
                    t.wait_ge(ctx.s("ev"), ev0 + 3 * (r - 2) + 1)
                t.matmul(pbank(bk), sb["rs2"][:, 0:128],
                         sb[tgt][:, 512 * c:512 * (c + 1)],
                         start=True, stop=True).then_inc(ctx.s("mm"), 1)

        @blk.vector
        def _(v):
            for r, (tgt, c) in enumerate(chunks):
                bk = r % 2
                cs = slice(512 * c, 512 * (c + 1))
                v.wait_ge(ctx.s("mm"), mm0 + r + 1)
                v.tensor_mul(sb["rs1"][:, 0:512], pbank(bk), sb["sinr"][:, cs]).then_inc(ctx.s("ev"), 1)
                v.tensor_mul(sb["rs1"][:, 512:1024], sb[tgt][:, cs],
                             sb["cosr"][:, cs]).then_inc(ctx.s("ev"), 1)
                v.wait_ge(ctx.s("ev"), ev0 + 3 * r + 2)
                v.tensor_add(sb[tgt][:, cs], sb["rs1"][:, 0:512],
                             sb["rs1"][:, 512:1024]).then_inc(ctx.s("ev"), 1)

        ctx.add("mm", 32)
        ctx.add("ev", 96)

    def ph_l1_attn(blk, b):
        ld0, mm0, ev0, st0 = ctx.v("ld"), ctx.v("mm"), ctx.v("ev"), ctx.v("st")

        @blk.gpsimd
        def _(g):
            g.dma_start(out=sb["l1m"][:, :], in_=dram["c_l1maskT"].ap()).then_inc(ctx.s("ld"), 16)
            g.dma_start(out=sb["l1q"][:, :], in_=dram["c_l1qdec"].ap()).then_inc(ctx.s("ld"), 16)
            g.dma_start(out=sb["l1k"][:, :], in_=dram["c_l1kdec"].ap()).then_inc(ctx.s("ld"), 16)
            g.dma_start(out=sb["l1b"][:, :], in_=dram["c_l1bdec"].ap()).then_inc(ctx.s("ld"), 16)
        ctx.add("ld", 16 * 4)
        ld_all = ctx.v("ld")

        @blk.tensor
        def _(t):
            t.wait_ge(ctx.s("ld"), ld_all)
            for c in range(64):
                bk = c % 2
                if c >= 2:
                    t.wait_ge(ctx.s("ev"), ev0 + (c - 2) + 1)
                t.transpose(pbank(bk, 0, 128, 32).bitcast(BF16),
                            sb["ktp"][0:64, 128 * c:128 * (c + 1)],
                            sb["ident"][0:64, 0:64]).then_inc(ctx.s("mm"), 1)

        @blk.vector
        def _(v):
            for c in range(64):
                v.wait_ge(ctx.s("mm"), mm0 + c + 1)
                v.tensor_copy(sb["ktok"][:, 64 * c:64 * (c + 1)],
                              pbank(c % 2, 0, 128, 32).bitcast(BF16)).then_inc(ctx.s("ev"), 1)
            v.wait_ge(ctx.s("ev"), ev0 + 64)
            qd3 = sb["l1q"][:, :].unsqueeze(1).broadcast_to([128, 64, 128])
            v.tensor_mul(sb["qdt"].rearrange("p (c w) -> p c w", w=128),
                         sb["qtp"].rearrange("p (c w) -> p c w", w=128), qd3)
            v.tensor_scalar(sb["kdt"][:, 0:4096], sb["ktok"][:, :],
                            sb["l1k"][:, 0:1], None, ALU.mult)
            v.tensor_scalar(sb["kdt"][:, 4096:8192], sb["ktok"][:, :],
                            sb["l1k"][:, 1:2], None, ALU.mult).then_inc(ctx.s("ev"), 1)
            v.memset(sb["kvf"][:, :], 0.0)
            v.memset(sb["kvp"][:, :], 0.0).then_inc(ctx.s("ev"), 1)

        ctx.add("mm", 64)
        ctx.add("ev", 66)
        pre_ev = ctx.v("ev")
        pre_mm = ctx.v("mm")

        @blk.tensor
        def _(t):
            t.wait_ge(ctx.s("ev"), pre_ev)
            for c in range(64):
                cs = slice(128 * c, 128 * (c + 1))
                vs = slice(64 * c, 64 * (c + 1))
                if c >= 1:
                    t.wait_ge(ctx.s("ev"), pre_ev + 4 * (c - 1) + 1)
                t.matmul(pbank(0, 0, 128, 128), sb["ktp"][0:64, cs], sb["qtp"][0:64, cs],
                         start=True, stop=True)
                t.matmul(pbank(1, 0, 128, 128), sb["ktp"][64:128, cs], sb["qtp"][64:128, cs],
                         start=True, stop=True).then_inc(ctx.s("mm"), 1)
                if c >= 1:
                    t.wait_ge(ctx.s("ev"), pre_ev + 4 * (c - 1) + 2)
                t.matmul(pbank(2, 0, 64, 64), sb["kdt"][:, vs], sb["vt1"][:, vs],
                         start=True, stop=True)
                t.matmul(pbank(3, 0, 64, 64), sb["kdt"][:, 4096 + 64 * c:4096 + 64 * (c + 1)],
                         sb["vt1"][:, vs], start=True, stop=True).then_inc(ctx.s("mm"), 1)
                t.wait_ge(ctx.s("ev"), pre_ev + 4 * c + 1)
                if c >= 1:
                    t.wait_ge(ctx.s("ev"), pre_ev + 4 * (c - 1) + 3)
                    t.wait_ge(ctx.s("ev"), pre_ev + 4 * (c - 1) + 4)
                t.matmul(pbank(4, 0, 64, 128), sb["vt1"][:, vs], sb["st1"][:, 0:128],
                         start=True, stop=False)
                t.matmul(pbank(4, 0, 64, 128), sb["kvp"][:, 0:64],
                         sb["qdt"][:, cs], start=False, stop=True)
                t.matmul(pbank(5, 0, 64, 128), sb["vt1"][:, vs], sb["st1"][:, 128:256],
                         start=True, stop=False)
                t.matmul(pbank(5, 0, 64, 128), sb["kvp"][:, 64:128],
                         sb["qdt"][:, cs], start=False, stop=True).then_inc(ctx.s("mm"), 1)

        @blk.vector
        def _(v):
            nst = 0
            for c in range(64):
                v.wait_ge(ctx.s("mm"), pre_mm + 3 * c + 1)
                if c >= 1:
                    v.wait_ge(ctx.s("mm"), pre_mm + 3 * (c - 1) + 3)
                v.tensor_mul(sb["st1"][:, 0:128], pbank(0, 0, 128, 128), sb["l1m"][:, 0:128])
                v.tensor_mul(sb["st1"][:, 128:256], pbank(1, 0, 128, 128),
                             sb["l1m"][:, 128:256]).then_inc(ctx.s("ev"), 1)
                v.wait_ge(ctx.s("mm"), pre_mm + 3 * c + 2)
                v.scalar_tensor_tensor(sb["kvf"][0:64, 0:64], sb["kvf"][0:64, 0:64],
                                       sb["l1b"][0:64, 0:1], pbank(2, 0, 64, 64),
                                       ALU.mult, ALU.add)
                v.scalar_tensor_tensor(sb["kvf"][64:128, 64:128], sb["kvf"][64:128, 64:128],
                                       sb["l1b"][64:128, 1:2], pbank(3, 0, 64, 64),
                                       ALU.mult, ALU.add).then_inc(ctx.s("ev"), 1)
                v.wait_ge(ctx.s("ev"), pre_ev + 4 * c + 2)
                v.tensor_copy(sb["kvp"][0:64, 0:64], sb["kvf"][0:64, 0:64])
                v.tensor_copy(sb["kvp"][64:128, 64:128],
                              sb["kvf"][64:128, 64:128]).then_inc(ctx.s("ev"), 1)
                v.wait_ge(ctx.s("mm"), pre_mm + 3 * c + 3)
                h = 512 * ((c // 4) % 2)
                if c >= 8 and c % 4 == 0:
                    v.wait_ge(ctx.s("st"), st0 + 16 * (c // 4 - 1))
                v.tensor_copy(sb["aot2"][0:64, h + 128 * (c % 4):h + 128 * (c % 4 + 1)],
                              pbank(4, 0, 64, 128))
                v.tensor_copy(sb["aot2"][64:128, h + 128 * (c % 4):h + 128 * (c % 4 + 1)],
                              pbank(5, 0, 64, 128)).then_inc(ctx.s("ev"), 1)

        @blk.gpsimd
        def _(g):
            for c4 in range(16):
                g.wait_ge(ctx.s("ev"), pre_ev + 4 * (4 * c4 + 3) + 4)
                h = 512 * (c4 % 2)
                blkv = dram["a2a_ao_in"].ap()[4 * b + c4 // 4, :].rearrange("(p n) -> p n", p=128)
                dst = blkv[:, 512 * (c4 % 4):512 * (c4 % 4 + 1)]
                g.dma_start(out=dst, in_=sb["aot2"][:, h:h + 512]).then_inc(ctx.s("st"), 16)

        ctx.add("mm", 3 * 64)
        ctx.add("ev", 4 * 64)
        ctx.add("st", 16 * 16)


    def ph_l1_sum(blk):
        ld0, mm0, ev0, st0 = ctx.v("ld"), ctx.v("mm"), ctx.v("ev"), ctx.v("st")
        parts = [("w_l1_w1T_part", 2, DFF), ("w_l1_w2T_part", 8, D)]
        abscol = sb["stat"][:, 0:10]
        par2 = sb["stat"][:, 10:12]

        @blk.gpsimd
        def _(g):
            k = 0
            for wname, n, W in parts:
                for t in range(n):
                    if k >= 1:
                        g.wait_ge(ctx.s("ev"), ev0 + 1 + k)
                    g.dma_start(out=sb["wfl1"].bitcast(F32)[:, 0:W],
                                in_=dram[wname].ap()[128 * t:128 * (t + 1), :]).then_inc(ctx.s("ld"), 16)
                    k += 1
            g.wait_ge(ctx.s("ev"), ev0 + 13)
            g.dma_start(out=dram["l1sum_io"].ap(),
                        in_=sb["rec"][0:1, 8:10]).then_inc(ctx.s("st"), 16)
            g.wait_ge(ctx.s("st"), st0 + 16)

        @blk.vector
        def _(v):
            v.memset(sb["cst"][:, 3:4], 1.0).then_inc(ctx.s("ev"), 1)
            k = 0
            for wname, n, W in parts:
                for t in range(n):
                    v.wait_ge(ctx.s("ld"), ld0 + 16 * (k + 1))
                    v.reduce_sum(abscol[:, k:k + 1], sb["wfl1"].bitcast(F32)[:, 0:W],
                                 axis=AX.X, apply_absolute_value=True).then_inc(ctx.s("ev"), 1)
                    k += 1
            v.wait_ge(ctx.s("ev"), ev0 + 11)
            v.reduce_sum(par2[:, 0:1], abscol[:, 0:2], axis=AX.X)
            v.reduce_sum(par2[:, 1:2], abscol[:, 2:10], axis=AX.X).then_inc(ctx.s("ev"), 1)
            v.wait_ge(ctx.s("mm"), mm0 + 1)
            v.tensor_copy(sb["rec"][0:1, 8:10], ps[0:1, 0:2]).then_inc(ctx.s("ev"), 1)

        @blk.tensor
        def _(t):
            t.wait_ge(ctx.s("ev"), ev0 + 12)
            t.matmul(ps[0:1, 0:2], sb["cst"][:, 3:4], par2,
                     start=True, stop=True).then_inc(ctx.s("mm"), 1)

        ctx.add("ld", 16 * 10)
        ctx.add("st", 16)
        ctx.add("mm", 1)
        ctx.add("ev", 13)

    def ph_l1_ar(blk):
        st_all = ctx.v("st")

        @blk.gpsimd
        def _(g):
            g.wait_ge(ctx.s("st"), st_all)
            with allow_ici_all_to_all():
                g.collective_compute(
                    "AllReduce", ALU.add,
                    replica_groups=[[0, 1, 2, 3], [4, 5, 6, 7]],
                    ins=[dram["l1sum_io"].ap().opt()],
                    outs=[dram["l1sum_out"].ap().opt()],
                ).then_inc(ctx.s("cc"), 1)
            g.wait_ge(ctx.s("cc"), ctx.add("cc"))

    def ph_l1_tern(blk, wpart, n_tiles, W, qpart, sw_col, sum_col):
        ld0, mm0, ev0, st0 = ctx.v("ld"), ctx.v("mm"), ctx.v("ev"), ctx.v("st")
        n = n_tiles
        swb = sb["stat"][:, 12:13]
        wf = sb["wfl1"].bitcast(F32)[:, 0:W]
        qf = sb["qfl1"].bitcast(F32)[:, 0:W]
        i8 = sb["i8l1"].bitcast(I8)[:, 0:W]
        qb = sb["qbl1"][:, 0:W]
        base2 = ev0 + 4

        def p1_mark(t):
            return base2 + 3 * t + 1

        def p2_mark(t):
            return base2 + 3 * t + 2

        def cp_mark(t):
            return base2 + 3 * t + 3

        @blk.gpsimd
        def _(g):
            g.dma_start(out=sb["rec"][0:1, 12:14],
                        in_=dram["l1sum_out"].ap()).then_inc(ctx.s("ld"), 16)
            for t in range(n):
                if t >= 1:
                    g.wait_ge(ctx.s("ev"), p1_mark(t - 1))
                g.dma_start(out=wf,
                            in_=dram[wpart].ap()[128 * t:128 * (t + 1), :]).then_inc(ctx.s("ld"), 16)

        @blk.sync
        def _(s):
            for t in range(n):
                s.wait_ge(ctx.s("ev"), cp_mark(t))
                s.dma_start(out=dram[qpart].ap()[128 * t:128 * (t + 1), :],
                            in_=qb).then_inc(ctx.s("st"), 16)

        @blk.tensor
        def _(t):
            t.wait_ge(ctx.s("ev"), ev0 + 3)
            t.matmul(pbank(1, 0, 128, 1), sb["onesf"][0:1, 0:128],
                     sb["rec"][0:1, 0:1], start=True, stop=True)
            t.matmul(pbank(2, 0, 128, 1), sb["onesf"][0:1, 0:128],
                     sb["rec"][0:1, 1:2], start=True, stop=True).then_inc(ctx.s("mm"), 1)

        @blk.vector
        def _(v):
            v.wait_ge(ctx.s("ld"), ld0 + 16)
            v.tensor_scalar(sb["rec"][0:1, 1:2],
                            sb["rec"][0:1, 12 + sum_col:13 + sum_col],
                            1.0 / (D * DFF), 1e-5, ALU.mult,
                            ALU.max).then_inc(ctx.s("ev"), 1)
            v.wait_ge(ctx.s("ev"), ev0 + 1)
            v.reciprocal(sb["rec"][0:1, 0:1], sb["rec"][0:1, 1:2]).then_inc(ctx.s("ev"), 1)
            v.wait_ge(ctx.s("ev"), ev0 + 2)
            v.engine_nop().then_inc(ctx.s("ev"), 1)
            v.wait_ge(ctx.s("mm"), mm0 + 1)
            v.tensor_copy(swb, pbank(1, 0, 128, 1))
            v.tensor_copy(sb["swc"][:, sw_col:sw_col + 1],
                          pbank(2, 0, 128, 1)).then_inc(ctx.s("ev"), 1)
            v.wait_ge(ctx.s("ev"), base2)
            for t in range(n):
                v.wait_ge(ctx.s("ld"), ld0 + 16 * (t + 2))
                v.tensor_scalar(qf, wf, swb, -1.0, ALU.mult,
                                ALU.max).then_inc(ctx.s("ev"), 1)
                v.wait_ge(ctx.s("ev"), p1_mark(t))
                v.tensor_scalar(i8, qf, 1.0, None, ALU.min).then_inc(ctx.s("ev"), 1)
                v.wait_ge(ctx.s("ev"), p2_mark(t))
                if t >= 1:
                    v.wait_ge(ctx.s("st"), st0 + 16 * t)
                v.tensor_copy(qb, i8).then_inc(ctx.s("ev"), 1)

        ctx.add("ld", 16 * (n + 1))
        ctx.add("st", 16 * n)
        ctx.add("mm", 1)
        ctx.add("ev", 4 + 3 * n)

    def ph_l1_agq(blk, part, dstq, prev=None):
        st_all = ctx.v("st")

        @blk.gpsimd
        def _(g):
            g.wait_ge(ctx.s("st"), st_all)
            if prev is not None:
                g.wait_ge(ctx.s("cc"), cc_marks[prev])
            with allow_ici_all_to_all():
                g.collective_compute(
                    "AllGather", ALU.bypass,
                    replica_groups=[[0, 1, 2, 3], [4, 5, 6, 7]],
                    ins=[dram[part].ap().opt()],
                    outs=[dram[dstq].ap().opt()],
                ).then_inc(ctx.s("cc"), 1)
            cc_marks[dstq] = ctx.add("cc")
            g.wait_ge(ctx.s("cc"), cc_marks[dstq])

    def ph_l1_quantx(blk):
        ld0, ev0, sc0, st0 = ctx.v("ld"), ctx.v("ev"), ctx.v("sc"), ctx.v("st")
        ss = sb["stat"][:, 64:80]
        rstd1 = sb["stat"][:, 80:96]
        sA = sb["stat"][:, 96:112]
        sq1 = sb["stat"][:, 112:128]
        mcol = sb["stat"][:, 128:144]
        xb = sb["xb1"].bitcast(F32)[:, 0:D]
        xh = sb["xh1"].bitcast(F32)[:, 0:D]
        i8 = sb["i81"].bitcast(I8)[:, 0:D]
        xq = sb["xqb1"][:, 0:D]
        n1 = sb["n1r"].bitcast(F32)[:, 0:D]

        def e_(t):
            return ev0 + 10 * t

        @blk.gpsimd
        def _(g):
            g.dma_start(out=n1, in_=dram["c_n1rep"].ap()).then_inc(ctx.s("ld"), 16)
            g.dma_start(out=sb["l1s"][:, :], in_=dram["c_l1s"].ap()).then_inc(ctx.s("ld"), 16)
            for t in range(NT):
                if t >= 1:
                    g.wait_ge(ctx.s("ev"), e_(t - 1) + 3)
                g.dma_start(out=xb,
                            in_=dram["stage_xn32"].ap()[128 * t:128 * (t + 1), :]).then_inc(ctx.s("ld"), 16)

        @blk.scalar
        def _(sc):
            for t in range(NT):
                sc.wait_ge(ctx.s("ld"), ld0 + 16 * (t + 3))
                sc.activation(ps[:, 2048:3072], xb, AF.Square,
                              accum_out=ss[:, t:t + 1]).then_inc(ctx.s("sc"), 1)
                sc.wait_ge(ctx.s("ev"), e_(t) + 1)
                sc.activation(mcol[:, t:t + 1], rstd1[:, t:t + 1], AF.Sqrt,
                              bias=np.float32(0.0), scale=1.0).then_inc(ctx.s("sc"), 1)

        @blk.vector
        def _(v):
            for t in range(NT):
                v.wait_ge(ctx.s("sc"), sc0 + 2 * t + 1)
                v.tensor_scalar(rstd1[:, t:t + 1], ss[:, t:t + 1], 1.0 / D, 1e-8,
                                ALU.mult, ALU.add).then_inc(ctx.s("ev"), 1)
                v.wait_ge(ctx.s("sc"), sc0 + 2 * t + 2)
                v.reciprocal(rstd1[:, t:t + 1], mcol[:, t:t + 1]).then_inc(ctx.s("ev"), 1)
                v.wait_ge(ctx.s("ev"), e_(t) + 2)
                v.scalar_tensor_tensor(xh, xb, rstd1[:, t:t + 1], n1,
                                       ALU.mult, ALU.mult).then_inc(ctx.s("ev"), 1)
                v.wait_ge(ctx.s("ev"), e_(t) + 3)
                v.reduce_max(mcol[:, t:t + 1], xh, axis=AX.X,
                             apply_absolute_value=True).then_inc(ctx.s("ev"), 1)
                v.wait_ge(ctx.s("ev"), e_(t) + 4)
                v.tensor_scalar(mcol[:, t:t + 1], mcol[:, t:t + 1], 1e-5, None,
                                ALU.max).then_inc(ctx.s("ev"), 1)
                v.wait_ge(ctx.s("ev"), e_(t) + 5)
                v.reciprocal(sq1[:, t:t + 1], mcol[:, t:t + 1]).then_inc(ctx.s("ev"), 1)
                v.wait_ge(ctx.s("ev"), e_(t) + 6)
                v.tensor_scalar(sq1[:, t:t + 1], sq1[:, t:t + 1], 127.0, None,
                                ALU.mult)
                v.scalar_tensor_tensor(mcol[:, t:t + 1], mcol[:, t:t + 1],
                                       1.0 / 127.0, sb["swc"][:, 0:1],
                                       ALU.mult, ALU.mult).then_inc(ctx.s("ev"), 1)
                v.wait_ge(ctx.s("ev"), e_(t) + 7)
                v.tensor_mul(sA[:, t:t + 1], mcol[:, t:t + 1],
                             mcol[:, t:t + 1]).then_inc(ctx.s("ev"), 1)
                v.wait_ge(ctx.s("ev"), e_(t) + 8)
                v.tensor_scalar(sA[:, t:t + 1], sA[:, t:t + 1],
                                sb["l1s"][:, 0:1], None, ALU.mult)
                v.tensor_scalar(i8, xh, sq1[:, t:t + 1], None,
                                ALU.mult).then_inc(ctx.s("ev"), 1)
                v.wait_ge(ctx.s("ev"), e_(t) + 9)
                if t >= 1:
                    v.wait_ge(ctx.s("st"), st0 + 16 * t)
                v.tensor_copy(xq, i8).then_inc(ctx.s("ev"), 1)

        @blk.sync
        def _(s):
            for t in range(NT):
                s.wait_ge(ctx.s("ev"), e_(t) + 10)
                s.dma_start(out=dram["stage_xn"].ap()[128 * t:128 * (t + 1), :],
                            in_=xq).then_inc(ctx.s("st"), 16)
            s.wait_ge(ctx.s("st"), st0 + 16 * NT)
            for dc in range(8):
                s.dma_start(out=sb["xnT"][:, 2048 * dc:2048 * (dc + 1)],
                            in_=dram["stage_xn"].ap()[:, 128 * dc:128 * (dc + 1)],
                            transpose=True).then_inc(ctx.s("tr"), 16)

        ctx.add("ld", 16 * (NT + 2))
        ctx.add("st", 16 * NT)
        ctx.add("sc", 2 * NT)
        ctx.add("ev", 10 * NT)
        ctx.add("tr", 16 * 8)

    def ph_l1_mm1(blk):
        ld0, mm0, ev0, sc0, st0 = (ctx.v("ld"), ctx.v("mm"), ctx.v("ev"),
                                   ctx.v("sc"), ctx.v("st"))
        tr_ready = ctx.v("tr")
        ss2a = sb["stat"][:, 0:16]
        rstd2 = sb["stat"][:, 16:32]
        m2 = sb["stat"][:, 32:48]
        sq2 = sb["stat"][:, 48:64]
        ss2b = sb["stat"][:, 64:80]
        sA = sb["stat"][:, 96:112]
        mcol = sb["stat"][:, 128:144]
        rs2 = sb["stat"][:, 144:160]
        n2 = sb["n2r"].bitcast(F32)[:, 0:DFF]
        i8 = sb["i8h"].bitcast(I8)[:, 0:DFF]

        def aF(sub):
            return (sb["a0"] if sub == 0 else sb["a1"]).bitcast(F32)[:, 0:DFF]

        def araw(sub):
            return (sb["a0"] if sub == 0 else sb["a1"])[:, 0:DFF]

        def evtp0(tp):
            return ev0 + 68 * tp

        def qb_(tp, sub):
            return evtp0(tp) + 48 + 10 * sub

        @blk.gpsimd
        def _(g):
            g.dma_start(out=n2, in_=dram["c_n2rep"].ap()).then_inc(ctx.s("ld"), 16)
            if "wq1q" in cc_marks:
                g.wait_ge(ctx.s("cc"), cc_marks["wq1q"])
            for k in range(64):
                if k >= 2:
                    g.wait_ge(ctx.s("mm"), mm0 + 2 * (k - 1))
                hs = k % 8
                slab = sb["w1sl"].rearrange("p (c n) -> p c n", n=512)
                srcw = dram["wq1q"].ap()[:, 512 * hs:512 * (hs + 1)]
                srcw = srcw.rearrange("(c p) n -> p c n", p=128)
                g.dma_start(out=slab[:, 8 * (k % 2):8 * (k % 2) + 8, :],
                            in_=srcw).then_inc(ctx.s("ld"), 16)

        @blk.tensor
        def _(t):
            t.wait_ge(ctx.s("tr"), tr_ready)
            for tp in range(8):
                for hs in range(8):
                    k = 8 * tp + hs
                    t.wait_ge(ctx.s("ld"), ld0 + 16 * (k + 2))
                    slab = sb["w1sl"][:, 4096 * (k % 2):4096 * (k % 2) + 4096]
                    for sub in range(2):
                        tt = 2 * tp + sub
                        gidx = 16 * tp + 2 * hs + sub
                        bk = gidx % 4
                        if gidx >= 4:
                            g4 = gidx - 4
                            t.wait_ge(ctx.s("ev"),
                                      evtp0(g4 // 16) + 3 * (g4 % 16) + 1)
                        last = None
                        for dc in range(8):
                            last = t.matmul(pbank(bk),
                                            sb["xnT"][:, 2048 * dc + 128 * tt:2048 * dc + 128 * (tt + 1)],
                                            slab[:, 512 * dc:512 * (dc + 1)],
                                            start=(dc == 0), stop=(dc == 7))
                        last.then_inc(ctx.s("mm"), 1)

        @blk.scalar
        def _(sc):
            for tp in range(8):
                for sub in range(2):
                    tt = 2 * tp + sub
                    w = evtp0(tp) + 45 + 3 * sub
                    if sub == 1:
                        w = max(w, qb_(tp, 0) + 10)
                    elif tp >= 1:
                        w = max(w, qb_(tp - 1, 1) + 10)
                    sc.wait_ge(ctx.s("ev"), w)
                    sc.activation(ps[:, 2048:4096], aF(sub)[:, 0:2048], AF.Square,
                                  accum_out=ss2a[:, tt:tt + 1]).then_inc(ctx.s("sc"), 1)
                    sc.activation(ps[:, 2048:4096], aF(sub)[:, 2048:DFF], AF.Square,
                                  accum_out=ss2b[:, tt:tt + 1]).then_inc(ctx.s("sc"), 1)
                    sc.wait_ge(ctx.s("ev"), qb_(tp, sub) + 2)
                    sc.activation(m2[:, tt:tt + 1], rstd2[:, tt:tt + 1], AF.Sqrt,
                                  bias=np.float32(0.0), scale=1.0).then_inc(ctx.s("sc"), 1)

        @blk.vector
        def _(v):
            for tp in range(8):
                for hs in range(8):
                    for sub in range(2):
                        tt = 2 * tp + sub
                        gidx = 16 * tp + 2 * hs + sub
                        bk = gidx % 4
                        v.wait_ge(ctx.s("mm"), mm0 + gidx + 1)
                        if hs == 0 and tp >= 1:
                            v.wait_ge(ctx.s("st"), st0 + 16 * (2 * tp - 1 + sub))
                        asl = aF(sub)[:, 512 * hs:512 * (hs + 1)]
                        p0 = evtp0(tp) + 3 * (2 * hs + sub)
                        v.tensor_copy(asl, pbank(bk)).then_inc(ctx.s("ev"), 1)
                        v.wait_ge(ctx.s("ev"), p0 + 1)
                        v.scalar_tensor_tensor(asl, asl, 0.0, asl, ALU.max,
                                               ALU.mult).then_inc(ctx.s("ev"), 1)
                        v.wait_ge(ctx.s("ev"), p0 + 2)
                        v.tensor_scalar(asl, asl, sA[:, tt:tt + 1],
                                        sb["l1s"][:, 1:2], ALU.mult,
                                        ALU.add).then_inc(ctx.s("ev"), 1)
                for sub in range(2):
                    tt = 2 * tp + sub
                    q0 = qb_(tp, sub)
                    v.wait_ge(ctx.s("sc"), sc0 + 6 * tp + (2 if sub == 0 else 5))
                    v.scalar_tensor_tensor(rstd2[:, tt:tt + 1], ss2a[:, tt:tt + 1],
                                           1.0, ss2b[:, tt:tt + 1], ALU.mult,
                                           ALU.add).then_inc(ctx.s("ev"), 1)
                    v.wait_ge(ctx.s("ev"), q0 + 1)
                    v.tensor_scalar(rstd2[:, tt:tt + 1], rstd2[:, tt:tt + 1],
                                    1.0 / DFF, 1e-8, ALU.mult,
                                    ALU.add).then_inc(ctx.s("ev"), 1)
                    v.wait_ge(ctx.s("sc"), sc0 + 6 * tp + (3 if sub == 0 else 6))
                    v.reciprocal(rstd2[:, tt:tt + 1], m2[:, tt:tt + 1]).then_inc(ctx.s("ev"), 1)
                    v.wait_ge(ctx.s("ev"), q0 + 3)
                    v.scalar_tensor_tensor(aF(sub), aF(sub), rstd2[:, tt:tt + 1],
                                           n2, ALU.mult, ALU.mult).then_inc(ctx.s("ev"), 1)
                    v.wait_ge(ctx.s("ev"), q0 + 4)
                    v.reduce_max(m2[:, tt:tt + 1], aF(sub), axis=AX.X,
                                 apply_absolute_value=True).then_inc(ctx.s("ev"), 1)
                    v.wait_ge(ctx.s("ev"), q0 + 5)
                    v.tensor_scalar(m2[:, tt:tt + 1], m2[:, tt:tt + 1], 1e-5,
                                    None, ALU.max).then_inc(ctx.s("ev"), 1)
                    v.wait_ge(ctx.s("ev"), q0 + 6)
                    v.scalar_tensor_tensor(rs2[:, tt:tt + 1], m2[:, tt:tt + 1],
                                           1.0 / 127.0, sb["swc"][:, 1:2],
                                           ALU.mult, ALU.mult)
                    v.reciprocal(mcol[:, tt:tt + 1], m2[:, tt:tt + 1]).then_inc(ctx.s("ev"), 1)
                    v.wait_ge(ctx.s("ev"), q0 + 7)
                    v.tensor_scalar(sq2[:, tt:tt + 1], mcol[:, tt:tt + 1], 127.0,
                                    None, ALU.mult).then_inc(ctx.s("ev"), 1)
                    v.wait_ge(ctx.s("ev"), q0 + 8)
                    v.tensor_scalar(i8, aF(sub), sq2[:, tt:tt + 1], None,
                                    ALU.mult).then_inc(ctx.s("ev"), 1)
                    v.wait_ge(ctx.s("ev"), q0 + 9)
                    v.tensor_copy(araw(sub)[:, 0:4096], i8).then_inc(ctx.s("ev"), 1)

        @blk.sync
        def _(s):
            for tp in range(8):
                for sub in range(2):
                    tt = 2 * tp + sub
                    s.wait_ge(ctx.s("ev"), qb_(tp, sub) + 10)
                    s.dma_start(out=dram["stage_hq1"].ap()[128 * tt:128 * (tt + 1), :],
                                in_=araw(sub)[:, 0:4096]).then_inc(ctx.s("st"), 16)

        ctx.add("ld", 16 * 65)
        ctx.add("mm", 128)
        ctx.add("sc", 48)
        ctx.add("st", 16 * NT)
        ctx.add("ev", 68 * 8)

    def ph_l1_mm2(blk):
        ld0, mm0, ev0, tr0 = ctx.v("ld"), ctx.v("mm"), ctx.v("ev"), ctx.v("tr")
        rs2 = sb["stat"][:, 144:160]

        @blk.sync
        def _(s):
            for grp in range(4):
                for fc in range(32):
                    k = 32 * grp + fc
                    if k >= 4:
                        s.wait_ge(ctx.s("mm"), mm0 + k - 3)
                    s.dma_start(out=sb["hq1s"][:, 1024 * (k % 4):1024 * (k % 4) + 512],
                                in_=dram["stage_hq1"].ap()[512 * grp:512 * (grp + 1),
                                                           128 * fc:128 * (fc + 1)],
                                transpose=True).then_inc(ctx.s("tr"), 16)

        @blk.gpsimd
        def _(g):
            if "wq2q" in cc_marks:
                g.wait_ge(ctx.s("cc"), cc_marks["wq2q"])
            for grp in range(4):
                for fc in range(32):
                    k = 32 * grp + fc
                    if k >= 4:
                        g.wait_ge(ctx.s("mm"), mm0 + k - 3)
                    g.dma_start(out=sb["w2sl"][:, 1024 * (k % 4):1024 * (k % 4 + 1)],
                                in_=dram["wq2q"].ap()[128 * fc:128 * (fc + 1), :]).then_inc(ctx.s("ld"), 16)

        @blk.tensor
        def _(t):
            for grp in range(4):
                if grp >= 1:
                    t.wait_ge(ctx.s("ev"), ev0 + 8 * grp)
                for fc in range(32):
                    k = 32 * grp + fc
                    t.wait_ge(ctx.s("tr"), tr0 + 16 * (k + 1))
                    t.wait_ge(ctx.s("ld"), ld0 + 16 * (k + 1))
                    hq = sb["hq1s"][:, 1024 * (k % 4):1024 * (k % 4) + 512]
                    w2_ = sb["w2sl"][:, 1024 * (k % 4):1024 * (k % 4 + 1)]
                    last = None
                    for tt4 in range(4):
                        for ds in range(2):
                            last = t.matmul(pbank(2 * tt4 + ds),
                                            hq[:, 128 * tt4:128 * (tt4 + 1)],
                                            w2_[:, 512 * ds:512 * (ds + 1)],
                                            start=(fc == 0), stop=(fc == 31))
                    last.then_inc(ctx.s("mm"), 1)

        @blk.vector
        def _(v):
            for grp in range(4):
                v.wait_ge(ctx.s("mm"), mm0 + 32 * (grp + 1))
                for tt4 in range(4):
                    tt = 4 * grp + tt4
                    for ds in range(2):
                        dst = sb["stream"][:, 1024 * tt + 512 * ds:1024 * tt + 512 * (ds + 1)]
                        v.scalar_tensor_tensor(dst, pbank(2 * tt4 + ds),
                                               rs2[:, tt:tt + 1], dst,
                                               ALU.mult, ALU.add).then_inc(ctx.s("ev"), 1)

        ctx.add("mm", 128)
        ctx.add("tr", 16 * 128)
        ctx.add("ld", 16 * 128)
        ctx.add("ev", 32)

    def emit_fp4_quant(v, sc_ops, W, pairs):
        nb = W // 16
        ns = len(pairs)
        st = []
        for k, (src_ap, out_ap) in enumerate(pairs):
            st.append(dict(
                src3=src_ap.rearrange("p (b w) -> p b w", w=16),
                out3=out_ap.rearrange("p (b w) -> p b w", w=16),
                qf1=sb["qf1"].bitcast(F32)[:, k * W:(k + 1) * W],
                qf2=sb["qf2"].bitcast(F32)[:, k * W:(k + 1) * W],
                bm=sb["bmx"].bitcast(F32)[:, k * nb:(k + 1) * nb],
                scl=sb["sclq"].bitcast(F32)[:, k * nb:(k + 1) * nb],
                rcp=sb["rcpq"].bitcast(F32)[:, k * nb:(k + 1) * nb],
                i8a=sb["i8a"].bitcast(I8)[:, k * W:(k + 1) * W],
                i8b=sb["i8b"].bitcast(I8)[:, k * W:(k + 1) * W],
                i8c=sb["i8c"].bitcast(I8)[:, k * W:(k + 1) * W],
                mark=[0],
            ))

        def step(fn, wait_prev=True):
            for k in range(ns):
                s = st[k]
                if wait_prev and s["mark"][0]:
                    v.wait_ge(ctx.s("ev"), s["mark"][0])
                fn(s).then_inc(ctx.s("ev"), 1)
                s["mark"][0] = ctx.add("ev")

        step(lambda s: v.reduce_max(s["bm"], s["src3"], axis=AX.X,
                                    apply_absolute_value=True), wait_prev=False)
        step(lambda s: v.tensor_scalar(s["scl"], s["bm"], 1.0 / 6.0, 1e-12,
                                       ALU.mult, ALU.max))
        step(lambda s: v.reciprocal(s["rcp"], s["scl"]))

        def _xn(s):
            rcp3 = s["rcp"].unsqueeze(2).broadcast_to([128, nb, 16])
            return v.tensor_mul(s["qf1"].rearrange("p (b w) -> p b w", w=16),
                                s["src3"], rcp3)
        step(_xn)
        xns = []
        for k in range(ns):
            s = st[k]
            sc_ops.append((s["mark"][0], s["qf1"], s["i8a"], s["i8b"], s["i8c"]))
            s["sc_tgt"] = ctx.add("sc")
            xns.append(s["mark"][0])

        def _c1(s):
            v.wait_ge(ctx.s("sc"), s["sc_tgt"])
            v.tensor_scalar(s["qf1"], s["i8b"], -4.0, 4.0, ALU.max, ALU.min)
            return v.tensor_scalar(s["qf2"], s["i8b"], -2.0, 2.0, ALU.max, ALU.min)
        step(_c1, wait_prev=False)

        def _c2(s):
            v.tensor_sub(s["qf1"], s["qf1"], s["qf2"])
            return v.tensor_scalar(s["qf2"], s["i8c"], -3.0, 3.0, ALU.max, ALU.min)
        step(_c2)

        def _c3(s):
            v.scalar_tensor_tensor(s["qf1"], s["qf2"], 2.0, s["qf1"], ALU.mult, ALU.add)
            return v.tensor_scalar(s["qf2"], s["i8c"], -2.0, 2.0, ALU.max, ALU.min)
        step(_c3)

        def _c4(s):
            v.scalar_tensor_tensor(s["qf1"], s["qf2"], -2.0, s["qf1"], ALU.mult, ALU.add)
            return v.tensor_scalar(s["qf2"], s["i8a"], -4.0, 4.0, ALU.max, ALU.min)
        step(_c4)
        step(lambda s: v.scalar_tensor_tensor(s["qf1"], s["qf2"], 0.5, s["qf1"],
                                              ALU.mult, ALU.add))

        def _out(s):
            scl3 = s["scl"].unsqueeze(2).broadcast_to([128, nb, 16])
            return v.tensor_mul(s["out3"], s["qf1"].rearrange("p (b w) -> p b w", w=16),
                                scl3)
        step(_out)
        return [(xns[k], st[k]["mark"][0]) for k in range(ns)]

    TIE_EPS = 1.0 - 2.0 ** -11

    def emit_fp4_scalar(sc, sc_ops):
        for (xn_ev, qf1, i8a, i8b, i8c) in sc_ops:
            sc.wait_ge(ctx.s("ev"), xn_ev)
            sc.mul(i8a, qf1, 2.0 * TIE_EPS)
            sc.mul(i8b, qf1, 1.0 * TIE_EPS)
            sc.mul(i8c, qf1, 0.5 * TIE_EPS).then_inc(ctx.s("sc"), 1)

    def ph_l3_prep(blk, wname, n_tiles, Wl, _unused=0):
        ld0, st0 = ctx.v("ld"), ctx.v("st")
        sc_ops = []
        ev_marks = []
        wst = dram["wstage"].ap()

        npair = n_tiles // 2

        @blk.vector
        def _(v):
            for tp in range(npair):
                for k in (0, 1):
                    v.wait_ge(ctx.s("ld"), ld0 + 16 * (2 * tp + k + 1))
                    if tp >= 1:
                        v.wait_ge(ctx.s("st"), st0 + 16 * (2 * tp - 1 + k))
                wf = sb["wf32"].bitcast(F32)
                marks = emit_fp4_quant(v, sc_ops, Wl,
                                       [(wf[:, 0:Wl], sb["qbf"][:, 0:Wl]),
                                        (wf[:, Wl:2 * Wl], sb["qbf"][:, Wl:2 * Wl])])
                ev_marks.append(marks)

        @blk.scalar
        def _(sc):
            emit_fp4_scalar(sc, sc_ops)

        @blk.gpsimd
        def _(g):
            for tp in range(npair):
                for k in (0, 1):
                    t = 2 * tp + k
                    if tp >= 1:
                        g.wait_ge(ctx.s("ev"), ev_marks[tp - 1][k][0])
                    g.dma_start(out=sb["wf32"].bitcast(F32)[:, Wl * k:Wl * (k + 1)],
                                in_=dram[wname].ap()[128 * t:128 * (t + 1), :]).then_inc(ctx.s("ld"), 16)
                for k in (0, 1):
                    t = 2 * tp + k
                    g.wait_ge(ctx.s("ev"), ev_marks[tp][k][1])
                    dst = wst[128 * Wl * t: 128 * Wl * (t + 1)]
                    g.dma_start(out=dst.rearrange("(p n) -> p n", p=128),
                                in_=sb["qbf"][:, Wl * k:Wl * (k + 1)]).then_inc(ctx.s("st"), 16)

        ctx.add("ld", 16 * n_tiles)
        ctx.add("st", 16 * n_tiles)

    def ph_l3_prepT(blk, part, R, Wl):
        tr0, st0 = ctx.v("tr"), ctx.v("st")
        wst = dram["wstage"].ap()[0:R * Wl].rearrange("(r c) -> r c", c=Wl)
        chunks = []
        c0 = 0
        while c0 < Wl:
            chunks.append((c0, min(128, Wl - c0)))
            c0 += 128

        @blk.sync
        def _(s):
            s.wait_ge(ctx.s("st"), st0)
            for i, (c0, w) in enumerate(chunks):
                if i >= 1:
                    s.wait_ge(ctx.s("st"), st0 + 16 * i)
                s.dma_start(out=sb["trs"][0:w, 0:R], in_=wst[:, c0:c0 + w],
                            transpose=True).then_inc(ctx.s("tr"), 16)

        @blk.gpsimd
        def _(g):
            for i, (c0, w) in enumerate(chunks):
                g.wait_ge(ctx.s("tr"), tr0 + 16 * (i + 1))
                g.dma_start(out=dram[part].ap()[c0:c0 + w, :],
                            in_=sb["trs"][0:w, 0:R]).then_inc(ctx.s("st"), 16)
            g.wait_ge(ctx.s("st"), st0 + 16 * len(chunks))

        ctx.add("tr", 16 * len(chunks))
        ctx.add("st", 16 * len(chunks))

    cc_marks = {}

    def ph_l3_ag(blk, part, dstT, prev=None):
        st_all = ctx.v("st")

        @blk.gpsimd
        def _(g):
            g.wait_ge(ctx.s("st"), st_all)
            if prev is not None:
                g.wait_ge(ctx.s("cc"), cc_marks[prev])
            with allow_ici_all_to_all():
                g.collective_compute(
                    "AllGather", ALU.bypass,
                    replica_groups=[[0, 1, 2, 3], [4, 5, 6, 7]],
                    ins=[dram[part].ap().opt()],
                    outs=[dram[dstT].ap().opt()],
                ).then_inc(ctx.s("cc"), 1)
            cc_marks[dstT] = ctx.add("cc")
            g.wait_ge(ctx.s("cc"), cc_marks[dstT])

    def ph_l3_quantx(blk):
        ld0, st0 = ctx.v("ld"), ctx.v("st")
        sc_ops = []
        ev_marks = []

        @blk.vector
        def _(v):
            for tp in range(NT // 2):
                for k in (0, 1):
                    v.wait_ge(ctx.s("ld"), ld0 + 16 * (2 * tp + k + 1))
                    if tp >= 1:
                        v.wait_ge(ctx.s("st"), st0 + 16 * (2 * tp - 1 + k))
                wf = sb["wf32"].bitcast(F32)
                marks = emit_fp4_quant(v, sc_ops, 1024,
                                       [(wf[:, 0:1024], sb["qbf"][:, 0:1024]),
                                        (wf[:, 1024:2048], sb["qbf"][:, 1024:2048])])
                ev_marks.append(marks)

        @blk.scalar
        def _(sc):
            emit_fp4_scalar(sc, sc_ops)

        @blk.gpsimd
        def _(g):
            for tp in range(NT // 2):
                for k in (0, 1):
                    t = 2 * tp + k
                    if tp >= 1:
                        g.wait_ge(ctx.s("ev"), ev_marks[tp - 1][k][0])
                    g.dma_start(out=sb["wf32"].bitcast(F32)[:, 1024 * k:1024 * (k + 1)],
                                in_=dram["stage_xn32"].ap()[128 * t:128 * (t + 1), :]).then_inc(ctx.s("ld"), 16)
                for k in (0, 1):
                    t = 2 * tp + k
                    g.wait_ge(ctx.s("ev"), ev_marks[tp][k][1])
                    g.dma_start(out=dram["stage_xn"].ap()[128 * t:128 * (t + 1), :],
                                in_=sb["qbf"][:, 1024 * k:1024 * (k + 1)]).then_inc(ctx.s("st"), 16)

        @blk.sync
        def _(s):
            s.wait_ge(ctx.s("st"), st0 + 16 * NT)
            for dc in range(8):
                s.dma_start(out=sb["xnT"][:, 2048 * dc:2048 * (dc + 1)],
                            in_=dram["stage_xn"].ap()[:, 128 * dc:128 * (dc + 1)],
                            transpose=True).then_inc(ctx.s("tr"), 16)

        ctx.add("ld", 16 * NT)
        ctx.add("st", 16 * NT)
        ctx.add("tr", 16 * 8)

    def ph_l3_gateup(blk):
        ld0, mm0, ev0, sc0, st0 = (ctx.v("ld"), ctx.v("mm"), ctx.v("ev"),
                                   ctx.v("sc"), ctx.v("st"))
        tr_ready = ctx.v("tr")
        GS = [(i * 512, 512) for i in range(5)] + [(2560, 256)]

        @blk.sync
        def _(s):
            if "wuqT" in cc_marks:
                s.wait_ge(ctx.s("cc"), cc_marks["wuqT"])
            n = 0
            for gi, (goff, gw) in enumerate(GS):
                for src_t, dst in (("wgqT", "wgs"), ("wuqT", "wus")):
                    for dc in range(8):
                        if gi >= 1:
                            s.wait_ge(ctx.s("mm"), mm0 + 16 * gi)
                        s.dma_start(out=sb[dst][:, 512 * dc:512 * dc + gw],
                                    in_=dram[src_t].ap()[128 * dc:128 * (dc + 1), goff:goff + gw]).then_inc(ctx.s("ld"), 16)
                        n += 1
        NLD = 6 * 16
        ctx.add("ld", 16 * NLD)

        @blk.tensor
        def _(t):
            t.wait_ge(ctx.s("tr"), tr_ready)
            for gi, (goff, gw) in enumerate(GS):
                t.wait_ge(ctx.s("ld"), ld0 + 16 * 16 * (gi + 1))
                for tt in range(NT):
                    gidx = gi * NT + tt
                    bkG = 2 * (tt % 2)
                    bkU = bkG + 1
                    if gidx >= 2:
                        t.wait_ge(ctx.s("ev"), ev0 + gidx - 1)
                    for dc in range(8):
                        t.matmul(pbank(bkG, 0, 128, gw),
                                 sb["xnT"][:, 2048 * dc + 128 * tt:2048 * dc + 128 * (tt + 1)],
                                 sb["wgs"][:, 512 * dc:512 * dc + gw],
                                 start=(dc == 0), stop=(dc == 7))
                    last = None
                    for dc in range(8):
                        last = t.matmul(pbank(bkU, 0, 128, gw),
                                        sb["xnT"][:, 2048 * dc + 128 * tt:2048 * dc + 128 * (tt + 1)],
                                        sb["wus"][:, 512 * dc:512 * dc + gw],
                                        start=(dc == 0), stop=(dc == 7))
                    last.then_inc(ctx.s("mm"), 1)

        @blk.scalar
        def _(sc):
            for gi, (goff, gw) in enumerate(GS):
                for tt in range(NT):
                    gidx = gi * NT + tt
                    sc.wait_ge(ctx.s("mm"), mm0 + gidx + 1)
                    if gidx >= 1:
                        sc.wait_ge(ctx.s("ev"), ev0 + gidx)
                    bkG = 2 * (tt % 2)
                    sc.activation(sb["sg1"].bitcast(F32)[:, 0:gw], pbank(bkG, 0, 128, gw),
                                  AF.Silu, bias=0.0, scale=1.0).then_inc(ctx.s("sc"), 1)

        @blk.vector
        def _(v):
            for gi, (goff, gw) in enumerate(GS):
                for tt in range(NT):
                    gidx = gi * NT + tt
                    v.wait_ge(ctx.s("sc"), sc0 + gidx + 1)
                    if gidx >= 1:
                        v.wait_ge(ctx.s("st"), st0 + 16 * gidx)
                    bkU = 2 * (tt % 2) + 1
                    v.tensor_mul(sb["sg2"].bitcast(F32)[:, 0:gw],
                                 sb["sg1"].bitcast(F32)[:, 0:gw],
                                 pbank(bkU, 0, 128, gw)).then_inc(ctx.s("ev"), 1)

        @blk.gpsimd
        def _(g):
            for gi, (goff, gw) in enumerate(GS):
                for tt in range(NT):
                    gidx = gi * NT + tt
                    g.wait_ge(ctx.s("ev"), ev0 + gidx + 1)
                    g.dma_start(out=dram["stage_h"].ap()[128 * tt:128 * (tt + 1), goff:goff + gw],
                                in_=sb["sg2"].bitcast(F32)[:, 0:gw]).then_inc(ctx.s("st"), 16)

        NG = 6 * NT
        ctx.add("mm", NG)
        ctx.add("sc", NG)
        ctx.add("ev", NG)
        ctx.add("st", 16 * NG)

    def ph_l3_quanth(blk):
        ld0, st0 = ctx.v("ld"), ctx.v("st")
        sc_ops = []
        ev_marks = []

        @blk.vector
        def _(v):
            for t in range(NT):
                v.wait_ge(ctx.s("ld"), ld0 + 16 * (t + 1))
                if t >= 1:
                    v.wait_ge(ctx.s("st"), st0 + 16 * t)
                wf = sb["wf32"].bitcast(F32)
                marks = emit_fp4_quant(v, sc_ops, 1408,
                                       [(wf[:, 0:1408], sb["qbf"][:, 0:1408]),
                                        (wf[:, 1408:2816], sb["qbf"][:, 1408:2816])])
                ev_marks.append(marks)

        @blk.scalar
        def _(sc):
            emit_fp4_scalar(sc, sc_ops)

        @blk.gpsimd
        def _(g):
            for t in range(NT):
                if t >= 1:
                    g.wait_ge(ctx.s("ev"), max(ev_marks[t - 1][0][0], ev_marks[t - 1][1][0]))
                g.dma_start(out=sb["wf32"].bitcast(F32)[:, 0:2816],
                            in_=dram["stage_h"].ap()[128 * t:128 * (t + 1), :]).then_inc(ctx.s("ld"), 16)
                g.wait_ge(ctx.s("ev"), max(ev_marks[t][0][1], ev_marks[t][1][1]))
                g.dma_start(out=dram["stage_hq"].ap()[128 * t:128 * (t + 1), :],
                            in_=sb["qbf"][:, 0:2816]).then_inc(ctx.s("st"), 16)

        ctx.add("ld", 16 * NT)
        ctx.add("st", 16 * NT)

    def ph_l3_down(blk):
        ld0, mm0, ev0, tr0 = ctx.v("ld"), ctx.v("mm"), ctx.v("ev"), ctx.v("tr")

        @blk.sync
        def _(s):
            for grp in range(4):
                for dd in range(22):
                    k = grp * 22 + dd
                    if k >= 2:
                        s.wait_ge(ctx.s("mm"), mm0 + k - 1)
                    s.dma_start(out=sb["hqs"][:, 1024 * (dd % 2):1024 * (dd % 2) + 512],
                                in_=dram["stage_hq"].ap()[512 * grp:512 * (grp + 1),
                                                          128 * dd:128 * (dd + 1)],
                                transpose=True).then_inc(ctx.s("tr"), 16)

        @blk.gpsimd
        def _(g):
            if "wdqT" in cc_marks:
                g.wait_ge(ctx.s("cc"), cc_marks["wdqT"])
            for grp in range(4):
                for dd in range(22):
                    k = grp * 22 + dd
                    if k >= 2:
                        g.wait_ge(ctx.s("mm"), mm0 + k - 1)
                    g.dma_start(out=sb["wds"][:, 1024 * (dd % 2):1024 * (dd % 2 + 1)],
                                in_=dram["wdqT"].ap()[128 * dd:128 * (dd + 1), :]).then_inc(ctx.s("ld"), 16)

        @blk.tensor
        def _(t):
            for grp in range(4):
                if grp >= 1:
                    t.wait_ge(ctx.s("ev"), ev0 + 8 * grp)
                for dd in range(22):
                    k = grp * 22 + dd
                    t.wait_ge(ctx.s("tr"), tr0 + 16 * (k + 1))
                    t.wait_ge(ctx.s("ld"), ld0 + 16 * (k + 1))
                    hq = sb["hqs"][:, 1024 * (dd % 2):1024 * (dd % 2) + 512]
                    wd_ = sb["wds"][:, 1024 * (dd % 2):1024 * (dd % 2 + 1)]
                    last = None
                    for tt4 in range(4):
                        for ds in range(2):
                            bk = 2 * tt4 + ds
                            last = t.matmul(pbank(bk),
                                            hq[:, 128 * tt4:128 * (tt4 + 1)],
                                            wd_[:, 512 * ds:512 * (ds + 1)],
                                            start=(dd == 0), stop=(dd == 21))
                    last.then_inc(ctx.s("mm"), 1)

        @blk.vector
        def _(v):
            for grp in range(4):
                v.wait_ge(ctx.s("mm"), mm0 + 22 * (grp + 1))
                for tt4 in range(4):
                    for ds in range(2):
                        bk = 2 * tt4 + ds
                        tt = 4 * grp + tt4
                        dst = sb["stream"][:, 1024 * tt + 512 * ds:1024 * tt + 512 * (ds + 1)]
                        v.tensor_add(dst, dst, pbank(bk)).then_inc(ctx.s("ev"), 1)

        ctx.add("mm", 88)
        ctx.add("tr", 16 * 88)
        ctx.add("ld", 16 * 88)
        ctx.add("ev", 32)

    def ph_dump(blk, name, nelem):
        st0 = ctx.v("st")

        @blk.sync
        def _(s):
            s.wait_ge(ctx.s("st"), st0)
            src_ap = dram[name].ap()
            flat = src_ap
            if len(src_ap.shape) > 1:
                flat = src_ap.rearrange("a b -> (a b)") if len(src_ap.shape) == 2 else src_ap
            s.dma_start(out=dbg_out.ap().rearrange("a b -> (a b)")[0:nelem],
                        in_=flat[0:nelem] if len(flat.shape) == 1 else flat).then_inc(ctx.s("st"), 16)
            s.wait_ge(ctx.s("st"), st0 + 16)
        ctx.add("st", 16)

    def ph_dump_sbuf(blk):
        st0 = ctx.v("st")

        @blk.gpsimd
        def _(g):
            flat = dbg_out.ap().rearrange("a b -> (a b)")
            o = [2621440]

            def dump(t, n):
                g.dma_start(out=flat[o[0]:o[0] + 128 * n].rearrange("(p n) -> p n", p=128),
                            in_=t).then_inc(ctx.s("st"), 16)
                o[0] += 128 * n
            dump(sb["qtp"][:, :], 8192)
            dump(sb["ktp"][:, :], 8192)
            dump(sb["vt"][:, :], 8320)
            dump(sb["pt"][:, :], 2048)
            g.wait_ge(ctx.s("st"), st0 + 64)
        ctx.add("st", 16 * 4)

    def ph_dump_wx(blk):
        st0 = ctx.v("st")

        @blk.gpsimd
        def _(g):
            flat = dbg_out.ap().rearrange("a b -> (a b)")
            g.dma_start(out=flat[3948544:3948544 + 128 * 8192].rearrange("(p n) -> p n", p=128),
                        in_=sb["wa"][:, :]).then_inc(ctx.s("st"), 16)
            g.dma_start(out=flat[4997120:4997120 + 128 * 8192].rearrange("(p n) -> p n", p=128),
                        in_=sb["xnT"][:, 0:8192]).then_inc(ctx.s("st"), 16)
            g.wait_ge(ctx.s("st"), st0 + 32)
        ctx.add("st", 32)

    def ph_dump_flat(blk, name, nelem, off):
        st0 = ctx.v("st")

        @blk.sync
        def _(s):
            s.wait_ge(ctx.s("st"), st0)
            s.wait_ge(ctx.s("cc"), ctx.v("cc"))
            flat = dbg_out.ap().rearrange("a b -> (a b)")
            srcf = dram[name].ap().rearrange("a b -> (a b)")
            s.dma_start(out=flat[off:off + nelem], in_=srcf[0:nelem]).then_inc(ctx.s("st"), 16)
            s.wait_ge(ctx.s("st"), st0 + 16)
        ctx.add("st", 16)

    def ph_dump_stat(blk):
        st0 = ctx.v("st")

        @blk.gpsimd
        def _(g):
            g.dma_start(out=dbg_out.ap()[7, 0:128 * 160].rearrange("(p n) -> p n", p=128),
                        in_=sb["stat"][:, :]).then_inc(ctx.s("st"), 16)
            g.wait_ge(ctx.s("st"), st0 + 16)
        ctx.add("st", 16)

    run_phase(ph_init)
    def _emit_ln():
        run_phase(ph_ln)
        run_phase(ph_dump, "stage_xn", TPC * D)
        run_phase(ph_dump_stat)
    def _emit_qkv():
        run_phase(ph_ln)
        run_phase(ph_l0_qkv)
        run_phase(ph_dump, "a2a_qkv_in", 8 * BLK_QKV)
    def _emit_attn():
        run_phase(ph_ln)
        run_phase(ph_l0_qkv)
        run_phase(ph_dump_flat, "a2a_qkv_in", 262144, 2097152)
        run_phase(ph_dump_wx)
        run_phase(ph_a2a, "a2a_qkv_in", "a2a_qkv_out")
        run_phase(ph_dump_flat, "a2a_qkv_out", 262144, 2359296)
        run_phase(ph_l0_attn_batch, 0, C0_L0)
        run_phase(ph_dump_sbuf)
        run_phase(ph_l0_attn_batch, 1, C0_L0)
        run_phase(ph_dump, "a2a_ao_in", 8 * SEG_Q)
    def _emit_l0a():
        run_phase(ph_ln)
        run_phase(ph_l0_qkv)
        run_phase(ph_a2a, "a2a_qkv_in", "a2a_qkv_out")
        run_phase(ph_l0_attn_batch, 0, C0_L0)
        run_phase(ph_l0_attn_batch, 1, C0_L0)
        run_phase(ph_a2a, "a2a_ao_in", "a2a_ao_out")
        run_phase(ph_l0_wo)
    def _emit_l0f():
        run_phase(ph_ln)
        run_phase(ph_l0_ffn)
    def _emit_l1a():
        run_phase(ph_ln)
        run_phase(ph_l1_qkv)
        run_phase(ph_a2a, "a2a_qkv_in", "a2a_qkv_out")
        run_phase(ph_l1_rope, 0)
        run_phase(ph_l1_attn, 0)
        run_phase(ph_l1_rope, 1)
        run_phase(ph_l1_attn, 1)
        run_phase(ph_a2a, "a2a_ao_in", "a2a_ao_out")
        run_phase(ph_l0_wo, "w_l1_woT", "w_l1_bo")
    def _emit_l1fdbg():
        run_phase(ph_l1_sum)
        run_phase(ph_l1_ar)
        run_phase(ph_l1_tern, "w_l1_w1T_part", 2, DFF, "w1qpart", 0, 0)
        run_phase(ph_l1_tern, "w_l1_w2T_part", 8, D, "w2qpart", 1, 1)
        run_phase(ph_l1_agq, "w1qpart", "wq1q")
        run_phase(ph_l1_agq, "w2qpart", "wq2q", prev="wq1q")
        run_phase(ph_ln, "32")
        run_phase(ph_l1_quantx)
        run_phase(ph_l1_mm1)
        run_phase(ph_l1_mm2)
        run_phase(ph_dump_flat, "wq1q", 1024 * 4096, 262144)
    def _emit_l1f():
        run_phase(ph_l1_sum)
        run_phase(ph_l1_ar)
        run_phase(ph_l1_tern, "w_l1_w1T_part", 2, DFF, "w1qpart", 0, 0)
        run_phase(ph_l1_tern, "w_l1_w2T_part", 8, D, "w2qpart", 1, 1)
        run_phase(ph_l1_agq, "w1qpart", "wq1q")
        run_phase(ph_l1_agq, "w2qpart", "wq2q", prev="wq1q")
        run_phase(ph_ln, "32")
        run_phase(ph_l1_quantx)
        run_phase(ph_l1_mm1)
        run_phase(ph_l1_mm2)
    def _emit_l3p():
        run_phase(ph_l3_prep, "w_l3_wg_part", 22, 256, 0)
        run_phase(ph_l3_prepT, "wtpartA", DI3, 256)
        run_phase(ph_l3_ag, "wtpartA", "wgqT")
        run_phase(ph_l3_prep, "w_l3_wu_part", 22, 256, 0)
        run_phase(ph_l3_prepT, "wtpartC", DI3, 256)
        run_phase(ph_l3_ag, "wtpartC", "wuqT", prev="wgqT")
        run_phase(ph_l3_prep, "w_l3_wd_part", 8, 768, 0)
        run_phase(ph_l3_prepT, "wtpartB", D, 768)
        run_phase(ph_l3_ag, "wtpartB", "wdqT", prev="wuqT")
    def _emit_l3f():
        run_phase(ph_ln, "32")
        run_phase(ph_l3_quantx)
        run_phase(ph_l3_gateup)
        run_phase(ph_l3_quanth)
        run_phase(ph_l3_down)
    def _emit_l3f1():
        run_phase(ph_ln, "32")
        run_phase(ph_l3_quantx)
        run_phase(ph_dump, "stage_xn", TPC * D)
    def _emit_l3f3():
        run_phase(ph_ln, "32")
        run_phase(ph_l3_quantx)
        run_phase(ph_l3_gateup)
        run_phase(ph_l3_quanth)
    def _emit_l3a():
        run_phase(ph_ln)
        run_phase(ph_l0_qkv, "w_l3_wqT", "w_l3_wkT", "w_l3_wvT")
        run_phase(ph_a2a, "a2a_qkv_in", "a2a_qkv_out")
        run_phase(ph_rope, 0)
        run_phase(ph_l0_attn_batch, 0, C0_L3, False)
        run_phase(ph_rope, 1)
        run_phase(ph_l0_attn_batch, 1, C0_L3, False)
        run_phase(ph_a2a, "a2a_ao_in", "a2a_ao_out")
        run_phase(ph_l0_wo, "w_l3_woT", "w_l3_bo")
    def _emit_l2a():
        run_phase(ph_ln)
        run_phase(ph_l2_comb, "w_l2_wq", "w_l2_bqc", 0)
        run_phase(ph_l2_qf)
        run_phase(ph_l2_comb, "w_l2_wk", "w_l2_bkc", 1024)
        run_phase(ph_l2_kv)
        run_phase(ph_l2_ar)
        run_phase(ph_l2_out)
        run_phase(ph_l2_wo)
    def _emit_l2f():
        run_phase(ph_ln)
        run_phase(ph_l2_ffn)
    _emitters = {"ln": _emit_ln, "qkv": _emit_qkv, "attn": _emit_attn, "l0a": _emit_l0a, "l0f": _emit_l0f, "l1a": _emit_l1a, "l1fdbg": _emit_l1fdbg, "l1f": _emit_l1f, "l3p": _emit_l3p, "l3f": _emit_l3f, "l3f1": _emit_l3f1, "l3f3": _emit_l3f3, "l3a": _emit_l3a, "l2a": _emit_l2a, "l2f": _emit_l2f}
    for _p in phases:
        if _p in _emitters:
            _emitters[_p]()
    run_phase(ph_store)

    es.__exit__(None, None, None)
    return nc


LAST_RESULTS = None


def kernel(**inp):
    global LAST_RESULTS
    in_maps = _host_prep(inp)
    nc = build_program(ALL_PHASES if PHASES is None else PHASES)
    res = run_bass_kernel_spmd(nc, in_maps, core_ids=list(range(N_CORES)))
    LAST_RESULTS = res.results
    out = np.zeros((B, S, D), np.float32)
    for j in range(N_CORES):
        b, r = j // 4, (j % 4) * TPC
        out[b, r:r + TPC, :] = res.results[j]["y"]
    return out


# version 12 (final)
# speedup vs baseline: 1.1413x; 1.0246x over previous
import contextlib
import numpy as np
import ml_dtypes

import concourse.bass as bass
import concourse.mybir as mybir
from concourse.bass_utils import run_bass_kernel_spmd
from concourse.aot_env import allow_ici_all_to_all

F32 = mybir.dt.float32
BF16 = mybir.dt.bfloat16
I8 = mybir.dt.int8
AF = mybir.ActivationFunctionType
ALU = mybir.AluOpType
AX = mybir.AxisListType

B, S, D = 2, 8192, 1024
TPC = 2048
NT = TPC // 128
NH, HD = 16, 64
DFF = 4096
RANK = 256
DI3 = 2816
LN_EPS = 1e-5
C0_L0 = 6.0
C0_L3 = 20.0

PHASES = None
ALL_PHASES = ["l0a", "l0f", "l1a", "l1f", "l2a", "l2f", "l3a", "l3p", "l3f"]
N_CORES = 8


def bf16(x):
    return np.asarray(x, dtype=ml_dtypes.bfloat16)



def _host_constants():
    c = {}
    c["c_ident"] = bf16(np.eye(128, dtype=np.float32))
    c["c_onesb"] = bf16(np.ones((1, 2048), np.float32))
    c["c_onesf"] = np.ones((33, 512), np.float32)
    mfull = np.zeros((4, 128, 512), np.float32)
    for r in range(4):
        kpos = 128 * r + np.arange(128)[:, None]
        qpos = np.arange(512)[None, :]
        mfull[r] = np.where(kpos > qpos, -1e9, 0.0)
    lo = np.concatenate([mfull[:, 0:64], mfull[:, 0:64]], axis=1)
    hi = np.concatenate([mfull[:, 64:128], mfull[:, 64:128]], axis=1)
    c["c_cmask_lo"] = bf16(np.concatenate(list(lo), axis=1))
    c["c_cmask_hi"] = bf16(np.concatenate(list(hi), axis=1))
    j1 = np.zeros((64, 128), np.float32)
    j1[np.arange(64), np.arange(64)] = 1.0
    j2 = np.zeros((64, 128), np.float32)
    j2[np.arange(64), 64 + np.arange(64)] = 1.0
    c["c_jlo"] = bf16(np.concatenate([j1, j1], axis=0))
    c["c_jhi"] = bf16(np.concatenate([j2, j2], axis=0))
    inv = 1.0 / (10000.0 ** (np.arange(0, HD, 2, dtype=np.float64) / HD))
    freqs = np.outer(np.arange(S, dtype=np.float64), inv)
    emb = np.concatenate([freqs, freqs], axis=-1)
    cosf = np.cos(emb).T
    sinf = np.sin(emb).T
    c["c_cosd"] = bf16(np.concatenate([cosf, cosf], axis=0))
    c["c_sind"] = bf16(np.concatenate([sinf, sinf], axis=0))
    rotm = np.zeros((128, 128), np.float32)
    for b0 in (0, 64):
        for j in range(32):
            rotm[b0 + 32 + j, b0 + j] = -1.0
            rotm[b0 + j, b0 + 32 + j] = 1.0
    c["c_rotm"] = bf16(rotm)
    return c


def _host_prep(inp):
    T = lambda a: np.ascontiguousarray(np.asarray(a).T)
    A = lambda a: np.ascontiguousarray(np.asarray(a))
    base = {}
    base["w_l0_wq"] = A(inp["L0_attn_Wq"])
    base["w_l0_wk"] = A(inp["L0_attn_Wk"])
    base["w_l0_wv"] = A(inp["L0_attn_Wv"])
    base["w_l0_wo"] = A(inp["L0_attn_Wo"])
    base["w_l0_bo"] = A(inp["L0_attn_bo"]).reshape(1, D)
    base["w_l0_u1"] = A(inp["L0_ffn_U1"])
    base["w_l0_v1"] = A(inp["L0_ffn_V1"])
    base["w_l0_b1"] = A(inp["L0_ffn_b1"]).reshape(1, DFF)
    base["w_l0_u2"] = A(inp["L0_ffn_U2"])
    base["w_l0_v2"] = A(inp["L0_ffn_V2"])
    base["w_l0_b2"] = A(inp["L0_ffn_b2"]).reshape(1, D)
    base["w_l2_wq"] = A(inp["L2_attn_Wq"])
    base["w_l2_wk"] = A(inp["L2_attn_Wk"])
    base["w_l2_wvT"] = T(inp["L2_attn_Wv"])
    base["w_l2_woT"] = T(inp["L2_attn_Wo"])

    def _bcol(b):
        arr = np.zeros((128, 8), np.float32)
        b = np.asarray(b)
        for h in range(16):
            arr[64 * (h % 2):64 * (h % 2) + 64, h // 2] = b[64 * h:64 * (h + 1)]
        return arr
    base["w_l2_bqc"] = _bcol(inp["L2_attn_bq"])
    base["w_l2_bkc"] = _bcol(inp["L2_attn_bk"])
    base["w_l2_bv"] = A(inp["L2_attn_bv"]).reshape(1, D)
    base["w_l2_bo"] = A(inp["L2_attn_bo"]).reshape(1, D)
    pt_ = np.ascontiguousarray(np.asarray(inp["L2_attn_P"]).T)
    base["w_l2_pT"] = np.concatenate([pt_, pt_], axis=0).copy()
    base["w_l2_w1"] = A(inp["L2_ffn_w1"])
    base["w_l2_b1"] = A(inp["L2_ffn_b1"]).reshape(1, DFF)
    base["w_l2_w2"] = A(inp["L2_ffn_w2"])
    base["w_l2_b2"] = A(inp["L2_ffn_b2"]).reshape(1, D)
    base["w_l3_wqT"] = T(inp["L3_attn_Wq"])
    base["w_l3_wkT"] = T(inp["L3_attn_Wk"])
    base["w_l3_wvT"] = T(inp["L3_attn_Wv"])
    base["w_l3_woT"] = T(inp["L3_attn_Wo"])
    base["w_l3_bo"] = np.zeros((1, D), np.float32)
    wg_ = np.asarray(inp["L3_ffn_w_gate"])
    wu_ = np.asarray(inp["L3_ffn_w_up"])
    wd_ = np.asarray(inp["L3_ffn_w_down"])
    base["w_l1_wqT"] = T(inp["L1_attn_Wq"])
    base["w_l1_wkT"] = T(inp["L1_attn_Wk"])
    base["w_l1_wvT"] = T(inp["L1_attn_Wv"])
    base["w_l1_woT"] = T(inp["L1_attn_Wo"])
    base["w_l1_bo"] = np.zeros((1, D), np.float32)
    w1T_ = np.ascontiguousarray(np.asarray(inp["L1_ffn_w1"]).T)
    w2T_ = np.ascontiguousarray(np.asarray(inp["L1_ffn_w2"]).T)
    base["c_n1rep"] = np.broadcast_to(np.asarray(inp["L1_ffn_norm1_w"], np.float32), (128, D)).copy()
    base["c_n2rep"] = np.broadcast_to(np.asarray(inp["L1_ffn_norm2_w"], np.float32), (128, DFF)).copy()
    sc_ = np.zeros((128, 2), np.float32)
    sc_[:, 0] = np.float32(np.asarray(inp["L1_ffn_act_scale"]).reshape(-1)[0])
    sc_[:, 1] = np.float32(np.asarray(inp["L1_ffn_act_bias"]).reshape(-1)[0])
    base["c_l1s"] = sc_
    base.update(_host_constants())
    in_maps = []
    for j in range(N_CORES):
        m = dict(base)
        b, r = j // 4, (j % 4) * TPC
        m["x"] = A(np.asarray(inp["L0_attn_x"])[b, r:r + TPC, :])
        rk = j % 4
        m["w_l3_wg_part"] = A(wg_[:, 256 * rk:256 * (rk + 1)])
        m["w_l1_w1T_part"] = A(w1T_[256 * rk:256 * (rk + 1), :])
        m["w_l1_w2T_part"] = A(w2T_[1024 * rk:1024 * (rk + 1), :])
        m["w_l3_wu_part"] = A(wu_[:, 256 * rk:256 * (rk + 1)])
        if rk < 3:
            m["w_l3_wd_part"] = A(wd_[:, 768 * rk:768 * (rk + 1)])
        else:
            pad = np.zeros((D, 768), wd_.dtype)
            pad[:, 0:512] = wd_[:, 2304:2816]
            m["w_l3_wd_part"] = pad
        t_ = np.arange(128, dtype=np.float64)
        maskT = np.zeros((128, 256), np.float32)
        qdec = np.zeros((128, 128), np.float32)
        kdec = np.zeros((128, 2), np.float32)
        bdec = np.zeros((128, 2), np.float32)
        for hh in range(2):
            h = 2 * j + hh
            sl = 2.0 ** (-8.0 / 16 * (h + 1))
            mT = np.where(t_[None, :] >= t_[:, None],
                          np.exp(-sl * (t_[None, :] - t_[:, None])), 0.0)
            maskT[:, 128 * hh:128 * (hh + 1)] = mT
            qdec[64 * hh:64 * (hh + 1), :] = np.exp(-sl * (t_ + 1.0))[None, :]
            kdec[:, hh] = np.exp(-sl * (127.0 - t_))
            bdec[:, hh] = np.exp(-sl * 128.0)
        m["c_l1maskT"] = bf16(maskT)
        m["c_l1qdec"] = bf16(qdec)
        m["c_l1kdec"] = kdec.astype(np.float32)
        m["c_l1bdec"] = bdec.astype(np.float32)
        in_maps.append(m)
    return in_maps



class Ctx:
    def __init__(self, nc, es):
        self.nc = nc
        self.cnt = {}
        self.sems = {}
        for name in ["ld", "mm", "pv", "ev", "sc", "st", "tr", "cc"]:
            self.sems[name] = es.enter_context(nc.semaphore("s_" + name))

    def s(self, name):
        return self.sems[name]

    def add(self, name, n=1):
        self.cnt[name] = self.cnt.get(name, 0) + n
        return self.cnt[name]

    def v(self, name):
        return self.cnt.get(name, 0)


def build_program(phases):
    nc = bass.Bass("TRN2", target_bir_lowering=False)
    es = contextlib.ExitStack()
    es.__enter__()
    ctx = Ctx(nc, es)
    E = es.enter_context

    dram = {}

    def din(name, shape, dtype=F32):
        dram[name] = nc.dram_tensor(name, list(shape), dtype, kind="ExternalInput")

    def dint(name, shape, dtype=BF16):
        dram[name] = nc.dram_tensor(name, list(shape), dtype)

    din("x", [TPC, D])
    din("w_l0_wq", [D, D]); din("w_l0_wk", [D, D]); din("w_l0_wv", [D, D])
    din("w_l0_wo", [D, D]); din("w_l0_bo", [1, D])
    din("w_l0_u1", [D, RANK]); din("w_l0_v1", [RANK, DFF]); din("w_l0_b1", [1, DFF])
    din("w_l0_u2", [DFF, RANK]); din("w_l0_v2", [RANK, D]); din("w_l0_b2", [1, D])
    din("c_ident", [128, 128], BF16)
    din("c_onesb", [1, 2048], BF16)
    din("c_onesf", [33, 512], F32)
    din("c_cmask_lo", [128, 4 * 512], BF16)
    din("c_cmask_hi", [128, 4 * 512], BF16)
    din("c_jlo", [128, 128], BF16)
    din("c_jhi", [128, 128], BF16)
    din("c_cosd", [128, S], BF16)
    din("c_sind", [128, S], BF16)
    din("c_rotm", [128, 128], BF16)
    for nm in ("w_l3_wqT", "w_l3_wkT", "w_l3_wvT", "w_l3_woT"):
        din(nm, [D, D])
    din("w_l3_bo", [1, D])
    din("w_l3_wg_part", [DI3, 256]); din("w_l3_wu_part", [DI3, 256])
    din("w_l3_wd_part", [D, 768])
    din("w_l1_w1T_part", [256, DFF]); din("w_l1_w2T_part", [1024, D])
    din("w_l1_wqT", [D, D]); din("w_l1_wkT", [D, 256]); din("w_l1_wvT", [D, 256])
    din("w_l1_woT", [D, D]); din("w_l1_bo", [1, D])
    din("c_l1maskT", [128, 256], BF16)
    din("c_l1qdec", [128, 128], BF16)
    din("c_l1kdec", [128, 2])
    din("c_l1bdec", [128, 2])
    din("c_n1rep", [128, D]); din("c_n2rep", [128, DFF]); din("c_l1s", [128, 2])

    for nm in ("w_l2_wq", "w_l2_wk", "w_l2_wvT", "w_l2_woT"):
        din(nm, [D, D])
    for nm in ("w_l2_bv", "w_l2_bo", "w_l2_b2"):
        din(nm, [1, D])
    din("w_l2_bqc", [128, 8]); din("w_l2_bkc", [128, 8])
    din("w_l2_pT", [128, HD])
    din("w_l2_w1", [D, DFF]); din("w_l2_b1", [1, DFF]); din("w_l2_w2", [DFF, D])

    y_out = nc.dram_tensor("y", [TPC, D], F32, kind="ExternalOutput")
    dbg_out = nc.dram_tensor("dbg", [8, 786432], BF16, kind="ExternalOutput")

    dint("stage_xn", [TPC, D])
    SEG_Q = 128 * 2048
    SEG_V = 2048 * 128
    BLK_QKV = SEG_Q + SEG_Q + SEG_V
    dint("a2a_qkv_in", [8, BLK_QKV])
    dint("a2a_qkv_out", [8, BLK_QKV])
    dint("a2a_ao_in", [8, SEG_Q])
    dint("a2a_ao_out", [8, SEG_Q])
    dint("ar_in", [NH, HD, 65], F32)
    dint("ar_out", [NH, HD, 65], F32)
    dint("wstage", [DI3 * D])
    dint("stage_xn32", [TPC, D], F32)
    dint("stage_hq", [TPC, DI3])
    dint("wq1q", [D, DFF]); dint("wq2q", [DFF, D])
    dint("stage_hq1", [TPC, DFF])
    dint("wgqT", [D, DI3]); dint("wuqT", [D, DI3]); dint("wdqT", [3072, D])
    dint("wtpartA", [256, DI3]); dint("wtpartB", [768, D])
    dint("wtpartC", [256, DI3])
    dint("w1qpart", [256, DFF]); dint("w2qpart", [1024, D])
    dint("l1sum_io", [1, 2], F32); dint("l1sum_out", [1, 2], F32)
    dint("stage_h", [TPC, DI3], F32)

    sb = {}
    sb["stream"] = E(nc.sbuf_tensor("stream", [128, NT * D], F32))
    sb["scr"] = E(nc.sbuf_tensor("scr", [128, 2048], BF16))
    sb["stat"] = E(nc.sbuf_tensor("stat", [128, 10 * NT], F32))
    sb["rec"] = E(nc.sbuf_tensor("rec", [33, 1024], F32))
    sb["bcb"] = E(nc.sbuf_tensor("bcb", [128, 2 * 512], F32))
    sb["ident"] = E(nc.sbuf_tensor("ident", [128, 128], BF16))
    sb["onesb"] = E(nc.sbuf_tensor("onesb", [1, 2048], BF16))
    sb["onesf"] = E(nc.sbuf_tensor("onesf", [33, 512], F32))
    sb["cml"] = E(nc.sbuf_tensor("cml", [128, 4 * 512], BF16))
    sb["cmh"] = E(nc.sbuf_tensor("cmh", [128, 4 * 512], BF16))
    sb["jlo"] = E(nc.sbuf_tensor("jlo", [128, 128], BF16))
    sb["jhi"] = E(nc.sbuf_tensor("jhi", [128, 128], BF16))
    sb["bias"] = E(nc.sbuf_tensor("bias", [1, DFF], BF16))
    sb["bias2"] = E(nc.sbuf_tensor("bias2", [1, D], BF16))
    sb["cst"] = E(nc.sbuf_tensor("cst", [128, 8], F32))
    sb["pl2"] = E(nc.sbuf_tensor("pl2", [128, HD], BF16))
    sb["kvs"] = E(nc.sbuf_tensor("kvs", [64, 1040], F32))
    sb["bcl"] = E(nc.sbuf_tensor("bcl", [128, 8], BF16))
    sb["kvf"] = E(nc.sbuf_tensor("kvf", [128, 128], F32))
    sb["kvp"] = E(nc.sbuf_tensor("kvp", [128, 128], BF16))
    sb["l1m"] = E(nc.sbuf_tensor("l1m", [128, 256], BF16))
    sb["l1q"] = E(nc.sbuf_tensor("l1q", [128, 128], BF16))
    sb["l1k"] = E(nc.sbuf_tensor("l1k", [128, 2], F32))
    sb["l1b"] = E(nc.sbuf_tensor("l1b", [128, 2], F32))
    sb["swc"] = E(nc.sbuf_tensor("swc", [128, 4], F32))
    sb["l1s"] = E(nc.sbuf_tensor("l1s", [128, 2], F32))
    pool = E(nc.sbuf_tensor("pool", [128, 51216], BF16))
    pool_map = {
        "xnT": (0, 16384),
        "wa": (16384, 8192), "wb": (24576, 8192), "wc": (32768, 8192),
        "qtp": (16384, 8192), "ktp": (24576, 8192), "vt": (32768, 8320),
        "pt": (41088, 4096), "aot2": (45184, 1024),
        "cosr": (0, 8192), "sinr": (8192, 8192),
        "qdt": (0, 8192),
        "kdt": (8192, 8192),
        "vt1": (32768, 4096), "ktok": (36864, 4096),
        "st1": (41088, 256),
        "rs1": (44160, 2048), "rs2": (46208, 2048),
        "aoT": (16384, 16384), "wd": (32768, 8192),
        "u1": (16384, 2048), "v1": (18432, 8192), "u2": (26624, 8192),
        "v2": (34816, 2048), "ht": (36864, 8192), "pt2": (45056, 4096),
        "qt2": (49152, 1024),
        "aotl2": (0, 16384),
        "qft": (16384, 16384),
        "l2wv": (32768, 8192),
        "wraw": (32768, 8192),
        "wcmb": (40960, 8192),
        "kf2": (49152, 1024), "v22": (50176, 1040), "kvb": (49152, 520),
        "aoq2": (32768, 2048),
        "wfl1": (0, 8192), "qfl1": (8192, 8192),
        "i8l1": (16384, 2048), "qbl1": (18432, 4096),
        "xb1": (16384, 2048), "xh1": (18432, 2048), "i81": (20480, 512),
        "xqb1": (20992, 1024), "n1r": (22016, 2048),
        "w1sl": (16384, 8192), "a0": (24576, 8192), "a1": (32768, 8192),
        "n2r": (40960, 8192), "i8h": (49152, 2048),
        "hq1s": (16384, 4096), "w2sl": (20480, 4096),
        "wf32": (0, 5632), "i8a": (5632, 1408), "i8b": (7040, 1408),
        "i8c": (8448, 1408), "qf1": (9856, 5632), "qf2": (15488, 5632),
        "bmx": (21120, 352), "sclq": (21472, 352), "rcpq": (21824, 352),
        "qbf": (22176, 2816), "trs": (24992, 2816),
        "wgs": (16384, 4096), "wus": (20480, 4096),
        "sg1": (24576, 1024), "sg2": (25600, 1024),
        "hqs": (16384, 2048), "wds": (18432, 2048),
        "w1s": (36864, 4096),
        "w2s": (40960, 4096),
        "htl2": (20480, 16384),
    }

    class SBuf(dict):
        def __getitem__(self, k):
            v = dict.get(self, k, None)
            if v is None and k in pool_map:
                off, n = pool_map[k]
                return pool[:, off:off + n]
            return v

    sb = SBuf(sb)

    ps = E(nc.psum_tensor("ps", [128, 4096], F32))

    def pbank(i, r0=0, r1=128, w=512):
        return ps[r0:r1, 512 * i:512 * i + w]

    def run_phase(fn, *a, **kw):
        with nc.Block() as blk:
            fn(blk, *a, **kw)
        nc.all_engine_barrier()

    do_zero = sorted(set(phases)) != sorted(set(ALL_PHASES))

    def ph_init(blk):
        nl = [0]

        @blk.sync
        def _(s):
            for tt in range(NT):
                s.dma_start(out=sb["stream"][:, 1024 * tt:1024 * (tt + 1)],
                            in_=dram["x"].ap()[128 * tt:128 * (tt + 1), :]).then_inc(ctx.s("ld"), 16)
            s.dma_start(out=sb["ident"][:, :], in_=dram["c_ident"].ap()).then_inc(ctx.s("ld"), 16)
            s.dma_start(out=sb["onesb"][:, :], in_=dram["c_onesb"].ap()).then_inc(ctx.s("ld"), 16)
            s.dma_start(out=sb["onesf"][:, :], in_=dram["c_onesf"].ap()).then_inc(ctx.s("ld"), 16)
            s.dma_start(out=sb["cml"][:, :], in_=dram["c_cmask_lo"].ap()).then_inc(ctx.s("ld"), 16)
            s.dma_start(out=sb["cmh"][:, :], in_=dram["c_cmask_hi"].ap()).then_inc(ctx.s("ld"), 16)
            s.dma_start(out=sb["jlo"][:, :], in_=dram["c_jlo"].ap()).then_inc(ctx.s("ld"), 16)
            s.dma_start(out=sb["jhi"][:, :], in_=dram["c_jhi"].ap()).then_inc(ctx.s("ld"), 16)
            nl[0] = NT + 7

        @blk.vector
        def _(v):
            v.memset(sb["vt"].rearrange("p (c w) -> p c w", w=130)[:, :, 64:130:65], 1.0)
            v.memset(sb["scr"][:, :], 0.0).then_inc(ctx.s("ev"), 1)
        ctx.add("ev", 1)

        @blk.gpsimd
        def _(g):
            if not do_zero:
                return
            g.wait_ge(ctx.s("ev"), ctx.v("ev"))
            nz = 0
            for name in ("a2a_qkv_in", "a2a_qkv_out", "a2a_ao_in", "a2a_ao_out", "stage_xn"):
                flat = dram[name].ap().rearrange("a b -> (a b)")
                tot = flat.shape[0]
                CH = 128 * 2048
                for off in range(0, tot, CH):
                    n = min(CH, tot - off)
                    g.dma_start(out=flat[off:off + n].rearrange("(p n) -> p n", p=128),
                                in_=sb["scr"][:, 0:n // 128]).then_inc(ctx.s("st"), 16)
                    nz += 1
            g.wait_ge(ctx.s("st"), ctx.v("st") + 16 * nz)
            ctx.add("st", 16 * nz)

        @blk.gpsimd
        def _(g):
            g.wait_ge(ctx.s("ld"), ctx.v("ld") + 16 * nl[0])

        ctx.add("ld", 16 * nl[0])

    def ph_ln(blk, mode="T"):
        ev0, sc0, st0, mm0, tr0 = (ctx.v("ev"), ctx.v("sc"), ctx.v("st"),
                                   ctx.v("mm"), ctx.v("tr"))
        sums = sb["stat"][:, 0:NT]
        sumsq = sb["stat"][:, NT:2 * NT]
        mcol = sb["stat"][:, 2 * NT:3 * NT]
        e2 = sb["stat"][:, 3 * NT:4 * NT]
        var0 = sb["stat"][:, 4 * NT:5 * NT]
        var = sb["stat"][:, 5 * NT:6 * NT]
        stdc = sb["stat"][:, 6 * NT:7 * NT]
        rstd = sb["stat"][:, 7 * NT:8 * NT]
        nb = sb["stat"][:, 8 * NT:9 * NT]

        def psB(dc):
            return ps.bitcast(BF16)[:, 1024 * dc:1024 * dc + 128]

        @blk.scalar
        def _(sc):
            for tt in range(NT):
                st = sb["stream"][:, 1024 * tt:1024 * (tt + 1)]
                ins = sc.activation(sb["scr"][:, 0:1024], st, AF.Square,
                                    accum_out=sumsq[:, tt:tt + 1])
                if tt == NT - 1:
                    ins.then_inc(ctx.s("sc"), 1)
            sc.wait_ge(ctx.s("ev"), ev0 + 4)
            sc.activation(stdc, var, AF.Sqrt, bias=np.float32(LN_EPS),
                          scale=1.0).then_inc(ctx.s("sc"), 1)

        @blk.vector
        def _(v):
            for tt in range(NT):
                st = sb["stream"][:, 1024 * tt:1024 * (tt + 1)]
                ins = v.reduce_sum(sums[:, tt:tt + 1], st, axis=AX.X)
                if tt == NT - 1:
                    ins.then_inc(ctx.s("ev"), 1)
            v.wait_ge(ctx.s("ev"), ev0 + 1)
            v.tensor_scalar(mcol, sums, 1.0 / D, None, ALU.mult).then_inc(ctx.s("ev"), 1)
            v.wait_ge(ctx.s("sc"), sc0 + 1)
            v.tensor_scalar(e2, sumsq, 1.0 / D, None, ALU.mult)
            v.wait_ge(ctx.s("ev"), ev0 + 2)
            v.tensor_mul(var0, mcol, mcol).then_inc(ctx.s("ev"), 1)
            v.wait_ge(ctx.s("ev"), ev0 + 3)
            v.tensor_sub(var, e2, var0).then_inc(ctx.s("ev"), 1)
            v.wait_ge(ctx.s("sc"), sc0 + 2)
            v.reciprocal(rstd, stdc).then_inc(ctx.s("ev"), 1)
            v.wait_ge(ctx.s("ev"), ev0 + 5)
            v.scalar_tensor_tensor(nb, mcol, -1.0, rstd, ALU.mult, ALU.mult).then_inc(ctx.s("ev"), 1)
            v.wait_ge(ctx.s("ev"), ev0 + 6)
            if mode == "T":
                for tt in range(NT):
                    if tt >= 2:
                        v.wait_ge(ctx.s("mm"), mm0 + 8 * (tt - 1))
                    st = sb["stream"][:, 1024 * tt:1024 * (tt + 1)]
                    half = sb["scr"][:, 1024 * (tt % 2):1024 * (tt % 2) + 1024]
                    v.tensor_scalar(half, st, rstd[:, tt:tt + 1],
                                    nb[:, tt:tt + 1], ALU.mult,
                                    ALU.add).then_inc(ctx.s("ev"), 1)
                    for dc in range(8):
                        v.wait_ge(ctx.s("mm"), mm0 + 8 * tt + dc + 1)
                        v.tensor_copy(sb["xnT"][:, 2048 * dc + 128 * tt:2048 * dc + 128 * (tt + 1)],
                                      psB(dc)).then_inc(ctx.s("tr"), 1)
            else:
                for tt in range(NT):
                    if tt >= 2:
                        v.wait_ge(ctx.s("st"), st0 + 16 * (tt - 1))
                    st = sb["stream"][:, 1024 * tt:1024 * (tt + 1)]
                    dst32 = (sb["scr"].bitcast(F32)[:, 0:1024] if tt % 2 == 0
                             else sb["bcb"][:, 0:1024])
                    v.tensor_scalar(dst32, st,
                                    rstd[:, tt:tt + 1], nb[:, tt:tt + 1],
                                    ALU.mult, ALU.add).then_inc(ctx.s("ev"), 1)

        if mode == "T":
            @blk.tensor
            def _(t):
                for tt in range(NT):
                    t.wait_ge(ctx.s("ev"), ev0 + 7 + tt)
                    half = sb["scr"][:, 1024 * (tt % 2):1024 * (tt % 2) + 1024]
                    for dc in range(8):
                        if tt >= 1:
                            t.wait_ge(ctx.s("tr"), tr0 + 8 * (tt - 1) + dc + 1)
                        t.transpose(psB(dc), half[:, 128 * dc:128 * (dc + 1)],
                                    sb["ident"][:, :]).then_inc(ctx.s("mm"), 1)
            ctx.add("mm", 8 * NT)
            ctx.add("tr", 8 * NT)
        else:
            @blk.gpsimd
            def _(g):
                for tt in range(NT):
                    g.wait_ge(ctx.s("ev"), ev0 + 6 + tt + 1)
                    src32 = (sb["scr"].bitcast(F32)[:, 0:1024] if tt % 2 == 0
                             else sb["bcb"][:, 0:1024])
                    g.dma_start(out=dram["stage_xn32"].ap()[128 * tt:128 * (tt + 1), :],
                                in_=src32).then_inc(ctx.s("st"), 16)
            ctx.add("st", 16 * NT)

        ctx.add("ev", 6 + NT)
        ctx.add("sc", 2)

    def ph_l0_qkv(blk, wq="w_l0_wq", wk="w_l0_wk", wv="w_l0_wv"):
        ld0, mm0, ev0, st0 = ctx.v("ld"), ctx.v("mm"), ctx.v("ev"), ctx.v("st")
        tr_ready = ctx.v("tr")
        qkv_in = dram["a2a_qkv_in"].ap()

        @blk.gpsimd
        def _(g):
            for w, dst in ((wq, "wa"), (wk, "wb"), (wv, "wc")):
                for dc in range(8):
                    g.dma_start(out=sb[dst][:, 1024 * dc:1024 * (dc + 1)],
                                in_=dram[w].ap()[128 * dc:128 * (dc + 1), :]).then_inc(ctx.s("ld"), 16)
        ctx.add("ld", 16 * 24)

        cum_st = []
        tot = 0
        for g in range(96):
            tot += 16 if g < 64 else 64
            cum_st.append(tot)

        @blk.tensor
        def _(t):
            t.wait_ge(ctx.s("tr"), tr_ready)
            g = 0
            for which, wt in (("q", "wa"), ("k", "wb")):
                t.wait_ge(ctx.s("ld"), ld0 + 16 * 8 * (1 if which == "q" else 2))
                for ic in range(8):
                    for t4 in range(4):
                        bk = g % 2
                        if g >= 2:
                            t.wait_ge(ctx.s("ev"), ev0 + g - 1)
                        for dc in range(8):
                            last = t.matmul(pbank(bk),
                                            sb[wt][:, 1024 * dc + 128 * ic:1024 * dc + 128 * (ic + 1)],
                                            sb["xnT"][:, 2048 * dc + 512 * t4:2048 * dc + 512 * (t4 + 1)],
                                            start=(dc == 0), stop=(dc == 7))
                        last.then_inc(ctx.s("mm"), 1)
                        g += 1
            t.wait_ge(ctx.s("ld"), ld0 + 16 * 24)
            for tt in range(NT):
                for vc in range(2):
                    bk = g % 2
                    t.wait_ge(ctx.s("ev"), ev0 + g - 1)
                    for dc in range(8):
                        last = t.matmul(pbank(bk),
                                        sb["xnT"][:, 2048 * dc + 128 * tt:2048 * dc + 128 * (tt + 1)],
                                        sb["wc"][:, 1024 * dc + 512 * vc:1024 * dc + 512 * (vc + 1)],
                                        start=(dc == 0), stop=(dc == 7))
                    last.then_inc(ctx.s("mm"), 1)
                    g += 1

        @blk.vector
        def _(v):
            for g in range(96):
                v.wait_ge(ctx.s("mm"), mm0 + g + 1)
                if g >= 4:
                    v.wait_ge(ctx.s("st"), st0 + cum_st[g - 4])
                q = 512 * (g % 4)
                v.tensor_copy(sb["scr"][:, q:q + 512], pbank(g % 2)).then_inc(ctx.s("ev"), 1)

        @blk.sync
        def _(s):
            g = 0
            for which in ("q", "k"):
                off0 = 0 if which == "q" else SEG_Q
                for ic in range(8):
                    for t4 in range(4):
                        s.wait_ge(ctx.s("ev"), ev0 + g + 1)
                        q = 512 * (g % 4)
                        dst = qkv_in[ic, off0:off0 + SEG_Q].rearrange("(p n) -> p n", p=128)[:, 512 * t4:512 * (t4 + 1)]
                        s.dma_start(out=dst, in_=sb["scr"][:, q:q + 512]).then_inc(ctx.s("st"), 16)
                        g += 1
            for tt in range(NT):
                for vc in range(2):
                    s.wait_ge(ctx.s("ev"), ev0 + g + 1)
                    q = 512 * (g % 4)
                    for blki in range(4):
                        p = 4 * vc + blki
                        dst = qkv_in[p, 2 * SEG_Q + 16384 * tt:2 * SEG_Q + 16384 * (tt + 1)]
                        dst = dst.rearrange("(p n) -> p n", p=128)
                        s.dma_start(out=dst,
                                    in_=sb["scr"][:, q + 128 * blki:q + 128 * (blki + 1)]).then_inc(ctx.s("st"), 16)
                    g += 1

        ctx.add("mm", 96)
        ctx.add("ev", 96)
        ctx.add("st", 16 * (64 + 32 * 4))

    def ph_a2a(blk, name_in, name_out):
        st_all = ctx.v("st")

        @blk.gpsimd
        def _(g):
            g.wait_ge(ctx.s("st"), st_all)
            with allow_ici_all_to_all():
                g.collective_compute(
                    "AllToAll", ALU.bypass,
                    replica_groups=[list(range(8))],
                    ins=[dram[name_in].ap().opt()],
                    outs=[dram[name_out].ap().opt()],
                ).then_inc(ctx.s("cc"), 1)
            g.wait_ge(ctx.s("cc"), ctx.add("cc"))

    def emit_attn_loads(s, b):
        qkv_out = dram["a2a_qkv_out"].ap()
        for i in range(4):
            srcb = qkv_out[4 * b + i, 0:SEG_Q].rearrange("(p n) -> p n", p=128)
            s.dma_start(out=sb["qtp"][:, 2048 * i:2048 * (i + 1)], in_=srcb).then_inc(ctx.s("ld"), 16)
        for i in range(4):
            srcb = qkv_out[4 * b + i, SEG_Q:2 * SEG_Q].rearrange("(p n) -> p n", p=128)
            s.dma_start(out=sb["ktp"][:, 2048 * i:2048 * (i + 1)], in_=srcb).then_inc(ctx.s("ld"), 16)
        vtv = sb["vt"].rearrange("p (c w) -> p c w", w=130)
        for i in range(4):
            srcb = qkv_out[4 * b + i, 2 * SEG_Q:].rearrange("(c p n) -> c p n", c=16, p=128)
            s.dma_start(out=vtv[:, 16 * i:16 * (i + 1), 0:64],
                        in_=srcb[:, :, 0:64].transpose([1, 0, 2])).then_inc(ctx.s("ld"), 16)
            s.dma_start(out=vtv[:, 16 * i:16 * (i + 1), 65:129],
                        in_=srcb[:, :, 64:128].transpose([1, 0, 2])).then_inc(ctx.s("ld"), 16)

    def ph_rope(blk, b):
        ld0, mm0, ev0 = ctx.v("ld"), ctx.v("mm"), ctx.v("ev")

        @blk.sync
        def _(s):
            emit_attn_loads(s, b)
            s.dma_start(out=sb["cosr"][:, :], in_=dram["c_cosd"].ap()).then_inc(ctx.s("ld"), 16)
            s.dma_start(out=sb["sinr"][:, :], in_=dram["c_sind"].ap()).then_inc(ctx.s("ld"), 16)
            s.dma_start(out=sb["rs2"][:, 0:128], in_=dram["c_rotm"].ap()).then_inc(ctx.s("ld"), 16)
        ctx.add("ld", 16 * 19)

        chunks = [(tgt, c) for tgt in ("qtp", "ktp") for c in range(16)]

        @blk.tensor
        def _(t):
            t.wait_ge(ctx.s("ld"), ld0 + 16 * 19)
            for r, (tgt, c) in enumerate(chunks):
                bk = r % 2
                if r >= 2:
                    t.wait_ge(ctx.s("ev"), ev0 + 3 * (r - 2) + 1)
                t.matmul(pbank(bk), sb["rs2"][:, 0:128],
                         sb[tgt][:, 512 * c:512 * (c + 1)],
                         start=True, stop=True).then_inc(ctx.s("mm"), 1)

        @blk.vector
        def _(v):
            for r, (tgt, c) in enumerate(chunks):
                bk = r % 2
                cs = slice(512 * c, 512 * (c + 1))
                v.wait_ge(ctx.s("mm"), mm0 + r + 1)
                v.tensor_mul(sb["rs1"][:, 0:512], pbank(bk), sb["sinr"][:, cs]).then_inc(ctx.s("ev"), 1)
                v.tensor_mul(sb["rs1"][:, 512:1024], sb[tgt][:, cs],
                             sb["cosr"][:, cs]).then_inc(ctx.s("ev"), 1)
                v.wait_ge(ctx.s("ev"), ev0 + 3 * r + 2)
                v.tensor_add(sb[tgt][:, cs], sb["rs1"][:, 0:512],
                             sb["rs1"][:, 512:1024]).then_inc(ctx.s("ev"), 1)

        ctx.add("mm", 32)
        ctx.add("ev", 96)

    def ph_l0_attn_batch(blk, b, c0, do_loads=True):
        ld0, mm0, pv0, ev0, sc0, st0, tr0 = (ctx.v("ld"), ctx.v("mm"), ctx.v("pv"),
                                             ctx.v("ev"), ctx.v("sc"), ctx.v("st"),
                                             ctx.v("tr"))
        cc0 = ctx.v("cc")

        @blk.vector
        def _(v):
            v.memset(sb["vt"].rearrange("p (c w) -> p c w", w=130)[:, :, 64:130:65],
                     1.0).then_inc(ctx.s("cc"), 1)
        ctx.add("cc", 1)

        if do_loads:
            @blk.sync
            def _(s):
                emit_attn_loads(s, b)
            ctx.add("ld", 16 * 16)

        qts = list(range(16))
        nks = [4 * (qt + 1) for qt in qts]
        total_exp = sum(nks)
        cum_pv = np.cumsum(nks)

        @blk.tensor
        def _(t):
            t.wait_ge(ctx.s("ld"), ctx.v("ld"))
            t.wait_ge(ctx.s("cc"), cc0 + 1)
            t.wait_ge(ctx.s("ev"), ev0)
            e = 0
            for qi, qt in enumerate(qts):
                nk = nks[qt]
                for kc in range(nk):
                    bA, bB = 2 * (kc % 2), 2 * (kc % 2) + 1
                    if e >= 2:
                        t.wait_ge(ctx.s("sc"), sc0 + e - 1)
                    last = t.matmul(pbank(bA), sb["ktp"][0:64, 128 * kc:128 * (kc + 1)],
                                    sb["qtp"][0:64, 512 * qt:512 * (qt + 1)],
                                    start=True, stop=(kc < 4 * qt))
                    last = t.matmul(pbank(bB), sb["ktp"][64:128, 128 * kc:128 * (kc + 1)],
                                    sb["qtp"][64:128, 512 * qt:512 * (qt + 1)],
                                    start=True, stop=(kc < 4 * qt))
                    if kc >= 4 * qt:
                        r = kc - 4 * qt
                        t.matmul(pbank(bA), sb["jlo"][0:64, :], sb["cml"][0:64, 512 * r:512 * (r + 1)],
                                 start=False, stop=False)
                        t.matmul(pbank(bA), sb["jhi"][0:64, :], sb["cmh"][0:64, 512 * r:512 * (r + 1)],
                                 start=False, stop=True)
                        t.matmul(pbank(bB), sb["jlo"][64:128, :], sb["cml"][64:128, 512 * r:512 * (r + 1)],
                                 start=False, stop=False)
                        last = t.matmul(pbank(bB), sb["jhi"][64:128, :],
                                        sb["cmh"][64:128, 512 * r:512 * (r + 1)],
                                        start=False, stop=True)
                    last.then_inc(ctx.s("mm"), 1)
                    if kc >= 1:
                        t.wait_ge(ctx.s("sc"), sc0 + e)
                        pc = kc - 1
                        if pc == 0 and qi > 0:
                            t.wait_ge(ctx.s("ev"), ev0 + 3 * qi)
                        half = 1024 * (pc % 4)
                        t.matmul(pbank(4, 0, 65), sb["vt"][:, 130 * pc:130 * pc + 65],
                                 sb["pt"][:, half:half + 512], start=(pc == 0), stop=False)
                        t.matmul(pbank(5, 0, 65), sb["vt"][:, 130 * pc + 65:130 * pc + 130],
                                 sb["pt"][:, half + 512:half + 1024],
                                 start=(pc == 0), stop=False).then_inc(ctx.s("pv"), 1)
                    e += 1
                t.wait_ge(ctx.s("sc"), sc0 + e)
                pc = nk - 1
                if pc == 0 and qi > 0:
                    t.wait_ge(ctx.s("ev"), ev0 + 3 * qi)
                half = 1024 * (pc % 4)
                t.matmul(pbank(4, 0, 65), sb["vt"][:, 130 * pc:130 * pc + 65],
                         sb["pt"][:, half:half + 512], start=(pc == 0), stop=True)
                t.matmul(pbank(5, 0, 65), sb["vt"][:, 130 * pc + 65:130 * pc + 130],
                         sb["pt"][:, half + 512:half + 1024],
                         start=(pc == 0), stop=True).then_inc(ctx.s("pv"), 1)
                t.wait_ge(ctx.s("ev"), ev0 + 3 * qi + 1)
                t.matmul(pbank(6, 0, 64), sb["onesf"][0:1, 0:64], sb["rec"][0:1, 0:512],
                         start=True, stop=True)
                t.matmul(pbank(7, 0, 64), sb["onesf"][32:33, 0:64], sb["rec"][32:33, 0:512],
                         start=True, stop=True).then_inc(ctx.s("tr"), 1)

        @blk.scalar
        def _(sc):
            e = 0
            for qi, qt in enumerate(qts):
                nk = nks[qt]
                for kc in range(nk):
                    sc.wait_ge(ctx.s("mm"), mm0 + e + 1)
                    if e >= 4:
                        sc.wait_ge(ctx.s("pv"), pv0 + e - 3)
                    half = 1024 * (kc % 4)
                    stp = kc % 2
                    sc.activation(sb["pt"][:, half:half + 1024],
                                  ps[:, 1024 * stp:1024 * stp + 1024],
                                  AF.Exp, bias=np.float32(-c0),
                                  scale=0.125).then_inc(ctx.s("sc"), 1)
                    e += 1

        @blk.vector
        def _(v):
            for qi, qt in enumerate(qts):
                v.wait_ge(ctx.s("pv"), pv0 + int(cum_pv[qi]))
                v.tensor_copy(sb["rec"][0:1, 0:512], ps[64:65, 512 * 4:512 * 4 + 512])
                v.tensor_copy(sb["rec"][32:33, 0:512], ps[64:65, 512 * 5:512 * 5 + 512]).then_inc(ctx.s("ev"), 1)
                v.wait_ge(ctx.s("tr"), tr0 + qi + 1)
                v.reciprocal(sb["bcb"][0:64, 0:512], pbank(6, 0, 64))
                v.reciprocal(sb["bcb"][0:64, 512:1024], pbank(7, 0, 64)).then_inc(ctx.s("ev"), 1)
                v.wait_ge(ctx.s("ev"), ev0 + 3 * qi + 2)
                if qi >= 2:
                    v.wait_ge(ctx.s("st"), st0 + 16 * (qi - 1))
                h = 512 * (qi % 2)
                v.tensor_mul(sb["aot2"][0:64, h:h + 512],
                             pbank(4, 0, 64), sb["bcb"][0:64, 0:512])
                v.tensor_mul(sb["aot2"][64:128, h:h + 512],
                             pbank(5, 0, 64), sb["bcb"][0:64, 512:1024]).then_inc(ctx.s("ev"), 1)

        @blk.gpsimd
        def _(g):
            for qi, qt in enumerate(qts):
                g.wait_ge(ctx.s("ev"), ev0 + 3 * qi + 3)
                h = 512 * (qi % 2)
                blkv = dram["a2a_ao_in"].ap()[4 * b + qt // 4, :].rearrange("(p n) -> p n", p=128)
                dst = blkv[:, 512 * (qt % 4):512 * (qt % 4 + 1)]
                g.dma_start(out=dst, in_=sb["aot2"][:, h:h + 512]).then_inc(ctx.s("st"), 16)

        ctx.add("mm", total_exp)
        ctx.add("sc", total_exp)
        ctx.add("pv", total_exp)
        ctx.add("tr", 16)
        ctx.add("ev", 3 * 16)
        ctx.add("st", 16 * 16)

    def ph_l0_wo(blk, wo="w_l0_wo", bo="w_l0_bo"):
        ld0, mm0, ev0 = ctx.v("ld"), ctx.v("mm"), ctx.v("ev")
        ao = dram["a2a_ao_out"].ap()

        @blk.gpsimd
        def _(g):
            for dc in range(8):
                g.dma_start(out=sb["wd"][:, 1024 * dc:1024 * (dc + 1)],
                            in_=dram[wo].ap()[128 * dc:128 * (dc + 1), :]).then_inc(ctx.s("ld"), 16)
            g.dma_start(out=sb["bias"][0:1, 0:D], in_=dram[bo].ap()).then_inc(ctx.s("ld"), 16)

        @blk.sync
        def _(s):
            s.wait_ge(ctx.s("cc"), ctx.v("cc"))
            for ic in range(8):
                src = ao[ic, :].rearrange("(p n) -> p n", p=128)
                s.dma_start(out=sb["aoT"][:, 2048 * ic:2048 * (ic + 1)], in_=src).then_inc(ctx.s("ld"), 16)
        ctx.add("ld", 16 * 17)

        @blk.tensor
        def _(t):
            t.wait_ge(ctx.s("ld"), ld0 + 16 * 17)
            g = 0
            for tt in range(NT):
                for ds in range(2):
                    bk = g % 2
                    if g >= 2:
                        t.wait_ge(ctx.s("ev"), ev0 + g - 1)
                    for ic in range(8):
                        t.matmul(pbank(bk),
                                 sb["aoT"][:, 2048 * ic + 128 * tt:2048 * ic + 128 * (tt + 1)],
                                 sb["wd"][:, 1024 * ic + 512 * ds:1024 * ic + 512 * (ds + 1)],
                                 start=(ic == 0), stop=False)
                    t.matmul(pbank(bk), sb["onesb"][0:1, 0:128],
                             sb["bias"][0:1, 512 * ds:512 * (ds + 1)],
                             start=False, stop=True).then_inc(ctx.s("mm"), 1)
                    g += 1

        @blk.vector
        def _(v):
            for g in range(32):
                v.wait_ge(ctx.s("mm"), mm0 + g + 1)
                tt, ds = g // 2, g % 2
                dst = sb["stream"][:, 1024 * tt + 512 * ds:1024 * tt + 512 * (ds + 1)]
                v.tensor_add(dst, dst, pbank(g % 2)).then_inc(ctx.s("ev"), 1)

        ctx.add("mm", 32)
        ctx.add("ev", 32)

    def ph_l0_ffn(blk):
        ld0, mm0, ev0, sc0 = ctx.v("ld"), ctx.v("mm"), ctx.v("ev"), ctx.v("sc")
        tr_ready = ctx.v("tr")

        events = []
        ev_idx = {}
        n = 0
        for g in range(8):
            n += 1
            ev_idx[("pt", g)] = n
        for t4 in range(4):
            for fc in range(16):
                n += 1
                ev_idx[("ht", t4, fc)] = n
            n += 1
            ev_idx[("qpart", t4)] = n
            for fc in range(16, 32):
                n += 1
                ev_idx[("ht", t4, fc)] = n
            for rc in range(2):
                n += 1
                ev_idx[("qfin", t4, rc)] = n
            for tt4 in range(4):
                for ds in range(2):
                    n += 1
                    ev_idx[("y", t4, tt4, ds)] = n
        n_mm_total = n

        ve_idx = {}
        n = 0
        for g in range(8):
            n += 1
            ve_idx[("pt", g)] = n
        for t4 in range(4):
            for rc in range(2):
                n += 1
                ve_idx[("qt2", t4, rc)] = n
            for tt4 in range(4):
                for ds in range(2):
                    n += 1
                    ve_idx[("y", t4, tt4, ds)] = n
        n_ev_total = n

        sc_idx = {}
        n = 0
        for t4 in range(4):
            for fc in range(32):
                n += 1
                sc_idx[(t4, fc)] = n
        n_sc_total = n

        @blk.gpsimd
        def _(g):
            for dc in range(8):
                g.dma_start(out=sb["u1"][:, 256 * dc:256 * (dc + 1)],
                            in_=dram["w_l0_u1"].ap()[128 * dc:128 * (dc + 1), :]).then_inc(ctx.s("ld"), 16)
            for rc in range(2):
                g.dma_start(out=sb["v1"][:, 4096 * rc:4096 * (rc + 1)],
                            in_=dram["w_l0_v1"].ap()[128 * rc:128 * (rc + 1), :]).then_inc(ctx.s("ld"), 16)
            for fc in range(32):
                g.dma_start(out=sb["u2"][:, 256 * fc:256 * (fc + 1)],
                            in_=dram["w_l0_u2"].ap()[128 * fc:128 * (fc + 1), :]).then_inc(ctx.s("ld"), 16)
            for rc in range(2):
                g.dma_start(out=sb["v2"][:, 1024 * rc:1024 * (rc + 1)],
                            in_=dram["w_l0_v2"].ap()[128 * rc:128 * (rc + 1), :]).then_inc(ctx.s("ld"), 16)
            g.dma_start(out=sb["bias"][0:1, :], in_=dram["w_l0_b1"].ap()).then_inc(ctx.s("ld"), 16)
            g.dma_start(out=sb["bias2"][0:1, 0:D], in_=dram["w_l0_b2"].ap()).then_inc(ctx.s("ld"), 16)
        NLD = 8 + 2 + 32 + 2 + 2
        ctx.add("ld", 16 * NLD)

        @blk.tensor
        def _(t):
            t.wait_ge(ctx.s("tr"), tr_ready)
            t.wait_ge(ctx.s("ld"), ld0 + 16 * NLD)
            for g in range(8):
                rc, t4 = g // 4, g % 4
                bk = g % 2
                if g >= 2:
                    t.wait_ge(ctx.s("ev"), ev0 + ve_idx[("pt", g - 2)])
                last = None
                for dc in range(8):
                    last = t.matmul(pbank(bk),
                                    sb["u1"][:, 256 * dc + 128 * rc:256 * dc + 128 * (rc + 1)],
                                    sb["xnT"][:, 2048 * dc + 512 * t4:2048 * dc + 512 * (t4 + 1)],
                                    start=(dc == 0), stop=(dc == 7))
                last.then_inc(ctx.s("mm"), 1)
            for t4 in range(4):
                if t4 == 0:
                    t.wait_ge(ctx.s("ev"), ev0 + ve_idx[("pt", 7)])
                else:
                    t.wait_ge(ctx.s("ev"), ev0 + ve_idx[("qt2", t4 - 1, 1)])
                for half in range(2):
                    for fci in range(16):
                        fc = 16 * half + fci
                        bk = 2 + (fc % 4)
                        if fc >= 4:
                            t.wait_ge(ctx.s("sc"), sc0 + sc_idx[(t4, fc)] - 4)
                        for rc in range(2):
                            t.matmul(pbank(bk),
                                     sb["v1"][:, 4096 * rc + 128 * fc:4096 * rc + 128 * (fc + 1)],
                                     sb["pt2"][:, 2048 * rc + 512 * t4:2048 * rc + 512 * (t4 + 1)],
                                     start=(rc == 0), stop=False)
                        t.matmul(pbank(bk), sb["bias"][0:1, 128 * fc:128 * (fc + 1)],
                                 sb["onesb"][0:1, 0:512], start=False,
                                 stop=True).then_inc(ctx.s("mm"), 1)
                    t.wait_ge(ctx.s("sc"), sc0 + sc_idx[(t4, 16 * half + 15)])
                    last = None
                    for rc in range(2):
                        for fci in range(16):
                            fc = 16 * half + fci
                            last = t.matmul(pbank(rc),
                                            sb["u2"][:, 256 * fc + 128 * rc:256 * fc + 128 * (rc + 1)],
                                            sb["ht"][:, 512 * fci:512 * (fci + 1)],
                                            start=(fc == 0), stop=(fc == 31))
                        if half == 1:
                            last.then_inc(ctx.s("mm"), 1)
                    if half == 0:
                        last.then_inc(ctx.s("mm"), 1)
                for tt4 in range(4):
                    for ds in range(2):
                        gy = 2 * tt4 + ds
                        bk = 6 + (gy % 2)
                        t.wait_ge(ctx.s("ev"), ev0 + ve_idx[("qt2", t4, 1)])
                        if gy >= 2:
                            t.wait_ge(ctx.s("ev"), ev0 + ve_idx[("y", t4, (gy - 2) // 2, (gy - 2) % 2)])
                        elif t4 > 0:
                            t.wait_ge(ctx.s("ev"), ev0 + ve_idx[("y", t4 - 1, 3, 1)])
                        for rc in range(2):
                            t.matmul(pbank(bk),
                                     sb["qt2"][:, 512 * rc + 128 * tt4:512 * rc + 128 * (tt4 + 1)],
                                     sb["v2"][:, 1024 * rc + 512 * ds:1024 * rc + 512 * (ds + 1)],
                                     start=(rc == 0), stop=False)
                        t.matmul(pbank(bk), sb["onesb"][0:1, 0:128],
                                 sb["bias2"][0:1, 512 * ds:512 * (ds + 1)], start=False,
                                 stop=True).then_inc(ctx.s("mm"), 1)

        @blk.scalar
        def _(sc):
            for t4 in range(4):
                for fc in range(32):
                    sc.wait_ge(ctx.s("mm"), mm0 + ev_idx[("ht", t4, fc)])
                    if fc == 16:
                        sc.wait_ge(ctx.s("mm"), mm0 + ev_idx[("qpart", t4)])
                    bk = 2 + (fc % 4)
                    sc.activation(sb["ht"][:, 512 * (fc % 16):512 * (fc % 16 + 1)], pbank(bk),
                                  AF.Gelu_apprx_tanh, bias=0.0, scale=1.0).then_inc(ctx.s("sc"), 1)

        @blk.vector
        def _(v):
            for g in range(8):
                v.wait_ge(ctx.s("mm"), mm0 + ev_idx[("pt", g)])
                rc, t4 = g // 4, g % 4
                v.tensor_copy(sb["pt2"][:, 2048 * rc + 512 * t4:2048 * rc + 512 * (t4 + 1)],
                              pbank(g % 2)).then_inc(ctx.s("ev"), 1)
            for t4 in range(4):
                for rc in range(2):
                    v.wait_ge(ctx.s("mm"), mm0 + ev_idx[("qfin", t4, rc)])
                    v.tensor_copy(sb["qt2"][:, 512 * rc:512 * (rc + 1)], pbank(rc)).then_inc(ctx.s("ev"), 1)
                for tt4 in range(4):
                    for ds in range(2):
                        v.wait_ge(ctx.s("mm"), mm0 + ev_idx[("y", t4, tt4, ds)])
                        bk = 6 + ((2 * tt4 + ds) % 2)
                        tt = 4 * t4 + tt4
                        dst = sb["stream"][:, 1024 * tt + 512 * ds:1024 * tt + 512 * (ds + 1)]
                        v.tensor_add(dst, dst, pbank(bk)).then_inc(ctx.s("ev"), 1)

        ctx.add("mm", n_mm_total)
        ctx.add("sc", n_sc_total)
        ctx.add("ev", n_ev_total)

    def ph_store(blk):
        st0 = ctx.v("st")

        @blk.sync
        def _(s):
            for tt in range(NT):
                s.dma_start(out=y_out.ap()[128 * tt:128 * (tt + 1), :],
                            in_=sb["stream"][:, 1024 * tt:1024 * (tt + 1)]).then_inc(ctx.s("st"), 16)
            s.wait_ge(ctx.s("st"), st0 + 16 * NT)
        ctx.add("st", 16 * NT)


    NORM_L2 = float(1.0 / np.sqrt(np.sqrt(64.0)))

    def ph_l2_comb(blk, wname, bcname, bias_off):
        ld0, mm0, ev0 = ctx.v("ld"), ctx.v("mm"), ctx.v("ev")

        @blk.gpsimd
        def _(g):
            for rc in range(8):
                g.dma_start(out=sb["wraw"][:, 1024 * rc:1024 * (rc + 1)],
                            in_=dram[wname].ap()[128 * rc:128 * (rc + 1), :]).then_inc(ctx.s("ld"), 16)
            g.dma_start(out=sb["bcl"][:, :], in_=dram[bcname].ap()).then_inc(ctx.s("ld"), 16)
            g.dma_start(out=sb["pl2"][:, :], in_=dram["w_l2_pT"].ap()).then_inc(ctx.s("ld"), 16)
        ctx.add("ld", 16 * 10)

        @blk.tensor
        def _(t):
            t.wait_ge(ctx.s("ld"), ld0 + 16 * 10)
            for dc in range(8):
                bkE, bkO = (0, 1) if dc % 2 == 0 else (2, 3)
                if dc >= 2:
                    t.wait_ge(ctx.s("ev"), ev0 + dc - 1)
                last = None
                for h in range(16):
                    base = 64 * (h % 2)
                    bk = bkE if h % 2 == 0 else bkO
                    lhs = sb["wraw"][base:base + 64,
                                     1024 * (h // 2) + 128 * dc:1024 * (h // 2) + 128 * (dc + 1)]
                    last = t.matmul(pbank(bk)[:, 64 * (h // 2):64 * (h // 2) + 64],
                                    lhs, sb["pl2"][base:base + 64, :],
                                    start=True, stop=True)
                last.then_inc(ctx.s("mm"), 1)
            last = None
            for h in range(16):
                base = 64 * (h % 2)
                bkb = 6 if h % 2 == 0 else 7
                last = t.matmul(ps[0:1, 512 * bkb + 64 * (h // 2):512 * bkb + 64 * (h // 2) + 64],
                                sb["bcl"][base:base + 64, h // 2:h // 2 + 1],
                                sb["pl2"][base:base + 64, :], start=True, stop=True)
            last.then_inc(ctx.s("mm"), 1)

        @blk.vector
        def _(v):
            for dc in range(8):
                v.wait_ge(ctx.s("mm"), mm0 + dc + 1)
                bkE, bkO = (0, 1) if dc % 2 == 0 else (2, 3)
                dst3 = sb["wcmb"].rearrange("p (c w) -> p c w", w=64)
                psE = pbank(bkE).rearrange("p (c w) -> p c w", w=64)
                psO = pbank(bkO).rearrange("p (c w) -> p c w", w=64)
                v.tensor_copy(dst3[:, 16 * dc:16 * (dc + 1):2, :], psE[:, 0:8, :])
                v.tensor_copy(dst3[:, 16 * dc + 1:16 * (dc + 1):2, :],
                              psO[:, 0:8, :]).then_inc(ctx.s("ev"), 1)
            v.wait_ge(ctx.s("mm"), mm0 + 9)
            bias3 = sb["bias"].rearrange("o (c w) -> o c w", w=64)
            ps3a = ps.rearrange("p (b w) -> p b w", w=64)
            v.tensor_copy(bias3[0:1, bias_off // 64:bias_off // 64 + 16:2, :],
                          ps3a[0:1, 48:56, :])
            v.tensor_copy(bias3[0:1, bias_off // 64 + 1:bias_off // 64 + 16:2, :],
                          ps3a[0:1, 56:64, :]).then_inc(ctx.s("ev"), 1)

        ctx.add("mm", 9)
        ctx.add("ev", 9)

    def ph_l2_qf(blk):
        mm0, ev0 = ctx.v("mm"), ctx.v("ev")
        tr_ready = ctx.v("tr")

        @blk.tensor
        def _(t):
            t.wait_ge(ctx.s("tr"), tr_ready)
            g = 0
            for ic in range(8):
                for t4 in range(4):
                    bk = g % 2
                    if g >= 2:
                        t.wait_ge(ctx.s("ev"), ev0 + 2 * (g - 1))
                    for dc in range(8):
                        t.matmul(pbank(bk),
                                 sb["wcmb"][:, 1024 * dc + 128 * ic:1024 * dc + 128 * (ic + 1)],
                                 sb["xnT"][:, 2048 * dc + 512 * t4:2048 * dc + 512 * (t4 + 1)],
                                 start=(dc == 0), stop=False)
                    t.matmul(pbank(bk), sb["bias"][0:1, 128 * ic:128 * (ic + 1)],
                             sb["onesb"][0:1, 0:512], start=False,
                             stop=True).then_inc(ctx.s("mm"), 1)
                    g += 1

        @blk.vector
        def _(v):
            for g in range(32):
                ic, t4 = g // 4, g % 4
                v.wait_ge(ctx.s("mm"), mm0 + g + 1)
                dst = sb["qft"][:, 2048 * ic + 512 * t4:2048 * ic + 512 * (t4 + 1)]
                v.tensor_scalar(dst, pbank(g % 2), NORM_L2, 0.0,
                                ALU.mult, ALU.max).then_inc(ctx.s("ev"), 1)
                v.wait_ge(ctx.s("ev"), ev0 + 2 * g + 1)
                v.tensor_scalar(dst, dst, 1e-6, None, ALU.add).then_inc(ctx.s("ev"), 1)

        ctx.add("mm", 32)
        ctx.add("ev", 64)

    def ph_l2_kv(blk):
        ld0, mm0, ev0 = ctx.v("ld"), ctx.v("mm"), ctx.v("ev")

        @blk.gpsimd
        def _(g):
            for dc in range(8):
                g.dma_start(out=sb["l2wv"][:, 1024 * dc:1024 * (dc + 1)],
                            in_=dram["w_l2_wvT"].ap()[128 * dc:128 * (dc + 1), :]).then_inc(ctx.s("ld"), 16)
            g.dma_start(out=sb["bias"][0:1, 2048:3072],
                        in_=dram["w_l2_bv"].ap()).then_inc(ctx.s("ld"), 16)
            g.dma_start(out=sb["bias"][0:1, 3072:4096],
                        in_=dram["w_l2_bo"].ap()).then_inc(ctx.s("ld"), 16)
        ctx.add("ld", 16 * 10)

        @blk.vector
        def _(v):
            v.memset(sb["v22"].rearrange("p (c w) -> p c w", w=65)[:, :, 64:65],
                     1.0).then_inc(ctx.s("ev"), 1)
        ev1 = ev0 + 1
        ctx.add("ev", 1)

        @blk.tensor
        def _(t):
            t.wait_ge(ctx.s("ld"), ld0 + 16 * 10)
            t.wait_ge(ctx.s("ev"), ev1)
            for tt in range(NT):
                for which in range(2):
                    for vc in range(2):
                        if tt >= 1:
                            t.wait_ge(ctx.s("ev"), ev1 + 5 * tt)
                        bk = 2 * which + vc
                        wsrc = "l2wv" if which == 0 else "wcmb"
                        for dc in range(8):
                            t.matmul(pbank(bk),
                                     sb["xnT"][:, 2048 * dc + 128 * tt:2048 * dc + 128 * (tt + 1)],
                                     sb[wsrc][:, 1024 * dc + 512 * vc:1024 * dc + 512 * (vc + 1)],
                                     start=(dc == 0), stop=False)
                        boff = 2048 if which == 0 else 1024
                        t.matmul(pbank(bk), sb["onesb"][0:1, 0:128],
                                 sb["bias"][0:1, boff + 512 * vc:boff + 512 * (vc + 1)],
                                 start=False, stop=True).then_inc(ctx.s("mm"), 1)
                t.wait_ge(ctx.s("ev"), ev1 + 5 * tt + 5)
                last = None
                for h in range(16):
                    bkv = 5 + h // 7
                    coff = 65 * (h % 7)
                    last = t.matmul(ps[0:64, 512 * bkv + coff:512 * bkv + coff + 65],
                                    sb["kf2"][:, 64 * h:64 * (h + 1)],
                                    sb["v22"][:, 65 * h:65 * (h + 1)],
                                    start=(tt == 0), stop=(tt == NT - 1))
                last.then_inc(ctx.s("mm"), 1)

        @blk.vector
        def _(v):
            v2v = sb["v22"].rearrange("p (c w) -> p c w", w=65)
            for tt in range(NT):
                for vc in range(2):
                    v.wait_ge(ctx.s("mm"), mm0 + 5 * tt + vc + 1)
                    v.tensor_copy(v2v[:, 8 * vc:8 * (vc + 1), 0:64],
                                  pbank(vc)).then_inc(ctx.s("ev"), 1)
                for vc in range(2):
                    v.wait_ge(ctx.s("mm"), mm0 + 5 * tt + 2 + vc + 1)
                    v.tensor_scalar(sb["kf2"][:, 512 * vc:512 * (vc + 1)],
                                    pbank(2 + vc), NORM_L2, 0.0,
                                    ALU.mult, ALU.max).then_inc(ctx.s("ev"), 1)
                v.wait_ge(ctx.s("ev"), ev1 + 5 * tt + 4)
                v.tensor_scalar(sb["kf2"][:, :], sb["kf2"][:, :], 1e-6, None,
                                ALU.add).then_inc(ctx.s("ev"), 1)

        ctx.add("mm", NT * 5)
        ctx.add("ev", NT * 5)

    def ph_l2_ar(blk):
        mm_all, ev0, st0 = ctx.v("mm"), ctx.v("ev"), ctx.v("st")

        @blk.vector
        def _(v):
            v.wait_ge(ctx.s("mm"), mm_all)
            v.tensor_copy(sb["kvs"][0:64, 0:455], ps[0:64, 512 * 5:512 * 5 + 455])
            v.tensor_copy(sb["kvs"][0:64, 455:910], ps[0:64, 512 * 6:512 * 6 + 455])
            v.tensor_copy(sb["kvs"][0:64, 910:1040],
                          ps[0:64, 512 * 7:512 * 7 + 130]).then_inc(ctx.s("ev"), 1)

        @blk.gpsimd
        def _(g):
            g.wait_ge(ctx.s("ev"), ev0 + 1)
            for h in range(16):
                g.dma_start(out=dram["ar_in"].ap()[h],
                            in_=sb["kvs"][0:64, 65 * h:65 * (h + 1)]).then_inc(ctx.s("st"), 16)
            g.wait_ge(ctx.s("st"), st0 + 16 * 16)
            g.collective_compute(
                "AllReduce", ALU.add,
                replica_groups=[[0, 1, 2, 3], [4, 5, 6, 7]],
                ins=[dram["ar_in"].ap().opt()],
                outs=[dram["ar_out"].ap().opt()],
            ).then_inc(ctx.s("cc"), 1)
            g.wait_ge(ctx.s("cc"), ctx.add("cc"))

        ctx.add("ev", 1)
        ctx.add("st", 16 * 16)

    def ph_l2_out(blk):
        ld0, mm0, ev0, tr0 = ctx.v("ld"), ctx.v("mm"), ctx.v("ev"), ctx.v("tr")
        dn = sb["stat"][:, 0:16]
        rc = sb["stat"][:, 16:32]

        def slot(h):
            p, half = h // 2, h % 2
            bank = (0 if half == 0 else 2) + (p // 6)
            return 512 * bank + 66 * (p % 6)

        @blk.gpsimd
        def _(g):
            for p in range(8):
                g.dma_start(out=sb["kvb"][0:64, 65 * p:65 * (p + 1)],
                            in_=dram["ar_out"].ap()[2 * p]).then_inc(ctx.s("ld"), 16)
                g.dma_start(out=sb["kvb"][64:128, 65 * p:65 * (p + 1)],
                            in_=dram["ar_out"].ap()[2 * p + 1]).then_inc(ctx.s("ld"), 16)
        ctx.add("ld", 16 * 16)

        @blk.tensor
        def _(t):
            t.wait_ge(ctx.s("ld"), ld0 + 16 * 16)
            for tt in range(NT):
                if tt >= 1:
                    t.wait_ge(ctx.s("ev"), ev0 + 5 * (tt - 1) + 4)
                last = None
                for h in range(16):
                    p, half = h // 2, h % 2
                    r0 = 64 * half
                    last = t.matmul(ps[:, slot(h):slot(h) + 65],
                                    sb["qft"][r0:r0 + 64,
                                              2048 * p + 128 * tt:2048 * p + 128 * (tt + 1)],
                                    sb["kvb"][r0:r0 + 64, 65 * p:65 * (p + 1)],
                                    start=True, stop=True)
                last.then_inc(ctx.s("mm"), 1)
                t.wait_ge(ctx.s("ev"), ev0 + 5 * tt + 4)
                if tt >= 1:
                    t.wait_ge(ctx.s("ev"), ev0 + 5 * (tt - 1) + 5)
                hq = 1024 * (tt % 2)
                ins = None
                for ic in range(8):
                    ins = t.transpose(ps.bitcast(BF16)[:, 6144 + 128 * ic:6144 + 128 * (ic + 1)],
                                      sb["aoq2"][:, hq + 128 * ic:hq + 128 * (ic + 1)],
                                      sb["ident"][:, :])
                ins.then_inc(ctx.s("tr"), 1)

        @blk.vector
        def _(v):
            for tt in range(NT):
                v.wait_ge(ctx.s("mm"), mm0 + tt + 1)
                ins = None
                for h in range(16):
                    ins = v.tensor_copy(dn[:, h:h + 1], ps[:, slot(h) + 64:slot(h) + 65])
                ins.then_inc(ctx.s("ev"), 1)
                v.wait_ge(ctx.s("ev"), ev0 + 5 * tt + 1)
                v.tensor_scalar(dn, dn, 1e-6, None, ALU.add).then_inc(ctx.s("ev"), 1)
                v.wait_ge(ctx.s("ev"), ev0 + 5 * tt + 2)
                v.reciprocal(rc, dn).then_inc(ctx.s("ev"), 1)
                v.wait_ge(ctx.s("ev"), ev0 + 5 * tt + 3)
                if tt >= 2:
                    v.wait_ge(ctx.s("tr"), tr0 + tt - 1)
                hq = 1024 * (tt % 2)
                ins = None
                for h in range(16):
                    p, half = h // 2, h % 2
                    ins = v.tensor_scalar(sb["aoq2"][:, hq + 128 * p + 64 * half:hq + 128 * p + 64 * (half + 1)],
                                          ps[:, slot(h):slot(h) + 64],
                                          rc[:, h:h + 1], None, ALU.mult)
                ins.then_inc(ctx.s("ev"), 1)
                v.wait_ge(ctx.s("tr"), tr0 + tt + 1)
                ins = None
                for ic in range(8):
                    ins = v.tensor_copy(sb["aotl2"][:, 2048 * ic + 128 * tt:2048 * ic + 128 * (tt + 1)],
                                        ps.bitcast(BF16)[:, 6144 + 128 * ic:6144 + 128 * (ic + 1)])
                ins.then_inc(ctx.s("ev"), 1)

        ctx.add("mm", NT)
        ctx.add("tr", NT)
        ctx.add("ev", 5 * NT)

    def ph_l2_wo(blk):
        ld0, mm0, ev0 = ctx.v("ld"), ctx.v("mm"), ctx.v("ev")

        @blk.gpsimd
        def _(g):
            for dc in range(8):
                g.dma_start(out=sb["wd"][:, 1024 * dc:1024 * (dc + 1)],
                            in_=dram["w_l2_woT"].ap()[128 * dc:128 * (dc + 1), :]).then_inc(ctx.s("ld"), 16)
        ctx.add("ld", 16 * 8)

        @blk.tensor
        def _(t):
            t.wait_ge(ctx.s("ld"), ld0 + 16 * 8)
            g = 0
            for tt in range(NT):
                for ds in range(2):
                    bk = g % 2
                    if g >= 2:
                        t.wait_ge(ctx.s("ev"), ev0 + g - 1)
                    for ic in range(8):
                        t.matmul(pbank(bk),
                                 sb["aotl2"][:, 2048 * ic + 128 * tt:2048 * ic + 128 * (tt + 1)],
                                 sb["wd"][:, 1024 * ic + 512 * ds:1024 * ic + 512 * (ds + 1)],
                                 start=(ic == 0), stop=False)
                    t.matmul(pbank(bk), sb["onesb"][0:1, 0:128],
                             sb["bias"][0:1, 3072 + 512 * ds:3072 + 512 * (ds + 1)],
                             start=False, stop=True).then_inc(ctx.s("mm"), 1)
                    g += 1

        @blk.vector
        def _(v):
            for g in range(32):
                v.wait_ge(ctx.s("mm"), mm0 + g + 1)
                tt, ds = g // 2, g % 2
                dst = sb["stream"][:, 1024 * tt + 512 * ds:1024 * tt + 512 * (ds + 1)]
                v.tensor_add(dst, dst, pbank(g % 2)).then_inc(ctx.s("ev"), 1)

        ctx.add("mm", 32)
        ctx.add("ev", 32)

    def ph_l2_ffn(blk):
        ld0, mm0, ev0, sc0 = ctx.v("ld"), ctx.v("mm"), ctx.v("ev"), ctx.v("sc")
        tr_ready = ctx.v("tr")

        def ht_mm_idx(t4, fc):
            return t4 * 65 + fc + 1

        def y_mm_idx(t4, fc):
            return t4 * 65 + 32 + fc + 1

        def ybias_mm_idx(t4):
            return t4 * 65 + 65

        @blk.gpsimd
        def _(g):
            g.dma_start(out=sb["bias"][0:1, :], in_=dram["w_l2_b1"].ap()).then_inc(ctx.s("ld"), 16)
            g.dma_start(out=sb["bias2"][0:1, :], in_=dram["w_l2_b2"].ap()).then_inc(ctx.s("ld"), 16)
            for t4 in range(4):
                for q in range(16):
                    if t4 * 16 + q >= 2:
                        pf = t4 * 16 + q - 2
                        g.wait_ge(ctx.s("mm"), mm0 + ht_mm_idx(pf // 16, 2 * (pf % 16) + 1))
                    srcw = dram["w_l2_w1"].ap()[:, 256 * q:256 * (q + 1)]
                    srcw = srcw.rearrange("(c p) m -> p c m", p=128)
                    dst = sb["w1s"][:, 2048 * (q % 2):2048 * (q % 2 + 1)]
                    g.dma_start(out=dst.rearrange("p (c m) -> p c m", m=256),
                                in_=srcw).then_inc(ctx.s("ld"), 16)
                for q in range(16):
                    if t4 * 16 + q >= 2:
                        pf = t4 * 16 + q - 2
                        g.wait_ge(ctx.s("mm"), mm0 + y_mm_idx(pf // 16, 2 * (pf % 16) + 1))
                    srcw = dram["w_l2_w2"].ap()[256 * q:256 * (q + 1), :]
                    srcw = srcw.rearrange("(f p) n -> p f n", p=128)
                    dst = sb["w2s"][:, 2048 * (q % 2):2048 * (q % 2 + 1)]
                    g.dma_start(out=dst.rearrange("p (f n) -> p f n", f=2),
                                in_=srcw).then_inc(ctx.s("ld"), 16)

        NLD_T = 2 + 4 * 32
        ctx.add("ld", 16 * NLD_T)

        def ld_cnt_w1(t4, fc):
            return 2 + t4 * 32 + fc // 2 + 1

        def ld_cnt_w2(t4, fc):
            return 2 + t4 * 32 + 16 + fc // 2 + 1

        @blk.tensor
        def _(t):
            t.wait_ge(ctx.s("tr"), tr_ready)
            t.wait_ge(ctx.s("ld"), ld0 + 16 * 2)
            for t4 in range(4):
                if t4 >= 1:
                    t.wait_ge(ctx.s("ev"), ev0 + 8 * t4)
                for fc in range(32):
                    t.wait_ge(ctx.s("ld"), ld0 + 16 * ld_cnt_w1(t4, fc))
                    bk = 2 + (fc % 4)
                    if t4 * 32 + fc >= 4:
                        t.wait_ge(ctx.s("sc"), sc0 + t4 * 32 + fc - 3)
                    base1 = 2048 * ((fc // 2) % 2) + 128 * (fc % 2)
                    for dc in range(8):
                        t.matmul(pbank(bk), sb["w1s"][:, base1 + 256 * dc:base1 + 256 * dc + 128],
                                 sb["xnT"][:, 2048 * dc + 512 * t4:2048 * dc + 512 * (t4 + 1)],
                                 start=(dc == 0), stop=False)
                    t.matmul(pbank(bk), sb["bias"][0:1, 128 * fc:128 * (fc + 1)],
                             sb["onesb"][0:1, 0:512], start=False,
                             stop=True).then_inc(ctx.s("mm"), 1)
                t.wait_ge(ctx.s("sc"), sc0 + 32 * (t4 + 1))
                if t4 >= 1:
                    t.wait_ge(ctx.s("ev"), ev0 + 8 * t4)
                for fc in range(32):
                    t.wait_ge(ctx.s("ld"), ld0 + 16 * ld_cnt_w2(t4, fc))
                    slot = sb["w2s"][:, 2048 * ((fc // 2) % 2) + 1024 * (fc % 2):
                                     2048 * ((fc // 2) % 2) + 1024 * (fc % 2) + 1024]
                    last = None
                    for tt4 in range(4):
                        for ds in range(2):
                            bk = 2 * tt4 + ds
                            last = t.matmul(pbank(bk),
                                            sb["htl2"][:, 512 * fc + 128 * tt4:512 * fc + 128 * (tt4 + 1)],
                                            slot[:, 512 * ds:512 * (ds + 1)],
                                            start=(fc == 0), stop=False)
                    last.then_inc(ctx.s("mm"), 1)
                last = None
                for tt4 in range(4):
                    for ds in range(2):
                        bk = 2 * tt4 + ds
                        last = t.matmul(pbank(bk), sb["onesb"][0:1, 0:128],
                                        sb["bias2"][0:1, 512 * ds:512 * (ds + 1)],
                                        start=False, stop=True)
                last.then_inc(ctx.s("mm"), 1)

        @blk.scalar
        def _(sc):
            for t4 in range(4):
                for fc in range(32):
                    sc.wait_ge(ctx.s("mm"), mm0 + ht_mm_idx(t4, fc))
                    bk = 2 + (fc % 4)
                    sc.activation(sb["htl2"][:, 512 * fc:512 * (fc + 1)], pbank(bk),
                                  AF.Gelu, bias=0.0, scale=1.0).then_inc(ctx.s("sc"), 1)

        @blk.vector
        def _(v):
            for t4 in range(4):
                v.wait_ge(ctx.s("mm"), mm0 + ybias_mm_idx(t4))
                for tt4 in range(4):
                    for ds in range(2):
                        bk = 2 * tt4 + ds
                        tt = 4 * t4 + tt4
                        dst = sb["stream"][:, 1024 * tt + 512 * ds:1024 * tt + 512 * (ds + 1)]
                        v.tensor_add(dst, dst, pbank(bk)).then_inc(ctx.s("ev"), 1)

        ctx.add("mm", 4 * 65)
        ctx.add("sc", 4 * 32)
        ctx.add("ev", 4 * 8)



    def ph_l1_qkv(blk):
        ld0, mm0, ev0, st0 = ctx.v("ld"), ctx.v("mm"), ctx.v("ev"), ctx.v("st")
        tr_ready = ctx.v("tr")
        qkv_in = dram["a2a_qkv_in"].ap()

        @blk.gpsimd
        def _(g):
            for dc in range(8):
                g.dma_start(out=sb["wa"][:, 1024 * dc:1024 * (dc + 1)],
                            in_=dram["w_l1_wqT"].ap()[128 * dc:128 * (dc + 1), :]).then_inc(ctx.s("ld"), 16)
            for dc in range(8):
                g.dma_start(out=sb["wb"][:, 256 * dc:256 * (dc + 1)],
                            in_=dram["w_l1_wkT"].ap()[128 * dc:128 * (dc + 1), :]).then_inc(ctx.s("ld"), 16)
            for dc in range(8):
                g.dma_start(out=sb["wb"][:, 2048 + 256 * dc:2048 + 256 * (dc + 1)],
                            in_=dram["w_l1_wvT"].ap()[128 * dc:128 * (dc + 1), :]).then_inc(ctx.s("ld"), 16)
        ctx.add("ld", 16 * 24)

        cum_st = []
        tot = 0
        for gq in range(32):
            tot += 16
            cum_st.append(tot)
        for gk in range(8):
            tot += 16 * 4
            cum_st.append(tot)
        for gv in range(16):
            tot += 16 * 8
            cum_st.append(tot)

        @blk.tensor
        def _(t):
            t.wait_ge(ctx.s("tr"), tr_ready)
            t.wait_ge(ctx.s("ld"), ld0 + 16 * 8)
            g = 0
            for ic in range(8):
                for t4 in range(4):
                    bk = g % 2
                    if g >= 2:
                        t.wait_ge(ctx.s("ev"), ev0 + g - 1)
                    last = None
                    for dc in range(8):
                        last = t.matmul(pbank(bk),
                                        sb["wa"][:, 1024 * dc + 128 * ic:1024 * dc + 128 * (ic + 1)],
                                        sb["xnT"][:, 2048 * dc + 512 * t4:2048 * dc + 512 * (t4 + 1)],
                                        start=(dc == 0), stop=(dc == 7))
                    last.then_inc(ctx.s("mm"), 1)
                    g += 1
            t.wait_ge(ctx.s("ld"), ld0 + 16 * 16)
            for ic2 in range(2):
                for t4 in range(4):
                    bk = g % 2
                    t.wait_ge(ctx.s("ev"), ev0 + g - 1)
                    last = None
                    for dc in range(8):
                        last = t.matmul(pbank(bk),
                                        sb["wb"][:, 256 * dc + 128 * ic2:256 * dc + 128 * (ic2 + 1)],
                                        sb["xnT"][:, 2048 * dc + 512 * t4:2048 * dc + 512 * (t4 + 1)],
                                        start=(dc == 0), stop=(dc == 7))
                    last.then_inc(ctx.s("mm"), 1)
                    g += 1
            t.wait_ge(ctx.s("ld"), ld0 + 16 * 24)
            for tt in range(NT):
                bk = g % 2
                t.wait_ge(ctx.s("ev"), ev0 + g - 1)
                last = None
                for dc in range(8):
                    last = t.matmul(pbank(bk, 0, 128, 256),
                                    sb["xnT"][:, 2048 * dc + 128 * tt:2048 * dc + 128 * (tt + 1)],
                                    sb["wb"][:, 2048 + 256 * dc:2048 + 256 * (dc + 1)],
                                    start=(dc == 0), stop=(dc == 7))
                last.then_inc(ctx.s("mm"), 1)
                g += 1

        @blk.vector
        def _(v):
            for g in range(56):
                v.wait_ge(ctx.s("mm"), mm0 + g + 1)
                if g >= 4:
                    v.wait_ge(ctx.s("st"), st0 + cum_st[g - 4])
                q = 512 * (g % 4)
                w = 512 if g < 40 else 256
                v.tensor_copy(sb["scr"][:, q:q + w], pbank(g % 2, 0, 128, w)).then_inc(ctx.s("ev"), 1)

        @blk.sync
        def _(s):
            g = 0
            for ic in range(8):
                for t4 in range(4):
                    s.wait_ge(ctx.s("ev"), ev0 + g + 1)
                    q = 512 * (g % 4)
                    dst = qkv_in[ic, 0:SEG_Q].rearrange("(p n) -> p n", p=128)[:, 512 * t4:512 * (t4 + 1)]
                    s.dma_start(out=dst, in_=sb["scr"][:, q:q + 512]).then_inc(ctx.s("st"), 16)
                    g += 1
            for ic2 in range(2):
                for t4 in range(4):
                    s.wait_ge(ctx.s("ev"), ev0 + g + 1)
                    q = 512 * (g % 4)
                    for half in range(2):
                        for dup in range(2):
                            blkid = 4 * ic2 + 2 * half + dup
                            dst = qkv_in[blkid, SEG_Q:SEG_Q + 64 * 2048]
                            dst = dst.rearrange("(p n) -> p n", p=64)[:, 512 * t4:512 * (t4 + 1)]
                            s.dma_start(out=dst,
                                        in_=sb["scr"][64 * half:64 * (half + 1), q:q + 512]).then_inc(ctx.s("st"), 16)
                    g += 1
            for tt in range(NT):
                s.wait_ge(ctx.s("ev"), ev0 + g + 1)
                q = 512 * (g % 4)
                for gv in range(4):
                    for dup in range(2):
                        blkid = 2 * gv + dup
                        dst = qkv_in[blkid, SEG_Q + 64 * 2048 + 8192 * tt:
                                     SEG_Q + 64 * 2048 + 8192 * (tt + 1)]
                        dst = dst.rearrange("(p n) -> p n", p=128)
                        s.dma_start(out=dst,
                                    in_=sb["scr"][:, q + 64 * gv:q + 64 * (gv + 1)]).then_inc(ctx.s("st"), 16)
                g += 1

        ctx.add("mm", 56)
        ctx.add("ev", 56)
        ctx.add("st", 16 * (32 + 32 + 128))

    def ph_l1_rope(blk, b):
        ld0, mm0, ev0 = ctx.v("ld"), ctx.v("mm"), ctx.v("ev")
        qkv_out = dram["a2a_qkv_out"].ap()

        @blk.sync
        def _(s):
            for i in range(4):
                srcb = qkv_out[4 * b + i, 0:SEG_Q].rearrange("(p n) -> p n", p=128)
                s.dma_start(out=sb["qtp"][:, 2048 * i:2048 * (i + 1)], in_=srcb).then_inc(ctx.s("ld"), 16)
            for i in range(4):
                srcb = qkv_out[4 * b + i, SEG_Q:SEG_Q + 64 * 2048].rearrange("(p n) -> p n", p=64)
                s.dma_start(out=sb["ktp"][0:64, 2048 * i:2048 * (i + 1)], in_=srcb).then_inc(ctx.s("ld"), 16)
                s.dma_start(out=sb["ktp"][64:128, 2048 * i:2048 * (i + 1)], in_=srcb).then_inc(ctx.s("ld"), 16)
            vtv = sb["vt1"].rearrange("p (c w) -> p c w", w=64)
            for i in range(4):
                srcb = qkv_out[4 * b + i, SEG_Q + 64 * 2048:SEG_Q + 64 * 2048 + 16 * 8192]
                srcb = srcb.rearrange("(c p n) -> c p n", c=16, p=128)
                s.dma_start(out=vtv[:, 16 * i:16 * (i + 1), :],
                            in_=srcb.transpose([1, 0, 2])).then_inc(ctx.s("ld"), 16)
            s.dma_start(out=sb["cosr"][:, :], in_=dram["c_cosd"].ap()).then_inc(ctx.s("ld"), 16)
            s.dma_start(out=sb["sinr"][:, :], in_=dram["c_sind"].ap()).then_inc(ctx.s("ld"), 16)
            s.dma_start(out=sb["rs2"][:, 0:128], in_=dram["c_rotm"].ap()).then_inc(ctx.s("ld"), 16)
        ctx.add("ld", 16 * 19)

        chunks = [(tgt, c) for tgt in ("qtp", "ktp") for c in range(16)]

        @blk.tensor
        def _(t):
            t.wait_ge(ctx.s("ld"), ld0 + 16 * 19)
            for r, (tgt, c) in enumerate(chunks):
                bk = r % 2
                if r >= 2:
                    t.wait_ge(ctx.s("ev"), ev0 + 3 * (r - 2) + 1)
                t.matmul(pbank(bk), sb["rs2"][:, 0:128],
                         sb[tgt][:, 512 * c:512 * (c + 1)],
                         start=True, stop=True).then_inc(ctx.s("mm"), 1)

        @blk.vector
        def _(v):
            for r, (tgt, c) in enumerate(chunks):
                bk = r % 2
                cs = slice(512 * c, 512 * (c + 1))
                v.wait_ge(ctx.s("mm"), mm0 + r + 1)
                v.tensor_mul(sb["rs1"][:, 0:512], pbank(bk), sb["sinr"][:, cs]).then_inc(ctx.s("ev"), 1)
                v.tensor_mul(sb["rs1"][:, 512:1024], sb[tgt][:, cs],
                             sb["cosr"][:, cs]).then_inc(ctx.s("ev"), 1)
                v.wait_ge(ctx.s("ev"), ev0 + 3 * r + 2)
                v.tensor_add(sb[tgt][:, cs], sb["rs1"][:, 0:512],
                             sb["rs1"][:, 512:1024]).then_inc(ctx.s("ev"), 1)

        ctx.add("mm", 32)
        ctx.add("ev", 96)

    def ph_l1_attn(blk, b):
        ld0, mm0, ev0, st0 = ctx.v("ld"), ctx.v("mm"), ctx.v("ev"), ctx.v("st")

        @blk.gpsimd
        def _(g):
            g.dma_start(out=sb["l1m"][:, :], in_=dram["c_l1maskT"].ap()).then_inc(ctx.s("ld"), 16)
            g.dma_start(out=sb["l1q"][:, :], in_=dram["c_l1qdec"].ap()).then_inc(ctx.s("ld"), 16)
            g.dma_start(out=sb["l1k"][:, :], in_=dram["c_l1kdec"].ap()).then_inc(ctx.s("ld"), 16)
            g.dma_start(out=sb["l1b"][:, :], in_=dram["c_l1bdec"].ap()).then_inc(ctx.s("ld"), 16)
        ctx.add("ld", 16 * 4)
        ld_all = ctx.v("ld")

        @blk.tensor
        def _(t):
            t.wait_ge(ctx.s("ld"), ld_all)
            for c in range(64):
                bk = c % 2
                if c >= 2:
                    t.wait_ge(ctx.s("ev"), ev0 + (c - 2) + 1)
                t.transpose(pbank(bk, 0, 128, 32).bitcast(BF16),
                            sb["ktp"][0:64, 128 * c:128 * (c + 1)],
                            sb["ident"][0:64, 0:64]).then_inc(ctx.s("mm"), 1)

        @blk.vector
        def _(v):
            for c in range(64):
                v.wait_ge(ctx.s("mm"), mm0 + c + 1)
                v.tensor_copy(sb["ktok"][:, 64 * c:64 * (c + 1)],
                              pbank(c % 2, 0, 128, 32).bitcast(BF16)).then_inc(ctx.s("ev"), 1)
            v.wait_ge(ctx.s("ev"), ev0 + 64)
            qd3 = sb["l1q"][:, :].unsqueeze(1).broadcast_to([128, 64, 128])
            v.tensor_mul(sb["qdt"].rearrange("p (c w) -> p c w", w=128),
                         sb["qtp"].rearrange("p (c w) -> p c w", w=128), qd3)
            v.tensor_scalar(sb["kdt"][:, 0:4096], sb["ktok"][:, :],
                            sb["l1k"][:, 0:1], None, ALU.mult)
            v.tensor_scalar(sb["kdt"][:, 4096:8192], sb["ktok"][:, :],
                            sb["l1k"][:, 1:2], None, ALU.mult).then_inc(ctx.s("ev"), 1)
            v.memset(sb["kvf"][:, :], 0.0)
            v.memset(sb["kvp"][:, :], 0.0).then_inc(ctx.s("ev"), 1)

        ctx.add("mm", 64)
        ctx.add("ev", 66)
        pre_ev = ctx.v("ev")
        pre_mm = ctx.v("mm")

        @blk.tensor
        def _(t):
            t.wait_ge(ctx.s("ev"), pre_ev)
            for c in range(64):
                cs = slice(128 * c, 128 * (c + 1))
                vs = slice(64 * c, 64 * (c + 1))
                if c >= 1:
                    t.wait_ge(ctx.s("ev"), pre_ev + 4 * (c - 1) + 1)
                t.matmul(pbank(0, 0, 128, 128), sb["ktp"][0:64, cs], sb["qtp"][0:64, cs],
                         start=True, stop=True)
                t.matmul(pbank(1, 0, 128, 128), sb["ktp"][64:128, cs], sb["qtp"][64:128, cs],
                         start=True, stop=True).then_inc(ctx.s("mm"), 1)
                if c >= 1:
                    t.wait_ge(ctx.s("ev"), pre_ev + 4 * (c - 1) + 2)
                t.matmul(pbank(2, 0, 64, 64), sb["kdt"][:, vs], sb["vt1"][:, vs],
                         start=True, stop=True)
                t.matmul(pbank(3, 0, 64, 64), sb["kdt"][:, 4096 + 64 * c:4096 + 64 * (c + 1)],
                         sb["vt1"][:, vs], start=True, stop=True).then_inc(ctx.s("mm"), 1)
                t.wait_ge(ctx.s("ev"), pre_ev + 4 * c + 1)
                if c >= 1:
                    t.wait_ge(ctx.s("ev"), pre_ev + 4 * (c - 1) + 3)
                    t.wait_ge(ctx.s("ev"), pre_ev + 4 * (c - 1) + 4)
                t.matmul(pbank(4, 0, 64, 128), sb["vt1"][:, vs], sb["st1"][:, 0:128],
                         start=True, stop=False)
                t.matmul(pbank(4, 0, 64, 128), sb["kvp"][:, 0:64],
                         sb["qdt"][:, cs], start=False, stop=True)
                t.matmul(pbank(5, 0, 64, 128), sb["vt1"][:, vs], sb["st1"][:, 128:256],
                         start=True, stop=False)
                t.matmul(pbank(5, 0, 64, 128), sb["kvp"][:, 64:128],
                         sb["qdt"][:, cs], start=False, stop=True).then_inc(ctx.s("mm"), 1)

        @blk.vector
        def _(v):
            nst = 0
            for c in range(64):
                v.wait_ge(ctx.s("mm"), pre_mm + 3 * c + 1)
                if c >= 1:
                    v.wait_ge(ctx.s("mm"), pre_mm + 3 * (c - 1) + 3)
                v.tensor_mul(sb["st1"][:, 0:128], pbank(0, 0, 128, 128), sb["l1m"][:, 0:128])
                v.tensor_mul(sb["st1"][:, 128:256], pbank(1, 0, 128, 128),
                             sb["l1m"][:, 128:256]).then_inc(ctx.s("ev"), 1)
                v.wait_ge(ctx.s("mm"), pre_mm + 3 * c + 2)
                v.scalar_tensor_tensor(sb["kvf"][0:64, 0:64], sb["kvf"][0:64, 0:64],
                                       sb["l1b"][0:64, 0:1], pbank(2, 0, 64, 64),
                                       ALU.mult, ALU.add)
                v.scalar_tensor_tensor(sb["kvf"][64:128, 64:128], sb["kvf"][64:128, 64:128],
                                       sb["l1b"][64:128, 1:2], pbank(3, 0, 64, 64),
                                       ALU.mult, ALU.add).then_inc(ctx.s("ev"), 1)
                v.wait_ge(ctx.s("ev"), pre_ev + 4 * c + 2)
                v.tensor_copy(sb["kvp"][0:64, 0:64], sb["kvf"][0:64, 0:64])
                v.tensor_copy(sb["kvp"][64:128, 64:128],
                              sb["kvf"][64:128, 64:128]).then_inc(ctx.s("ev"), 1)
                v.wait_ge(ctx.s("mm"), pre_mm + 3 * c + 3)
                h = 512 * ((c // 4) % 2)
                if c >= 8 and c % 4 == 0:
                    v.wait_ge(ctx.s("st"), st0 + 16 * (c // 4 - 1))
                v.tensor_copy(sb["aot2"][0:64, h + 128 * (c % 4):h + 128 * (c % 4 + 1)],
                              pbank(4, 0, 64, 128))
                v.tensor_copy(sb["aot2"][64:128, h + 128 * (c % 4):h + 128 * (c % 4 + 1)],
                              pbank(5, 0, 64, 128)).then_inc(ctx.s("ev"), 1)

        @blk.gpsimd
        def _(g):
            for c4 in range(16):
                g.wait_ge(ctx.s("ev"), pre_ev + 4 * (4 * c4 + 3) + 4)
                h = 512 * (c4 % 2)
                blkv = dram["a2a_ao_in"].ap()[4 * b + c4 // 4, :].rearrange("(p n) -> p n", p=128)
                dst = blkv[:, 512 * (c4 % 4):512 * (c4 % 4 + 1)]
                g.dma_start(out=dst, in_=sb["aot2"][:, h:h + 512]).then_inc(ctx.s("st"), 16)

        ctx.add("mm", 3 * 64)
        ctx.add("ev", 4 * 64)
        ctx.add("st", 16 * 16)


    def ph_l1_sum(blk):
        ld0, mm0, ev0, st0 = ctx.v("ld"), ctx.v("mm"), ctx.v("ev"), ctx.v("st")
        parts = [("w_l1_w1T_part", 2, DFF), ("w_l1_w2T_part", 8, D)]
        abscol = sb["stat"][:, 0:10]
        par2 = sb["stat"][:, 10:12]

        @blk.gpsimd
        def _(g):
            k = 0
            for wname, n, W in parts:
                for t in range(n):
                    if k >= 1:
                        g.wait_ge(ctx.s("ev"), ev0 + 1 + k)
                    g.dma_start(out=sb["wfl1"].bitcast(F32)[:, 0:W],
                                in_=dram[wname].ap()[128 * t:128 * (t + 1), :]).then_inc(ctx.s("ld"), 16)
                    k += 1
            g.wait_ge(ctx.s("ev"), ev0 + 13)
            g.dma_start(out=dram["l1sum_io"].ap(),
                        in_=sb["rec"][0:1, 8:10]).then_inc(ctx.s("st"), 16)
            g.wait_ge(ctx.s("st"), st0 + 16)

        @blk.vector
        def _(v):
            v.memset(sb["cst"][:, 3:4], 1.0).then_inc(ctx.s("ev"), 1)
            k = 0
            for wname, n, W in parts:
                for t in range(n):
                    v.wait_ge(ctx.s("ld"), ld0 + 16 * (k + 1))
                    v.reduce_sum(abscol[:, k:k + 1], sb["wfl1"].bitcast(F32)[:, 0:W],
                                 axis=AX.X, apply_absolute_value=True).then_inc(ctx.s("ev"), 1)
                    k += 1
            v.wait_ge(ctx.s("ev"), ev0 + 11)
            v.reduce_sum(par2[:, 0:1], abscol[:, 0:2], axis=AX.X)
            v.reduce_sum(par2[:, 1:2], abscol[:, 2:10], axis=AX.X).then_inc(ctx.s("ev"), 1)
            v.wait_ge(ctx.s("mm"), mm0 + 1)
            v.tensor_copy(sb["rec"][0:1, 8:10], ps[0:1, 0:2]).then_inc(ctx.s("ev"), 1)

        @blk.tensor
        def _(t):
            t.wait_ge(ctx.s("ev"), ev0 + 12)
            t.matmul(ps[0:1, 0:2], sb["cst"][:, 3:4], par2,
                     start=True, stop=True).then_inc(ctx.s("mm"), 1)

        ctx.add("ld", 16 * 10)
        ctx.add("st", 16)
        ctx.add("mm", 1)
        ctx.add("ev", 13)

    def ph_l1_ar(blk):
        st_all = ctx.v("st")

        @blk.gpsimd
        def _(g):
            g.wait_ge(ctx.s("st"), st_all)
            with allow_ici_all_to_all():
                g.collective_compute(
                    "AllReduce", ALU.add,
                    replica_groups=[[0, 1, 2, 3], [4, 5, 6, 7]],
                    ins=[dram["l1sum_io"].ap().opt()],
                    outs=[dram["l1sum_out"].ap().opt()],
                ).then_inc(ctx.s("cc"), 1)
            g.wait_ge(ctx.s("cc"), ctx.add("cc"))

    def ph_l1_tern(blk, wpart, n_tiles, W, qpart, sw_col, sum_col):
        ld0, mm0, ev0, st0 = ctx.v("ld"), ctx.v("mm"), ctx.v("ev"), ctx.v("st")
        n = n_tiles
        swb = sb["stat"][:, 12:13]
        wf = sb["wfl1"].bitcast(F32)[:, 0:W]
        qf = sb["qfl1"].bitcast(F32)[:, 0:W]
        i8 = sb["i8l1"].bitcast(I8)[:, 0:W]
        qb = sb["qbl1"][:, 0:W]
        base2 = ev0 + 4

        def p1_mark(t):
            return base2 + 3 * t + 1

        def p2_mark(t):
            return base2 + 3 * t + 2

        def cp_mark(t):
            return base2 + 3 * t + 3

        @blk.gpsimd
        def _(g):
            g.dma_start(out=sb["rec"][0:1, 12:14],
                        in_=dram["l1sum_out"].ap()).then_inc(ctx.s("ld"), 16)
            for t in range(n):
                if t >= 1:
                    g.wait_ge(ctx.s("ev"), p1_mark(t - 1))
                g.dma_start(out=wf,
                            in_=dram[wpart].ap()[128 * t:128 * (t + 1), :]).then_inc(ctx.s("ld"), 16)

        @blk.sync
        def _(s):
            for t in range(n):
                s.wait_ge(ctx.s("ev"), cp_mark(t))
                s.dma_start(out=dram[qpart].ap()[128 * t:128 * (t + 1), :],
                            in_=qb).then_inc(ctx.s("st"), 16)

        @blk.tensor
        def _(t):
            t.wait_ge(ctx.s("ev"), ev0 + 3)
            t.matmul(pbank(1, 0, 128, 1), sb["onesf"][0:1, 0:128],
                     sb["rec"][0:1, 0:1], start=True, stop=True)
            t.matmul(pbank(2, 0, 128, 1), sb["onesf"][0:1, 0:128],
                     sb["rec"][0:1, 1:2], start=True, stop=True).then_inc(ctx.s("mm"), 1)

        @blk.vector
        def _(v):
            v.wait_ge(ctx.s("ld"), ld0 + 16)
            v.tensor_scalar(sb["rec"][0:1, 1:2],
                            sb["rec"][0:1, 12 + sum_col:13 + sum_col],
                            1.0 / (D * DFF), 1e-5, ALU.mult,
                            ALU.max).then_inc(ctx.s("ev"), 1)
            v.wait_ge(ctx.s("ev"), ev0 + 1)
            v.reciprocal(sb["rec"][0:1, 0:1], sb["rec"][0:1, 1:2]).then_inc(ctx.s("ev"), 1)
            v.wait_ge(ctx.s("ev"), ev0 + 2)
            v.engine_nop().then_inc(ctx.s("ev"), 1)
            v.wait_ge(ctx.s("mm"), mm0 + 1)
            v.tensor_copy(swb, pbank(1, 0, 128, 1))
            v.tensor_copy(sb["swc"][:, sw_col:sw_col + 1],
                          pbank(2, 0, 128, 1)).then_inc(ctx.s("ev"), 1)
            v.wait_ge(ctx.s("ev"), base2)
            for t in range(n):
                v.wait_ge(ctx.s("ld"), ld0 + 16 * (t + 2))
                v.tensor_scalar(qf, wf, swb, -1.0, ALU.mult,
                                ALU.max).then_inc(ctx.s("ev"), 1)
                v.wait_ge(ctx.s("ev"), p1_mark(t))
                v.tensor_scalar(i8, qf, 1.0, None, ALU.min).then_inc(ctx.s("ev"), 1)
                v.wait_ge(ctx.s("ev"), p2_mark(t))
                if t >= 1:
                    v.wait_ge(ctx.s("st"), st0 + 16 * t)
                v.tensor_copy(qb, i8).then_inc(ctx.s("ev"), 1)

        ctx.add("ld", 16 * (n + 1))
        ctx.add("st", 16 * n)
        ctx.add("mm", 1)
        ctx.add("ev", 4 + 3 * n)

    def ph_l1_agq(blk, part, dstq, prev=None):
        st_all = ctx.v("st")

        @blk.gpsimd
        def _(g):
            g.wait_ge(ctx.s("st"), st_all)
            if prev is not None:
                g.wait_ge(ctx.s("cc"), cc_marks[prev])
            with allow_ici_all_to_all():
                g.collective_compute(
                    "AllGather", ALU.bypass,
                    replica_groups=[[0, 1, 2, 3], [4, 5, 6, 7]],
                    ins=[dram[part].ap().opt()],
                    outs=[dram[dstq].ap().opt()],
                ).then_inc(ctx.s("cc"), 1)
            cc_marks[dstq] = ctx.add("cc")
            g.wait_ge(ctx.s("cc"), cc_marks[dstq])

    def ph_l1_quantx(blk):
        ld0, ev0, sc0, st0 = ctx.v("ld"), ctx.v("ev"), ctx.v("sc"), ctx.v("st")
        ss = sb["stat"][:, 64:80]
        rstd1 = sb["stat"][:, 80:96]
        sA = sb["stat"][:, 96:112]
        sq1 = sb["stat"][:, 112:128]
        mcol = sb["stat"][:, 128:144]
        xb = sb["xb1"].bitcast(F32)[:, 0:D]
        xh = sb["xh1"].bitcast(F32)[:, 0:D]
        i8 = sb["i81"].bitcast(I8)[:, 0:D]
        xq = sb["xqb1"][:, 0:D]
        n1 = sb["n1r"].bitcast(F32)[:, 0:D]

        def e_(t):
            return ev0 + 10 * t

        @blk.gpsimd
        def _(g):
            g.dma_start(out=n1, in_=dram["c_n1rep"].ap()).then_inc(ctx.s("ld"), 16)
            g.dma_start(out=sb["l1s"][:, :], in_=dram["c_l1s"].ap()).then_inc(ctx.s("ld"), 16)
            for t in range(NT):
                if t >= 1:
                    g.wait_ge(ctx.s("ev"), e_(t - 1) + 3)
                g.dma_start(out=xb,
                            in_=dram["stage_xn32"].ap()[128 * t:128 * (t + 1), :]).then_inc(ctx.s("ld"), 16)

        @blk.scalar
        def _(sc):
            for t in range(NT):
                sc.wait_ge(ctx.s("ld"), ld0 + 16 * (t + 3))
                sc.activation(ps[:, 2048:3072], xb, AF.Square,
                              accum_out=ss[:, t:t + 1]).then_inc(ctx.s("sc"), 1)
                sc.wait_ge(ctx.s("ev"), e_(t) + 1)
                sc.activation(mcol[:, t:t + 1], rstd1[:, t:t + 1], AF.Sqrt,
                              bias=np.float32(0.0), scale=1.0).then_inc(ctx.s("sc"), 1)

        @blk.vector
        def _(v):
            for t in range(NT):
                v.wait_ge(ctx.s("sc"), sc0 + 2 * t + 1)
                v.tensor_scalar(rstd1[:, t:t + 1], ss[:, t:t + 1], 1.0 / D, 1e-8,
                                ALU.mult, ALU.add).then_inc(ctx.s("ev"), 1)
                v.wait_ge(ctx.s("sc"), sc0 + 2 * t + 2)
                v.reciprocal(rstd1[:, t:t + 1], mcol[:, t:t + 1]).then_inc(ctx.s("ev"), 1)
                v.wait_ge(ctx.s("ev"), e_(t) + 2)
                v.scalar_tensor_tensor(xh, xb, rstd1[:, t:t + 1], n1,
                                       ALU.mult, ALU.mult).then_inc(ctx.s("ev"), 1)
                v.wait_ge(ctx.s("ev"), e_(t) + 3)
                v.reduce_max(mcol[:, t:t + 1], xh, axis=AX.X,
                             apply_absolute_value=True).then_inc(ctx.s("ev"), 1)
                v.wait_ge(ctx.s("ev"), e_(t) + 4)
                v.tensor_scalar(mcol[:, t:t + 1], mcol[:, t:t + 1], 1e-5, None,
                                ALU.max).then_inc(ctx.s("ev"), 1)
                v.wait_ge(ctx.s("ev"), e_(t) + 5)
                v.reciprocal(sq1[:, t:t + 1], mcol[:, t:t + 1]).then_inc(ctx.s("ev"), 1)
                v.wait_ge(ctx.s("ev"), e_(t) + 6)
                v.tensor_scalar(sq1[:, t:t + 1], sq1[:, t:t + 1], 127.0, None,
                                ALU.mult)
                v.scalar_tensor_tensor(mcol[:, t:t + 1], mcol[:, t:t + 1],
                                       1.0 / 127.0, sb["swc"][:, 0:1],
                                       ALU.mult, ALU.mult).then_inc(ctx.s("ev"), 1)
                v.wait_ge(ctx.s("ev"), e_(t) + 7)
                v.tensor_mul(sA[:, t:t + 1], mcol[:, t:t + 1],
                             mcol[:, t:t + 1]).then_inc(ctx.s("ev"), 1)
                v.wait_ge(ctx.s("ev"), e_(t) + 8)
                v.tensor_scalar(sA[:, t:t + 1], sA[:, t:t + 1],
                                sb["l1s"][:, 0:1], None, ALU.mult)
                v.tensor_scalar(i8, xh, sq1[:, t:t + 1], None,
                                ALU.mult).then_inc(ctx.s("ev"), 1)
                v.wait_ge(ctx.s("ev"), e_(t) + 9)
                if t >= 1:
                    v.wait_ge(ctx.s("st"), st0 + 16 * t)
                v.tensor_copy(xq, i8).then_inc(ctx.s("ev"), 1)

        @blk.sync
        def _(s):
            for t in range(NT):
                s.wait_ge(ctx.s("ev"), e_(t) + 10)
                s.dma_start(out=dram["stage_xn"].ap()[128 * t:128 * (t + 1), :],
                            in_=xq).then_inc(ctx.s("st"), 16)
            s.wait_ge(ctx.s("st"), st0 + 16 * NT)
            for dc in range(8):
                s.dma_start(out=sb["xnT"][:, 2048 * dc:2048 * (dc + 1)],
                            in_=dram["stage_xn"].ap()[:, 128 * dc:128 * (dc + 1)],
                            transpose=True).then_inc(ctx.s("tr"), 16)

        ctx.add("ld", 16 * (NT + 2))
        ctx.add("st", 16 * NT)
        ctx.add("sc", 2 * NT)
        ctx.add("ev", 10 * NT)
        ctx.add("tr", 16 * 8)

    def ph_l1_mm1(blk):
        ld0, mm0, ev0, sc0, st0 = (ctx.v("ld"), ctx.v("mm"), ctx.v("ev"),
                                   ctx.v("sc"), ctx.v("st"))
        tr_ready = ctx.v("tr")
        ss2a = sb["stat"][:, 0:16]
        rstd2 = sb["stat"][:, 16:32]
        m2 = sb["stat"][:, 32:48]
        sq2 = sb["stat"][:, 48:64]
        ss2b = sb["stat"][:, 64:80]
        sA = sb["stat"][:, 96:112]
        mcol = sb["stat"][:, 128:144]
        rs2 = sb["stat"][:, 144:160]
        n2 = sb["n2r"].bitcast(F32)[:, 0:DFF]
        i8 = sb["i8h"].bitcast(I8)[:, 0:DFF]

        def aF(sub):
            return (sb["a0"] if sub == 0 else sb["a1"]).bitcast(F32)[:, 0:DFF]

        def araw(sub):
            return (sb["a0"] if sub == 0 else sb["a1"])[:, 0:DFF]

        def evtp0(tp):
            return ev0 + 68 * tp

        def qb_(tp, sub):
            return evtp0(tp) + 48 + 10 * sub

        @blk.gpsimd
        def _(g):
            g.dma_start(out=n2, in_=dram["c_n2rep"].ap()).then_inc(ctx.s("ld"), 16)
            if "wq1q" in cc_marks:
                g.wait_ge(ctx.s("cc"), cc_marks["wq1q"])
            for k in range(64):
                if k >= 2:
                    g.wait_ge(ctx.s("mm"), mm0 + 2 * (k - 1))
                hs = k % 8
                slab = sb["w1sl"].rearrange("p (c n) -> p c n", n=512)
                srcw = dram["wq1q"].ap()[:, 512 * hs:512 * (hs + 1)]
                srcw = srcw.rearrange("(c p) n -> p c n", p=128)
                g.dma_start(out=slab[:, 8 * (k % 2):8 * (k % 2) + 8, :],
                            in_=srcw).then_inc(ctx.s("ld"), 16)

        @blk.tensor
        def _(t):
            t.wait_ge(ctx.s("tr"), tr_ready)
            for tp in range(8):
                for hs in range(8):
                    k = 8 * tp + hs
                    t.wait_ge(ctx.s("ld"), ld0 + 16 * (k + 2))
                    slab = sb["w1sl"][:, 4096 * (k % 2):4096 * (k % 2) + 4096]
                    for sub in range(2):
                        tt = 2 * tp + sub
                        gidx = 16 * tp + 2 * hs + sub
                        bk = gidx % 4
                        if gidx >= 4:
                            g4 = gidx - 4
                            t.wait_ge(ctx.s("ev"),
                                      evtp0(g4 // 16) + 3 * (g4 % 16) + 1)
                        last = None
                        for dc in range(8):
                            last = t.matmul(pbank(bk),
                                            sb["xnT"][:, 2048 * dc + 128 * tt:2048 * dc + 128 * (tt + 1)],
                                            slab[:, 512 * dc:512 * (dc + 1)],
                                            start=(dc == 0), stop=(dc == 7))
                        last.then_inc(ctx.s("mm"), 1)

        @blk.scalar
        def _(sc):
            for tp in range(8):
                for sub in range(2):
                    tt = 2 * tp + sub
                    w = evtp0(tp) + 45 + 3 * sub
                    if sub == 1:
                        w = max(w, qb_(tp, 0) + 10)
                    elif tp >= 1:
                        w = max(w, qb_(tp - 1, 1) + 10)
                    sc.wait_ge(ctx.s("ev"), w)
                    sc.activation(ps[:, 2048:4096], aF(sub)[:, 0:2048], AF.Square,
                                  accum_out=ss2a[:, tt:tt + 1]).then_inc(ctx.s("sc"), 1)
                    sc.activation(ps[:, 2048:4096], aF(sub)[:, 2048:DFF], AF.Square,
                                  accum_out=ss2b[:, tt:tt + 1]).then_inc(ctx.s("sc"), 1)
                    sc.wait_ge(ctx.s("ev"), qb_(tp, sub) + 2)
                    sc.activation(m2[:, tt:tt + 1], rstd2[:, tt:tt + 1], AF.Sqrt,
                                  bias=np.float32(0.0), scale=1.0).then_inc(ctx.s("sc"), 1)

        @blk.vector
        def _(v):
            for tp in range(8):
                for hs in range(8):
                    for sub in range(2):
                        tt = 2 * tp + sub
                        gidx = 16 * tp + 2 * hs + sub
                        bk = gidx % 4
                        v.wait_ge(ctx.s("mm"), mm0 + gidx + 1)
                        if hs == 0 and tp >= 1:
                            v.wait_ge(ctx.s("st"), st0 + 16 * (2 * tp - 1 + sub))
                        asl = aF(sub)[:, 512 * hs:512 * (hs + 1)]
                        p0 = evtp0(tp) + 3 * (2 * hs + sub)
                        v.tensor_copy(asl, pbank(bk)).then_inc(ctx.s("ev"), 1)
                        v.wait_ge(ctx.s("ev"), p0 + 1)
                        v.scalar_tensor_tensor(asl, asl, 0.0, asl, ALU.max,
                                               ALU.mult).then_inc(ctx.s("ev"), 1)
                        v.wait_ge(ctx.s("ev"), p0 + 2)
                        v.tensor_scalar(asl, asl, sA[:, tt:tt + 1],
                                        sb["l1s"][:, 1:2], ALU.mult,
                                        ALU.add).then_inc(ctx.s("ev"), 1)
                for sub in range(2):
                    tt = 2 * tp + sub
                    q0 = qb_(tp, sub)
                    v.wait_ge(ctx.s("sc"), sc0 + 6 * tp + (2 if sub == 0 else 5))
                    v.scalar_tensor_tensor(rstd2[:, tt:tt + 1], ss2a[:, tt:tt + 1],
                                           1.0, ss2b[:, tt:tt + 1], ALU.mult,
                                           ALU.add).then_inc(ctx.s("ev"), 1)
                    v.wait_ge(ctx.s("ev"), q0 + 1)
                    v.tensor_scalar(rstd2[:, tt:tt + 1], rstd2[:, tt:tt + 1],
                                    1.0 / DFF, 1e-8, ALU.mult,
                                    ALU.add).then_inc(ctx.s("ev"), 1)
                    v.wait_ge(ctx.s("sc"), sc0 + 6 * tp + (3 if sub == 0 else 6))
                    v.reciprocal(rstd2[:, tt:tt + 1], m2[:, tt:tt + 1]).then_inc(ctx.s("ev"), 1)
                    v.wait_ge(ctx.s("ev"), q0 + 3)
                    v.scalar_tensor_tensor(aF(sub), aF(sub), rstd2[:, tt:tt + 1],
                                           n2, ALU.mult, ALU.mult).then_inc(ctx.s("ev"), 1)
                    v.wait_ge(ctx.s("ev"), q0 + 4)
                    v.reduce_max(m2[:, tt:tt + 1], aF(sub), axis=AX.X,
                                 apply_absolute_value=True).then_inc(ctx.s("ev"), 1)
                    v.wait_ge(ctx.s("ev"), q0 + 5)
                    v.tensor_scalar(m2[:, tt:tt + 1], m2[:, tt:tt + 1], 1e-5,
                                    None, ALU.max).then_inc(ctx.s("ev"), 1)
                    v.wait_ge(ctx.s("ev"), q0 + 6)
                    v.scalar_tensor_tensor(rs2[:, tt:tt + 1], m2[:, tt:tt + 1],
                                           1.0 / 127.0, sb["swc"][:, 1:2],
                                           ALU.mult, ALU.mult)
                    v.reciprocal(mcol[:, tt:tt + 1], m2[:, tt:tt + 1]).then_inc(ctx.s("ev"), 1)
                    v.wait_ge(ctx.s("ev"), q0 + 7)
                    v.tensor_scalar(sq2[:, tt:tt + 1], mcol[:, tt:tt + 1], 127.0,
                                    None, ALU.mult).then_inc(ctx.s("ev"), 1)
                    v.wait_ge(ctx.s("ev"), q0 + 8)
                    v.tensor_scalar(i8, aF(sub), sq2[:, tt:tt + 1], None,
                                    ALU.mult).then_inc(ctx.s("ev"), 1)
                    v.wait_ge(ctx.s("ev"), q0 + 9)
                    v.tensor_copy(araw(sub)[:, 0:4096], i8).then_inc(ctx.s("ev"), 1)

        @blk.sync
        def _(s):
            for tp in range(8):
                for sub in range(2):
                    tt = 2 * tp + sub
                    s.wait_ge(ctx.s("ev"), qb_(tp, sub) + 10)
                    s.dma_start(out=dram["stage_hq1"].ap()[128 * tt:128 * (tt + 1), :],
                                in_=araw(sub)[:, 0:4096]).then_inc(ctx.s("st"), 16)

        ctx.add("ld", 16 * 65)
        ctx.add("mm", 128)
        ctx.add("sc", 48)
        ctx.add("st", 16 * NT)
        ctx.add("ev", 68 * 8)

    def ph_l1_mm2(blk):
        ld0, mm0, ev0, tr0 = ctx.v("ld"), ctx.v("mm"), ctx.v("ev"), ctx.v("tr")
        rs2 = sb["stat"][:, 144:160]

        @blk.sync
        def _(s):
            for grp in range(4):
                for fc in range(32):
                    k = 32 * grp + fc
                    if k >= 4:
                        s.wait_ge(ctx.s("mm"), mm0 + k - 3)
                    s.dma_start(out=sb["hq1s"][:, 1024 * (k % 4):1024 * (k % 4) + 512],
                                in_=dram["stage_hq1"].ap()[512 * grp:512 * (grp + 1),
                                                           128 * fc:128 * (fc + 1)],
                                transpose=True).then_inc(ctx.s("tr"), 16)

        @blk.gpsimd
        def _(g):
            if "wq2q" in cc_marks:
                g.wait_ge(ctx.s("cc"), cc_marks["wq2q"])
            for grp in range(4):
                for fc in range(32):
                    k = 32 * grp + fc
                    if k >= 4:
                        g.wait_ge(ctx.s("mm"), mm0 + k - 3)
                    g.dma_start(out=sb["w2sl"][:, 1024 * (k % 4):1024 * (k % 4 + 1)],
                                in_=dram["wq2q"].ap()[128 * fc:128 * (fc + 1), :]).then_inc(ctx.s("ld"), 16)

        @blk.tensor
        def _(t):
            for grp in range(4):
                if grp >= 1:
                    t.wait_ge(ctx.s("ev"), ev0 + 8 * grp)
                for fc in range(32):
                    k = 32 * grp + fc
                    t.wait_ge(ctx.s("tr"), tr0 + 16 * (k + 1))
                    t.wait_ge(ctx.s("ld"), ld0 + 16 * (k + 1))
                    hq = sb["hq1s"][:, 1024 * (k % 4):1024 * (k % 4) + 512]
                    w2_ = sb["w2sl"][:, 1024 * (k % 4):1024 * (k % 4 + 1)]
                    last = None
                    for tt4 in range(4):
                        for ds in range(2):
                            last = t.matmul(pbank(2 * tt4 + ds),
                                            hq[:, 128 * tt4:128 * (tt4 + 1)],
                                            w2_[:, 512 * ds:512 * (ds + 1)],
                                            start=(fc == 0), stop=(fc == 31))
                    last.then_inc(ctx.s("mm"), 1)

        @blk.vector
        def _(v):
            for grp in range(4):
                v.wait_ge(ctx.s("mm"), mm0 + 32 * (grp + 1))
                for tt4 in range(4):
                    tt = 4 * grp + tt4
                    for ds in range(2):
                        dst = sb["stream"][:, 1024 * tt + 512 * ds:1024 * tt + 512 * (ds + 1)]
                        v.scalar_tensor_tensor(dst, pbank(2 * tt4 + ds),
                                               rs2[:, tt:tt + 1], dst,
                                               ALU.mult, ALU.add).then_inc(ctx.s("ev"), 1)

        ctx.add("mm", 128)
        ctx.add("tr", 16 * 128)
        ctx.add("ld", 16 * 128)
        ctx.add("ev", 32)

    def emit_fp4_quant(v, sc_ops, W, pairs):
        nb = W // 16
        ns = len(pairs)
        st = []
        for k, (src_ap, out_ap) in enumerate(pairs):
            st.append(dict(
                src3=src_ap.rearrange("p (b w) -> p b w", w=16),
                out3=out_ap.rearrange("p (b w) -> p b w", w=16),
                qf1=sb["qf1"].bitcast(F32)[:, k * W:(k + 1) * W],
                qf2=sb["qf2"].bitcast(F32)[:, k * W:(k + 1) * W],
                bm=sb["bmx"].bitcast(F32)[:, k * nb:(k + 1) * nb],
                scl=sb["sclq"].bitcast(F32)[:, k * nb:(k + 1) * nb],
                rcp=sb["rcpq"].bitcast(F32)[:, k * nb:(k + 1) * nb],
                i8a=sb["i8a"].bitcast(I8)[:, k * W:(k + 1) * W],
                i8b=sb["i8b"].bitcast(I8)[:, k * W:(k + 1) * W],
                i8c=sb["i8c"].bitcast(I8)[:, k * W:(k + 1) * W],
                mark=[0],
            ))

        def step(fn, wait_prev=True):
            for k in range(ns):
                s = st[k]
                if wait_prev and s["mark"][0]:
                    v.wait_ge(ctx.s("ev"), s["mark"][0])
                fn(s).then_inc(ctx.s("ev"), 1)
                s["mark"][0] = ctx.add("ev")

        step(lambda s: v.reduce_max(s["bm"], s["src3"], axis=AX.X,
                                    apply_absolute_value=True), wait_prev=False)
        step(lambda s: v.tensor_scalar(s["scl"], s["bm"], 1.0 / 6.0, 1e-12,
                                       ALU.mult, ALU.max))
        step(lambda s: v.reciprocal(s["rcp"], s["scl"]))

        def _xn(s):
            rcp3 = s["rcp"].unsqueeze(2).broadcast_to([128, nb, 16])
            return v.tensor_mul(s["qf1"].rearrange("p (b w) -> p b w", w=16),
                                s["src3"], rcp3)
        step(_xn)
        xns = []
        for k in range(ns):
            s = st[k]
            sc_ops.append((s["mark"][0], s["qf1"], s["i8a"], s["i8b"], s["i8c"]))
            s["sc_tgt"] = ctx.add("sc")
            xns.append(s["mark"][0])

        def _c1(s):
            v.wait_ge(ctx.s("sc"), s["sc_tgt"])
            v.tensor_scalar(s["qf1"], s["i8b"], -4.0, 4.0, ALU.max, ALU.min)
            return v.tensor_scalar(s["qf2"], s["i8b"], -2.0, 2.0, ALU.max, ALU.min)
        step(_c1, wait_prev=False)

        def _c2(s):
            v.tensor_sub(s["qf1"], s["qf1"], s["qf2"])
            return v.tensor_scalar(s["qf2"], s["i8c"], -3.0, 3.0, ALU.max, ALU.min)
        step(_c2)

        def _c3(s):
            v.scalar_tensor_tensor(s["qf1"], s["qf2"], 2.0, s["qf1"], ALU.mult, ALU.add)
            return v.tensor_scalar(s["qf2"], s["i8c"], -2.0, 2.0, ALU.max, ALU.min)
        step(_c3)

        def _c4(s):
            v.scalar_tensor_tensor(s["qf1"], s["qf2"], -2.0, s["qf1"], ALU.mult, ALU.add)
            return v.tensor_scalar(s["qf2"], s["i8a"], -4.0, 4.0, ALU.max, ALU.min)
        step(_c4)
        step(lambda s: v.scalar_tensor_tensor(s["qf1"], s["qf2"], 0.5, s["qf1"],
                                              ALU.mult, ALU.add))

        def _out(s):
            scl3 = s["scl"].unsqueeze(2).broadcast_to([128, nb, 16])
            return v.tensor_mul(s["out3"], s["qf1"].rearrange("p (b w) -> p b w", w=16),
                                scl3)
        step(_out)
        return [(xns[k], st[k]["mark"][0]) for k in range(ns)]

    TIE_EPS = 1.0 - 2.0 ** -11

    def emit_fp4_scalar(sc, sc_ops):
        for (xn_ev, qf1, i8a, i8b, i8c) in sc_ops:
            sc.wait_ge(ctx.s("ev"), xn_ev)
            sc.mul(i8a, qf1, 2.0 * TIE_EPS)
            sc.mul(i8b, qf1, 1.0 * TIE_EPS)
            sc.mul(i8c, qf1, 0.5 * TIE_EPS).then_inc(ctx.s("sc"), 1)

    def ph_l3_prep(blk, wname, n_tiles, Wl, _unused=0):
        ld0, st0 = ctx.v("ld"), ctx.v("st")
        sc_ops = []
        ev_marks = []
        wst = dram["wstage"].ap()

        npair = n_tiles // 2

        @blk.vector
        def _(v):
            for tp in range(npair):
                for k in (0, 1):
                    v.wait_ge(ctx.s("ld"), ld0 + 16 * (2 * tp + k + 1))
                    if tp >= 1:
                        v.wait_ge(ctx.s("st"), st0 + 16 * (2 * tp - 1 + k))
                wf = sb["wf32"].bitcast(F32)
                marks = emit_fp4_quant(v, sc_ops, Wl,
                                       [(wf[:, 0:Wl], sb["qbf"][:, 0:Wl]),
                                        (wf[:, Wl:2 * Wl], sb["qbf"][:, Wl:2 * Wl])])
                ev_marks.append(marks)

        @blk.scalar
        def _(sc):
            emit_fp4_scalar(sc, sc_ops)

        @blk.gpsimd
        def _(g):
            for tp in range(npair):
                for k in (0, 1):
                    t = 2 * tp + k
                    if tp >= 1:
                        g.wait_ge(ctx.s("ev"), ev_marks[tp - 1][k][0])
                    g.dma_start(out=sb["wf32"].bitcast(F32)[:, Wl * k:Wl * (k + 1)],
                                in_=dram[wname].ap()[128 * t:128 * (t + 1), :]).then_inc(ctx.s("ld"), 16)
                for k in (0, 1):
                    t = 2 * tp + k
                    g.wait_ge(ctx.s("ev"), ev_marks[tp][k][1])
                    dst = wst[128 * Wl * t: 128 * Wl * (t + 1)]
                    g.dma_start(out=dst.rearrange("(p n) -> p n", p=128),
                                in_=sb["qbf"][:, Wl * k:Wl * (k + 1)]).then_inc(ctx.s("st"), 16)

        ctx.add("ld", 16 * n_tiles)
        ctx.add("st", 16 * n_tiles)

    def ph_l3_prepT(blk, part, R, Wl):
        tr0, st0 = ctx.v("tr"), ctx.v("st")
        wst = dram["wstage"].ap()[0:R * Wl].rearrange("(r c) -> r c", c=Wl)
        chunks = []
        c0 = 0
        while c0 < Wl:
            chunks.append((c0, min(128, Wl - c0)))
            c0 += 128

        @blk.sync
        def _(s):
            s.wait_ge(ctx.s("st"), st0)
            for i, (c0, w) in enumerate(chunks):
                if i >= 1:
                    s.wait_ge(ctx.s("st"), st0 + 16 * i)
                s.dma_start(out=sb["trs"][0:w, 0:R], in_=wst[:, c0:c0 + w],
                            transpose=True).then_inc(ctx.s("tr"), 16)

        @blk.gpsimd
        def _(g):
            for i, (c0, w) in enumerate(chunks):
                g.wait_ge(ctx.s("tr"), tr0 + 16 * (i + 1))
                g.dma_start(out=dram[part].ap()[c0:c0 + w, :],
                            in_=sb["trs"][0:w, 0:R]).then_inc(ctx.s("st"), 16)
            g.wait_ge(ctx.s("st"), st0 + 16 * len(chunks))

        ctx.add("tr", 16 * len(chunks))
        ctx.add("st", 16 * len(chunks))

    cc_marks = {}

    def ph_l3_ag(blk, part, dstT, prev=None):
        st_all = ctx.v("st")

        @blk.gpsimd
        def _(g):
            g.wait_ge(ctx.s("st"), st_all)
            if prev is not None:
                g.wait_ge(ctx.s("cc"), cc_marks[prev])
            with allow_ici_all_to_all():
                g.collective_compute(
                    "AllGather", ALU.bypass,
                    replica_groups=[[0, 1, 2, 3], [4, 5, 6, 7]],
                    ins=[dram[part].ap().opt()],
                    outs=[dram[dstT].ap().opt()],
                ).then_inc(ctx.s("cc"), 1)
            cc_marks[dstT] = ctx.add("cc")
            g.wait_ge(ctx.s("cc"), cc_marks[dstT])

    def ph_l3_quantx(blk):
        ld0, st0 = ctx.v("ld"), ctx.v("st")
        sc_ops = []
        ev_marks = []

        @blk.vector
        def _(v):
            for tp in range(NT // 2):
                for k in (0, 1):
                    v.wait_ge(ctx.s("ld"), ld0 + 16 * (2 * tp + k + 1))
                    if tp >= 1:
                        v.wait_ge(ctx.s("st"), st0 + 16 * (2 * tp - 1 + k))
                wf = sb["wf32"].bitcast(F32)
                marks = emit_fp4_quant(v, sc_ops, 1024,
                                       [(wf[:, 0:1024], sb["qbf"][:, 0:1024]),
                                        (wf[:, 1024:2048], sb["qbf"][:, 1024:2048])])
                ev_marks.append(marks)

        @blk.scalar
        def _(sc):
            emit_fp4_scalar(sc, sc_ops)

        @blk.gpsimd
        def _(g):
            for tp in range(NT // 2):
                for k in (0, 1):
                    t = 2 * tp + k
                    if tp >= 1:
                        g.wait_ge(ctx.s("ev"), ev_marks[tp - 1][k][0])
                    g.dma_start(out=sb["wf32"].bitcast(F32)[:, 1024 * k:1024 * (k + 1)],
                                in_=dram["stage_xn32"].ap()[128 * t:128 * (t + 1), :]).then_inc(ctx.s("ld"), 16)
                for k in (0, 1):
                    t = 2 * tp + k
                    g.wait_ge(ctx.s("ev"), ev_marks[tp][k][1])
                    g.dma_start(out=dram["stage_xn"].ap()[128 * t:128 * (t + 1), :],
                                in_=sb["qbf"][:, 1024 * k:1024 * (k + 1)]).then_inc(ctx.s("st"), 16)

        @blk.sync
        def _(s):
            s.wait_ge(ctx.s("st"), st0 + 16 * NT)
            for dc in range(8):
                s.dma_start(out=sb["xnT"][:, 2048 * dc:2048 * (dc + 1)],
                            in_=dram["stage_xn"].ap()[:, 128 * dc:128 * (dc + 1)],
                            transpose=True).then_inc(ctx.s("tr"), 16)

        ctx.add("ld", 16 * NT)
        ctx.add("st", 16 * NT)
        ctx.add("tr", 16 * 8)

    def ph_l3_gateup(blk):
        ld0, mm0, ev0, sc0, st0 = (ctx.v("ld"), ctx.v("mm"), ctx.v("ev"),
                                   ctx.v("sc"), ctx.v("st"))
        tr_ready = ctx.v("tr")
        GS = [(i * 512, 512) for i in range(5)] + [(2560, 256)]

        @blk.sync
        def _(s):
            if "wuqT" in cc_marks:
                s.wait_ge(ctx.s("cc"), cc_marks["wuqT"])
            n = 0
            for gi, (goff, gw) in enumerate(GS):
                for src_t, dst in (("wgqT", "wgs"), ("wuqT", "wus")):
                    for dc in range(8):
                        if gi >= 1:
                            s.wait_ge(ctx.s("mm"), mm0 + 16 * gi)
                        s.dma_start(out=sb[dst][:, 512 * dc:512 * dc + gw],
                                    in_=dram[src_t].ap()[128 * dc:128 * (dc + 1), goff:goff + gw]).then_inc(ctx.s("ld"), 16)
                        n += 1
        NLD = 6 * 16
        ctx.add("ld", 16 * NLD)

        @blk.tensor
        def _(t):
            t.wait_ge(ctx.s("tr"), tr_ready)
            for gi, (goff, gw) in enumerate(GS):
                t.wait_ge(ctx.s("ld"), ld0 + 16 * 16 * (gi + 1))
                for tt in range(NT):
                    gidx = gi * NT + tt
                    bkG = 2 * (tt % 2)
                    bkU = bkG + 1
                    if gidx >= 2:
                        t.wait_ge(ctx.s("ev"), ev0 + gidx - 1)
                    for dc in range(8):
                        t.matmul(pbank(bkG, 0, 128, gw),
                                 sb["xnT"][:, 2048 * dc + 128 * tt:2048 * dc + 128 * (tt + 1)],
                                 sb["wgs"][:, 512 * dc:512 * dc + gw],
                                 start=(dc == 0), stop=(dc == 7))
                    last = None
                    for dc in range(8):
                        last = t.matmul(pbank(bkU, 0, 128, gw),
                                        sb["xnT"][:, 2048 * dc + 128 * tt:2048 * dc + 128 * (tt + 1)],
                                        sb["wus"][:, 512 * dc:512 * dc + gw],
                                        start=(dc == 0), stop=(dc == 7))
                    last.then_inc(ctx.s("mm"), 1)

        @blk.scalar
        def _(sc):
            for gi, (goff, gw) in enumerate(GS):
                for tt in range(NT):
                    gidx = gi * NT + tt
                    sc.wait_ge(ctx.s("mm"), mm0 + gidx + 1)
                    if gidx >= 1:
                        sc.wait_ge(ctx.s("ev"), ev0 + gidx)
                    bkG = 2 * (tt % 2)
                    sc.activation(sb["sg1"].bitcast(F32)[:, 0:gw], pbank(bkG, 0, 128, gw),
                                  AF.Silu, bias=0.0, scale=1.0).then_inc(ctx.s("sc"), 1)

        @blk.vector
        def _(v):
            for gi, (goff, gw) in enumerate(GS):
                for tt in range(NT):
                    gidx = gi * NT + tt
                    v.wait_ge(ctx.s("sc"), sc0 + gidx + 1)
                    if gidx >= 1:
                        v.wait_ge(ctx.s("st"), st0 + 16 * gidx)
                    bkU = 2 * (tt % 2) + 1
                    v.tensor_mul(sb["sg2"].bitcast(F32)[:, 0:gw],
                                 sb["sg1"].bitcast(F32)[:, 0:gw],
                                 pbank(bkU, 0, 128, gw)).then_inc(ctx.s("ev"), 1)

        @blk.gpsimd
        def _(g):
            for gi, (goff, gw) in enumerate(GS):
                for tt in range(NT):
                    gidx = gi * NT + tt
                    g.wait_ge(ctx.s("ev"), ev0 + gidx + 1)
                    g.dma_start(out=dram["stage_h"].ap()[128 * tt:128 * (tt + 1), goff:goff + gw],
                                in_=sb["sg2"].bitcast(F32)[:, 0:gw]).then_inc(ctx.s("st"), 16)

        NG = 6 * NT
        ctx.add("mm", NG)
        ctx.add("sc", NG)
        ctx.add("ev", NG)
        ctx.add("st", 16 * NG)

    def ph_l3_quanth(blk):
        ld0, st0 = ctx.v("ld"), ctx.v("st")
        sc_ops = []
        ev_marks = []

        @blk.vector
        def _(v):
            for t in range(NT):
                v.wait_ge(ctx.s("ld"), ld0 + 16 * (t + 1))
                if t >= 1:
                    v.wait_ge(ctx.s("st"), st0 + 16 * t)
                wf = sb["wf32"].bitcast(F32)
                marks = emit_fp4_quant(v, sc_ops, 1408,
                                       [(wf[:, 0:1408], sb["qbf"][:, 0:1408]),
                                        (wf[:, 1408:2816], sb["qbf"][:, 1408:2816])])
                ev_marks.append(marks)

        @blk.scalar
        def _(sc):
            emit_fp4_scalar(sc, sc_ops)

        @blk.gpsimd
        def _(g):
            for t in range(NT):
                if t >= 1:
                    g.wait_ge(ctx.s("ev"), max(ev_marks[t - 1][0][0], ev_marks[t - 1][1][0]))
                g.dma_start(out=sb["wf32"].bitcast(F32)[:, 0:2816],
                            in_=dram["stage_h"].ap()[128 * t:128 * (t + 1), :]).then_inc(ctx.s("ld"), 16)
                g.wait_ge(ctx.s("ev"), max(ev_marks[t][0][1], ev_marks[t][1][1]))
                g.dma_start(out=dram["stage_hq"].ap()[128 * t:128 * (t + 1), :],
                            in_=sb["qbf"][:, 0:2816]).then_inc(ctx.s("st"), 16)

        ctx.add("ld", 16 * NT)
        ctx.add("st", 16 * NT)

    def ph_l3_down(blk):
        ld0, mm0, ev0, tr0 = ctx.v("ld"), ctx.v("mm"), ctx.v("ev"), ctx.v("tr")

        @blk.sync
        def _(s):
            for grp in range(4):
                for dd in range(22):
                    k = grp * 22 + dd
                    if k >= 2:
                        s.wait_ge(ctx.s("mm"), mm0 + k - 1)
                    s.dma_start(out=sb["hqs"][:, 1024 * (dd % 2):1024 * (dd % 2) + 512],
                                in_=dram["stage_hq"].ap()[512 * grp:512 * (grp + 1),
                                                          128 * dd:128 * (dd + 1)],
                                transpose=True).then_inc(ctx.s("tr"), 16)

        @blk.gpsimd
        def _(g):
            if "wdqT" in cc_marks:
                g.wait_ge(ctx.s("cc"), cc_marks["wdqT"])
            for grp in range(4):
                for dd in range(22):
                    k = grp * 22 + dd
                    if k >= 2:
                        g.wait_ge(ctx.s("mm"), mm0 + k - 1)
                    g.dma_start(out=sb["wds"][:, 1024 * (dd % 2):1024 * (dd % 2 + 1)],
                                in_=dram["wdqT"].ap()[128 * dd:128 * (dd + 1), :]).then_inc(ctx.s("ld"), 16)

        @blk.tensor
        def _(t):
            for grp in range(4):
                if grp >= 1:
                    t.wait_ge(ctx.s("ev"), ev0 + 8 * grp)
                for dd in range(22):
                    k = grp * 22 + dd
                    t.wait_ge(ctx.s("tr"), tr0 + 16 * (k + 1))
                    t.wait_ge(ctx.s("ld"), ld0 + 16 * (k + 1))
                    hq = sb["hqs"][:, 1024 * (dd % 2):1024 * (dd % 2) + 512]
                    wd_ = sb["wds"][:, 1024 * (dd % 2):1024 * (dd % 2 + 1)]
                    last = None
                    for tt4 in range(4):
                        for ds in range(2):
                            bk = 2 * tt4 + ds
                            last = t.matmul(pbank(bk),
                                            hq[:, 128 * tt4:128 * (tt4 + 1)],
                                            wd_[:, 512 * ds:512 * (ds + 1)],
                                            start=(dd == 0), stop=(dd == 21))
                    last.then_inc(ctx.s("mm"), 1)

        @blk.vector
        def _(v):
            for grp in range(4):
                v.wait_ge(ctx.s("mm"), mm0 + 22 * (grp + 1))
                for tt4 in range(4):
                    for ds in range(2):
                        bk = 2 * tt4 + ds
                        tt = 4 * grp + tt4
                        dst = sb["stream"][:, 1024 * tt + 512 * ds:1024 * tt + 512 * (ds + 1)]
                        v.tensor_add(dst, dst, pbank(bk)).then_inc(ctx.s("ev"), 1)

        ctx.add("mm", 88)
        ctx.add("tr", 16 * 88)
        ctx.add("ld", 16 * 88)
        ctx.add("ev", 32)

    def ph_dump(blk, name, nelem):
        st0 = ctx.v("st")

        @blk.sync
        def _(s):
            s.wait_ge(ctx.s("st"), st0)
            src_ap = dram[name].ap()
            flat = src_ap
            if len(src_ap.shape) > 1:
                flat = src_ap.rearrange("a b -> (a b)") if len(src_ap.shape) == 2 else src_ap
            s.dma_start(out=dbg_out.ap().rearrange("a b -> (a b)")[0:nelem],
                        in_=flat[0:nelem] if len(flat.shape) == 1 else flat).then_inc(ctx.s("st"), 16)
            s.wait_ge(ctx.s("st"), st0 + 16)
        ctx.add("st", 16)

    def ph_dump_sbuf(blk):
        st0 = ctx.v("st")

        @blk.gpsimd
        def _(g):
            flat = dbg_out.ap().rearrange("a b -> (a b)")
            o = [2621440]

            def dump(t, n):
                g.dma_start(out=flat[o[0]:o[0] + 128 * n].rearrange("(p n) -> p n", p=128),
                            in_=t).then_inc(ctx.s("st"), 16)
                o[0] += 128 * n
            dump(sb["qtp"][:, :], 8192)
            dump(sb["ktp"][:, :], 8192)
            dump(sb["vt"][:, :], 8320)
            dump(sb["pt"][:, :], 2048)
            g.wait_ge(ctx.s("st"), st0 + 64)
        ctx.add("st", 16 * 4)

    def ph_dump_wx(blk):
        st0 = ctx.v("st")

        @blk.gpsimd
        def _(g):
            flat = dbg_out.ap().rearrange("a b -> (a b)")
            g.dma_start(out=flat[3948544:3948544 + 128 * 8192].rearrange("(p n) -> p n", p=128),
                        in_=sb["wa"][:, :]).then_inc(ctx.s("st"), 16)
            g.dma_start(out=flat[4997120:4997120 + 128 * 8192].rearrange("(p n) -> p n", p=128),
                        in_=sb["xnT"][:, 0:8192]).then_inc(ctx.s("st"), 16)
            g.wait_ge(ctx.s("st"), st0 + 32)
        ctx.add("st", 32)

    def ph_dump_flat(blk, name, nelem, off):
        st0 = ctx.v("st")

        @blk.sync
        def _(s):
            s.wait_ge(ctx.s("st"), st0)
            s.wait_ge(ctx.s("cc"), ctx.v("cc"))
            flat = dbg_out.ap().rearrange("a b -> (a b)")
            srcf = dram[name].ap().rearrange("a b -> (a b)")
            s.dma_start(out=flat[off:off + nelem], in_=srcf[0:nelem]).then_inc(ctx.s("st"), 16)
            s.wait_ge(ctx.s("st"), st0 + 16)
        ctx.add("st", 16)

    def ph_dump_stat(blk):
        st0 = ctx.v("st")

        @blk.gpsimd
        def _(g):
            g.dma_start(out=dbg_out.ap()[7, 0:128 * 160].rearrange("(p n) -> p n", p=128),
                        in_=sb["stat"][:, :]).then_inc(ctx.s("st"), 16)
            g.wait_ge(ctx.s("st"), st0 + 16)
        ctx.add("st", 16)

    run_phase(ph_init)
    def _emit_ln():
        run_phase(ph_ln)
        run_phase(ph_dump, "stage_xn", TPC * D)
        run_phase(ph_dump_stat)
    def _emit_qkv():
        run_phase(ph_ln)
        run_phase(ph_l0_qkv)
        run_phase(ph_dump, "a2a_qkv_in", 8 * BLK_QKV)
    def _emit_attn():
        run_phase(ph_ln)
        run_phase(ph_l0_qkv)
        run_phase(ph_dump_flat, "a2a_qkv_in", 262144, 2097152)
        run_phase(ph_dump_wx)
        run_phase(ph_a2a, "a2a_qkv_in", "a2a_qkv_out")
        run_phase(ph_dump_flat, "a2a_qkv_out", 262144, 2359296)
        run_phase(ph_l0_attn_batch, 0, C0_L0)
        run_phase(ph_dump_sbuf)
        run_phase(ph_l0_attn_batch, 1, C0_L0)
        run_phase(ph_dump, "a2a_ao_in", 8 * SEG_Q)
    def _emit_l0a():
        run_phase(ph_ln)
        run_phase(ph_l0_qkv)
        run_phase(ph_a2a, "a2a_qkv_in", "a2a_qkv_out")
        run_phase(ph_l0_attn_batch, 0, C0_L0)
        run_phase(ph_l0_attn_batch, 1, C0_L0)
        run_phase(ph_a2a, "a2a_ao_in", "a2a_ao_out")
        run_phase(ph_l0_wo)
    def _emit_l0f():
        run_phase(ph_ln)
        run_phase(ph_l0_ffn)
    def _emit_l1a():
        run_phase(ph_ln)
        run_phase(ph_l1_qkv)
        run_phase(ph_a2a, "a2a_qkv_in", "a2a_qkv_out")
        run_phase(ph_l1_rope, 0)
        run_phase(ph_l1_attn, 0)
        run_phase(ph_l1_rope, 1)
        run_phase(ph_l1_attn, 1)
        run_phase(ph_a2a, "a2a_ao_in", "a2a_ao_out")
        run_phase(ph_l0_wo, "w_l1_woT", "w_l1_bo")
    def _emit_l1fdbg():
        run_phase(ph_l1_sum)
        run_phase(ph_l1_ar)
        run_phase(ph_l1_tern, "w_l1_w1T_part", 2, DFF, "w1qpart", 0, 0)
        run_phase(ph_l1_tern, "w_l1_w2T_part", 8, D, "w2qpart", 1, 1)
        run_phase(ph_l1_agq, "w1qpart", "wq1q")
        run_phase(ph_l1_agq, "w2qpart", "wq2q", prev="wq1q")
        run_phase(ph_ln, "32")
        run_phase(ph_l1_quantx)
        run_phase(ph_l1_mm1)
        run_phase(ph_l1_mm2)
        run_phase(ph_dump_flat, "wq1q", 1024 * 4096, 262144)
    def _emit_l1f():
        run_phase(ph_l1_sum)
        run_phase(ph_l1_ar)
        run_phase(ph_l1_tern, "w_l1_w1T_part", 2, DFF, "w1qpart", 0, 0)
        run_phase(ph_l1_tern, "w_l1_w2T_part", 8, D, "w2qpart", 1, 1)
        run_phase(ph_l1_agq, "w1qpart", "wq1q")
        run_phase(ph_l1_agq, "w2qpart", "wq2q", prev="wq1q")
        run_phase(ph_ln, "32")
        run_phase(ph_l1_quantx)
        run_phase(ph_l1_mm1)
        run_phase(ph_l1_mm2)
    def _emit_l3p():
        run_phase(ph_l3_prep, "w_l3_wg_part", 22, 256, 0)
        run_phase(ph_l3_prepT, "wtpartA", DI3, 256)
        run_phase(ph_l3_ag, "wtpartA", "wgqT")
        run_phase(ph_l3_prep, "w_l3_wu_part", 22, 256, 0)
        run_phase(ph_l3_prepT, "wtpartC", DI3, 256)
        run_phase(ph_l3_ag, "wtpartC", "wuqT", prev="wgqT")
        run_phase(ph_l3_prep, "w_l3_wd_part", 8, 768, 0)
        run_phase(ph_l3_prepT, "wtpartB", D, 768)
        run_phase(ph_l3_ag, "wtpartB", "wdqT", prev="wuqT")
    def _emit_l3f():
        run_phase(ph_ln, "32")
        run_phase(ph_l3_quantx)
        run_phase(ph_l3_gateup)
        run_phase(ph_l3_quanth)
        run_phase(ph_l3_down)
    def _emit_l3f1():
        run_phase(ph_ln, "32")
        run_phase(ph_l3_quantx)
        run_phase(ph_dump, "stage_xn", TPC * D)
    def _emit_l3f3():
        run_phase(ph_ln, "32")
        run_phase(ph_l3_quantx)
        run_phase(ph_l3_gateup)
        run_phase(ph_l3_quanth)
    def _emit_l3a():
        run_phase(ph_ln)
        run_phase(ph_l0_qkv, "w_l3_wqT", "w_l3_wkT", "w_l3_wvT")
        run_phase(ph_a2a, "a2a_qkv_in", "a2a_qkv_out")
        run_phase(ph_rope, 0)
        run_phase(ph_l0_attn_batch, 0, C0_L3, False)
        run_phase(ph_rope, 1)
        run_phase(ph_l0_attn_batch, 1, C0_L3, False)
        run_phase(ph_a2a, "a2a_ao_in", "a2a_ao_out")
        run_phase(ph_l0_wo, "w_l3_woT", "w_l3_bo")
    def _emit_l2a():
        run_phase(ph_ln)
        run_phase(ph_l2_comb, "w_l2_wq", "w_l2_bqc", 0)
        run_phase(ph_l2_qf)
        run_phase(ph_l2_comb, "w_l2_wk", "w_l2_bkc", 1024)
        run_phase(ph_l2_kv)
        run_phase(ph_l2_ar)
        run_phase(ph_l2_out)
        run_phase(ph_l2_wo)
    def _emit_l2f():
        run_phase(ph_ln)
        run_phase(ph_l2_ffn)
    _emitters = {"ln": _emit_ln, "qkv": _emit_qkv, "attn": _emit_attn, "l0a": _emit_l0a, "l0f": _emit_l0f, "l1a": _emit_l1a, "l1fdbg": _emit_l1fdbg, "l1f": _emit_l1f, "l3p": _emit_l3p, "l3f": _emit_l3f, "l3f1": _emit_l3f1, "l3f3": _emit_l3f3, "l3a": _emit_l3a, "l2a": _emit_l2a, "l2f": _emit_l2f}
    for _p in phases:
        if _p in _emitters:
            _emitters[_p]()
    run_phase(ph_store)

    es.__exit__(None, None, None)
    return nc


LAST_RESULTS = None


def kernel(**inp):
    global LAST_RESULTS
    in_maps = _host_prep(inp)
    nc = build_program(ALL_PHASES if PHASES is None else PHASES)
    res = run_bass_kernel_spmd(nc, in_maps, core_ids=list(range(N_CORES)))
    LAST_RESULTS = res.results
    out = np.zeros((B, S, D), np.float32)
    for j in range(N_CORES):
        b, r = j // 4, (j % 4) * TPC
        out[b, r:r + TPC, :] = res.results[j]["y"]
    return out
